# version 01
# baseline (speedup 1.0000x reference)
import sys
if "/opt/trn_rl_repo" not in sys.path:
    sys.path.insert(0, "/opt/trn_rl_repo")

from contextlib import ExitStack

import numpy as np
import ml_dtypes

import concourse.bass as bass
import concourse.mybir as mybir
from concourse.bass_utils import run_bass_kernel_spmd

F32 = mybir.dt.float32
BF16 = mybir.dt.bfloat16
I32 = mybir.dt.int32
AX = mybir.AxisListType
ALU = mybir.AluOpType
ACT = mybir.ActivationFunctionType

D = 1024
NH = 16
FFI = 2816
NCH = 8
CWS = (21, 41)
PADS = (10, 20)
UPAD = 64
GAP = 192
SLOTP = [CWS[0] + GAP, CWS[1] + GAP]
PRB = [GAP // 2, GAP // 2 + 8 * SLOTP[0]]
PRP = GAP // 2 + 8 * SLOTP[0] + 8 * SLOTP[1] + GAP // 2
M22 = float(3 * 2 ** 21)
M24 = float(3 * 2 ** 22)
BFREQ = 10000.0
MIN_L = 0.83
SDT_L = (BFREQ - MIN_L * BFREQ) * 2.0
EPS_LN = 1e-5
PI = float(np.pi)
NEG4M = -4194304


def ts(i, size):
    return slice(i * size, (i + 1) * size)


class Sem:
    def __init__(self, nc, es, name):
        self.h = es.enter_context(nc.semaphore(name))
        self.n = 0

    def inc(self, inst, k=1):
        inst.then_inc(self.h, k)
        self.n += k
        return self.n

    def dinc(self, inst):
        inst.then_inc(self.h, 16)
        self.n += 16
        return self.n


def build(cfg):
    T = cfg["T"]
    NSEQ = cfg["NSEQ"]
    N = T * NSEQ
    NT = N // 128
    NTS = T // 128
    QB = min(512, T)
    NQB = T // QB
    NKT = T // 128
    NKTP = (T + 2 * UPAD) // 128
    LNG = min(8, NT)
    dbg = set(cfg.get("debug", ()))
    stop = cfg.get("stop_after", 99)

    nc = bass.Bass("TRN2", target_bir_lowering=False, debug=False)

    def inp(name, shape, dt=F32):
        return nc.dram_tensor(name, shape, dt, kind="ExternalInput").ap()

    x_in = inp("x", [N, D])
    Wq = inp("Wq", [D, D]); Wk = inp("Wk", [D, D]); Wv = inp("Wv", [D, D])
    Wo = inp("Wo", [D, D])
    bq = inp("bq", [D]); bk = inp("bk", [D]); bv = inp("bv", [D]); bo = inp("bo", [D])
    nrope = inp("nrope", [1])
    w_gate = inp("w_gate", [FFI, D]); w_up = inp("w_up", [FFI, D])
    w_down = inp("w_down", [D, FFI])
    W_in = inp("W_in", [D, D]); W_attn = inp("W_attn", [656, D])
    W_out = inp("W_out", [D, D]); sw_in = inp("sw", [2])
    W_l1 = inp("W_l1", [D, D])
    iota_t = inp("iota_t", [128, T])
    rope_e = inp("rope_e", [128, 1])
    rope_sm = inp("rope_sm", [128, 1])
    sperm = nc.dram_tensor("sperm", [128, 128], BF16, kind="ExternalInput").ap()
    e2sel = nc.dram_tensor("e2sel", [16, 1024], BF16, kind="ExternalInput").ap()

    out_t = nc.dram_tensor("out", [N, D], F32, kind="ExternalOutput").ap()
    dbg_t = {}
    for name, shape, dt in [
        ("d_xn", [N, D], F32), ("d_xq", [N, D], BF16),
        ("d_qT", [D, T], BF16), ("d_kT", [D, T], BF16),
        ("d_v", [128, NH * 65], BF16), ("d_att", [N, D], F32),
        ("d_hq", [N, FFI], BF16), ("d_xn2", [N, D], F32),
        ("d_u", [D, T], BF16), ("d_ow", [D, T], BF16),
        ("d_aq", [N, D], BF16), ("d_wgq", [FFI, D], BF16),
        ("d_wl1q", [D, D], BF16), ("d_ws", [1, 2], F32),
    ]:
        if name in dbg:
            dbg_t[name] = nc.dram_tensor(name, shape, dt, kind="ExternalOutput").ap()

    def scr(name, shape, dt=BF16):
        return nc.dram_tensor(name, shape, dt, kind="Internal").ap()

    xn_s = scr("xn_s", [N, D])
    xq_s = scr("xq_s", [N, D])
    h_s = scr("h_s", [N, FFI], F32)
    hq_s = scr("hq_s", [N, FFI])
    x1_s = scr("x1_s", [N, D], F32)
    xn2_s = scr("xn2_s", [N, D])
    aq_s = scr("aq_s", [N, D])
    wq_bs = scr("wq_bs", [D, D]); wk_bs = scr("wk_bs", [D, D])
    wv_bs = scr("wv_bs", [D, D]); wo_bs = scr("wo_bs", [D, D])
    wgq_s = scr("wgq_s", [FFI, D]); wuq_s = scr("wuq_s", [FFI, D])
    wdq_s = scr("wdq_s", [D, FFI])
    wl1q_s = scr("wl1q_s", [D, D])
    wi_s = scr("wi_s", [D, D]); wa_s = scr("wa_s", [656, D]); wo2_s = scr("wo2_s", [D, D])
    u_s = scr("u_s", [NSEQ, D, T + 128])
    pr_flat = scr("pr_s", [NSEQ * T * PRP])
    scal_s = scr("scal_s", [8], F32)

    es = ExitStack()

    def sb(name, shape, dt=F32):
        return es.enter_context(nc.sbuf_tensor(name, shape, dt))

    R1N = NCH * T
    R1 = es.enter_context(nc.sbuf_tensor("R1", [128, max(R1N, 4 * N, 8 * T, 2 * LNG * D)], BF16))
    R2 = es.enter_context(nc.sbuf_tensor("R2", [128, max(2 * NKT * QB, 4 * N, 8 * T, 8192)], BF16))
    R3 = es.enter_context(nc.sbuf_tensor("R3", [128, max(NKT * NH * 65, NKTP * 1024, 22 * 512 + 2048, NCH * T)], BF16))
    R4 = es.enter_context(nc.sbuf_tensor("R4", [128, max(NCH * T, 16384)], BF16))
    R5 = es.enter_context(nc.sbuf_tensor("R5", [128, max(4 * T, NCH * 656 + PRP, 8192 + 1024)], BF16))

    xg = R1[:].bitcast(F32)
    xnT = R1
    Upr = R1
    xqT_lo = R1
    uT = R1
    aqT = R1
    ET = R2
    xqT_hi = R2
    ruT = R2
    wl1qT = R2
    vA = R3
    hqTb = R3
    gsil_v = R3[:, 22 * 512:22 * 512 + 2048].bitcast(F32)
    xn2T = R3
    upad = R3
    WVr = R4[:, 0:NCH * 1024]
    WQs = R4[:, 8192:12288]
    WKs = R4[:, 12288:16384]
    rope_t1 = R4[:, 8192:8192 + T]
    rope_sq = R4[:, 8192 + T:8192 + 2 * T]
    WOr = R4[:, 0:NCH * 1024]
    wstr = R4
    wdT_a = R4
    owT = R4
    qkbuf = R5
    rbb = R5[:, 2 * T:3 * T]
    UstA = R5[:, 3 * T:3 * T + QB]
    UstB = R5[:, 3 * T + QB:3 * T + 2 * QB]
    wiT = R5
    waT = R5[:, 0:NCH * 656]
    PW = R5[:, NCH * 656:NCH * 656 + PRP]
    woutT = R5
    wdT_b = R5

    q_vI = es.enter_context(nc.sbuf_tensor("q_vI", [128, FFI], I32))
    q_2I = es.enter_context(nc.sbuf_tensor("q_2I", [128, FFI], I32))
    q_out = es.enter_context(nc.sbuf_tensor("q_out", [128, FFI], BF16))
    q_bm = sb("q_bm", [128, FFI // 16])
    q_sc = sb("q_sc", [128, FFI // 16])

    junk = sb("junk", [128, max(FFI + 32, T + 64)])
    junkb = junk[:].bitcast(BF16)
    q_hi = junk[:, 0:FFI]
    Rr = junk[:].bitcast(BF16)[0:16, 0:T]
    Rrb = junk[:].bitcast(BF16)[0:16, T:2 * T]
    stat = sb("stat", [128, 4 * LNG + 4])
    cbc = sb("cbc", [128, 24])
    tiny = sb("tiny", [128, 16])
    ifr = sb("ifr", [128, 4])
    bqt = sb("bqt", [128, 4 * NCH])
    CC = es.enter_context(nc.sbuf_tensor("CC", [128, T], BF16))
    SSb = es.enter_context(nc.sbuf_tensor("SSb", [128, T], BF16))
    spermt = es.enter_context(nc.sbuf_tensor("spermt", [128, 128], BF16))
    e2t = es.enter_context(nc.sbuf_tensor("e2t", [16, 1024], BF16))
    biasrow = es.enter_context(nc.sbuf_tensor("biasrow", [1, D], BF16))
    onesrow = es.enter_context(nc.sbuf_tensor("onesrow", [1, 128], BF16))
    Btile = es.enter_context(nc.sbuf_tensor("Btile", [128, 4 * 128], BF16))
    asws = sb("asws", [128, max(NT, 1)])
    x1t = sb("x1t", [128, D])
    xnt = R2[:].bitcast(F32)[:, 0:D]
    wfp = x1t
    wbuf = R1[:, 0:2048]
    prz = sb("prz", [128, 16])
    wl1r = sb("wl1r", [128, 1])
    zpad = es.enter_context(nc.sbuf_tensor("zpad", [128, 512], BF16))
    wl1s = sb("wl1s", [128, 1])

    pw0 = es.enter_context(nc.psum_tensor("pw0", [128, 1024], F32))
    pw1 = es.enter_context(nc.psum_tensor("pw1", [128, 1024], F32))
    p4 = es.enter_context(nc.psum_tensor("p4", [128, 512], F32))
    p5 = es.enter_context(nc.psum_tensor("p5", [128, 512], F32))
    p6 = es.enter_context(nc.psum_tensor("p6", [128, 512], F32))
    p7 = es.enter_context(nc.psum_tensor("p7", [128, 512], F32))

    s_ld = Sem(nc, es, "s_ld")
    s_lw = Sem(nc, es, "s_lw")
    s_lx = Sem(nc, es, "s_lx")
    s_th = Sem(nc, es, "s_th")
    s_st = Sem(nc, es, "s_st")
    s_tr = Sem(nc, es, "s_tr")
    s_mv = Sem(nc, es, "s_mv")
    s_v = Sem(nc, es, "s_v")
    s_s = Sem(nc, es, "s_s")
    s_t = Sem(nc, es, "s_t")
    s_g = Sem(nc, es, "s_g")
    s_bar = Sem(nc, es, "s_bar")
    s_out = Sem(nc, es, "s_out")

    blk = es.enter_context(nc.Block())

    CONSTS = {}

    def reg_const(*vals):
        for v in vals:
            v = float(v)
            if v not in CONSTS:
                CONSTS[v] = len(CONSTS)
                assert len(CONSTS) <= 24

    def C(v):
        return cbc[:, ts(CONSTS[float(v)], 1)]

    reg_const(0.0, 1.0, -8.0, 0.125, SDT_L, MIN_L * BFREQ)

    nbar = [0]

    def barrier():
        nbar[0] += 1
        tgt = 5 * nbar[0]
        for ename in ("sync", "vector", "scalar", "tensor", "gpsimd"):
            @getattr(blk, ename)
            def _(eng):
                eng.sem_inc(s_bar.h, 1)
                eng.wait_ge(s_bar.h, tgt)
        s_bar.n = tgt

    def emit_fp4(v, W, src_ap):
        J = W // 16
        vF = q_vI[:, 0:W].bitcast(F32)
        q2I = q_2I[:, 0:W]
        q2F = q_2I[:, 0:W].bitcast(F32)
        hiF = q_hi[:, 0:W]

        def st(inst):
            s_v.inc(inst)
            v.wait_ge(s_v.h, s_v.n)

        st(v.reduce_max(out=q_bm[:, 0:J].unsqueeze(2),
                        in_=src_ap.rearrange("p (c s) -> p c s", s=16),
                        axis=AX.X, apply_absolute_value=True))
        st(v.tensor_scalar(out=q_sc[:, 0:J], in0=q_bm[:, 0:J], scalar1=6e-12,
                           scalar2=1.0 / 6.0, op0=ALU.max, op1=ALU.mult))
        st(v.reciprocal(out=q_bm[:, 0:J], in_=q_sc[:, 0:J]))
        bm3 = q_bm[:, 0:J].unsqueeze(2).broadcast_to([128, J, 16])
        sc3 = q_sc[:, 0:J].unsqueeze(2).broadcast_to([128, J, 16])
        st(v.tensor_mul(out=vF.rearrange("p (c s) -> p c s", s=16),
                        in0=src_ap.rearrange("p (c s) -> p c s", s=16), in1=bm3))
        st(v.tensor_scalar_add(out=q2I, in0=q_vI[:, 0:W], scalar1=0x00200000))
        st(v.tensor_single_scalar(out=q2I, in_=q2I, scalar=NEG4M, op=ALU.bitwise_and))
        st(v.tensor_scalar(out=vF, in0=vF, scalar1=M22, scalar2=M22,
                           op0=ALU.add, op1=ALU.subtract))
        st(v.tensor_scalar(out=vF, in0=vF, scalar1=-2.0, scalar2=2.0,
                           op0=ALU.max, op1=ALU.min))
        st(v.tensor_scalar(out=hiF, in0=q2F, scalar1=-2.0, scalar2=2.0,
                           op0=ALU.max, op1=ALU.min))
        st(v.tensor_sub(out=hiF, in0=q2F, in1=hiF))
        st(v.tensor_add(out=hiF, in0=hiF, in1=vF))
        st(v.tensor_mul(out=q_out[:, 0:W].rearrange("p (c s) -> p c s", s=16),
                        in0=hiF.rearrange("p (c s) -> p c s", s=16), in1=sc3))

    def phase0a():
        @blk.sync
        def _(sync):
            for i, b in enumerate((bq, bk, bv, bo)):
                s_ld.dinc(sync.dma_start(out=bqt[:, ts(i, NCH)],
                                         in_=b.rearrange("(c p) -> p c", p=128),
                                         allow_slow_non_contiguous=True))
            s_ld.dinc(sync.dma_start(out=tiny[:, 0:1],
                                     in_=nrope.unsqueeze(0).broadcast_to([128, 1])))
            s_ld.dinc(sync.dma_start(out=ifr[:, 0:1], in_=rope_e[:, :]))
            s_ld.dinc(sync.dma_start(out=ifr[:, 1:2], in_=rope_sm[:, :]))
            s_ld.dinc(sync.dma_start(out=junk[:, 0:T], in_=iota_t[:, 0:T]))
            s_ld.dinc(sync.dma_start(out=tiny[:, 4:6],
                                     in_=sw_in.unsqueeze(0).broadcast_to([128, 2])))
            s_ld.dinc(sync.dma_start(out=e2t[:, :], in_=e2sel[:, :]))
            s_ld.dinc(sync.dma_start(out=spermt[:, :], in_=sperm[:, :]))
        ld0 = s_ld.n

        ncst = [0]

        @blk.vector
        def _(v):
            v.wait_ge(s_ld.h, ld0)
            for val, col in CONSTS.items():
                v.memset(cbc[:, ts(col, 1)], val).then_inc(s_v.h, 1)
                ncst[0] += 1
            v.memset(onesrow[:, :], 1.0).then_inc(s_v.h, 1)
            ncst[0] += 1
            v.memset(zpad[:, :], 0.0).then_inc(s_v.h, 1)
            ncst[0] += 1
        s_v.n += ncst[0]
        vc0 = s_v.n

        @blk.scalar
        def _(s):
            s.wait_ge(s_v.h, vc0)
            s.activation(out=tiny[:, 1:2], in_=tiny[:, 0:1], func=ACT.Sigmoid).then_inc(s_s.h, 1)
            s.wait_ge(s_s.h, s_s.n + 1)
            s.activation(out=tiny[:, 2:3], in_=tiny[:, 1:2], func=ACT.Ln,
                         scale=C(SDT_L), bias=C(MIN_L * BFREQ)).then_inc(s_s.h, 1)
            s.activation(out=tiny[:, 6:8], in_=tiny[:, 4:6], func=ACT.Exp).then_inc(s_s.h, 1)
        s_s.n += 3
        sc0 = s_s.n

        @blk.vector
        def _(v):
            v.wait_ge(s_s.h, sc0)
            v.reduce_sum(out=tiny[:, 10:11], in_=tiny[:, 6:8], axis=AX.X).then_inc(s_v.h, 1)
            v.wait_ge(s_v.h, s_v.n + 1)
            v.reciprocal(out=tiny[:, 11:12], in_=tiny[:, 10:11]).then_inc(s_v.h, 1)
            v.wait_ge(s_v.h, s_v.n + 2)
            v.tensor_mul(out=tiny[:, 8:10], in0=tiny[:, 6:8],
                         in1=tiny[:, 11:12].broadcast_to([128, 2])).then_inc(s_v.h, 1)
            v.tensor_mul(out=ifr[:, 2:3], in0=ifr[:, 0:1], in1=tiny[:, 2:3]).then_inc(s_v.h, 1)
        s_v.n += 4
        vc1 = s_v.n

        @blk.scalar
        def _(s):
            s.wait_ge(s_v.h, vc1)
            s.activation(out=ifr[:, 3:4], in_=ifr[:, 2:3], func=ACT.Exp).then_inc(s_s.h, 1)
        s_s.n += 1
        sc1 = s_s.n

        @blk.vector
        def _(v):
            v.wait_ge(s_s.h, sc1)
            v.tensor_mul(out=ifr[:, 3:4], in0=ifr[:, 3:4], in1=ifr[:, 1:2]).then_inc(s_v.h, 1)
            v.wait_ge(s_v.h, s_v.n + 1)
            v.tensor_scalar(out=junk[:, 0:T], in0=junk[:, 0:T], scalar1=ifr[:, 3:4],
                            scalar2=None, op0=ALU.mult).then_inc(s_v.h, 1)
            v.wait_ge(s_v.h, s_v.n + 2)
            v.tensor_single_scalar(out=q_vI[:, 0:T].bitcast(F32), in_=junk[:, 0:T],
                                   scalar=PI, op=ALU.is_gt).then_inc(s_v.h, 1)
            v.wait_ge(s_v.h, s_v.n + 3)
            v.scalar_tensor_tensor(out=junk[:, 0:T], in0=q_vI[:, 0:T].bitcast(F32),
                                   scalar=-2.0 * PI, in1=junk[:, 0:T],
                                   op0=ALU.mult, op1=ALU.add).then_inc(s_v.h, 1)
            v.wait_ge(s_v.h, s_v.n + 4)
            v.tensor_single_scalar(out=q_vI[:, 0:T].bitcast(F32), in_=junk[:, 0:T],
                                   scalar=-PI, op=ALU.is_lt).then_inc(s_v.h, 1)
            v.wait_ge(s_v.h, s_v.n + 5)
            v.scalar_tensor_tensor(out=junk[:, 0:T], in0=q_vI[:, 0:T].bitcast(F32),
                                   scalar=2.0 * PI, in1=junk[:, 0:T],
                                   op0=ALU.mult, op1=ALU.add).then_inc(s_v.h, 1)
        s_v.n += 6
        vc2 = s_v.n

        @blk.scalar
        def _(s):
            s.wait_ge(s_v.h, vc2)
            s.activation(out=SSb[:, 0:T], in_=junk[:, 0:T], func=ACT.Sin).then_inc(s_s.h, 1)
        s_s.n += 1
        sc2 = s_s.n

        @blk.vector
        def _(v):
            v.wait_ge(s_s.h, sc2)
            v.tensor_scalar_mul(out=xg[:, 0:T], in0=junk[:, 0:T],
                                scalar1=-1.0).then_inc(s_v.h, 1)
            v.wait_ge(s_v.h, s_v.n + 1)
            v.tensor_max(out=junk[:, 0:T], in0=junk[:, 0:T],
                         in1=xg[:, 0:T]).then_inc(s_v.h, 1)
            v.wait_ge(s_v.h, s_v.n + 2)
            v.tensor_scalar(out=junk[:, 0:T], in0=junk[:, 0:T], scalar1=-1.0,
                            scalar2=PI / 2, op0=ALU.mult, op1=ALU.add).then_inc(s_v.h, 1)
        s_v.n += 3
        vc3 = s_v.n

        @blk.scalar
        def _(s):
            s.wait_ge(s_v.h, vc3)
            s.activation(out=CC[:, 0:T], in_=junk[:, 0:T], func=ACT.Sin).then_inc(s_s.h, 1)
        s_s.n += 1

    def phase0b():
        for wsrc, wdst in ((Wq, wq_bs), (Wk, wk_bs), (Wv, wv_bs), (Wo, wo_bs)):
            for c in range(NCH):
                @blk.sync
                def _(sync, wsrc=wsrc, c=c):
                    sync.wait_ge(s_v.h, s_v.n)
                    s_ld.dinc(sync.dma_start(out=wfp[:, :], in_=wsrc[ts(c, 128), :]))
                ldc = s_ld.n

                @blk.vector
                def _(v):
                    v.wait_ge(s_ld.h, ldc)
                    v.tensor_copy(out=wbuf[:, 0:1024], in_=wfp[:, :]).then_inc(s_v.h, 1)
                s_v.n += 1
                vcc = s_v.n

                @blk.scalar
                def _(s, wdst=wdst, c=c):
                    s.wait_ge(s_v.h, vcc)
                    s_st.dinc(s.dma_start(out=wdst[ts(c, 128), :], in_=wbuf[:, 0:1024]))
                stc = s_st.n

                @blk.vector
                def _(v):
                    v.wait_ge(s_st.h, stc)
                    v.sem_inc(s_v.h, 1)
                s_v.n += 1

        @blk.vector
        def _(v):
            v.tensor_copy(out=wbuf[:, 0:NCH], in_=bqt[:, ts(2, NCH)]).then_inc(s_v.h, 1)
        s_v.n += 1
        vb = s_v.n
        for c in range(NCH):
            @blk.sync
            def _(sync, c=c):
                sync.wait_ge(s_v.h, vb)
                if c > 0:
                    sync.wait_ge(s_t.h, s_t.n)
                s_tr.dinc(sync.dma_start(out=wbuf[:, 1024:2048],
                                         in_=wo_bs[ts(c, 128), :]))
            trc = s_tr.n

            @blk.tensor
            def _(t, c=c):
                t.wait_ge(s_tr.h, trc)
                t.matmul(pw0[0:1, 0:512], wbuf[:, c:c + 1], wbuf[:, 1024:1536],
                         start=(c == 0), stop=(c == NCH - 1))
                t.matmul(pw0[0:1, 512:1024], wbuf[:, c:c + 1], wbuf[:, 1536:2048],
                         start=(c == 0), stop=(c == NCH - 1)).then_inc(s_t.h, 1)
            s_t.n += 1
        tb = s_t.n

        @blk.sync
        def _(sync):
            sync.wait_ge(s_t.h, tb)
            s_tr.dinc(sync.dma_start(out=wfp[0:1, 0:D], in_=bo.unsqueeze(0)))
        trb = s_tr.n

        @blk.vector
        def _(v):
            v.wait_ge(s_tr.h, trb)
            v.tensor_add(out=wfp[0:1, 0:D], in0=wfp[0:1, 0:D],
                         in1=pw0[0:1, 0:1024]).then_inc(s_v.h, 1)
            v.wait_ge(s_v.h, s_v.n + 1)
            v.tensor_copy(out=biasrow[0:1, :], in_=wfp[0:1, 0:D]).then_inc(s_v.h, 1)
        s_v.n += 2

    def phase_ln(src_dram, xnbf_dram, q_dram, layer):
        ngrp = NT // LNG
        for g in range(ngrp):
            @blk.sync
            def _(sync, g=g):
                sync.wait_ge(s_v.h, s_v.n)
                for i in range(LNG):
                    s_lx.dinc(sync.dma_start(out=xg[:, ts(i, D)],
                                             in_=src_dram[ts(g * LNG + i, 128), :]))
            ldg = s_lx.n

            @blk.vector
            def _(v):
                v.wait_ge(s_lx.h, ldg)
                for i in range(LNG):
                    v.reduce_sum(out=stat[:, ts(i, 1)], in_=xg[:, ts(i, D)],
                                 axis=AX.X).then_inc(s_v.h, 1)
            s_v.n += LNG
            vred = s_v.n

            @blk.scalar
            def _(s):
                s.wait_ge(s_lx.h, ldg)
                base = s_s.n
                for i in range(LNG):
                    if i > 0:
                        s.wait_ge(s_s.h, base + i)
                    s.activation(out=junk[:, 0:D], in_=xg[:, ts(i, D)], func=ACT.Square,
                                 accum_out=stat[:, ts(LNG + i, 1)]).then_inc(s_s.h, 1)
            s_s.n += LNG
            ssq = s_s.n

            @blk.vector
            def _(v):
                v.wait_ge(s_v.h, vred)
                v.wait_ge(s_s.h, ssq)
                L = slice(0, LNG)
                v.tensor_scalar_mul(out=stat[:, L], in0=stat[:, L],
                                    scalar1=1.0 / D).then_inc(s_v.h, 1)
                v.wait_ge(s_v.h, s_v.n + 1)
                v.tensor_mul(out=stat[:, 2 * LNG:3 * LNG], in0=stat[:, L],
                             in1=stat[:, L]).then_inc(s_v.h, 1)
                v.wait_ge(s_v.h, s_v.n + 2)
                v.scalar_tensor_tensor(out=stat[:, LNG:2 * LNG],
                                       in0=stat[:, LNG:2 * LNG], scalar=1.0 / D,
                                       in1=stat[:, 2 * LNG:3 * LNG],
                                       op0=ALU.mult, op1=ALU.subtract).then_inc(s_v.h, 1)
                v.wait_ge(s_v.h, s_v.n + 3)
                v.tensor_scalar_add(out=stat[:, LNG:2 * LNG], in0=stat[:, LNG:2 * LNG],
                                    scalar1=EPS_LN).then_inc(s_v.h, 1)
            s_v.n += 4
            vstat = s_v.n

            @blk.scalar
            def _(s):
                s.wait_ge(s_v.h, vstat)
                s.activation(out=stat[:, LNG:2 * LNG], in_=stat[:, LNG:2 * LNG],
                             func=ACT.Sqrt).then_inc(s_s.h, 1)
            s_s.n += 1
            ssd = s_s.n

            @blk.vector
            def _(v):
                v.wait_ge(s_s.h, ssd)
                v.reciprocal(out=stat[:, LNG:2 * LNG],
                             in_=stat[:, LNG:2 * LNG]).then_inc(s_v.h, 1)
                v.wait_ge(s_v.h, s_v.n + 1)
                v.tensor_scalar(out=stat[:, 2 * LNG:3 * LNG], in0=stat[:, 0:LNG],
                                scalar1=-1.0, scalar2=None,
                                op0=ALU.mult).then_inc(s_v.h, 1)
                v.wait_ge(s_v.h, s_v.n + 2)
                v.tensor_mul(out=stat[:, 2 * LNG:3 * LNG],
                             in0=stat[:, 2 * LNG:3 * LNG],
                             in1=stat[:, LNG:2 * LNG]).then_inc(s_v.h, 1)
            s_v.n += 3
            vrs = s_v.n

            for i in range(LNG):
                it = g * LNG + i

                @blk.scalar
                def _(s, i=i):
                    s.wait_ge(s_v.h, vrs)
                    s.wait_ge(s_v.h, s_v.n)
                    s.activation(out=xnt[:, :], in_=xg[:, ts(i, D)], func=ACT.Identity,
                                 bias=stat[:, ts(2 * LNG + i, 1)],
                                 scale=stat[:, ts(LNG + i, 1)]).then_inc(s_s.h, 1)
                s_s.n += 1
                sxn = s_s.n

                @blk.vector
                def _(v):
                    v.wait_ge(s_s.h, sxn)
                    v.tensor_copy(out=q_out[:, 0:D], in_=xnt[:, :]).then_inc(s_v.h, 1)
                s_v.n += 1
                vxb = s_v.n

                @blk.scalar
                def _(s, it=it):
                    s.wait_ge(s_v.h, vxb)
                    s_st.dinc(s.dma_start(out=xnbf_dram[ts(it, 128), :],
                                          in_=q_out[:, 0:D]))
                    if layer == 0 and "d_xn" in dbg_t:
                        s_st.dinc(s.dma_start(out=dbg_t["d_xn"][ts(it, 128), :],
                                              in_=xnt[:, :]))
                    if layer == 1 and "d_xn2" in dbg_t:
                        s_st.dinc(s.dma_start(out=dbg_t["d_xn2"][ts(it, 128), :],
                                              in_=xnt[:, :]))
                stq = s_st.n

                if layer == 0:
                    @blk.vector
                    def _(v):
                        v.wait_ge(s_st.h, stq)
                        emit_fp4(v, D, xnt[:, 0:D])
                    vq = s_v.n

                    @blk.scalar
                    def _(s, it=it):
                        s.wait_ge(s_v.h, vq)
                        s_st.dinc(s.dma_start(out=q_dram[ts(it, 128), :],
                                              in_=q_out[:, 0:D]))
                        if "d_xq" in dbg_t:
                            s_st.dinc(s.dma_start(out=dbg_t["d_xq"][ts(it, 128), :],
                                                  in_=q_out[:, 0:D]))
                else:
                    @blk.vector
                    def _(v, it=it):
                        v.wait_ge(s_st.h, stq)

                        def st(inst):
                            s_v.inc(inst)
                            v.wait_ge(s_v.h, s_v.n)
                        c0 = 3 * LNG
                        st(v.reduce_max(out=stat[:, ts(c0, 1)], in_=xnt[:, :],
                                        axis=AX.X, apply_absolute_value=True))
                        st(v.tensor_scalar_add(out=stat[:, ts(c0, 1)],
                                               in0=stat[:, ts(c0, 1)], scalar1=1e-8))
                        st(v.reciprocal(out=stat[:, ts(c0 + 1, 1)],
                                        in_=stat[:, ts(c0, 1)]))
                        st(v.tensor_scalar_mul(out=stat[:, ts(c0 + 1, 1)],
                                               in0=stat[:, ts(c0 + 1, 1)], scalar1=127.0))
                        st(v.tensor_mul(out=asws[:, ts(it, 1)],
                                        in0=stat[:, ts(c0, 1)], in1=wl1s[:, 0:1]))
                        st(v.tensor_scalar_mul(out=asws[:, ts(it, 1)],
                                               in0=asws[:, ts(it, 1)], scalar1=1.0 / 127.0))
                        st(v.tensor_scalar(out=q_vI[:, 0:D].bitcast(F32), in0=xnt[:, :],
                                           scalar1=stat[:, ts(c0 + 1, 1)], scalar2=-128.0,
                                           op0=ALU.mult, op1=ALU.max))
                        st(v.tensor_scalar(out=q_vI[:, 0:D].bitcast(F32),
                                           in0=q_vI[:, 0:D].bitcast(F32),
                                           scalar1=127.0, scalar2=M24,
                                           op0=ALU.min, op1=ALU.add))
                        st(v.tensor_scalar(out=q_out[:, 0:D],
                                           in0=q_vI[:, 0:D].bitcast(F32),
                                           scalar1=M24, scalar2=None, op0=ALU.subtract))
                    vq = s_v.n

                    @blk.scalar
                    def _(s, it=it):
                        s.wait_ge(s_v.h, vq)
                        s_st.dinc(s.dma_start(out=q_dram[ts(it, 128), :],
                                              in_=q_out[:, 0:D]))
                        if "d_aq" in dbg_t:
                            s_st.dinc(s.dma_start(out=dbg_t["d_aq"][ts(it, 128), :],
                                                  in_=q_out[:, 0:D]))
                stq2 = s_st.n

                @blk.vector
                def _(v):
                    v.wait_ge(s_st.h, stq2)
                    v.sem_inc(s_v.h, 1)
                s_v.n += 1

            @blk.vector
            def _(v):
                v.wait_ge(s_st.h, s_st.n)
                v.sem_inc(s_v.h, 1)
            s_v.n += 1

    def phase234(sq):
        t0 = sq * T
        psl = [pw0[:, 0:512], pw0[:, 512:1024], pw1[:, 0:512], pw1[:, 512:1024]]
        NB5 = T // 512
        UprH = R4[:, 0:4 * T]

        @blk.sync
        def _(sync):
            sync.wait_ge(s_st.h, s_st.n)
            sync.wait_ge(s_v.h, s_v.n)
            sync.wait_ge(s_out.h, s_out.n)
            for c in range(NCH):
                s_tr.dinc(sync.dma_start(out=xnT[:, ts(c, T)],
                                         in_=xn_s[t0:t0 + T, ts(c, 128)],
                                         transpose=True))
            for k in range(NCH):
                s_lw.dinc(sync.dma_start(out=R4[:, ts(k, 1024)],
                                         in_=wv_bs[ts(k, 128), :]))
        trx = s_tr.n
        ldv = s_lw.n

        @blk.vector
        def _(v):
            v.wait_ge(s_st.h, s_st.n)
            va3 = vA[:, 0:NKT * NH * 65].rearrange("p (t x) -> p t x", x=65)
            v.memset(va3[:, :, 64:65], 1.0).then_inc(s_v.h, 1)
            v.memset(Rr[:, :], 1.0).then_inc(s_v.h, 1)
        s_v.n += 2
        vones = s_v.n

        for tt in range(NTS):
            @blk.tensor
            def _(t, tt=tt):
                t.wait_ge(s_tr.h, trx)
                t.wait_ge(s_lw.h, ldv)
                if tt > 0:
                    t.wait_ge(s_s.h, s_s.n)
                for k in range(NCH):
                    sl = xnT[:, k * T + tt * 128: k * T + tt * 128 + 128]
                    t.matmul(pw1[:, 0:512], sl, R4[:, k * 1024:k * 1024 + 512],
                             start=(k == 0), stop=(k == NCH - 1))
                    mm = t.matmul(pw1[:, 512:1024], sl, R4[:, k * 1024 + 512:k * 1024 + 1024],
                                  start=(k == 0), stop=(k == NCH - 1))
                mm.then_inc(s_t.h, 1)
            s_t.n += 1
            tv = s_t.n

            @blk.scalar
            def _(s, tt=tt):
                s.wait_ge(s_t.h, tv)
                s.wait_ge(s_v.h, vones)
                dst = vA[:, ts(tt, NH * 65)].rearrange("p (h x) -> p h x", x=65)[:, :, 0:64]
                s.copy(out=dst,
                       in_=pw1[:, 0:1024].rearrange("p (h x) -> p h x", x=64)).then_inc(s_s.h, 1)
                if tt == 0 and "d_v" in dbg_t and sq == 0:
                    s.wait_ge(s_s.h, s_s.n + 1)
                    s_st.dinc(s.dma_start(out=dbg_t["d_v"][:, :],
                                          in_=vA[:, 0:NH * 65]))
            s_s.n += 1
        vdone_s = s_s.n

        for half in range(2):
            for ci in range(4):
                c = half * 4 + ci

                @blk.sync
                def _(sync, c=c):
                    sync.wait_ge(s_v.h, s_v.n)
                    sync.wait_ge(s_t.h, s_t.n)
                    for k in range(NCH):
                        s_lw.dinc(sync.dma_start(out=R4[:, 8192 + k * 128:8192 + k * 128 + 128],
                                                 in_=wq_bs[ts(k, 128), ts(c, 128)]))
                        s_lw.dinc(sync.dma_start(out=R4[:, 12288 + k * 128:12288 + k * 128 + 128],
                                                 in_=wk_bs[ts(k, 128), ts(c, 128)]))
                ldqk = s_lw.n

                for hf, woff, boff in ((0, 8192, 0), (1, 12288, NCH)):
                    @blk.tensor
                    def _(t, c=c, hf=hf, woff=woff):
                        t.wait_ge(s_lw.h, ldqk)
                        t.wait_ge(s_tr.h, trx)
                        t.wait_ge(s_s.h, s_s.n)
                        t.wait_ge(s_st.h, s_st.n)
                        for k in range(NCH):
                            for tb in range(NB5):
                                mm = t.matmul(psl[tb],
                                              R4[:, woff + k * 128:woff + k * 128 + 128],
                                              xnT[:, k * T + tb * 512:k * T + (tb + 1) * 512],
                                              start=(k == 0), stop=(k == NCH - 1))
                        mm.then_inc(s_t.h, 1)
                    s_t.n += 1
                    tqk = s_t.n

                    @blk.scalar
                    def _(s, c=c, hf=hf, boff=boff):
                        s.wait_ge(s_t.h, tqk)
                        for tb in range(NB5):
                            s.activation(out=qkbuf[:, hf * T + tb * 512: hf * T + (tb + 1) * 512],
                                         in_=psl[tb], func=ACT.Identity,
                                         bias=bqt[:, ts(boff + c, 1)],
                                         scale=C(1.0)).then_inc(s_s.h, 1)
                    s_s.n += NB5
                    sqk = s_s.n

                    @blk.tensor
                    def _(t, hf=hf):
                        t.wait_ge(s_s.h, sqk)
                        t.wait_ge(s_v.h, s_v.n)
                        for tb in range(NB5):
                            t.matmul(p6[:, 0:512], spermt[:, :],
                                     qkbuf[:, hf * T + tb * 512: hf * T + (tb + 1) * 512],
                                     start=True, stop=True).then_inc(s_t.h, 1)
                            t.wait_ge(s_s.h, s_s.n + tb + 1)
                    s_t.n += NB5
                    trp = s_t.n

                    @blk.scalar
                    def _(s, hf=hf):
                        for tb in range(NB5):
                            s.wait_ge(s_t.h, trp - NB5 + tb + 1)
                            s.copy(out=R4[:, 8192 + T + tb * 512: 8192 + T + (tb + 1) * 512],
                                   in_=p6[:, 0:512]).then_inc(s_s.h, 1)
                    s_s.n += NB5
                    srp = s_s.n

                    @blk.vector
                    def _(v, hf=hf):
                        v.wait_ge(s_s.h, srp)
                        hb = hf * T
                        v.tensor_mul(out=R4[:, 8192:8192 + T],
                                     in0=qkbuf[:, hb:hb + T], in1=CC[:, 0:T]).then_inc(s_v.h, 1)
                        v.wait_ge(s_v.h, s_v.n + 1)
                        v.tensor_mul(out=R4[:, 8192 + T:8192 + 2 * T],
                                     in0=R4[:, 8192 + T:8192 + 2 * T],
                                     in1=SSb[:, 0:T]).then_inc(s_v.h, 1)
                        v.wait_ge(s_v.h, s_v.n + 2)
                        v.tensor_add(out=qkbuf[:, hb:hb + T], in0=R4[:, 8192:8192 + T],
                                     in1=R4[:, 8192 + T:8192 + 2 * T]).then_inc(s_v.h, 1)
                    s_v.n += 3

                vrope = s_v.n
                if "d_qT" in dbg_t and sq == 0:
                    @blk.scalar
                    def _(s, c=c):
                        s.wait_ge(s_v.h, vrope)
                        s_st.dinc(s.dma_start(out=dbg_t["d_qT"][ts(c, 128), :],
                                              in_=qkbuf[:, 0:T]))
                        s_st.dinc(s.dma_start(out=dbg_t["d_kT"][ts(c, 128), :],
                                              in_=qkbuf[:, T:2 * T]))

                for qb in range(NQB):
                    for kt in range(NKT):
                        @blk.tensor
                        def _(t, qb=qb, kt=kt):
                            if kt == 0 and qb == 0:
                                t.wait_ge(s_s.h, vdone_s)
                                t.wait_ge(s_v.h, vrope)
                            t.wait_ge(s_s.h, s_s.n)
                            t.matmul(p4[:, 0:QB],
                                     qkbuf[0:64, T + kt * 128:T + kt * 128 + 128],
                                     qkbuf[0:64, qb * QB:(qb + 1) * QB],
                                     start=True, stop=True,
                                     tile_position=(0, 0)).then_inc(s_t.h, 1)
                            t.matmul(p5[:, 0:QB],
                                     qkbuf[64:128, T + kt * 128:T + kt * 128 + 128],
                                     qkbuf[64:128, qb * QB:(qb + 1) * QB],
                                     start=True, stop=True,
                                     tile_position=(64, 0)).then_inc(s_t.h, 1)
                        s_t.n += 2
                        tsc = s_t.n

                        @blk.scalar
                        def _(s, kt=kt):
                            s.wait_ge(s_t.h, tsc)
                            s.activation(out=ET[:, ts(kt, QB)], in_=p4[:, 0:QB],
                                         func=ACT.Exp, scale=C(0.125),
                                         bias=C(-8.0)).then_inc(s_s.h, 1)
                            s.wait_ge(s_s.h, s_s.n + 1)
                            s.activation(out=ET[:, ts(NKT + kt, QB)], in_=p5[:, 0:QB],
                                         func=ACT.Exp, scale=C(0.125),
                                         bias=C(-8.0)).then_inc(s_s.h, 1)
                        s_s.n += 2
                    sexp = s_s.n

                    @blk.tensor
                    def _(t, c=c, qb=qb):
                        t.wait_ge(s_s.h, sexp)
                        t.wait_ge(s_mv.h, s_mv.n)
                        for head in range(2):
                            pav = p6 if head == 0 else p7
                            h = 2 * c + head
                            for kt in range(NKT):
                                mm = t.matmul(pav[0:65, 0:QB],
                                              vA[:, (kt * NH + h) * 65:(kt * NH + h) * 65 + 65],
                                              ET[:, ts(head * NKT + kt, QB)],
                                              start=(kt == 0), stop=(kt == NKT - 1))
                        mm.then_inc(s_t.h, 1)
                    s_t.n += 1
                    tav = s_t.n

                    @blk.scalar
                    def _(s, qb=qb):
                        s.wait_ge(s_t.h, tav)
                        s.copy(out=R5[0:65, 3 * T:3 * T + QB],
                               in_=p6[0:65, 0:QB]).then_inc(s_s.h, 1)
                        s.wait_ge(s_s.h, s_s.n + 1)
                        s.copy(out=R5[0:65, 3 * T + QB:3 * T + 2 * QB],
                               in_=p7[0:65, 0:QB]).then_inc(s_s.h, 1)
                    s_s.n += 2
                    sust = s_s.n

                    @blk.sync
                    def _(sync, ci=ci, c=c, qb=qb):
                        sync.wait_ge(s_s.h, sust)
                        s_mv.dinc(sync.dma_start(
                            out=UprH[0:64, ci * T + qb * QB: ci * T + qb * QB + QB],
                            in_=R5[0:64, 3 * T:3 * T + QB]))
                        s_mv.dinc(sync.dma_start(
                            out=UprH[64:128, ci * T + qb * QB: ci * T + qb * QB + QB],
                            in_=R5[0:64, 3 * T + QB:3 * T + 2 * QB]))
                        s_mv.dinc(sync.dma_start(out=Rr[2 * c:2 * c + 1, ts(qb, QB)],
                                                 in_=R5[64:65, 3 * T:3 * T + QB]))
                        s_mv.dinc(sync.dma_start(out=Rr[2 * c + 1:2 * c + 2, ts(qb, QB)],
                                                 in_=R5[64:65, 3 * T + QB:3 * T + 2 * QB]))

            @blk.vector
            def _(v):
                v.wait_ge(s_mv.h, s_mv.n)
                with nc.allow_low_precision("1/r applied to bf16 matmul operands"):
                    v.reciprocal(out=Rrb[:, :], in_=Rr[:, :]).then_inc(s_v.h, 1)
            s_v.n += 1
            vr = s_v.n

            for ci in range(4):
                c = half * 4 + ci

                @blk.tensor
                def _(t, c=c):
                    t.wait_ge(s_v.h, vr)
                    t.wait_ge(s_v.h, s_v.n)
                    t.wait_ge(s_s.h, s_s.n)
                    for tb in range(T // 512):
                        t.matmul(p6[:, 0:512], e2t[:, ts(c, 128)],
                                 Rrb[:, tb * 512:(tb + 1) * 512],
                                 start=True, stop=True).then_inc(s_t.h, 1)
                        t.wait_ge(s_s.h, s_s.n + tb + 1)
                s_t.n += T // 512
                trb = s_t.n

                @blk.scalar
                def _(s, c=c):
                    for tb in range(T // 512):
                        s.wait_ge(s_t.h, trb - T // 512 + tb + 1)
                        s.copy(out=rbb[:, tb * 512:(tb + 1) * 512],
                               in_=p6[:, 0:512]).then_inc(s_s.h, 1)
                s_s.n += T // 512
                srb = s_s.n

                @blk.vector
                def _(v, ci=ci):
                    v.wait_ge(s_s.h, srb)
                    v.tensor_mul(out=UprH[:, ts(ci, T)], in0=UprH[:, ts(ci, T)],
                                 in1=rbb[:, 0:T]).then_inc(s_v.h, 1)
                s_v.n += 1
            unorm = s_v.n

            @blk.sync
            def _(sync, half=half):
                sync.wait_ge(s_v.h, unorm)
                for kc in range(4):
                    s_lw.dinc(sync.dma_start(out=R4[:, 8192 + kc * 1024:8192 + (kc + 1) * 1024],
                                             in_=wo_bs[ts(half * 4 + kc, 128), :]))
            ldo = s_lw.n

            for tt in range(NTS):
                @blk.sync
                def _(sync, tt=tt):
                    sync.wait_ge(s_v.h, s_v.n)
                    if half == 0:
                        s_lx.dinc(sync.dma_start(out=x1t[:, :],
                                                 in_=x_in[t0 + tt * 128:t0 + (tt + 1) * 128, :]))
                    else:
                        s_lx.dinc(sync.dma_start(out=x1t[:, :],
                                                 in_=x1_s[t0 + tt * 128:t0 + (tt + 1) * 128, :]))
                ldx = s_lx.n

                @blk.tensor
                def _(t, tt=tt, half=half):
                    t.wait_ge(s_lw.h, ldo)
                    if tt > 0:
                        t.wait_ge(s_v.h, s_v.n)
                    for kc in range(4):
                        t.matmul(pw0[:, 0:512],
                                 UprH[:, kc * T + tt * 128:kc * T + tt * 128 + 128],
                                 R4[:, 8192 + kc * 1024:8192 + kc * 1024 + 512],
                                 start=(kc == 0), stop=(half == 1 and kc == 3))
                        mm = t.matmul(pw0[:, 512:1024],
                                      UprH[:, kc * T + tt * 128:kc * T + tt * 128 + 128],
                                      R4[:, 8192 + kc * 1024 + 512:8192 + (kc + 1) * 1024],
                                      start=(kc == 0), stop=(half == 1 and kc == 3))
                    if half == 0:
                        t.matmul(pw0[:, 0:512], onesrow[0:1, :], biasrow[0:1, 0:512],
                                 start=False, stop=True)
                        mm = t.matmul(pw0[:, 512:1024], onesrow[0:1, :],
                                      biasrow[0:1, 512:1024], start=False, stop=True)
                    mm.then_inc(s_t.h, 1)
                s_t.n += 1
                two = s_t.n

                @blk.vector
                def _(v, tt=tt):
                    v.wait_ge(s_t.h, two)
                    v.wait_ge(s_lx.h, ldx)
                    v.tensor_add(out=x1t[:, :], in0=x1t[:, :],
                                 in1=pw0[:, 0:1024]).then_inc(s_v.h, 1)
                s_v.n += 1
                vx1 = s_v.n

                @blk.scalar
                def _(s, tt=tt, half=half):
                    s.wait_ge(s_v.h, vx1)
                    s_st.dinc(s.dma_start(out=x1_s[t0 + tt * 128:t0 + (tt + 1) * 128, :],
                                          in_=x1t[:, :]))
                    if "d_att" in dbg_t and half == 1:
                        s_st.dinc(s.dma_start(
                            out=dbg_t["d_att"][t0 + tt * 128:t0 + (tt + 1) * 128, :],
                            in_=x1t[:, :]))
                stx = s_st.n

                @blk.vector
                def _(v):
                    v.wait_ge(s_st.h, stx)
                    v.sem_inc(s_v.h, 1)
                s_v.n += 1

    def phase5():
        for wsrc, wdst, rows, wid in ((w_gate, wgq_s, FFI, D), (w_up, wuq_s, FFI, D),
                                      (w_down, wdq_s, D, FFI)):
            for rt in range(rows // 128):
                @blk.sync
                def _(sync, wsrc=wsrc, rt=rt, wid=wid):
                    sync.wait_ge(s_v.h, s_v.n)
                    s_lw.dinc(sync.dma_start(out=q_hi[:, 0:wid],
                                             in_=wsrc[ts(rt, 128), :]))
                ldw_ = s_lw.n

                @blk.vector
                def _(v, wid=wid):
                    v.wait_ge(s_lw.h, ldw_)
                    emit_fp4(v, wid, q_hi[:, 0:wid])
                vq = s_v.n

                @blk.scalar
                def _(s, wdst=wdst, rt=rt, wid=wid, wsrc=wsrc):
                    s.wait_ge(s_v.h, vq)
                    s_st.dinc(s.dma_start(out=wdst[ts(rt, 128), :], in_=q_out[:, 0:wid]))
                    if wsrc is w_gate and "d_wgq" in dbg_t:
                        s_st.dinc(s.dma_start(out=dbg_t["d_wgq"][ts(rt, 128), :],
                                              in_=q_out[:, 0:wid]))
                stw = s_st.n

                @blk.vector
                def _(v):
                    v.wait_ge(s_st.h, stw)
                    v.sem_inc(s_v.h, 1)
                s_v.n += 1

    def phase6():
        @blk.sync
        def _(sync):
            sync.wait_ge(s_bar.h, s_bar.n)
            for c in range(NCH):
                dstbuf = xqT_lo if c < 4 else xqT_hi
                s_tr.dinc(sync.dma_start(out=dstbuf[:, (c % 4) * N:(c % 4) * N + N],
                                         in_=xq_s[0:N, ts(c, 128)], transpose=True))
        trq = s_tr.n

        NPS = FFI // 1024 + (1 if FFI % 1024 else 0)
        for np_ in range(NPS):
            n0 = np_ * 1024
            nw = min(1024, FFI - n0)

            @blk.sync
            def _(sync, n0=n0, nw=nw):
                sync.wait_ge(s_t.h, s_t.n)
                sync.wait_ge(s_v.h, s_v.n)
                for k in range(NCH):
                    s_lw.dinc(sync.dma_start(out=wstr[:, k * 2048:k * 2048 + nw],
                                             in_=wgq_s[n0:n0 + nw, ts(k, 128)],
                                             transpose=True))
                    s_lw.dinc(sync.dma_start(out=wstr[:, k * 2048 + 1024:k * 2048 + 1024 + nw],
                                             in_=wuq_s[n0:n0 + nw, ts(k, 128)],
                                             transpose=True))
            ldgu = s_lw.n

            for it in range(NT):
                @blk.tensor
                def _(t, it=it, n0=n0, nw=nw):
                    t.wait_ge(s_lw.h, ldgu)
                    t.wait_ge(s_tr.h, trq)
                    if it > 0 or n0 > 0:
                        t.wait_ge(s_v.h, s_v.n)
                    for k in range(NCH):
                        src = xqT_lo if k < 4 else xqT_hi
                        sl = src[:, (k % 4) * N + it * 128:(k % 4) * N + it * 128 + 128]
                        for nb in range(0, nw, 512):
                            ww = min(512, nw - nb)
                            t.matmul(pw0[:, nb:nb + ww], sl,
                                     wstr[:, k * 2048 + nb:k * 2048 + nb + ww],
                                     start=(k == 0), stop=(k == NCH - 1))
                            mm = t.matmul(pw1[:, nb:nb + ww], sl,
                                          wstr[:, k * 2048 + 1024 + nb:k * 2048 + 1024 + nb + ww],
                                          start=(k == 0), stop=(k == NCH - 1))
                    mm.then_inc(s_t.h, 1)
                s_t.n += 1
                tgu = s_t.n

                @blk.scalar
                def _(s, nw=nw):
                    s.wait_ge(s_t.h, tgu)
                    s.activation(out=gsil_v[:, 0:nw], in_=pw0[:, 0:nw],
                                 func=ACT.Silu).then_inc(s_s.h, 1)
                s_s.n += 1
                sgs = s_s.n

                @blk.vector
                def _(v, nw=nw):
                    v.wait_ge(s_s.h, sgs)
                    v.tensor_mul(out=junk[:, 0:nw], in0=gsil_v[:, 0:nw],
                                 in1=pw1[:, 0:nw]).then_inc(s_v.h, 1)
                s_v.n += 1
                vh = s_v.n

                @blk.scalar
                def _(s, it=it, n0=n0, nw=nw):
                    s.wait_ge(s_v.h, vh)
                    s_st.dinc(s.dma_start(out=h_s[ts(it, 128), n0:n0 + nw],
                                          in_=junk[:, 0:nw]))
                sth = s_st.n

                @blk.vector
                def _(v):
                    v.wait_ge(s_st.h, sth)
                    v.sem_inc(s_v.h, 1)
                s_v.n += 1

        for it in range(NT):
            @blk.sync
            def _(sync, it=it):
                sync.wait_ge(s_v.h, s_v.n)
                s_lx.dinc(sync.dma_start(out=q_hi[:, 0:FFI], in_=h_s[ts(it, 128), :]))
            ldh = s_lx.n

            @blk.vector
            def _(v):
                v.wait_ge(s_lx.h, ldh)
                emit_fp4(v, FFI, q_hi[:, 0:FFI])
            vhq = s_v.n

            @blk.scalar
            def _(s, it=it):
                s.wait_ge(s_v.h, vhq)
                s_st.dinc(s.dma_start(out=hq_s[ts(it, 128), :], in_=q_out[:, 0:FFI]))
                if "d_hq" in dbg_t:
                    s_st.dinc(s.dma_start(out=dbg_t["d_hq"][ts(it, 128), :],
                                          in_=q_out[:, 0:FFI]))
            sthq = s_st.n

            @blk.vector
            def _(v):
                v.wait_ge(s_st.h, sthq)
                v.sem_inc(s_v.h, 1)
            s_v.n += 1

        @blk.sync
        def _(sync):
            sync.wait_ge(s_v.h, s_v.n)
            sync.wait_ge(s_t.h, s_t.n)
            for kc in range(FFI // 128):
                if kc < 16:
                    dst = wdT_a[:, kc * 1024:(kc + 1) * 1024]
                else:
                    dst = wdT_b[:, (kc - 16) * 1024:(kc - 15) * 1024]
                s_tr.dinc(sync.dma_start(out=dst, in_=wdq_s[0:D, ts(kc, 128)],
                                         transpose=True))
        trd = s_tr.n

        NB4 = NT // 4
        for b4 in range(NB4):
            @blk.sync
            def _(sync, b4=b4):
                sync.wait_ge(s_t.h, s_t.n)
                for kc in range(FFI // 128):
                    s_th.dinc(sync.dma_start(out=hqTb[:, kc * 512:(kc + 1) * 512],
                                             in_=hq_s[b4 * 512:(b4 + 1) * 512, ts(kc, 128)],
                                             transpose=True))
            trh = s_th.n

            for i4 in range(4):
                it = b4 * 4 + i4

                @blk.sync
                def _(sync, it=it):
                    sync.wait_ge(s_v.h, s_v.n)
                    s_lx.dinc(sync.dma_start(out=x1t[:, :], in_=x1_s[ts(it, 128), :]))
                ldx1 = s_lx.n

                @blk.tensor
                def _(t, i4=i4):
                    t.wait_ge(s_th.h, trh)
                    t.wait_ge(s_tr.h, trd)
                    if True:
                        t.wait_ge(s_v.h, s_v.n)
                    for kc in range(FFI // 128):
                        if kc < 16:
                            wsl = wdT_a[:, kc * 1024:(kc + 1) * 1024]
                        else:
                            wsl = wdT_b[:, (kc - 16) * 1024:(kc - 15) * 1024]
                        hsl = hqTb[:, kc * 512 + i4 * 128:kc * 512 + i4 * 128 + 128]
                        t.matmul(pw0[:, 0:512], hsl, wsl[:, 0:512],
                                 start=(kc == 0), stop=(kc == FFI // 128 - 1))
                        mm = t.matmul(pw0[:, 512:1024], hsl, wsl[:, 512:1024],
                                      start=(kc == 0), stop=(kc == FFI // 128 - 1))
                    mm.then_inc(s_t.h, 1)
                s_t.n += 1
                tdn = s_t.n

                @blk.vector
                def _(v, it=it):
                    v.wait_ge(s_t.h, tdn)
                    v.wait_ge(s_lx.h, ldx1)
                    v.tensor_add(out=x1t[:, :], in0=x1t[:, :],
                                 in1=pw0[:, 0:1024]).then_inc(s_v.h, 1)
                s_v.n += 1
                vf = s_v.n

                @blk.scalar
                def _(s, it=it):
                    s.wait_ge(s_v.h, vf)
                    s_st.dinc(s.dma_start(out=x1_s[ts(it, 128), :], in_=x1t[:, :]))
                stf = s_st.n

                @blk.vector
                def _(v):
                    v.wait_ge(s_st.h, stf)
                    v.sem_inc(s_v.h, 1)
                s_v.n += 1

    def phase8a():
        for wsrc, wdst, rows in ((W_in, wi_s, D), (W_attn, wa_s, 656), (W_out, wo2_s, D)):
            nrt = (rows + 127) // 128
            for rt in range(nrt):
                rr = min(128, rows - rt * 128)

                @blk.sync
                def _(sync, wsrc=wsrc, rt=rt, rr=rr):
                    sync.wait_ge(s_v.h, s_v.n)
                    s_lw.dinc(sync.dma_start(out=wfp[0:rr, :],
                                             in_=wsrc[rt * 128:rt * 128 + rr, :]))
                ldr = s_lw.n

                @blk.vector
                def _(v, rr=rr):
                    v.wait_ge(s_lw.h, ldr)
                    v.tensor_copy(out=wbuf[0:rr, 0:1024], in_=wfp[0:rr, :]).then_inc(s_v.h, 1)
                s_v.n += 1
                vcp = s_v.n

                @blk.scalar
                def _(s, wdst=wdst, rt=rt, rr=rr):
                    s.wait_ge(s_v.h, vcp)
                    s_st.dinc(s.dma_start(out=wdst[rt * 128:rt * 128 + rr, :],
                                          in_=wbuf[0:rr, 0:1024]))
                str_ = s_st.n

                @blk.vector
                def _(v):
                    v.wait_ge(s_st.h, str_)
                    v.sem_inc(s_v.h, 1)
                s_v.n += 1

        for rt in range(NCH):
            @blk.sync
            def _(sync, rt=rt):
                sync.wait_ge(s_v.h, s_v.n)
                s_lw.dinc(sync.dma_start(out=wfp[:, :], in_=W_l1[ts(rt, 128), :]))
            ldr = s_lw.n

            @blk.vector
            def _(v, rt=rt):
                v.wait_ge(s_lw.h, ldr)
                v.reduce_sum(out=stat[:, ts(rt, 1)], in_=wfp[:, :], axis=AX.X,
                             apply_absolute_value=True).then_inc(s_v.h, 1)
                v.wait_ge(s_v.h, s_v.n + 1)
                v.sem_inc(s_v.h, 1)
            s_v.n += 2
        vws = s_v.n

        @blk.vector
        def _(v):
            v.wait_ge(s_v.h, vws)
            v.reduce_sum(out=stat[:, NCH:NCH + 1], in_=stat[:, 0:NCH],
                         axis=AX.X).then_inc(s_v.h, 1)
        s_v.n += 1
        vws2 = s_v.n

        @blk.tensor
        def _(t):
            t.wait_ge(s_v.h, vws2)
            t.matmul(p4[0:1, 0:1], C(1.0), stat[:, NCH:NCH + 1],
                     start=True, stop=True).then_inc(s_t.h, 1)
        s_t.n += 1
        tws = s_t.n

        @blk.vector
        def _(v):
            v.wait_ge(s_t.h, tws)
            v.tensor_scalar(out=stat[0:1, 0:1], in0=p4[0:1, 0:1],
                            scalar1=1.0 / (D * D), scalar2=1e-8,
                            op0=ALU.mult, op1=ALU.add).then_inc(s_v.h, 1)
            v.wait_ge(s_v.h, s_v.n + 1)
            v.reciprocal(out=stat[0:1, 1:2], in_=stat[0:1, 0:1]).then_inc(s_v.h, 1)
        s_v.n += 2
        vws3 = s_v.n

        @blk.scalar
        def _(s):
            s.wait_ge(s_v.h, vws3)
            s_st.dinc(s.dma_start(out=scal_s[0:2], in_=stat[0:1, 0:2]))
            if "d_ws" in dbg_t:
                s_st.dinc(s.dma_start(out=dbg_t["d_ws"][0:1, :], in_=stat[0:1, 0:2]))
        stws = s_st.n

        @blk.sync
        def _(sync):
            sync.wait_ge(s_st.h, stws)
            s_lx.dinc(sync.dma_start(out=wl1s[:, 0:1],
                                     in_=scal_s[0:1].unsqueeze(0).broadcast_to([128, 1])))
            s_lx.dinc(sync.dma_start(out=wl1r[:, 0:1],
                                     in_=scal_s[1:2].unsqueeze(0).broadcast_to([128, 1])))
        ldws = s_lx.n

        for rt in range(NCH):
            @blk.sync
            def _(sync, rt=rt):
                sync.wait_ge(s_v.h, s_v.n)
                s_lw.dinc(sync.dma_start(out=wfp[:, :], in_=W_l1[ts(rt, 128), :]))
            ldr = s_lw.n

            @blk.vector
            def _(v):
                v.wait_ge(s_lw.h, ldr)
                v.wait_ge(s_lx.h, ldws)

                def st(inst):
                    s_v.inc(inst)
                    v.wait_ge(s_v.h, s_v.n)
                st(v.tensor_scalar(out=q_vI[:, 0:D].bitcast(F32), in0=wfp[:, :],
                                   scalar1=wl1r[:, 0:1], scalar2=M24,
                                   op0=ALU.mult, op1=ALU.add))
                st(v.tensor_scalar(out=q_vI[:, 0:D].bitcast(F32),
                                   in0=q_vI[:, 0:D].bitcast(F32),
                                   scalar1=M24, scalar2=-1.0,
                                   op0=ALU.subtract, op1=ALU.max))
                st(v.tensor_scalar(out=q_out[:, 0:D], in0=q_vI[:, 0:D].bitcast(F32),
                                   scalar1=1.0, scalar2=None, op0=ALU.min))
            vqw = s_v.n

            @blk.scalar
            def _(s, rt=rt):
                s.wait_ge(s_v.h, vqw)
                s_st.dinc(s.dma_start(out=wl1q_s[ts(rt, 128), :], in_=q_out[:, 0:D]))
                if "d_wl1q" in dbg_t:
                    s_st.dinc(s.dma_start(out=dbg_t["d_wl1q"][ts(rt, 128), :],
                                          in_=q_out[:, 0:D]))
            stw = s_st.n

            @blk.vector
            def _(v):
                v.wait_ge(s_st.h, stw)
                v.sem_inc(s_v.h, 1)
            s_v.n += 1

    def phase8(sq):
        t0 = sq * T
        prs = pr_flat.rearrange("(a) -> a")

        @blk.sync
        def _(sync):
            sync.wait_ge(s_bar.h, s_bar.n)
            sync.wait_ge(s_out.h, s_out.n)
            sync.wait_ge(s_v.h, s_v.n)
            sync.wait_ge(s_st.h, s_st.n)
            s_st.dinc(sync.dma_start(out=u_s[sq, 0:D, 0:64],
                                     in_=zpad[:, 0:512]))
            s_st.dinc(sync.dma_start(out=u_s[sq, 0:D, T + 64:T + 128],
                                     in_=zpad[:, 0:512]))
            for c in range(NCH):
                s_tr.dinc(sync.dma_start(out=xn2T[:, ts(c, T)],
                                         in_=xn2_s[t0:t0 + T, ts(c, 128)], transpose=True))
            for k in range(NCH):
                s_tr.dinc(sync.dma_start(out=wiT[:, ts(k, 1024)],
                                         in_=wi_s[0:D, ts(k, 128)], transpose=True))
        trwi = s_tr.n

        psl8 = [pw0[:, 0:512], pw0[:, 512:1024], pw1[:, 0:512], pw1[:, 512:1024]]
        for e in range(NCH):
            @blk.tensor
            def _(t, e=e):
                t.wait_ge(s_tr.h, trwi)
                if e > 0:
                    t.wait_ge(s_s.h, s_s.n)
                for k in range(NCH):
                    for tb in range(T // 512):
                        mm = t.matmul(psl8[tb],
                                      wiT[:, k * 1024 + e * 128:k * 1024 + e * 128 + 128],
                                      xn2T[:, k * T + tb * 512:k * T + (tb + 1) * 512],
                                      start=(k == 0), stop=(k == NCH - 1))
                mm.then_inc(s_t.h, 1)
            s_t.n += 1
            twi = s_t.n

            @blk.scalar
            def _(s, e=e):
                s.wait_ge(s_t.h, twi)
                for tb in range(T // 512):
                    s.copy(out=uT[:, e * T + tb * 512:e * T + (tb + 1) * 512],
                           in_=psl8[tb]).then_inc(s_s.h, 1)
                    s.activation(out=ruT[:, e * T + tb * 512:e * T + (tb + 1) * 512],
                                 in_=psl8[tb], func=ACT.Relu).then_inc(s_s.h, 1)
            s_s.n += 2 * (T // 512)
            swi = s_s.n

            @blk.scalar
            def _(s, e=e):
                s.wait_ge(s_s.h, swi)
                s_st.dinc(s.dma_start(out=u_s[sq, ts(e, 128), 64:64 + T], in_=uT[:, ts(e, T)]))
                if "d_u" in dbg_t and sq == 0:
                    s_st.dinc(s.dma_start(out=dbg_t["d_u"][ts(e, 128), :],
                                          in_=uT[:, ts(e, T)]))
        uwdone = s_st.n

        @blk.sync
        def _(sync):
            sync.wait_ge(s_st.h, uwdone)
            for k in range(NCH):
                s_tr.dinc(sync.dma_start(out=waT[:, ts(k, 656)],
                                         in_=wa_s[0:656, ts(k, 128)], transpose=True))
        trwa = s_tr.n

        for tt in range(NTS):
            @blk.tensor
            def _(t, tt=tt):
                t.wait_ge(s_tr.h, trwa)
                if tt > 0:
                    t.wait_ge(s_st.h, s_st.n)
                for k in range(NCH):
                    sl = ruT[:, k * T + tt * 128:k * T + tt * 128 + 128]
                    t.matmul(pw0[:, 0:512], sl, waT[:, k * 656:k * 656 + 512],
                             start=(k == 0), stop=(k == NCH - 1))
                    mm = t.matmul(pw0[:, 512:656], sl, waT[:, k * 656 + 512:k * 656 + 656],
                                  start=(k == 0), stop=(k == NCH - 1))
                mm.then_inc(s_t.h, 1)
            s_t.n += 1
            tal = s_t.n

            @blk.vector
            def _(v, tt=tt):
                if tt == 0:
                    v.wait_ge(s_st.h, uwdone)
                    v.memset(PW[:, :], 0.0).then_inc(s_v.h, 1)
                else:
                    v.sem_inc(s_v.h, 1)
            s_v.n += 1
            vpw0_ = s_v.n

            @blk.scalar
            def _(s, tt=tt):
                s.wait_ge(s_t.h, tal)
                s.wait_ge(s_v.h, vpw0_)
                for sc_i in range(2):
                    cw = CWS[sc_i]
                    srcap = pw0[:, (0 if sc_i == 0 else 328):(168 if sc_i == 0 else 656)]
                    dst = PW[:, PRB[sc_i]:PRB[sc_i] + 8 * SLOTP[sc_i]]
                    s.activation(out=dst.rearrange("p (h w) -> p h w", w=SLOTP[sc_i])[:, :, 0:cw],
                                 in_=srcap.rearrange("p (h w) -> p h w", w=cw),
                                 func=ACT.Exp).then_inc(s_s.h, 1)
            s_s.n += 2
            spw = s_s.n

            @blk.vector
            def _(v, tt=tt):
                v.wait_ge(s_s.h, spw)

                def st(inst):
                    s_v.inc(inst)
                    v.wait_ge(s_v.h, s_v.n)
                for sc_i in range(2):
                    cw = CWS[sc_i]
                    dst = PW[:, PRB[sc_i]:PRB[sc_i] + 8 * SLOTP[sc_i]]
                    view = dst.rearrange("p (h w) -> p h w", w=SLOTP[sc_i])[:, :, 0:cw]
                    st(v.reduce_sum(out=prz[:, ts(sc_i, 8)].unsqueeze(2),
                                    in_=view, axis=AX.X))
                st(v.reciprocal(out=prz[:, 0:16], in_=prz[:, 0:16]))
                st(v.tensor_scalar(out=prz[:, 0:8], in0=prz[:, 0:8],
                                   scalar1=tiny[:, 8:9], scalar2=None, op0=ALU.mult))
                st(v.tensor_scalar(out=prz[:, 8:16], in0=prz[:, 8:16],
                                   scalar1=tiny[:, 9:10], scalar2=None, op0=ALU.mult))
                for sc_i in range(2):
                    cw = CWS[sc_i]
                    dst = PW[:, PRB[sc_i]:PRB[sc_i] + 8 * SLOTP[sc_i]]
                    view = dst.rearrange("p (h w) -> p h w", w=SLOTP[sc_i])[:, :, 0:cw]
                    st(v.tensor_mul(out=view, in0=view,
                                    in1=prz[:, ts(sc_i, 8)].unsqueeze(2).broadcast_to([128, 8, cw])))
            vpw = s_v.n

            @blk.scalar
            def _(s, tt=tt):
                s.wait_ge(s_v.h, vpw)
                base = (sq * T + tt * 128) * PRP
                dst = prs[base:base + 128 * PRP].rearrange("(p w) -> p w", w=PRP)
                s_st.dinc(s.dma_start(out=dst, in_=PW[:, :]))

        aldone = s_st.n

        @blk.sync
        def _(sync):
            sync.wait_ge(s_st.h, aldone)
            sync.wait_ge(s_t.h, s_t.n)
            for kt in range(NKTP):
                s_tr.dinc(sync.dma_start(out=upad[:, ts(kt, 1024)],
                                         in_=u_s[sq, 0:D, kt * 128:kt * 128 + 128],
                                         transpose=True))
        trup = s_tr.n

        @blk.vector
        def _(v):
            v.wait_ge(s_st.h, aldone)
            v.sem_inc(s_v.h, 1)
        s_v.n += 1
        vup = s_v.n

        phase8_window(sq, vup)

    def phase8_window(sq, vup):
        prs = pr_flat
        for c in range(NCH):
            for tg in range(NTS // 4):
                @blk.tensor
                def _(t, c=c, tg=tg):
                    t.wait_ge(s_v.h, vup)
                    t.wait_ge(s_tr.h, s_tr.n)
                    if not (c == 0 and tg == 0):
                        t.wait_ge(s_s.h, s_s.n)
                    t.sem_inc(s_t.h, 1)
                s_t.n += 1
                gat = s_t.n

                for i4 in range(4):
                    tt = tg * 4 + i4

                    @blk.sync
                    def _(sync, c=c, tt=tt, i4=i4, gat=gat):
                        sync.wait_ge(s_t.h, gat + i4)
                        for sc_i in range(2):
                            for k2 in range(2):
                                ktp = tt + k2
                                base = ((sq * T + tt * 128) * PRP
                                        + PRB[sc_i] + c * SLOTP[sc_i] + PADS[sc_i]
                                        + (ktp - tt) * 128 - UPAD)
                                ap = bass.AP(prs.tensor, base,
                                             [[PRP - 1, 128], [1, 128]])
                                s_tr.dinc(sync.dma_start(
                                    out=Btile[:, ts(sc_i * 2 + k2, 128)],
                                    in_=ap, transpose=True))
                    trbt = s_tr.n

                    @blk.tensor
                    def _(t, c=c, tt=tt, i4=i4):
                        t.wait_ge(s_tr.h, trbt)
                        first = True
                        for sc_i in range(2):
                            for k2 in range(2):
                                ktp = tt + k2
                                mm = t.matmul(p6[:, ts(i4, 128)],
                                              upad[:, ktp * 1024 + c * 128:ktp * 1024 + c * 128 + 128],
                                              Btile[:, ts(sc_i * 2 + k2, 128)],
                                              start=first, stop=(sc_i == 1 and k2 == 1))
                                first = False
                        mm.then_inc(s_t.h, 1)
                    s_t.n += 1
                tw4 = s_t.n

                @blk.scalar
                def _(s, c=c, tg=tg):
                    s.wait_ge(s_t.h, tw4)
                    s.copy(out=owT[:, c * T + tg * 512:c * T + (tg + 1) * 512],
                           in_=p6[:, 0:512]).then_inc(s_s.h, 1)
                s_s.n += 1

        if "d_ow" in dbg_t and sq == 0:
            @blk.scalar
            def _(s):
                s.wait_ge(s_s.h, s_s.n)
                for c in range(NCH):
                    s_st.dinc(s.dma_start(out=dbg_t["d_ow"][ts(c, 128), :],
                                          in_=owT[:, ts(c, T)]))

    def phase9(sq):
        t0 = sq * T

        @blk.sync
        def _(sync):
            sync.wait_ge(s_s.h, s_s.n)
            for k in range(NCH):
                s_tr.dinc(sync.dma_start(out=woutT[:, ts(k, 1024)],
                                         in_=wo2_s[0:D, ts(k, 128)], transpose=True))
                s_tr.dinc(sync.dma_start(out=wl1qT[:, ts(k, 1024)],
                                         in_=wl1q_s[0:D, ts(k, 128)], transpose=True))
                s_tr.dinc(sync.dma_start(out=aqT[:, ts(k, T)],
                                         in_=aq_s[t0:t0 + T, ts(k, 128)], transpose=True))
        tr9 = s_tr.n

        for tt in range(NTS):
            it = sq * NTS + tt

            @blk.sync
            def _(sync, it=it):
                sync.wait_ge(s_v.h, s_v.n)
                s_lx.dinc(sync.dma_start(out=x1t[:, :], in_=x1_s[ts(it, 128), :]))
            ldx1 = s_lx.n

            @blk.tensor
            def _(t, tt=tt):
                t.wait_ge(s_tr.h, tr9)
                if tt > 0:
                    t.wait_ge(s_v.h, s_v.n)
                for k in range(NCH):
                    osl = owT[:, k * T + tt * 128:k * T + tt * 128 + 128]
                    t.matmul(pw0[:, 0:512], osl, woutT[:, k * 1024:k * 1024 + 512],
                             start=(k == 0), stop=(k == NCH - 1))
                    t.matmul(pw0[:, 512:1024], osl, woutT[:, k * 1024 + 512:k * 1024 + 1024],
                             start=(k == 0), stop=(k == NCH - 1))
                for k in range(NCH):
                    asl = aqT[:, k * T + tt * 128:k * T + tt * 128 + 128]
                    t.matmul(pw1[:, 0:512], asl, wl1qT[:, k * 1024:k * 1024 + 512],
                             start=(k == 0), stop=(k == NCH - 1))
                    mm = t.matmul(pw1[:, 512:1024], asl, wl1qT[:, k * 1024 + 512:k * 1024 + 1024],
                                  start=(k == 0), stop=(k == NCH - 1))
                mm.then_inc(s_t.h, 1)
            s_t.n += 1
            t9 = s_t.n

            @blk.vector
            def _(v, it=it):
                v.wait_ge(s_t.h, t9)
                v.wait_ge(s_lx.h, ldx1)
                v.scalar_tensor_tensor(out=junk[:, 0:D], in0=pw1[:, 0:1024],
                                       scalar=asws[:, ts(it, 1)], in1=x1t[:, :],
                                       op0=ALU.mult, op1=ALU.add).then_inc(s_v.h, 1)
                v.wait_ge(s_v.h, s_v.n + 1)
                v.tensor_add(out=junk[:, 0:D], in0=junk[:, 0:D],
                             in1=pw0[:, 0:1024]).then_inc(s_v.h, 1)
            s_v.n += 2
            v9 = s_v.n

            @blk.scalar
            def _(s, it=it):
                s.wait_ge(s_v.h, v9)
                s_out.dinc(s.dma_start(out=out_t[ts(it, 128), :], in_=junk[:, 0:D]))
            o9 = s_out.n

            @blk.vector
            def _(v):
                v.wait_ge(s_out.h, o9)
                v.sem_inc(s_v.h, 1)
            s_v.n += 1

    phase0a()
    phase0b()
    if stop >= 1:
        phase_ln(x_in, xn_s, xq_s, layer=0)
    barrier()
    if stop >= 2:
        for sq in range(NSEQ):
            phase234(sq)
        barrier()
    if stop >= 5:
        phase5()
        phase6()
        barrier()
    if stop >= 7:
        phase8a()
        phase_ln(x1_s, xn2_s, aq_s, layer=1)
        barrier()
    if stop >= 8:
        for sq in range(NSEQ):
            phase8(sq)
            phase9(sq)

    if stop < 8:
        @blk.vector
        def _(v):
            v.wait_ge(s_bar.h, s_bar.n)
            v.memset(junk[:, 0:D], 0.0).then_inc(s_v.h, 1)
        s_v.n += 1
        zv = s_v.n

        @blk.scalar
        def _(s):
            s.wait_ge(s_v.h, zv)
            for it in range(NT):
                s_out.dinc(s.dma_start(out=out_t[ts(it, 128), :], in_=junk[:, 0:D]))

    @blk.gpsimd
    def _(g):
        g.wait_ge(s_out.h, s_out.n)
        g.wait_ge(s_st.h, s_st.n)
        g.wait_ge(s_mv.h, s_mv.n)

    es.close()
    return nc


def _host_consts(T):
    iota = np.broadcast_to(np.arange(T, dtype=np.float32), (128, T)).copy()
    rope_e = np.zeros((128, 1), np.float32)
    rope_sm = np.zeros((128, 1), np.float32)
    for p in range(128):
        r = p % 64
        if r >= 32:
            j = r - 32
            sgn = -1.0
            if j >= 16:
                j -= 16
                sgn = 1.0
            rope_e[p, 0] = -j / 16.0
            rope_sm[p, 0] = sgn * 2.0 * np.pi / max(T - 1, 1)
    sperm = np.zeros((128, 128), np.float32)
    for m in range(128):
        r = m % 64
        if 32 <= r < 48:
            k = m + 16
        elif 48 <= r < 64:
            k = m - 16
        else:
            k = m
        sperm[k, m] = 1.0
    e2 = np.zeros((16, 1024), np.float32)
    for c in range(8):
        for m in range(128):
            e2[2 * c + (1 if m >= 64 else 0), c * 128 + m] = 1.0
    return {
        "iota_t": iota,
        "rope_e": rope_e,
        "rope_sm": rope_sm,
        "sperm": sperm.astype(ml_dtypes.bfloat16),
        "e2sel": e2.astype(ml_dtypes.bfloat16),
    }


def make_in_maps(inputs, cfg, n_cores=8):
    T, NSEQ = cfg["T"], cfg["NSEQ"]
    consts = _host_consts(T)
    x = np.ascontiguousarray(np.asarray(inputs["L0_attn_x"]), dtype=np.float32)
    B = x.shape[0]
    shared = {
        "Wq": inputs["L0_attn_Wq"], "Wk": inputs["L0_attn_Wk"],
        "Wv": inputs["L0_attn_Wv"], "Wo": inputs["L0_attn_Wo"],
        "bq": inputs["L0_attn_bq"], "bk": inputs["L0_attn_bk"],
        "bv": inputs["L0_attn_bv"], "bo": inputs["L0_attn_bo"],
        "nrope": inputs["L0_attn_nrope_freq"],
        "w_gate": inputs["L0_ffn_w_gate"], "w_up": inputs["L0_ffn_w_up"],
        "w_down": inputs["L0_ffn_w_down"],
        "W_in": inputs["L1_attn_W_in"], "W_attn": inputs["L1_attn_W_attn"],
        "W_out": inputs["L1_attn_W_out"], "sw": inputs["L1_attn_scale_weights"],
        "W_l1": inputs["L1_ffn_weight"],
    }
    shared = {k: np.ascontiguousarray(np.asarray(v), dtype=np.float32)
              for k, v in shared.items()}
    shared.update(consts)
    per_core = B // n_cores
    maps = []
    for c in range(n_cores):
        m = dict(shared)
        m["x"] = np.ascontiguousarray(
            x[c * per_core:(c + 1) * per_core].reshape(per_core * T, D))
        maps.append(m)
    return maps


def kernel(**inputs):
    cfg = {"T": 2048, "NSEQ": 2}
    nc = build(cfg)
    maps = make_in_maps(inputs, cfg, n_cores=8)
    res = run_bass_kernel_spmd(nc, maps, core_ids=list(range(8)))
    per_core = 16 // 8
    out = np.concatenate(
        [res.results[c]["out"].reshape(per_core, cfg["T"], D) for c in range(8)],
        axis=0)
    return out.astype(np.float32)


# version 02
# speedup vs baseline: 1.2060x; 1.2060x over previous
import sys
if "/opt/trn_rl_repo" not in sys.path:
    sys.path.insert(0, "/opt/trn_rl_repo")

from contextlib import ExitStack

import numpy as np
import ml_dtypes

import concourse.bass as bass
import concourse.mybir as mybir
from concourse.bass_utils import run_bass_kernel_spmd

F32 = mybir.dt.float32
BF16 = mybir.dt.bfloat16
I32 = mybir.dt.int32
AX = mybir.AxisListType
ALU = mybir.AluOpType
ACT = mybir.ActivationFunctionType

D = 1024
NH = 16
FFI = 2816
NCH = 8
CWS = (21, 41)
PADS = (10, 20)
UPAD = 64
GAP = 192
SLOTP = [CWS[0] + GAP, CWS[1] + GAP]
PRB = [GAP // 2, GAP // 2 + 8 * SLOTP[0]]
PRP = GAP // 2 + 8 * SLOTP[0] + 8 * SLOTP[1] + GAP // 2
M22 = float(3 * 2 ** 21)
M24 = float(3 * 2 ** 22)
BFREQ = 10000.0
MIN_L = 0.83
SDT_L = (BFREQ - MIN_L * BFREQ) * 2.0
EPS_LN = 1e-5
PI = float(np.pi)
NEG4M = -4194304


def ts(i, size):
    return slice(i * size, (i + 1) * size)


class Sem:
    def __init__(self, nc, es, name):
        self.h = es.enter_context(nc.semaphore(name))
        self.n = 0

    def inc(self, inst, k=1):
        inst.then_inc(self.h, k)
        self.n += k
        return self.n

    def dinc(self, inst):
        inst.then_inc(self.h, 16)
        self.n += 16
        return self.n


def build(cfg):
    T = cfg["T"]
    NSEQ = cfg["NSEQ"]
    N = T * NSEQ
    NT = N // 128
    NTS = T // 128
    QB = min(512, T)
    NQB = T // QB
    NKT = T // 128
    NKTP = (T + 2 * UPAD) // 128
    LNG = min(8, NT)
    dbg = set(cfg.get("debug", ()))
    stop = cfg.get("stop_after", 99)

    nc = bass.Bass("TRN2", target_bir_lowering=False, debug=False)

    def inp(name, shape, dt=F32):
        return nc.dram_tensor(name, shape, dt, kind="ExternalInput").ap()

    x_in = inp("x", [N, D])
    Wq = inp("Wq", [D, D]); Wk = inp("Wk", [D, D]); Wv = inp("Wv", [D, D])
    Wo = inp("Wo", [D, D])
    bq = inp("bq", [D]); bk = inp("bk", [D]); bv = inp("bv", [D]); bo = inp("bo", [D])
    nrope = inp("nrope", [1])
    w_gate = inp("w_gate", [FFI, D]); w_up = inp("w_up", [FFI, D])
    w_down = inp("w_down", [D, FFI])
    W_in = inp("W_in", [D, D]); W_attn = inp("W_attn", [656, D])
    W_out = inp("W_out", [D, D]); sw_in = inp("sw", [2])
    W_l1 = inp("W_l1", [D, D])
    iota_t = inp("iota_t", [128, T])
    rope_e = inp("rope_e", [128, 1])
    rope_sm = inp("rope_sm", [128, 1])
    sperm = nc.dram_tensor("sperm", [128, 128], BF16, kind="ExternalInput").ap()
    e2sel = nc.dram_tensor("e2sel", [16, 1024], BF16, kind="ExternalInput").ap()

    out_t = nc.dram_tensor("out", [N, D], F32, kind="ExternalOutput").ap()
    dbg_t = {}
    for name, shape, dt in [
        ("d_xn", [N, D], F32), ("d_xq", [N, D], BF16),
        ("d_qT", [D, T], BF16), ("d_kT", [D, T], BF16),
        ("d_v", [128, NH * 65], BF16), ("d_att", [N, D], F32),
        ("d_hq", [N, FFI], BF16), ("d_xn2", [N, D], F32),
        ("d_u", [D, T], BF16), ("d_ow", [D, T], BF16),
        ("d_aq", [N, D], BF16), ("d_wgq", [FFI, D], BF16),
        ("d_wl1q", [D, D], BF16), ("d_ws", [1, 2], F32),
    ]:
        if name in dbg:
            dbg_t[name] = nc.dram_tensor(name, shape, dt, kind="ExternalOutput").ap()

    def scr(name, shape, dt=BF16):
        return nc.dram_tensor(name, shape, dt, kind="Internal").ap()

    xn_s = scr("xn_s", [N, D])
    xq_s = scr("xq_s", [N, D])
    h_s = scr("h_s", [N, FFI], F32)
    hq_s = scr("hq_s", [N, FFI])
    x1_s = scr("x1_s", [N, D], F32)
    xn2_s = scr("xn2_s", [N, D])
    aq_s = scr("aq_s", [N, D])
    wq_bs = scr("wq_bs", [D, D]); wk_bs = scr("wk_bs", [D, D])
    wv_bs = scr("wv_bs", [D, D]); wo_bs = scr("wo_bs", [D, D])
    wgq_s = scr("wgq_s", [FFI, D]); wuq_s = scr("wuq_s", [FFI, D])
    wdq_s = scr("wdq_s", [D, FFI])
    wl1q_s = scr("wl1q_s", [D, D])
    wi_s = scr("wi_s", [D, D]); wa_s = scr("wa_s", [656, D]); wo2_s = scr("wo2_s", [D, D])
    u_s = scr("u_s", [NSEQ, D, T + 128])
    pr_flat = scr("pr_s", [NSEQ * T * PRP])
    scal_s = scr("scal_s", [8], F32)

    es = ExitStack()

    def sb(name, shape, dt=F32):
        return es.enter_context(nc.sbuf_tensor(name, shape, dt))

    R1N = NCH * T
    R1 = es.enter_context(nc.sbuf_tensor("R1", [128, max(R1N, 4 * N, 8 * T, 2 * LNG * D)], BF16))
    R2 = es.enter_context(nc.sbuf_tensor("R2", [128, max(2 * NKT * QB, 4 * N, 8 * T, 8192)], BF16))
    R3 = es.enter_context(nc.sbuf_tensor("R3", [128, max(NKT * NH * 65, NKTP * 1024, 22 * 512 + 2048, NCH * T)], BF16))
    R4 = es.enter_context(nc.sbuf_tensor("R4", [128, max(NCH * T, 16384)], BF16))
    R5 = es.enter_context(nc.sbuf_tensor("R5", [128, max(4 * T, NCH * 656 + PRP, 8192 + 1024)], BF16))

    xg = R1[:].bitcast(F32)
    xnT = R1
    Upr = R1
    xqT_lo = R1
    uT = R1
    aqT = R1
    ET = R2
    xqT_hi = R2
    ruT = R2
    wl1qT = R2
    vA = R3
    hqTb = R3
    gsil_v = R3[:, 22 * 512:22 * 512 + 2048].bitcast(F32)
    xn2T = R3
    upad = R3
    WVr = R4[:, 0:NCH * 1024]
    WQs = R4[:, 8192:12288]
    WKs = R4[:, 12288:16384]
    rope_t1 = R4[:, 8192:8192 + T]
    rope_sq = R4[:, 8192 + T:8192 + 2 * T]
    WOr = R4[:, 0:NCH * 1024]
    wstr = R4
    wdT_a = R4
    owT = R4
    qkbuf = R5
    rbb = R5[:, 2 * T:3 * T]
    UstA = R5[:, 3 * T:3 * T + QB]
    UstB = R5[:, 3 * T + QB:3 * T + 2 * QB]
    wiT = R5
    waT = R5[:, 0:NCH * 656]
    PW = R5[:, NCH * 656:NCH * 656 + PRP]
    woutT = R5
    wdT_b = R5

    q_vI = es.enter_context(nc.sbuf_tensor("q_vI", [128, FFI], I32))
    q_2I = es.enter_context(nc.sbuf_tensor("q_2I", [128, FFI], I32))
    q_out = es.enter_context(nc.sbuf_tensor("q_out", [128, FFI], BF16))
    q_bm = sb("q_bm", [128, FFI // 16])
    q_sc = sb("q_sc", [128, FFI // 16])

    junk = sb("junk", [128, max(FFI + 32, T + 64)])
    junkb = junk[:].bitcast(BF16)
    q_hi = junk[:, 0:FFI]
    Rr = junk[:].bitcast(BF16)[0:16, 0:T]
    Rrb = junk[:].bitcast(BF16)[0:16, T:2 * T]
    stat = sb("stat", [128, 4 * LNG + 4])
    cbc = sb("cbc", [128, 24])
    tiny = sb("tiny", [128, 16])
    ifr = sb("ifr", [128, 4])
    bqt = sb("bqt", [128, 4 * NCH])
    CC = es.enter_context(nc.sbuf_tensor("CC", [128, T], BF16))
    SSb = es.enter_context(nc.sbuf_tensor("SSb", [128, T], BF16))
    spermt = es.enter_context(nc.sbuf_tensor("spermt", [128, 128], BF16))
    e2t = es.enter_context(nc.sbuf_tensor("e2t", [16, 1024], BF16))
    biasrow = es.enter_context(nc.sbuf_tensor("biasrow", [1, D], BF16))
    onesrow = es.enter_context(nc.sbuf_tensor("onesrow", [1, 128], BF16))
    Btile = es.enter_context(nc.sbuf_tensor("Btile", [128, 4 * 128], BF16))
    asws = sb("asws", [128, max(NT, 1)])
    x1t = sb("x1t", [128, D])
    xnt = R2[:].bitcast(F32)[:, 0:D]
    wfp = x1t
    wbuf = R1[:, 0:2048]
    prz = sb("prz", [128, 16])
    wl1r = sb("wl1r", [128, 1])
    zpad = es.enter_context(nc.sbuf_tensor("zpad", [128, 512], BF16))
    wl1s = sb("wl1s", [128, 1])

    pw0 = es.enter_context(nc.psum_tensor("pw0", [128, 1024], F32))
    pw1 = es.enter_context(nc.psum_tensor("pw1", [128, 1024], F32))
    p4 = es.enter_context(nc.psum_tensor("p4", [128, 512], F32))
    p5 = es.enter_context(nc.psum_tensor("p5", [128, 512], F32))
    p6 = es.enter_context(nc.psum_tensor("p6", [128, 512], F32))
    p7 = es.enter_context(nc.psum_tensor("p7", [128, 512], F32))

    s_ld = Sem(nc, es, "s_ld")
    s_lw = Sem(nc, es, "s_lw")
    s_lx = Sem(nc, es, "s_lx")
    s_th = Sem(nc, es, "s_th")
    s_st = Sem(nc, es, "s_st")
    s_tr = Sem(nc, es, "s_tr")
    s_mv = Sem(nc, es, "s_mv")
    s_v = Sem(nc, es, "s_v")
    s_s = Sem(nc, es, "s_s")
    s_t = Sem(nc, es, "s_t")
    s_g = Sem(nc, es, "s_g")
    s_bar = Sem(nc, es, "s_bar")
    s_out = Sem(nc, es, "s_out")

    blk = es.enter_context(nc.Block())

    CONSTS = {}

    def reg_const(*vals):
        for v in vals:
            v = float(v)
            if v not in CONSTS:
                CONSTS[v] = len(CONSTS)
                assert len(CONSTS) <= 24

    def C(v):
        return cbc[:, ts(CONSTS[float(v)], 1)]

    reg_const(0.0, 1.0, -8.0, 0.125, SDT_L, MIN_L * BFREQ)

    nbar = [0]

    def barrier():
        nbar[0] += 1
        tgt = 5 * nbar[0]
        for ename in ("sync", "vector", "scalar", "tensor", "gpsimd"):
            @getattr(blk, ename)
            def _(eng):
                eng.sem_inc(s_bar.h, 1)
                eng.wait_ge(s_bar.h, tgt)
        s_bar.n = tgt

    def emit_fp4(v, W, src_ap):
        J = W // 16
        vF = q_vI[:, 0:W].bitcast(F32)
        q2I = q_2I[:, 0:W]
        q2F = q_2I[:, 0:W].bitcast(F32)
        hiF = q_hi[:, 0:W]

        def st(inst):
            s_v.inc(inst)
            v.wait_ge(s_v.h, s_v.n)

        st(v.reduce_max(out=q_bm[:, 0:J].unsqueeze(2),
                        in_=src_ap.rearrange("p (c s) -> p c s", s=16),
                        axis=AX.X, apply_absolute_value=True))
        st(v.tensor_scalar(out=q_sc[:, 0:J], in0=q_bm[:, 0:J], scalar1=6e-12,
                           scalar2=1.0 / 6.0, op0=ALU.max, op1=ALU.mult))
        st(v.reciprocal(out=q_bm[:, 0:J], in_=q_sc[:, 0:J]))
        bm3 = q_bm[:, 0:J].unsqueeze(2).broadcast_to([128, J, 16])
        sc3 = q_sc[:, 0:J].unsqueeze(2).broadcast_to([128, J, 16])
        st(v.tensor_mul(out=vF.rearrange("p (c s) -> p c s", s=16),
                        in0=src_ap.rearrange("p (c s) -> p c s", s=16), in1=bm3))
        st(v.tensor_scalar_add(out=q2I, in0=q_vI[:, 0:W], scalar1=0x00200000))
        st(v.tensor_single_scalar(out=q2I, in_=q2I, scalar=NEG4M, op=ALU.bitwise_and))
        st(v.tensor_scalar(out=vF, in0=vF, scalar1=M22, scalar2=M22,
                           op0=ALU.add, op1=ALU.subtract))
        st(v.tensor_scalar(out=vF, in0=vF, scalar1=-2.0, scalar2=2.0,
                           op0=ALU.max, op1=ALU.min))
        st(v.tensor_scalar(out=hiF, in0=q2F, scalar1=-2.0, scalar2=2.0,
                           op0=ALU.max, op1=ALU.min))
        st(v.tensor_sub(out=hiF, in0=q2F, in1=hiF))
        st(v.tensor_add(out=hiF, in0=hiF, in1=vF))
        st(v.tensor_mul(out=q_out[:, 0:W].rearrange("p (c s) -> p c s", s=16),
                        in0=hiF.rearrange("p (c s) -> p c s", s=16), in1=sc3))

    def phase0a():
        @blk.sync
        def _(sync):
            for i, b in enumerate((bq, bk, bv, bo)):
                s_ld.dinc(sync.dma_start(out=bqt[:, ts(i, NCH)],
                                         in_=b.rearrange("(c p) -> p c", p=128),
                                         allow_slow_non_contiguous=True))
            s_ld.dinc(sync.dma_start(out=tiny[:, 0:1],
                                     in_=nrope.unsqueeze(0).broadcast_to([128, 1])))
            s_ld.dinc(sync.dma_start(out=ifr[:, 0:1], in_=rope_e[:, :]))
            s_ld.dinc(sync.dma_start(out=ifr[:, 1:2], in_=rope_sm[:, :]))
            s_ld.dinc(sync.dma_start(out=junk[:, 0:T], in_=iota_t[:, 0:T]))
            s_ld.dinc(sync.dma_start(out=tiny[:, 4:6],
                                     in_=sw_in.unsqueeze(0).broadcast_to([128, 2])))
            s_ld.dinc(sync.dma_start(out=e2t[:, :], in_=e2sel[:, :]))
            s_ld.dinc(sync.dma_start(out=spermt[:, :], in_=sperm[:, :]))
        ld0 = s_ld.n

        ncst = [0]

        @blk.vector
        def _(v):
            v.wait_ge(s_ld.h, ld0)
            for val, col in CONSTS.items():
                v.memset(cbc[:, ts(col, 1)], val).then_inc(s_v.h, 1)
                ncst[0] += 1
            v.memset(onesrow[:, :], 1.0).then_inc(s_v.h, 1)
            ncst[0] += 1
            v.memset(zpad[:, :], 0.0).then_inc(s_v.h, 1)
            ncst[0] += 1
        s_v.n += ncst[0]
        vc0 = s_v.n

        @blk.scalar
        def _(s):
            s.wait_ge(s_v.h, vc0)
            s.activation(out=tiny[:, 1:2], in_=tiny[:, 0:1], func=ACT.Sigmoid).then_inc(s_s.h, 1)
            s.wait_ge(s_s.h, s_s.n + 1)
            s.activation(out=tiny[:, 2:3], in_=tiny[:, 1:2], func=ACT.Ln,
                         scale=C(SDT_L), bias=C(MIN_L * BFREQ)).then_inc(s_s.h, 1)
            s.activation(out=tiny[:, 6:8], in_=tiny[:, 4:6], func=ACT.Exp).then_inc(s_s.h, 1)
        s_s.n += 3
        sc0 = s_s.n

        @blk.vector
        def _(v):
            v.wait_ge(s_s.h, sc0)
            v.reduce_sum(out=tiny[:, 10:11], in_=tiny[:, 6:8], axis=AX.X).then_inc(s_v.h, 1)
            v.wait_ge(s_v.h, s_v.n + 1)
            v.reciprocal(out=tiny[:, 11:12], in_=tiny[:, 10:11]).then_inc(s_v.h, 1)
            v.wait_ge(s_v.h, s_v.n + 2)
            v.tensor_mul(out=tiny[:, 8:10], in0=tiny[:, 6:8],
                         in1=tiny[:, 11:12].broadcast_to([128, 2])).then_inc(s_v.h, 1)
            v.tensor_mul(out=ifr[:, 2:3], in0=ifr[:, 0:1], in1=tiny[:, 2:3]).then_inc(s_v.h, 1)
        s_v.n += 4
        vc1 = s_v.n

        @blk.scalar
        def _(s):
            s.wait_ge(s_v.h, vc1)
            s.activation(out=ifr[:, 3:4], in_=ifr[:, 2:3], func=ACT.Exp).then_inc(s_s.h, 1)
        s_s.n += 1
        sc1 = s_s.n

        @blk.vector
        def _(v):
            v.wait_ge(s_s.h, sc1)
            v.tensor_mul(out=ifr[:, 3:4], in0=ifr[:, 3:4], in1=ifr[:, 1:2]).then_inc(s_v.h, 1)
            v.wait_ge(s_v.h, s_v.n + 1)
            v.tensor_scalar(out=junk[:, 0:T], in0=junk[:, 0:T], scalar1=ifr[:, 3:4],
                            scalar2=None, op0=ALU.mult).then_inc(s_v.h, 1)
            v.wait_ge(s_v.h, s_v.n + 2)
            v.tensor_single_scalar(out=q_vI[:, 0:T].bitcast(F32), in_=junk[:, 0:T],
                                   scalar=PI, op=ALU.is_gt).then_inc(s_v.h, 1)
            v.wait_ge(s_v.h, s_v.n + 3)
            v.scalar_tensor_tensor(out=junk[:, 0:T], in0=q_vI[:, 0:T].bitcast(F32),
                                   scalar=-2.0 * PI, in1=junk[:, 0:T],
                                   op0=ALU.mult, op1=ALU.add).then_inc(s_v.h, 1)
            v.wait_ge(s_v.h, s_v.n + 4)
            v.tensor_single_scalar(out=q_vI[:, 0:T].bitcast(F32), in_=junk[:, 0:T],
                                   scalar=-PI, op=ALU.is_lt).then_inc(s_v.h, 1)
            v.wait_ge(s_v.h, s_v.n + 5)
            v.scalar_tensor_tensor(out=junk[:, 0:T], in0=q_vI[:, 0:T].bitcast(F32),
                                   scalar=2.0 * PI, in1=junk[:, 0:T],
                                   op0=ALU.mult, op1=ALU.add).then_inc(s_v.h, 1)
        s_v.n += 6
        vc2 = s_v.n

        @blk.scalar
        def _(s):
            s.wait_ge(s_v.h, vc2)
            s.activation(out=SSb[:, 0:T], in_=junk[:, 0:T], func=ACT.Sin).then_inc(s_s.h, 1)
        s_s.n += 1
        sc2 = s_s.n

        @blk.vector
        def _(v):
            v.wait_ge(s_s.h, sc2)
            v.tensor_scalar_mul(out=xg[:, 0:T], in0=junk[:, 0:T],
                                scalar1=-1.0).then_inc(s_v.h, 1)
            v.wait_ge(s_v.h, s_v.n + 1)
            v.tensor_max(out=junk[:, 0:T], in0=junk[:, 0:T],
                         in1=xg[:, 0:T]).then_inc(s_v.h, 1)
            v.wait_ge(s_v.h, s_v.n + 2)
            v.tensor_scalar(out=junk[:, 0:T], in0=junk[:, 0:T], scalar1=-1.0,
                            scalar2=PI / 2, op0=ALU.mult, op1=ALU.add).then_inc(s_v.h, 1)
        s_v.n += 3
        vc3 = s_v.n

        @blk.scalar
        def _(s):
            s.wait_ge(s_v.h, vc3)
            s.activation(out=CC[:, 0:T], in_=junk[:, 0:T], func=ACT.Sin).then_inc(s_s.h, 1)
        s_s.n += 1

    def phase0b():
        for wsrc, wdst in ((Wq, wq_bs), (Wk, wk_bs), (Wv, wv_bs), (Wo, wo_bs)):
            for c in range(NCH):
                @blk.sync
                def _(sync, wsrc=wsrc, c=c):
                    sync.wait_ge(s_v.h, s_v.n)
                    s_ld.dinc(sync.dma_start(out=wfp[:, :], in_=wsrc[ts(c, 128), :]))
                ldc = s_ld.n

                @blk.vector
                def _(v):
                    v.wait_ge(s_ld.h, ldc)
                    v.tensor_copy(out=wbuf[:, 0:1024], in_=wfp[:, :]).then_inc(s_v.h, 1)
                s_v.n += 1
                vcc = s_v.n

                @blk.scalar
                def _(s, wdst=wdst, c=c):
                    s.wait_ge(s_v.h, vcc)
                    s_st.dinc(s.dma_start(out=wdst[ts(c, 128), :], in_=wbuf[:, 0:1024]))
                stc = s_st.n

                @blk.vector
                def _(v):
                    v.wait_ge(s_st.h, stc)
                    v.sem_inc(s_v.h, 1)
                s_v.n += 1

        @blk.vector
        def _(v):
            v.tensor_copy(out=wbuf[:, 0:NCH], in_=bqt[:, ts(2, NCH)]).then_inc(s_v.h, 1)
        s_v.n += 1
        vb = s_v.n
        for c in range(NCH):
            @blk.sync
            def _(sync, c=c):
                sync.wait_ge(s_v.h, vb)
                if c > 0:
                    sync.wait_ge(s_t.h, s_t.n)
                s_tr.dinc(sync.dma_start(out=wbuf[:, 1024:2048],
                                         in_=wo_bs[ts(c, 128), :]))
            trc = s_tr.n

            @blk.tensor
            def _(t, c=c):
                t.wait_ge(s_tr.h, trc)
                t.matmul(pw0[0:1, 0:512], wbuf[:, c:c + 1], wbuf[:, 1024:1536],
                         start=(c == 0), stop=(c == NCH - 1))
                t.matmul(pw0[0:1, 512:1024], wbuf[:, c:c + 1], wbuf[:, 1536:2048],
                         start=(c == 0), stop=(c == NCH - 1)).then_inc(s_t.h, 1)
            s_t.n += 1
        tb = s_t.n

        @blk.sync
        def _(sync):
            sync.wait_ge(s_t.h, tb)
            s_tr.dinc(sync.dma_start(out=wfp[0:1, 0:D], in_=bo.unsqueeze(0)))
        trb = s_tr.n

        @blk.vector
        def _(v):
            v.wait_ge(s_tr.h, trb)
            v.tensor_add(out=wfp[0:1, 0:D], in0=wfp[0:1, 0:D],
                         in1=pw0[0:1, 0:1024]).then_inc(s_v.h, 1)
            v.wait_ge(s_v.h, s_v.n + 1)
            v.tensor_copy(out=biasrow[0:1, :], in_=wfp[0:1, 0:D]).then_inc(s_v.h, 1)
        s_v.n += 2

    def phase_ln(src_dram, xnbf_dram, q_dram, layer):
        ngrp = NT // LNG
        for g in range(ngrp):
            @blk.sync
            def _(sync, g=g):
                sync.wait_ge(s_v.h, s_v.n)
                for i in range(LNG):
                    s_lx.dinc(sync.dma_start(out=xg[:, ts(i, D)],
                                             in_=src_dram[ts(g * LNG + i, 128), :]))
            ldg = s_lx.n

            @blk.vector
            def _(v):
                v.wait_ge(s_lx.h, ldg)
                for i in range(LNG):
                    v.reduce_sum(out=stat[:, ts(i, 1)], in_=xg[:, ts(i, D)],
                                 axis=AX.X).then_inc(s_v.h, 1)
            s_v.n += LNG
            vred = s_v.n

            @blk.scalar
            def _(s):
                s.wait_ge(s_lx.h, ldg)
                base = s_s.n
                for i in range(LNG):
                    if i > 0:
                        s.wait_ge(s_s.h, base + i)
                    s.activation(out=junk[:, 0:D], in_=xg[:, ts(i, D)], func=ACT.Square,
                                 accum_out=stat[:, ts(LNG + i, 1)]).then_inc(s_s.h, 1)
            s_s.n += LNG
            ssq = s_s.n

            @blk.vector
            def _(v):
                v.wait_ge(s_v.h, vred)
                v.wait_ge(s_s.h, ssq)
                L = slice(0, LNG)
                v.tensor_scalar_mul(out=stat[:, L], in0=stat[:, L],
                                    scalar1=1.0 / D).then_inc(s_v.h, 1)
                v.wait_ge(s_v.h, s_v.n + 1)
                v.tensor_mul(out=stat[:, 2 * LNG:3 * LNG], in0=stat[:, L],
                             in1=stat[:, L]).then_inc(s_v.h, 1)
                v.wait_ge(s_v.h, s_v.n + 2)
                v.scalar_tensor_tensor(out=stat[:, LNG:2 * LNG],
                                       in0=stat[:, LNG:2 * LNG], scalar=1.0 / D,
                                       in1=stat[:, 2 * LNG:3 * LNG],
                                       op0=ALU.mult, op1=ALU.subtract).then_inc(s_v.h, 1)
                v.wait_ge(s_v.h, s_v.n + 3)
                v.tensor_scalar_add(out=stat[:, LNG:2 * LNG], in0=stat[:, LNG:2 * LNG],
                                    scalar1=EPS_LN).then_inc(s_v.h, 1)
            s_v.n += 4
            vstat = s_v.n

            @blk.scalar
            def _(s):
                s.wait_ge(s_v.h, vstat)
                s.activation(out=stat[:, LNG:2 * LNG], in_=stat[:, LNG:2 * LNG],
                             func=ACT.Sqrt).then_inc(s_s.h, 1)
            s_s.n += 1
            ssd = s_s.n

            @blk.vector
            def _(v):
                v.wait_ge(s_s.h, ssd)
                v.reciprocal(out=stat[:, LNG:2 * LNG],
                             in_=stat[:, LNG:2 * LNG]).then_inc(s_v.h, 1)
                v.wait_ge(s_v.h, s_v.n + 1)
                v.tensor_scalar(out=stat[:, 2 * LNG:3 * LNG], in0=stat[:, 0:LNG],
                                scalar1=-1.0, scalar2=None,
                                op0=ALU.mult).then_inc(s_v.h, 1)
                v.wait_ge(s_v.h, s_v.n + 2)
                v.tensor_mul(out=stat[:, 2 * LNG:3 * LNG],
                             in0=stat[:, 2 * LNG:3 * LNG],
                             in1=stat[:, LNG:2 * LNG]).then_inc(s_v.h, 1)
            s_v.n += 3
            vrs = s_v.n

            for i in range(LNG):
                it = g * LNG + i

                @blk.scalar
                def _(s, i=i):
                    s.wait_ge(s_v.h, vrs)
                    s.wait_ge(s_v.h, s_v.n)
                    s.activation(out=xnt[:, :], in_=xg[:, ts(i, D)], func=ACT.Identity,
                                 bias=stat[:, ts(2 * LNG + i, 1)],
                                 scale=stat[:, ts(LNG + i, 1)]).then_inc(s_s.h, 1)
                s_s.n += 1
                sxn = s_s.n

                @blk.vector
                def _(v):
                    v.wait_ge(s_s.h, sxn)
                    v.tensor_copy(out=q_out[:, 0:D], in_=xnt[:, :]).then_inc(s_v.h, 1)
                s_v.n += 1
                vxb = s_v.n

                @blk.scalar
                def _(s, it=it):
                    s.wait_ge(s_v.h, vxb)
                    s_st.dinc(s.dma_start(out=xnbf_dram[ts(it, 128), :],
                                          in_=q_out[:, 0:D]))
                    if layer == 0 and "d_xn" in dbg_t:
                        s_st.dinc(s.dma_start(out=dbg_t["d_xn"][ts(it, 128), :],
                                              in_=xnt[:, :]))
                    if layer == 1 and "d_xn2" in dbg_t:
                        s_st.dinc(s.dma_start(out=dbg_t["d_xn2"][ts(it, 128), :],
                                              in_=xnt[:, :]))
                stq = s_st.n

                if layer == 0:
                    @blk.vector
                    def _(v):
                        v.wait_ge(s_st.h, stq)
                        emit_fp4(v, D, xnt[:, 0:D])
                    vq = s_v.n

                    @blk.scalar
                    def _(s, it=it):
                        s.wait_ge(s_v.h, vq)
                        s_st.dinc(s.dma_start(out=q_dram[ts(it, 128), :],
                                              in_=q_out[:, 0:D]))
                        if "d_xq" in dbg_t:
                            s_st.dinc(s.dma_start(out=dbg_t["d_xq"][ts(it, 128), :],
                                                  in_=q_out[:, 0:D]))
                else:
                    @blk.vector
                    def _(v, it=it):
                        v.wait_ge(s_st.h, stq)

                        def st(inst):
                            s_v.inc(inst)
                            v.wait_ge(s_v.h, s_v.n)
                        c0 = 3 * LNG
                        st(v.reduce_max(out=stat[:, ts(c0, 1)], in_=xnt[:, :],
                                        axis=AX.X, apply_absolute_value=True))
                        st(v.tensor_scalar_add(out=stat[:, ts(c0, 1)],
                                               in0=stat[:, ts(c0, 1)], scalar1=1e-8))
                        st(v.reciprocal(out=stat[:, ts(c0 + 1, 1)],
                                        in_=stat[:, ts(c0, 1)]))
                        st(v.tensor_scalar_mul(out=stat[:, ts(c0 + 1, 1)],
                                               in0=stat[:, ts(c0 + 1, 1)], scalar1=127.0))
                        st(v.tensor_mul(out=asws[:, ts(it, 1)],
                                        in0=stat[:, ts(c0, 1)], in1=wl1s[:, 0:1]))
                        st(v.tensor_scalar_mul(out=asws[:, ts(it, 1)],
                                               in0=asws[:, ts(it, 1)], scalar1=1.0 / 127.0))
                        st(v.tensor_scalar(out=q_vI[:, 0:D].bitcast(F32), in0=xnt[:, :],
                                           scalar1=stat[:, ts(c0 + 1, 1)], scalar2=-128.0,
                                           op0=ALU.mult, op1=ALU.max))
                        st(v.tensor_scalar(out=q_vI[:, 0:D].bitcast(F32),
                                           in0=q_vI[:, 0:D].bitcast(F32),
                                           scalar1=127.0, scalar2=M24,
                                           op0=ALU.min, op1=ALU.add))
                        st(v.tensor_scalar(out=q_out[:, 0:D],
                                           in0=q_vI[:, 0:D].bitcast(F32),
                                           scalar1=M24, scalar2=None, op0=ALU.subtract))
                    vq = s_v.n

                    @blk.scalar
                    def _(s, it=it):
                        s.wait_ge(s_v.h, vq)
                        s_st.dinc(s.dma_start(out=q_dram[ts(it, 128), :],
                                              in_=q_out[:, 0:D]))
                        if "d_aq" in dbg_t:
                            s_st.dinc(s.dma_start(out=dbg_t["d_aq"][ts(it, 128), :],
                                                  in_=q_out[:, 0:D]))
                stq2 = s_st.n

                @blk.vector
                def _(v):
                    v.wait_ge(s_st.h, stq2)
                    v.sem_inc(s_v.h, 1)
                s_v.n += 1

            @blk.vector
            def _(v):
                v.wait_ge(s_st.h, s_st.n)
                v.sem_inc(s_v.h, 1)
            s_v.n += 1

    def phase234(sq):
        t0 = sq * T
        psl = [pw0[:, 0:512], pw0[:, 512:1024], pw1[:, 0:512], pw1[:, 512:1024]]
        NB5 = T // 512
        UprH = R4[:, 0:4 * T]

        @blk.sync
        def _(sync):
            sync.wait_ge(s_st.h, s_st.n)
            sync.wait_ge(s_v.h, s_v.n)
            sync.wait_ge(s_out.h, s_out.n)
            for c in range(NCH):
                s_tr.dinc(sync.dma_start(out=xnT[:, ts(c, T)],
                                         in_=xn_s[t0:t0 + T, ts(c, 128)],
                                         transpose=True))
            for k in range(NCH):
                s_lw.dinc(sync.dma_start(out=R4[:, ts(k, 1024)],
                                         in_=wv_bs[ts(k, 128), :]))
        trx = s_tr.n
        ldv = s_lw.n

        @blk.vector
        def _(v):
            v.wait_ge(s_st.h, s_st.n)
            va3 = vA[:, 0:NKT * NH * 65].rearrange("p (t x) -> p t x", x=65)
            v.memset(va3[:, :, 64:65], 1.0).then_inc(s_v.h, 1)
            v.memset(Rr[:, :], 1.0).then_inc(s_v.h, 1)
        s_v.n += 2
        vones = s_v.n

        tv_l = []
        base_vt = s_t.n
        for tt in range(NTS):
            s_t.n += 1
            tv_l.append(s_t.n)
        sv_base = s_s.n

        @blk.tensor
        def _(t, tv_l=tv_l):
            t.wait_ge(s_tr.h, trx)
            t.wait_ge(s_lw.h, ldv)
            for tt in range(NTS):
                if tt > 0:
                    t.wait_ge(s_s.h, sv_base + tt)
                for k in range(NCH):
                    sl = xnT[:, k * T + tt * 128: k * T + tt * 128 + 128]
                    t.matmul(pw1[:, 0:512], sl, R4[:, k * 1024:k * 1024 + 512],
                             start=(k == 0), stop=(k == NCH - 1))
                    mm = t.matmul(pw1[:, 512:1024], sl, R4[:, k * 1024 + 512:k * 1024 + 1024],
                                  start=(k == 0), stop=(k == NCH - 1))
                mm.then_inc(s_t.h, 1)

        @blk.scalar
        def _(s, tv_l=tv_l):
            s.wait_ge(s_v.h, vones)
            for tt in range(NTS):
                s.wait_ge(s_t.h, tv_l[tt])
                dst = vA[:, ts(tt, NH * 65)].rearrange("p (h x) -> p h x", x=65)[:, :, 0:64]
                s.copy(out=dst,
                       in_=pw1[:, 0:1024].rearrange("p (h x) -> p h x", x=64)).then_inc(s_s.h, 1)
        s_s.n += NTS

        if "d_v" in dbg_t and sq == 0:
            @blk.scalar
            def _(s):
                s.wait_ge(s_s.h, s_s.n)
                s_st.dinc(s.dma_start(out=dbg_t["d_v"][:, :], in_=vA[:, 0:NH * 65]))
        vdone_s = s_s.n

        for half in range(2):
            for ci in range(4):
                c = half * 4 + ci

                @blk.sync
                def _(sync, c=c):
                    sync.wait_ge(s_v.h, s_v.n)
                    sync.wait_ge(s_t.h, s_t.n)
                    for k in range(NCH):
                        s_lw.dinc(sync.dma_start(out=R4[:, 8192 + k * 128:8192 + k * 128 + 128],
                                                 in_=wq_bs[ts(k, 128), ts(c, 128)]))
                        s_lw.dinc(sync.dma_start(out=R4[:, 12288 + k * 128:12288 + k * 128 + 128],
                                                 in_=wk_bs[ts(k, 128), ts(c, 128)]))
                ldqk = s_lw.n

                for hf, woff, boff in ((0, 8192, 0), (1, 12288, NCH)):
                    @blk.tensor
                    def _(t, c=c, hf=hf, woff=woff):
                        t.wait_ge(s_lw.h, ldqk)
                        t.wait_ge(s_tr.h, trx)
                        t.wait_ge(s_st.h, s_st.n)
                        t.wait_ge(s_v.h, s_v.n)
                        for k in range(NCH):
                            for tb in range(NB5):
                                mm = t.matmul(psl[tb],
                                              R4[:, woff + k * 128:woff + k * 128 + 128],
                                              xnT[:, k * T + tb * 512:k * T + (tb + 1) * 512],
                                              start=(k == 0), stop=(k == NCH - 1))
                        mm.then_inc(s_t.h, 1)
                    s_t.n += 1
                    tqk = s_t.n

                    @blk.vector
                    def _(v, c=c, hf=hf, boff=boff):
                        v.wait_ge(s_t.h, tqk)
                        for tb in range(NB5):
                            v.tensor_scalar(out=qkbuf[:, hf * T + tb * 512: hf * T + (tb + 1) * 512],
                                            in0=psl[tb], scalar1=bqt[:, ts(boff + c, 1)],
                                            scalar2=None, op0=ALU.add).then_inc(s_v.h, 1)
                    s_v.n += NB5
                    sqk = s_v.n

                    @blk.tensor
                    def _(t, hf=hf):
                        t.wait_ge(s_v.h, sqk)
                        for tb in range(NB5):
                            t.matmul(p6[:, 0:512], spermt[:, :],
                                     qkbuf[:, hf * T + tb * 512: hf * T + (tb + 1) * 512],
                                     start=True, stop=True).then_inc(s_t.h, 1)
                            t.wait_ge(s_v.h, s_v.n + tb + 1)
                    s_t.n += NB5
                    trp = s_t.n

                    @blk.vector
                    def _(v, hf=hf):
                        for tb in range(NB5):
                            v.wait_ge(s_t.h, trp - NB5 + tb + 1)
                            v.tensor_copy(out=R4[:, 8192 + T + tb * 512: 8192 + T + (tb + 1) * 512],
                                          in_=p6[:, 0:512]).then_inc(s_v.h, 1)
                    s_v.n += NB5
                    srp = s_v.n

                    @blk.vector
                    def _(v, hf=hf):
                        v.wait_ge(s_v.h, srp)
                        hb = hf * T
                        v.tensor_mul(out=R4[:, 8192:8192 + T],
                                     in0=qkbuf[:, hb:hb + T], in1=CC[:, 0:T]).then_inc(s_v.h, 1)
                        v.wait_ge(s_v.h, s_v.n + 1)
                        v.tensor_mul(out=R4[:, 8192 + T:8192 + 2 * T],
                                     in0=R4[:, 8192 + T:8192 + 2 * T],
                                     in1=SSb[:, 0:T]).then_inc(s_v.h, 1)
                        v.wait_ge(s_v.h, s_v.n + 2)
                        v.tensor_add(out=qkbuf[:, hb:hb + T], in0=R4[:, 8192:8192 + T],
                                     in1=R4[:, 8192 + T:8192 + 2 * T]).then_inc(s_v.h, 1)
                    s_v.n += 3

                vrope = s_v.n
                if "d_qT" in dbg_t and sq == 0:
                    @blk.scalar
                    def _(s, c=c):
                        s.wait_ge(s_v.h, vrope)
                        s_st.dinc(s.dma_start(out=dbg_t["d_qT"][ts(c, 128), :],
                                              in_=qkbuf[:, 0:T]))
                        s_st.dinc(s.dma_start(out=dbg_t["d_kT"][ts(c, 128), :],
                                              in_=qkbuf[:, T:2 * T]))

                tsc_l = []
                sexp_l = []
                tav_l = []
                vust_l = []
                smv_l = []
                base_t = s_t.n
                base_s = s_s.n
                smv_prev = s_mv.n
                vust_prev = s_v.n
                for qb in range(NQB):
                    for kt in range(NKT):
                        s_t.n += 2
                        tsc_l.append(s_t.n)
                        s_s.n += 2
                        sexp_l.append(s_s.n)
                    s_t.n += 1
                    tav_l.append(s_t.n)
                    s_v.n += 2
                    vust_l.append(s_v.n)
                    s_mv.n += 64
                    smv_l.append(s_mv.n)

                @blk.tensor
                def _(t, c=c, vrope=vrope, tsc_l=tsc_l, sexp_l=sexp_l,
                      tav_l=tav_l, vust_l=vust_l, base_s=base_s):
                    t.wait_ge(s_s.h, vdone_s)
                    t.wait_ge(s_v.h, vrope)
                    for qb in range(NQB):
                        for kt in range(NKT):
                            i = qb * NKT + kt
                            pa = (p4, pw0)[kt % 2]
                            pb = (p5, pw1)[kt % 2]
                            if i >= 2:
                                t.wait_ge(s_s.h, sexp_l[i - 2])
                            t.matmul(pa[:, 0:QB],
                                     qkbuf[0:64, T + kt * 128:T + kt * 128 + 128],
                                     qkbuf[0:64, qb * QB:(qb + 1) * QB],
                                     start=True, stop=True,
                                     tile_position=(0, 0)).then_inc(s_t.h, 1)
                            t.matmul(pb[:, 0:QB],
                                     qkbuf[64:128, T + kt * 128:T + kt * 128 + 128],
                                     qkbuf[64:128, qb * QB:(qb + 1) * QB],
                                     start=True, stop=True,
                                     tile_position=(64, 0)).then_inc(s_t.h, 1)
                        t.wait_ge(s_s.h, sexp_l[qb * NKT + NKT - 1])
                        t.wait_ge(s_v.h, vust_l[qb - 1] if qb > 0 else vust_prev)
                        for head in range(2):
                            pav = p6 if head == 0 else p7
                            h = 2 * c + head
                            for kt in range(NKT):
                                mm = t.matmul(pav[0:65, 0:QB],
                                              vA[:, (kt * NH + h) * 65:(kt * NH + h) * 65 + 65],
                                              ET[:, ts(head * NKT + kt, QB)],
                                              start=(kt == 0), stop=(kt == NKT - 1))
                        mm.then_inc(s_t.h, 1)

                @blk.scalar
                def _(s, c=c, tsc_l=tsc_l, tav_l=tav_l):
                    for qb in range(NQB):
                        for kt in range(NKT):
                            i = qb * NKT + kt
                            pa = (p4, pw0)[kt % 2]
                            pb = (p5, pw1)[kt % 2]
                            s.wait_ge(s_t.h, tsc_l[i])
                            s.activation(out=ET[:, ts(kt, QB)], in_=pa[:, 0:QB],
                                         func=ACT.Exp, scale=C(0.125),
                                         bias=C(-8.0)).then_inc(s_s.h, 1)
                            s.activation(out=ET[:, ts(NKT + kt, QB)], in_=pb[:, 0:QB],
                                         func=ACT.Exp, scale=C(0.125),
                                         bias=C(-8.0)).then_inc(s_s.h, 1)

                @blk.vector
                def _(v, tav_l=tav_l, smv_l=smv_l, smv_prev=smv_prev):
                    for qb in range(NQB):
                        v.wait_ge(s_t.h, tav_l[qb])
                        v.wait_ge(s_mv.h, smv_l[qb - 1] if qb > 0 else smv_prev)
                        v.tensor_copy(out=R5[0:65, 3 * T:3 * T + QB],
                                      in_=p6[0:65, 0:QB]).then_inc(s_v.h, 1)
                        v.tensor_copy(out=R5[0:65, 3 * T + QB:3 * T + 2 * QB],
                                      in_=p7[0:65, 0:QB]).then_inc(s_v.h, 1)

                @blk.sync
                def _(sync, ci=ci, c=c, vust_l=vust_l):
                    for qb in range(NQB):
                        sync.wait_ge(s_v.h, vust_l[qb])
                        sync.dma_start(
                            out=UprH[0:64, ci * T + qb * QB: ci * T + qb * QB + QB],
                            in_=R5[0:64, 3 * T:3 * T + QB]).then_inc(s_mv.h, 16)
                        sync.dma_start(
                            out=UprH[64:128, ci * T + qb * QB: ci * T + qb * QB + QB],
                            in_=R5[0:64, 3 * T + QB:3 * T + 2 * QB]).then_inc(s_mv.h, 16)
                        sync.dma_start(out=Rr[2 * c:2 * c + 1, ts(qb, QB)],
                                       in_=R5[64:65, 3 * T:3 * T + QB]).then_inc(s_mv.h, 16)
                        sync.dma_start(out=Rr[2 * c + 1:2 * c + 2, ts(qb, QB)],
                                       in_=R5[64:65, 3 * T + QB:3 * T + 2 * QB]).then_inc(s_mv.h, 16)

            @blk.vector
            def _(v):
                v.wait_ge(s_mv.h, s_mv.n)
                with nc.allow_low_precision("1/r applied to bf16 matmul operands"):
                    v.reciprocal(out=Rrb[:, :], in_=Rr[:, :]).then_inc(s_v.h, 1)
            s_v.n += 1
            vr = s_v.n

            for ci in range(4):
                c = half * 4 + ci

                @blk.tensor
                def _(t, c=c):
                    t.wait_ge(s_v.h, vr)
                    t.wait_ge(s_v.h, s_v.n)
                    t.wait_ge(s_s.h, s_s.n)
                    for tb in range(T // 512):
                        t.matmul(p6[:, 0:512], e2t[:, ts(c, 128)],
                                 Rrb[:, tb * 512:(tb + 1) * 512],
                                 start=True, stop=True).then_inc(s_t.h, 1)
                        t.wait_ge(s_s.h, s_s.n + tb + 1)
                s_t.n += T // 512
                trb = s_t.n

                @blk.scalar
                def _(s, c=c):
                    for tb in range(T // 512):
                        s.wait_ge(s_t.h, trb - T // 512 + tb + 1)
                        s.copy(out=rbb[:, tb * 512:(tb + 1) * 512],
                               in_=p6[:, 0:512]).then_inc(s_s.h, 1)
                s_s.n += T // 512
                srb = s_s.n

                @blk.vector
                def _(v, ci=ci):
                    v.wait_ge(s_s.h, srb)
                    v.tensor_mul(out=UprH[:, ts(ci, T)], in0=UprH[:, ts(ci, T)],
                                 in1=rbb[:, 0:T]).then_inc(s_v.h, 1)
                s_v.n += 1
            unorm = s_v.n

            @blk.sync
            def _(sync, half=half):
                sync.wait_ge(s_v.h, unorm)
                for kc in range(4):
                    s_lw.dinc(sync.dma_start(out=R4[:, 8192 + kc * 1024:8192 + (kc + 1) * 1024],
                                             in_=wo_bs[ts(half * 4 + kc, 128), :]))
            ldo = s_lw.n

            for tt in range(NTS):
                @blk.sync
                def _(sync, tt=tt):
                    sync.wait_ge(s_v.h, s_v.n)
                    if half == 0:
                        s_lx.dinc(sync.dma_start(out=x1t[:, :],
                                                 in_=x_in[t0 + tt * 128:t0 + (tt + 1) * 128, :]))
                    else:
                        s_lx.dinc(sync.dma_start(out=x1t[:, :],
                                                 in_=x1_s[t0 + tt * 128:t0 + (tt + 1) * 128, :]))
                ldx = s_lx.n

                @blk.tensor
                def _(t, tt=tt, half=half):
                    t.wait_ge(s_lw.h, ldo)
                    if tt > 0:
                        t.wait_ge(s_v.h, s_v.n)
                    for kc in range(4):
                        t.matmul(pw0[:, 0:512],
                                 UprH[:, kc * T + tt * 128:kc * T + tt * 128 + 128],
                                 R4[:, 8192 + kc * 1024:8192 + kc * 1024 + 512],
                                 start=(kc == 0), stop=(half == 1 and kc == 3))
                        mm = t.matmul(pw0[:, 512:1024],
                                      UprH[:, kc * T + tt * 128:kc * T + tt * 128 + 128],
                                      R4[:, 8192 + kc * 1024 + 512:8192 + (kc + 1) * 1024],
                                      start=(kc == 0), stop=(half == 1 and kc == 3))
                    if half == 0:
                        t.matmul(pw0[:, 0:512], onesrow[0:1, :], biasrow[0:1, 0:512],
                                 start=False, stop=True)
                        mm = t.matmul(pw0[:, 512:1024], onesrow[0:1, :],
                                      biasrow[0:1, 512:1024], start=False, stop=True)
                    mm.then_inc(s_t.h, 1)
                s_t.n += 1
                two = s_t.n

                @blk.vector
                def _(v, tt=tt):
                    v.wait_ge(s_t.h, two)
                    v.wait_ge(s_lx.h, ldx)
                    v.tensor_add(out=x1t[:, :], in0=x1t[:, :],
                                 in1=pw0[:, 0:1024]).then_inc(s_v.h, 1)
                s_v.n += 1
                vx1 = s_v.n

                @blk.scalar
                def _(s, tt=tt, half=half):
                    s.wait_ge(s_v.h, vx1)
                    s_st.dinc(s.dma_start(out=x1_s[t0 + tt * 128:t0 + (tt + 1) * 128, :],
                                          in_=x1t[:, :]))
                    if "d_att" in dbg_t and half == 1:
                        s_st.dinc(s.dma_start(
                            out=dbg_t["d_att"][t0 + tt * 128:t0 + (tt + 1) * 128, :],
                            in_=x1t[:, :]))
                stx = s_st.n

                @blk.vector
                def _(v):
                    v.wait_ge(s_st.h, stx)
                    v.sem_inc(s_v.h, 1)
                s_v.n += 1

    def phase5():
        for wsrc, wdst, rows, wid in ((w_gate, wgq_s, FFI, D), (w_up, wuq_s, FFI, D),
                                      (w_down, wdq_s, D, FFI)):
            for rt in range(rows // 128):
                @blk.sync
                def _(sync, wsrc=wsrc, rt=rt, wid=wid):
                    sync.wait_ge(s_v.h, s_v.n)
                    s_lw.dinc(sync.dma_start(out=q_hi[:, 0:wid],
                                             in_=wsrc[ts(rt, 128), :]))
                ldw_ = s_lw.n

                @blk.vector
                def _(v, wid=wid):
                    v.wait_ge(s_lw.h, ldw_)
                    emit_fp4(v, wid, q_hi[:, 0:wid])
                vq = s_v.n

                @blk.scalar
                def _(s, wdst=wdst, rt=rt, wid=wid, wsrc=wsrc):
                    s.wait_ge(s_v.h, vq)
                    s_st.dinc(s.dma_start(out=wdst[ts(rt, 128), :], in_=q_out[:, 0:wid]))
                    if wsrc is w_gate and "d_wgq" in dbg_t:
                        s_st.dinc(s.dma_start(out=dbg_t["d_wgq"][ts(rt, 128), :],
                                              in_=q_out[:, 0:wid]))
                stw = s_st.n

                @blk.vector
                def _(v):
                    v.wait_ge(s_st.h, stw)
                    v.sem_inc(s_v.h, 1)
                s_v.n += 1

    def phase6():
        @blk.sync
        def _(sync):
            sync.wait_ge(s_bar.h, s_bar.n)
            for c in range(NCH):
                dstbuf = xqT_lo if c < 4 else xqT_hi
                s_tr.dinc(sync.dma_start(out=dstbuf[:, (c % 4) * N:(c % 4) * N + N],
                                         in_=xq_s[0:N, ts(c, 128)], transpose=True))
        trq = s_tr.n

        NPS = FFI // 1024 + (1 if FFI % 1024 else 0)
        for np_ in range(NPS):
            n0 = np_ * 1024
            nw = min(1024, FFI - n0)

            @blk.sync
            def _(sync, n0=n0, nw=nw):
                sync.wait_ge(s_t.h, s_t.n)
                sync.wait_ge(s_v.h, s_v.n)
                for k in range(NCH):
                    s_lw.dinc(sync.dma_start(out=wstr[:, k * 2048:k * 2048 + nw],
                                             in_=wgq_s[n0:n0 + nw, ts(k, 128)],
                                             transpose=True))
                    s_lw.dinc(sync.dma_start(out=wstr[:, k * 2048 + 1024:k * 2048 + 1024 + nw],
                                             in_=wuq_s[n0:n0 + nw, ts(k, 128)],
                                             transpose=True))
            ldgu = s_lw.n

            tgu_l = []
            sgs_l = []
            vh_l = []
            sth_l = []
            base6s = s_st.n
            for it in range(NT):
                s_t.n += 1
                tgu_l.append(s_t.n)
                s_s.n += 1
                sgs_l.append(s_s.n)
                s_v.n += 1
                vh_l.append(s_v.n)
                s_st.n += 16
                sth_l.append(s_st.n)

            @blk.tensor
            def _(t, n0=n0, nw=nw, tgu_l=tgu_l, vh_l=vh_l):
                t.wait_ge(s_lw.h, ldgu)
                t.wait_ge(s_tr.h, trq)
                for it in range(NT):
                    if it > 0:
                        t.wait_ge(s_v.h, vh_l[it - 1])
                    for k in range(NCH):
                        src = xqT_lo if k < 4 else xqT_hi
                        sl = src[:, (k % 4) * N + it * 128:(k % 4) * N + it * 128 + 128]
                        for nb in range(0, nw, 512):
                            ww = min(512, nw - nb)
                            t.matmul(pw0[:, nb:nb + ww], sl,
                                     wstr[:, k * 2048 + nb:k * 2048 + nb + ww],
                                     start=(k == 0), stop=(k == NCH - 1))
                            mm = t.matmul(pw1[:, nb:nb + ww], sl,
                                          wstr[:, k * 2048 + 1024 + nb:k * 2048 + 1024 + nb + ww],
                                          start=(k == 0), stop=(k == NCH - 1))
                    mm.then_inc(s_t.h, 1)

            @blk.scalar
            def _(s, n0=n0, nw=nw, tgu_l=tgu_l, vh_l=vh_l):
                for it in range(NT):
                    s.wait_ge(s_t.h, tgu_l[it])
                    s.activation(out=gsil_v[:, 0:nw], in_=pw0[:, 0:nw],
                                 func=ACT.Silu).then_inc(s_s.h, 1)
                    if it > 0:
                        s.wait_ge(s_v.h, vh_l[it - 1])
                        s.dma_start(out=h_s[ts(it - 1, 128), n0:n0 + nw],
                                    in_=junk[:, 0:nw]).then_inc(s_st.h, 16)
                s.wait_ge(s_v.h, vh_l[NT - 1])
                s.dma_start(out=h_s[ts(NT - 1, 128), n0:n0 + nw],
                            in_=junk[:, 0:nw]).then_inc(s_st.h, 16)

            @blk.vector
            def _(v, nw=nw, sgs_l=sgs_l, sth_l=sth_l, base6s=base6s):
                for it in range(NT):
                    v.wait_ge(s_s.h, sgs_l[it])
                    v.wait_ge(s_st.h, sth_l[it - 1] if it > 0 else base6s)
                    v.tensor_mul(out=junk[:, 0:nw], in0=gsil_v[:, 0:nw],
                                 in1=pw1[:, 0:nw]).then_inc(s_v.h, 1)

        for it in range(NT):
            @blk.sync
            def _(sync, it=it):
                sync.wait_ge(s_v.h, s_v.n)
                sync.wait_ge(s_st.h, s_st.n)
                s_lx.dinc(sync.dma_start(out=q_hi[:, 0:FFI], in_=h_s[ts(it, 128), :]))
            ldh = s_lx.n

            @blk.vector
            def _(v):
                v.wait_ge(s_lx.h, ldh)
                emit_fp4(v, FFI, q_hi[:, 0:FFI])
            vhq = s_v.n

            @blk.scalar
            def _(s, it=it):
                s.wait_ge(s_v.h, vhq)
                s_st.dinc(s.dma_start(out=hq_s[ts(it, 128), :], in_=q_out[:, 0:FFI]))
                if "d_hq" in dbg_t:
                    s_st.dinc(s.dma_start(out=dbg_t["d_hq"][ts(it, 128), :],
                                          in_=q_out[:, 0:FFI]))
            sthq = s_st.n

            @blk.vector
            def _(v):
                v.wait_ge(s_st.h, sthq)
                v.sem_inc(s_v.h, 1)
            s_v.n += 1

        @blk.sync
        def _(sync):
            sync.wait_ge(s_v.h, s_v.n)
            sync.wait_ge(s_t.h, s_t.n)
            for kc in range(FFI // 128):
                if kc < 16:
                    dst = wdT_a[:, kc * 1024:(kc + 1) * 1024]
                else:
                    dst = wdT_b[:, (kc - 16) * 1024:(kc - 15) * 1024]
                s_tr.dinc(sync.dma_start(out=dst, in_=wdq_s[0:D, ts(kc, 128)],
                                         transpose=True))
        trd = s_tr.n

        NB4 = NT // 4
        for b4 in range(NB4):
            @blk.sync
            def _(sync, b4=b4):
                sync.wait_ge(s_t.h, s_t.n)
                for kc in range(FFI // 128):
                    s_th.dinc(sync.dma_start(out=hqTb[:, kc * 512:(kc + 1) * 512],
                                             in_=hq_s[b4 * 512:(b4 + 1) * 512, ts(kc, 128)],
                                             transpose=True))
            trh = s_th.n

            for i4 in range(4):
                it = b4 * 4 + i4

                @blk.sync
                def _(sync, it=it):
                    sync.wait_ge(s_v.h, s_v.n)
                    s_lx.dinc(sync.dma_start(out=x1t[:, :], in_=x1_s[ts(it, 128), :]))
                ldx1 = s_lx.n

                @blk.tensor
                def _(t, i4=i4):
                    t.wait_ge(s_th.h, trh)
                    t.wait_ge(s_tr.h, trd)
                    if True:
                        t.wait_ge(s_v.h, s_v.n)
                    for kc in range(FFI // 128):
                        if kc < 16:
                            wsl = wdT_a[:, kc * 1024:(kc + 1) * 1024]
                        else:
                            wsl = wdT_b[:, (kc - 16) * 1024:(kc - 15) * 1024]
                        hsl = hqTb[:, kc * 512 + i4 * 128:kc * 512 + i4 * 128 + 128]
                        t.matmul(pw0[:, 0:512], hsl, wsl[:, 0:512],
                                 start=(kc == 0), stop=(kc == FFI // 128 - 1))
                        mm = t.matmul(pw0[:, 512:1024], hsl, wsl[:, 512:1024],
                                      start=(kc == 0), stop=(kc == FFI // 128 - 1))
                    mm.then_inc(s_t.h, 1)
                s_t.n += 1
                tdn = s_t.n

                @blk.vector
                def _(v, it=it):
                    v.wait_ge(s_t.h, tdn)
                    v.wait_ge(s_lx.h, ldx1)
                    v.tensor_add(out=x1t[:, :], in0=x1t[:, :],
                                 in1=pw0[:, 0:1024]).then_inc(s_v.h, 1)
                s_v.n += 1
                vf = s_v.n

                @blk.scalar
                def _(s, it=it):
                    s.wait_ge(s_v.h, vf)
                    s_st.dinc(s.dma_start(out=x1_s[ts(it, 128), :], in_=x1t[:, :]))
                stf = s_st.n

                @blk.vector
                def _(v):
                    v.wait_ge(s_st.h, stf)
                    v.sem_inc(s_v.h, 1)
                s_v.n += 1

    def phase8a():
        for wsrc, wdst, rows in ((W_in, wi_s, D), (W_attn, wa_s, 656), (W_out, wo2_s, D)):
            nrt = (rows + 127) // 128
            for rt in range(nrt):
                rr = min(128, rows - rt * 128)

                @blk.sync
                def _(sync, wsrc=wsrc, rt=rt, rr=rr):
                    sync.wait_ge(s_v.h, s_v.n)
                    s_lw.dinc(sync.dma_start(out=wfp[0:rr, :],
                                             in_=wsrc[rt * 128:rt * 128 + rr, :]))
                ldr = s_lw.n

                @blk.vector
                def _(v, rr=rr):
                    v.wait_ge(s_lw.h, ldr)
                    v.tensor_copy(out=wbuf[0:rr, 0:1024], in_=wfp[0:rr, :]).then_inc(s_v.h, 1)
                s_v.n += 1
                vcp = s_v.n

                @blk.scalar
                def _(s, wdst=wdst, rt=rt, rr=rr):
                    s.wait_ge(s_v.h, vcp)
                    s_st.dinc(s.dma_start(out=wdst[rt * 128:rt * 128 + rr, :],
                                          in_=wbuf[0:rr, 0:1024]))
                str_ = s_st.n

                @blk.vector
                def _(v):
                    v.wait_ge(s_st.h, str_)
                    v.sem_inc(s_v.h, 1)
                s_v.n += 1

        for rt in range(NCH):
            @blk.sync
            def _(sync, rt=rt):
                sync.wait_ge(s_v.h, s_v.n)
                s_lw.dinc(sync.dma_start(out=wfp[:, :], in_=W_l1[ts(rt, 128), :]))
            ldr = s_lw.n

            @blk.vector
            def _(v, rt=rt):
                v.wait_ge(s_lw.h, ldr)
                v.reduce_sum(out=stat[:, ts(rt, 1)], in_=wfp[:, :], axis=AX.X,
                             apply_absolute_value=True).then_inc(s_v.h, 1)
                v.wait_ge(s_v.h, s_v.n + 1)
                v.sem_inc(s_v.h, 1)
            s_v.n += 2
        vws = s_v.n

        @blk.vector
        def _(v):
            v.wait_ge(s_v.h, vws)
            v.reduce_sum(out=stat[:, NCH:NCH + 1], in_=stat[:, 0:NCH],
                         axis=AX.X).then_inc(s_v.h, 1)
        s_v.n += 1
        vws2 = s_v.n

        @blk.tensor
        def _(t):
            t.wait_ge(s_v.h, vws2)
            t.matmul(p4[0:1, 0:1], C(1.0), stat[:, NCH:NCH + 1],
                     start=True, stop=True).then_inc(s_t.h, 1)
        s_t.n += 1
        tws = s_t.n

        @blk.vector
        def _(v):
            v.wait_ge(s_t.h, tws)
            v.tensor_scalar(out=stat[0:1, 0:1], in0=p4[0:1, 0:1],
                            scalar1=1.0 / (D * D), scalar2=1e-8,
                            op0=ALU.mult, op1=ALU.add).then_inc(s_v.h, 1)
            v.wait_ge(s_v.h, s_v.n + 1)
            v.reciprocal(out=stat[0:1, 1:2], in_=stat[0:1, 0:1]).then_inc(s_v.h, 1)
        s_v.n += 2
        vws3 = s_v.n

        @blk.scalar
        def _(s):
            s.wait_ge(s_v.h, vws3)
            s_st.dinc(s.dma_start(out=scal_s[0:2], in_=stat[0:1, 0:2]))
            if "d_ws" in dbg_t:
                s_st.dinc(s.dma_start(out=dbg_t["d_ws"][0:1, :], in_=stat[0:1, 0:2]))
        stws = s_st.n

        @blk.sync
        def _(sync):
            sync.wait_ge(s_st.h, stws)
            s_lx.dinc(sync.dma_start(out=wl1s[:, 0:1],
                                     in_=scal_s[0:1].unsqueeze(0).broadcast_to([128, 1])))
            s_lx.dinc(sync.dma_start(out=wl1r[:, 0:1],
                                     in_=scal_s[1:2].unsqueeze(0).broadcast_to([128, 1])))
        ldws = s_lx.n

        for rt in range(NCH):
            @blk.sync
            def _(sync, rt=rt):
                sync.wait_ge(s_v.h, s_v.n)
                s_lw.dinc(sync.dma_start(out=wfp[:, :], in_=W_l1[ts(rt, 128), :]))
            ldr = s_lw.n

            @blk.vector
            def _(v):
                v.wait_ge(s_lw.h, ldr)
                v.wait_ge(s_lx.h, ldws)

                def st(inst):
                    s_v.inc(inst)
                    v.wait_ge(s_v.h, s_v.n)
                st(v.tensor_scalar(out=q_vI[:, 0:D].bitcast(F32), in0=wfp[:, :],
                                   scalar1=wl1r[:, 0:1], scalar2=M24,
                                   op0=ALU.mult, op1=ALU.add))
                st(v.tensor_scalar(out=q_vI[:, 0:D].bitcast(F32),
                                   in0=q_vI[:, 0:D].bitcast(F32),
                                   scalar1=M24, scalar2=-1.0,
                                   op0=ALU.subtract, op1=ALU.max))
                st(v.tensor_scalar(out=q_out[:, 0:D], in0=q_vI[:, 0:D].bitcast(F32),
                                   scalar1=1.0, scalar2=None, op0=ALU.min))
            vqw = s_v.n

            @blk.scalar
            def _(s, rt=rt):
                s.wait_ge(s_v.h, vqw)
                s_st.dinc(s.dma_start(out=wl1q_s[ts(rt, 128), :], in_=q_out[:, 0:D]))
                if "d_wl1q" in dbg_t:
                    s_st.dinc(s.dma_start(out=dbg_t["d_wl1q"][ts(rt, 128), :],
                                          in_=q_out[:, 0:D]))
            stw = s_st.n

            @blk.vector
            def _(v):
                v.wait_ge(s_st.h, stw)
                v.sem_inc(s_v.h, 1)
            s_v.n += 1

    def phase8(sq):
        t0 = sq * T
        prs = pr_flat.rearrange("(a) -> a")

        @blk.sync
        def _(sync):
            sync.wait_ge(s_bar.h, s_bar.n)
            sync.wait_ge(s_out.h, s_out.n)
            sync.wait_ge(s_v.h, s_v.n)
            sync.wait_ge(s_st.h, s_st.n)
            s_st.dinc(sync.dma_start(out=u_s[sq, 0:D, 0:64],
                                     in_=zpad[:, 0:512]))
            s_st.dinc(sync.dma_start(out=u_s[sq, 0:D, T + 64:T + 128],
                                     in_=zpad[:, 0:512]))
            for c in range(NCH):
                s_tr.dinc(sync.dma_start(out=xn2T[:, ts(c, T)],
                                         in_=xn2_s[t0:t0 + T, ts(c, 128)], transpose=True))
            for k in range(NCH):
                s_tr.dinc(sync.dma_start(out=wiT[:, ts(k, 1024)],
                                         in_=wi_s[0:D, ts(k, 128)], transpose=True))
        trwi = s_tr.n

        psl8 = [pw0[:, 0:512], pw0[:, 512:1024], pw1[:, 0:512], pw1[:, 512:1024]]
        for e in range(NCH):
            @blk.tensor
            def _(t, e=e):
                t.wait_ge(s_tr.h, trwi)
                if e > 0:
                    t.wait_ge(s_s.h, s_s.n)
                for k in range(NCH):
                    for tb in range(T // 512):
                        mm = t.matmul(psl8[tb],
                                      wiT[:, k * 1024 + e * 128:k * 1024 + e * 128 + 128],
                                      xn2T[:, k * T + tb * 512:k * T + (tb + 1) * 512],
                                      start=(k == 0), stop=(k == NCH - 1))
                mm.then_inc(s_t.h, 1)
            s_t.n += 1
            twi = s_t.n

            @blk.scalar
            def _(s, e=e):
                s.wait_ge(s_t.h, twi)
                for tb in range(T // 512):
                    s.copy(out=uT[:, e * T + tb * 512:e * T + (tb + 1) * 512],
                           in_=psl8[tb]).then_inc(s_s.h, 1)
                    s.activation(out=ruT[:, e * T + tb * 512:e * T + (tb + 1) * 512],
                                 in_=psl8[tb], func=ACT.Relu).then_inc(s_s.h, 1)
            s_s.n += 2 * (T // 512)
            swi = s_s.n

            @blk.scalar
            def _(s, e=e):
                s.wait_ge(s_s.h, swi)
                s_st.dinc(s.dma_start(out=u_s[sq, ts(e, 128), 64:64 + T], in_=uT[:, ts(e, T)]))
                if "d_u" in dbg_t and sq == 0:
                    s_st.dinc(s.dma_start(out=dbg_t["d_u"][ts(e, 128), :],
                                          in_=uT[:, ts(e, T)]))
        uwdone = s_st.n

        @blk.sync
        def _(sync):
            sync.wait_ge(s_st.h, uwdone)
            for k in range(NCH):
                s_tr.dinc(sync.dma_start(out=waT[:, ts(k, 656)],
                                         in_=wa_s[0:656, ts(k, 128)], transpose=True))
        trwa = s_tr.n

        for tt in range(NTS):
            @blk.tensor
            def _(t, tt=tt):
                t.wait_ge(s_tr.h, trwa)
                if tt > 0:
                    t.wait_ge(s_st.h, s_st.n)
                for k in range(NCH):
                    sl = ruT[:, k * T + tt * 128:k * T + tt * 128 + 128]
                    t.matmul(pw0[:, 0:512], sl, waT[:, k * 656:k * 656 + 512],
                             start=(k == 0), stop=(k == NCH - 1))
                    mm = t.matmul(pw0[:, 512:656], sl, waT[:, k * 656 + 512:k * 656 + 656],
                                  start=(k == 0), stop=(k == NCH - 1))
                mm.then_inc(s_t.h, 1)
            s_t.n += 1
            tal = s_t.n

            @blk.vector
            def _(v, tt=tt):
                if tt == 0:
                    v.wait_ge(s_st.h, uwdone)
                    v.memset(PW[:, :], 0.0).then_inc(s_v.h, 1)
                else:
                    v.sem_inc(s_v.h, 1)
            s_v.n += 1
            vpw0_ = s_v.n

            @blk.scalar
            def _(s, tt=tt):
                s.wait_ge(s_t.h, tal)
                s.wait_ge(s_v.h, vpw0_)
                for sc_i in range(2):
                    cw = CWS[sc_i]
                    srcap = pw0[:, (0 if sc_i == 0 else 328):(168 if sc_i == 0 else 656)]
                    dst = PW[:, PRB[sc_i]:PRB[sc_i] + 8 * SLOTP[sc_i]]
                    s.activation(out=dst.rearrange("p (h w) -> p h w", w=SLOTP[sc_i])[:, :, 0:cw],
                                 in_=srcap.rearrange("p (h w) -> p h w", w=cw),
                                 func=ACT.Exp).then_inc(s_s.h, 1)
            s_s.n += 2
            spw = s_s.n

            @blk.vector
            def _(v, tt=tt):
                v.wait_ge(s_s.h, spw)

                def st(inst):
                    s_v.inc(inst)
                    v.wait_ge(s_v.h, s_v.n)
                for sc_i in range(2):
                    cw = CWS[sc_i]
                    dst = PW[:, PRB[sc_i]:PRB[sc_i] + 8 * SLOTP[sc_i]]
                    view = dst.rearrange("p (h w) -> p h w", w=SLOTP[sc_i])[:, :, 0:cw]
                    st(v.reduce_sum(out=prz[:, ts(sc_i, 8)].unsqueeze(2),
                                    in_=view, axis=AX.X))
                st(v.reciprocal(out=prz[:, 0:16], in_=prz[:, 0:16]))
                st(v.tensor_scalar(out=prz[:, 0:8], in0=prz[:, 0:8],
                                   scalar1=tiny[:, 8:9], scalar2=None, op0=ALU.mult))
                st(v.tensor_scalar(out=prz[:, 8:16], in0=prz[:, 8:16],
                                   scalar1=tiny[:, 9:10], scalar2=None, op0=ALU.mult))
                for sc_i in range(2):
                    cw = CWS[sc_i]
                    dst = PW[:, PRB[sc_i]:PRB[sc_i] + 8 * SLOTP[sc_i]]
                    view = dst.rearrange("p (h w) -> p h w", w=SLOTP[sc_i])[:, :, 0:cw]
                    st(v.tensor_mul(out=view, in0=view,
                                    in1=prz[:, ts(sc_i, 8)].unsqueeze(2).broadcast_to([128, 8, cw])))
            vpw = s_v.n

            @blk.scalar
            def _(s, tt=tt):
                s.wait_ge(s_v.h, vpw)
                base = (sq * T + tt * 128) * PRP
                dst = prs[base:base + 128 * PRP].rearrange("(p w) -> p w", w=PRP)
                s_st.dinc(s.dma_start(out=dst, in_=PW[:, :]))

        aldone = s_st.n

        @blk.sync
        def _(sync):
            sync.wait_ge(s_st.h, aldone)
            sync.wait_ge(s_t.h, s_t.n)
            for kt in range(NKTP):
                s_tr.dinc(sync.dma_start(out=upad[:, ts(kt, 1024)],
                                         in_=u_s[sq, 0:D, kt * 128:kt * 128 + 128],
                                         transpose=True))
        trup = s_tr.n

        @blk.vector
        def _(v):
            v.wait_ge(s_st.h, aldone)
            v.sem_inc(s_v.h, 1)
        s_v.n += 1
        vup = s_v.n

        phase8_window(sq, vup)

    def phase8_window(sq, vup):
        prs = pr_flat
        for c in range(NCH):
            for tg in range(NTS // 4):
                @blk.tensor
                def _(t, c=c, tg=tg):
                    t.wait_ge(s_v.h, vup)
                    t.wait_ge(s_tr.h, s_tr.n)
                    if not (c == 0 and tg == 0):
                        t.wait_ge(s_s.h, s_s.n)
                    t.sem_inc(s_t.h, 1)
                s_t.n += 1
                gat = s_t.n

                for i4 in range(4):
                    tt = tg * 4 + i4

                    @blk.sync
                    def _(sync, c=c, tt=tt, i4=i4, gat=gat):
                        sync.wait_ge(s_t.h, gat + i4)
                        for sc_i in range(2):
                            for k2 in range(2):
                                ktp = tt + k2
                                base = ((sq * T + tt * 128) * PRP
                                        + PRB[sc_i] + c * SLOTP[sc_i] + PADS[sc_i]
                                        + (ktp - tt) * 128 - UPAD)
                                ap = bass.AP(prs.tensor, base,
                                             [[PRP - 1, 128], [1, 128]])
                                s_tr.dinc(sync.dma_start(
                                    out=Btile[:, ts(sc_i * 2 + k2, 128)],
                                    in_=ap, transpose=True))
                    trbt = s_tr.n

                    @blk.tensor
                    def _(t, c=c, tt=tt, i4=i4):
                        t.wait_ge(s_tr.h, trbt)
                        first = True
                        for sc_i in range(2):
                            for k2 in range(2):
                                ktp = tt + k2
                                mm = t.matmul(p6[:, ts(i4, 128)],
                                              upad[:, ktp * 1024 + c * 128:ktp * 1024 + c * 128 + 128],
                                              Btile[:, ts(sc_i * 2 + k2, 128)],
                                              start=first, stop=(sc_i == 1 and k2 == 1))
                                first = False
                        mm.then_inc(s_t.h, 1)
                    s_t.n += 1
                tw4 = s_t.n

                @blk.scalar
                def _(s, c=c, tg=tg):
                    s.wait_ge(s_t.h, tw4)
                    s.copy(out=owT[:, c * T + tg * 512:c * T + (tg + 1) * 512],
                           in_=p6[:, 0:512]).then_inc(s_s.h, 1)
                s_s.n += 1

        if "d_ow" in dbg_t and sq == 0:
            @blk.scalar
            def _(s):
                s.wait_ge(s_s.h, s_s.n)
                for c in range(NCH):
                    s_st.dinc(s.dma_start(out=dbg_t["d_ow"][ts(c, 128), :],
                                          in_=owT[:, ts(c, T)]))

    def phase9(sq):
        t0 = sq * T

        @blk.sync
        def _(sync):
            sync.wait_ge(s_s.h, s_s.n)
            for k in range(NCH):
                s_tr.dinc(sync.dma_start(out=woutT[:, ts(k, 1024)],
                                         in_=wo2_s[0:D, ts(k, 128)], transpose=True))
                s_tr.dinc(sync.dma_start(out=wl1qT[:, ts(k, 1024)],
                                         in_=wl1q_s[0:D, ts(k, 128)], transpose=True))
                s_tr.dinc(sync.dma_start(out=aqT[:, ts(k, T)],
                                         in_=aq_s[t0:t0 + T, ts(k, 128)], transpose=True))
        tr9 = s_tr.n

        for tt in range(NTS):
            it = sq * NTS + tt

            @blk.sync
            def _(sync, it=it):
                sync.wait_ge(s_v.h, s_v.n)
                s_lx.dinc(sync.dma_start(out=x1t[:, :], in_=x1_s[ts(it, 128), :]))
            ldx1 = s_lx.n

            @blk.tensor
            def _(t, tt=tt):
                t.wait_ge(s_tr.h, tr9)
                if tt > 0:
                    t.wait_ge(s_v.h, s_v.n)
                for k in range(NCH):
                    osl = owT[:, k * T + tt * 128:k * T + tt * 128 + 128]
                    t.matmul(pw0[:, 0:512], osl, woutT[:, k * 1024:k * 1024 + 512],
                             start=(k == 0), stop=(k == NCH - 1))
                    t.matmul(pw0[:, 512:1024], osl, woutT[:, k * 1024 + 512:k * 1024 + 1024],
                             start=(k == 0), stop=(k == NCH - 1))
                for k in range(NCH):
                    asl = aqT[:, k * T + tt * 128:k * T + tt * 128 + 128]
                    t.matmul(pw1[:, 0:512], asl, wl1qT[:, k * 1024:k * 1024 + 512],
                             start=(k == 0), stop=(k == NCH - 1))
                    mm = t.matmul(pw1[:, 512:1024], asl, wl1qT[:, k * 1024 + 512:k * 1024 + 1024],
                                  start=(k == 0), stop=(k == NCH - 1))
                mm.then_inc(s_t.h, 1)
            s_t.n += 1
            t9 = s_t.n

            @blk.vector
            def _(v, it=it):
                v.wait_ge(s_t.h, t9)
                v.wait_ge(s_lx.h, ldx1)
                v.scalar_tensor_tensor(out=junk[:, 0:D], in0=pw1[:, 0:1024],
                                       scalar=asws[:, ts(it, 1)], in1=x1t[:, :],
                                       op0=ALU.mult, op1=ALU.add).then_inc(s_v.h, 1)
                v.wait_ge(s_v.h, s_v.n + 1)
                v.tensor_add(out=junk[:, 0:D], in0=junk[:, 0:D],
                             in1=pw0[:, 0:1024]).then_inc(s_v.h, 1)
            s_v.n += 2
            v9 = s_v.n

            @blk.scalar
            def _(s, it=it):
                s.wait_ge(s_v.h, v9)
                s_out.dinc(s.dma_start(out=out_t[ts(it, 128), :], in_=junk[:, 0:D]))
            o9 = s_out.n

            @blk.vector
            def _(v):
                v.wait_ge(s_out.h, o9)
                v.sem_inc(s_v.h, 1)
            s_v.n += 1

    phase0a()
    phase0b()
    if stop >= 1:
        phase_ln(x_in, xn_s, xq_s, layer=0)
    barrier()
    if stop >= 2:
        for sq in range(NSEQ):
            phase234(sq)
        barrier()
    if stop >= 5:
        phase5()
        phase6()
        barrier()
    if stop >= 7:
        phase8a()
        phase_ln(x1_s, xn2_s, aq_s, layer=1)
        barrier()
    if stop >= 8:
        for sq in range(NSEQ):
            phase8(sq)
            phase9(sq)

    if stop < 8:
        @blk.vector
        def _(v):
            v.wait_ge(s_bar.h, s_bar.n)
            v.memset(junk[:, 0:D], 0.0).then_inc(s_v.h, 1)
        s_v.n += 1
        zv = s_v.n

        @blk.scalar
        def _(s):
            s.wait_ge(s_v.h, zv)
            for it in range(NT):
                s_out.dinc(s.dma_start(out=out_t[ts(it, 128), :], in_=junk[:, 0:D]))

    @blk.gpsimd
    def _(g):
        g.wait_ge(s_out.h, s_out.n)
        g.wait_ge(s_st.h, s_st.n)
        g.wait_ge(s_mv.h, s_mv.n)

    es.close()
    return nc


def _host_consts(T):
    iota = np.broadcast_to(np.arange(T, dtype=np.float32), (128, T)).copy()
    rope_e = np.zeros((128, 1), np.float32)
    rope_sm = np.zeros((128, 1), np.float32)
    for p in range(128):
        r = p % 64
        if r >= 32:
            j = r - 32
            sgn = -1.0
            if j >= 16:
                j -= 16
                sgn = 1.0
            rope_e[p, 0] = -j / 16.0
            rope_sm[p, 0] = sgn * 2.0 * np.pi / max(T - 1, 1)
    sperm = np.zeros((128, 128), np.float32)
    for m in range(128):
        r = m % 64
        if 32 <= r < 48:
            k = m + 16
        elif 48 <= r < 64:
            k = m - 16
        else:
            k = m
        sperm[k, m] = 1.0
    e2 = np.zeros((16, 1024), np.float32)
    for c in range(8):
        for m in range(128):
            e2[2 * c + (1 if m >= 64 else 0), c * 128 + m] = 1.0
    return {
        "iota_t": iota,
        "rope_e": rope_e,
        "rope_sm": rope_sm,
        "sperm": sperm.astype(ml_dtypes.bfloat16),
        "e2sel": e2.astype(ml_dtypes.bfloat16),
    }


def make_in_maps(inputs, cfg, n_cores=8):
    T, NSEQ = cfg["T"], cfg["NSEQ"]
    consts = _host_consts(T)
    x = np.ascontiguousarray(np.asarray(inputs["L0_attn_x"]), dtype=np.float32)
    B = x.shape[0]
    shared = {
        "Wq": inputs["L0_attn_Wq"], "Wk": inputs["L0_attn_Wk"],
        "Wv": inputs["L0_attn_Wv"], "Wo": inputs["L0_attn_Wo"],
        "bq": inputs["L0_attn_bq"], "bk": inputs["L0_attn_bk"],
        "bv": inputs["L0_attn_bv"], "bo": inputs["L0_attn_bo"],
        "nrope": inputs["L0_attn_nrope_freq"],
        "w_gate": inputs["L0_ffn_w_gate"], "w_up": inputs["L0_ffn_w_up"],
        "w_down": inputs["L0_ffn_w_down"],
        "W_in": inputs["L1_attn_W_in"], "W_attn": inputs["L1_attn_W_attn"],
        "W_out": inputs["L1_attn_W_out"], "sw": inputs["L1_attn_scale_weights"],
        "W_l1": inputs["L1_ffn_weight"],
    }
    shared = {k: np.ascontiguousarray(np.asarray(v), dtype=np.float32)
              for k, v in shared.items()}
    shared.update(consts)
    per_core = B // n_cores
    maps = []
    for c in range(n_cores):
        m = dict(shared)
        m["x"] = np.ascontiguousarray(
            x[c * per_core:(c + 1) * per_core].reshape(per_core * T, D))
        maps.append(m)
    return maps


def kernel(**inputs):
    cfg = {"T": 2048, "NSEQ": 2}
    nc = build(cfg)
    maps = make_in_maps(inputs, cfg, n_cores=8)
    res = run_bass_kernel_spmd(nc, maps, core_ids=list(range(8)))
    per_core = 16 // 8
    out = np.concatenate(
        [res.results[c]["out"].reshape(per_core, cfg["T"], D) for c in range(8)],
        axis=0)
    return out.astype(np.float32)


# version 03
# speedup vs baseline: 1.2812x; 1.0624x over previous
import sys
if "/opt/trn_rl_repo" not in sys.path:
    sys.path.insert(0, "/opt/trn_rl_repo")

from contextlib import ExitStack

import numpy as np
import ml_dtypes

import concourse.bass as bass
import concourse.mybir as mybir
from concourse.bass_utils import run_bass_kernel_spmd

F32 = mybir.dt.float32
BF16 = mybir.dt.bfloat16
I32 = mybir.dt.int32
AX = mybir.AxisListType
ALU = mybir.AluOpType
ACT = mybir.ActivationFunctionType

D = 1024
NH = 16
FFI = 2816
NCH = 8
CWS = (21, 41)
PADS = (10, 20)
UPAD = 64
GAP = 192
SLOTP = [CWS[0] + GAP, CWS[1] + GAP]
PRB = [GAP // 2, GAP // 2 + 8 * SLOTP[0]]
PRP = GAP // 2 + 8 * SLOTP[0] + 8 * SLOTP[1] + GAP // 2
M22 = float(3 * 2 ** 21)
M24 = float(3 * 2 ** 22)
BFREQ = 10000.0
MIN_L = 0.83
SDT_L = (BFREQ - MIN_L * BFREQ) * 2.0
EPS_LN = 1e-5
PI = float(np.pi)
NEG4M = -4194304


def ts(i, size):
    return slice(i * size, (i + 1) * size)


class Sem:
    def __init__(self, nc, es, name):
        self.h = es.enter_context(nc.semaphore(name))
        self.n = 0

    def inc(self, inst, k=1):
        inst.then_inc(self.h, k)
        self.n += k
        return self.n

    def dinc(self, inst):
        inst.then_inc(self.h, 16)
        self.n += 16
        return self.n


def build(cfg):
    T = cfg["T"]
    NSEQ = cfg["NSEQ"]
    N = T * NSEQ
    NT = N // 128
    NTS = T // 128
    QB = min(512, T)
    NQB = T // QB
    NKT = T // 128
    NKTP = (T + 2 * UPAD) // 128
    LNG = min(8, NT)
    dbg = set(cfg.get("debug", ()))
    stop = cfg.get("stop_after", 99)

    nc = bass.Bass("TRN2", target_bir_lowering=False, debug=False)

    def inp(name, shape, dt=F32):
        return nc.dram_tensor(name, shape, dt, kind="ExternalInput").ap()

    x_in = inp("x", [N, D])
    Wq = inp("Wq", [D, D]); Wk = inp("Wk", [D, D]); Wv = inp("Wv", [D, D])
    Wo = inp("Wo", [D, D])
    bq = inp("bq", [D]); bk = inp("bk", [D]); bv = inp("bv", [D]); bo = inp("bo", [D])
    nrope = inp("nrope", [1])
    w_gate = inp("w_gate", [FFI, D]); w_up = inp("w_up", [FFI, D])
    w_down = inp("w_down", [D, FFI])
    W_in = inp("W_in", [D, D]); W_attn = inp("W_attn", [656, D])
    W_out = inp("W_out", [D, D]); sw_in = inp("sw", [2])
    W_l1 = inp("W_l1", [D, D])
    iota_t = inp("iota_t", [128, T])
    rope_e = inp("rope_e", [128, 1])
    rope_sm = inp("rope_sm", [128, 1])
    sperm = nc.dram_tensor("sperm", [128, 128], BF16, kind="ExternalInput").ap()
    e2sel = nc.dram_tensor("e2sel", [16, 1024], BF16, kind="ExternalInput").ap()

    out_t = nc.dram_tensor("out", [N, D], F32, kind="ExternalOutput").ap()
    dbg_t = {}
    for name, shape, dt in [
        ("d_xn", [N, D], F32), ("d_xq", [N, D], BF16),
        ("d_qT", [D, T], BF16), ("d_kT", [D, T], BF16),
        ("d_v", [128, NH * 65], BF16), ("d_att", [N, D], F32),
        ("d_hq", [N, FFI], BF16), ("d_xn2", [N, D], F32),
        ("d_u", [D, T], BF16), ("d_ow", [D, T], BF16),
        ("d_aq", [N, D], BF16), ("d_wgq", [FFI, D], BF16),
        ("d_wl1q", [D, D], BF16), ("d_ws", [1, 2], F32),
    ]:
        if name in dbg:
            dbg_t[name] = nc.dram_tensor(name, shape, dt, kind="ExternalOutput").ap()

    def scr(name, shape, dt=BF16):
        return nc.dram_tensor(name, shape, dt, kind="Internal").ap()

    xn_s = scr("xn_s", [N, D])
    xq_s = scr("xq_s", [N, D])
    h_s = scr("h_s", [N, FFI], F32)
    hq_s = scr("hq_s", [N, FFI])
    x1_s = scr("x1_s", [N, D], F32)
    xn2_s = scr("xn2_s", [N, D])
    aq_s = scr("aq_s", [N, D])
    wq_bs = scr("wq_bs", [D, D]); wk_bs = scr("wk_bs", [D, D])
    wv_bs = scr("wv_bs", [D, D]); wo_bs = scr("wo_bs", [D, D])
    wgq_s = scr("wgq_s", [FFI, D]); wuq_s = scr("wuq_s", [FFI, D])
    wdq_s = scr("wdq_s", [D, FFI])
    wl1q_s = scr("wl1q_s", [D, D])
    wi_s = scr("wi_s", [D, D]); wa_s = scr("wa_s", [656, D]); wo2_s = scr("wo2_s", [D, D])
    u_s = scr("u_s", [NSEQ, D, T + 128])
    pr_flat = scr("pr_s", [NSEQ * T * PRP])
    scal_s = scr("scal_s", [8], F32)

    es = ExitStack()

    def sb(name, shape, dt=F32):
        return es.enter_context(nc.sbuf_tensor(name, shape, dt))

    R1N = NCH * T
    R1 = es.enter_context(nc.sbuf_tensor("R1", [128, max(R1N, 4 * N, 8 * T, 2 * LNG * D)], BF16))
    R2 = es.enter_context(nc.sbuf_tensor("R2", [128, max(2 * NKT * QB, 4 * N, 8 * T, 8192)], BF16))
    R3 = es.enter_context(nc.sbuf_tensor("R3", [128, max(NKT * NH * 65, NKTP * 1024, 22 * 512 + 2048, NCH * T)], BF16))
    R4 = es.enter_context(nc.sbuf_tensor("R4", [128, max(NCH * T, 16384)], BF16))
    R5 = es.enter_context(nc.sbuf_tensor("R5", [128, max(4 * T, NCH * 656 + PRP, 8192 + 1024)], BF16))

    xg = R1[:].bitcast(F32)
    xnT = R1
    Upr = R1
    xqT_lo = R1
    uT = R1
    aqT = R1
    ET = R2
    xqT_hi = R2
    ruT = R2
    wl1qT = R2
    vA = R3
    hqTb = R3
    gsil_v = R3[:, 22 * 512:22 * 512 + 2048].bitcast(F32)
    xn2T = R3
    upad = R3
    WVr = R4[:, 0:NCH * 1024]
    WQs = R4[:, 8192:12288]
    WKs = R4[:, 12288:16384]
    rope_t1 = R4[:, 8192:8192 + T]
    rope_sq = R4[:, 8192 + T:8192 + 2 * T]
    WOr = R4[:, 0:NCH * 1024]
    wstr = R4
    wdT_a = R4
    owT = R4
    qkbuf = R5
    rbb = R5[:, 2 * T:3 * T]
    UstA = R5[:, 3 * T:3 * T + QB]
    UstB = R5[:, 3 * T + QB:3 * T + 2 * QB]
    wiT = R5
    waT = R5[:, 0:NCH * 656]
    PW = R5[:, NCH * 656:NCH * 656 + PRP]
    woutT = R5
    wdT_b = R5

    q_vI = es.enter_context(nc.sbuf_tensor("q_vI", [128, FFI], I32))
    q_2I = es.enter_context(nc.sbuf_tensor("q_2I", [128, FFI], I32))
    q_out = es.enter_context(nc.sbuf_tensor("q_out", [128, FFI], BF16))
    q_bm = sb("q_bm", [128, FFI // 16])
    q_sc = sb("q_sc", [128, FFI // 16])

    junk = sb("junk", [128, max(FFI + 32, T + 64)])
    junkb = junk[:].bitcast(BF16)
    q_hi = junk[:, 0:FFI]
    Rr = junk[:].bitcast(BF16)[0:16, 0:T]
    Rrb = junk[:].bitcast(BF16)[0:16, T:2 * T]
    stat = sb("stat", [128, 4 * LNG + 4])
    cbc = sb("cbc", [128, 24])
    tiny = sb("tiny", [128, 16])
    ifr = sb("ifr", [128, 4])
    bqt = sb("bqt", [128, 4 * NCH])
    CC = es.enter_context(nc.sbuf_tensor("CC", [128, T], BF16))
    SSb = es.enter_context(nc.sbuf_tensor("SSb", [128, T], BF16))
    spermt = es.enter_context(nc.sbuf_tensor("spermt", [128, 128], BF16))
    e2t = es.enter_context(nc.sbuf_tensor("e2t", [16, 1024], BF16))
    biasrow = es.enter_context(nc.sbuf_tensor("biasrow", [1, D], BF16))
    onesrow = es.enter_context(nc.sbuf_tensor("onesrow", [1, 128], BF16))
    Btile = es.enter_context(nc.sbuf_tensor("Btile", [128, 4 * 128], BF16))
    asws = sb("asws", [128, max(NT, 1)])
    x1t = sb("x1t", [128, D])
    xnt = R2[:].bitcast(F32)[:, 0:D]
    wfp = x1t
    wbuf = R1[:, 0:2048]
    prz = sb("prz", [128, 16])
    wl1r = sb("wl1r", [128, 1])
    zpad = es.enter_context(nc.sbuf_tensor("zpad", [128, 512], BF16))
    wl1s = sb("wl1s", [128, 1])

    pw0 = es.enter_context(nc.psum_tensor("pw0", [128, 1024], F32))
    pw1 = es.enter_context(nc.psum_tensor("pw1", [128, 1024], F32))
    p4 = es.enter_context(nc.psum_tensor("p4", [128, 512], F32))
    p5 = es.enter_context(nc.psum_tensor("p5", [128, 512], F32))
    p6 = es.enter_context(nc.psum_tensor("p6", [128, 512], F32))
    p7 = es.enter_context(nc.psum_tensor("p7", [128, 512], F32))

    s_ld = Sem(nc, es, "s_ld")
    s_lw = Sem(nc, es, "s_lw")
    s_lx = Sem(nc, es, "s_lx")
    s_th = Sem(nc, es, "s_th")
    s_st = Sem(nc, es, "s_st")
    s_tr = Sem(nc, es, "s_tr")
    s_mv = Sem(nc, es, "s_mv")
    s_v = Sem(nc, es, "s_v")
    s_s = Sem(nc, es, "s_s")
    s_t = Sem(nc, es, "s_t")
    s_g = Sem(nc, es, "s_g")
    s_bar = Sem(nc, es, "s_bar")
    s_out = Sem(nc, es, "s_out")
    s_qs = Sem(nc, es, "s_qs")
    s_lq = Sem(nc, es, "s_lq")

    blk = es.enter_context(nc.Block())

    CONSTS = {}

    def reg_const(*vals):
        for v in vals:
            v = float(v)
            if v not in CONSTS:
                CONSTS[v] = len(CONSTS)
                assert len(CONSTS) <= 24

    def C(v):
        return cbc[:, ts(CONSTS[float(v)], 1)]

    reg_const(0.0, 1.0, -8.0, 0.125, SDT_L, MIN_L * BFREQ)

    nbar = [0]

    def barrier():
        nbar[0] += 1
        tgt = 5 * nbar[0]
        for ename in ("sync", "vector", "scalar", "tensor", "gpsimd"):
            @getattr(blk, ename)
            def _(eng):
                eng.sem_inc(s_bar.h, 1)
                eng.wait_ge(s_bar.h, tgt)
        s_bar.n = tgt

    def emit_fp4(v, W, src_ap):
        J = W // 16
        vF = q_vI[:, 0:W].bitcast(F32)
        q2I = q_2I[:, 0:W]
        q2F = q_2I[:, 0:W].bitcast(F32)
        hiF = q_hi[:, 0:W]

        def st(inst):
            s_v.inc(inst)
            v.wait_ge(s_v.h, s_v.n)

        st(v.reduce_max(out=q_bm[:, 0:J].unsqueeze(2),
                        in_=src_ap.rearrange("p (c s) -> p c s", s=16),
                        axis=AX.X, apply_absolute_value=True))
        st(v.tensor_scalar(out=q_sc[:, 0:J], in0=q_bm[:, 0:J], scalar1=6e-12,
                           scalar2=1.0 / 6.0, op0=ALU.max, op1=ALU.mult))
        st(v.reciprocal(out=q_bm[:, 0:J], in_=q_sc[:, 0:J]))
        bm3 = q_bm[:, 0:J].unsqueeze(2).broadcast_to([128, J, 16])
        sc3 = q_sc[:, 0:J].unsqueeze(2).broadcast_to([128, J, 16])
        st(v.tensor_mul(out=vF.rearrange("p (c s) -> p c s", s=16),
                        in0=src_ap.rearrange("p (c s) -> p c s", s=16), in1=bm3))
        st(v.tensor_scalar_add(out=q2I, in0=q_vI[:, 0:W], scalar1=0x00200000))
        st(v.tensor_single_scalar(out=q2I, in_=q2I, scalar=NEG4M, op=ALU.bitwise_and))
        st(v.tensor_scalar(out=vF, in0=vF, scalar1=M22, scalar2=M22,
                           op0=ALU.add, op1=ALU.subtract))
        st(v.tensor_scalar(out=vF, in0=vF, scalar1=-2.0, scalar2=2.0,
                           op0=ALU.max, op1=ALU.min))
        st(v.tensor_scalar(out=hiF, in0=q2F, scalar1=-2.0, scalar2=2.0,
                           op0=ALU.max, op1=ALU.min))
        st(v.tensor_sub(out=hiF, in0=q2F, in1=hiF))
        st(v.tensor_add(out=hiF, in0=hiF, in1=vF))
        st(v.tensor_mul(out=q_out[:, 0:W].rearrange("p (c s) -> p c s", s=16),
                        in0=hiF.rearrange("p (c s) -> p c s", s=16), in1=sc3))

    QSLOT = 5
    qsrcF = q_vI[:, 0:QSLOT * 512].bitcast(F32)

    def emit_fp4s(v, W, si):
        J = W // 16
        src = qsrcF[:, si * 512:si * 512 + W]
        vF = q_2I[:, 0:W].bitcast(F32)
        vI = q_2I[:, 0:W]
        q2I = q_2I[:, 512:512 + W]
        q2F = q_2I[:, 512:512 + W].bitcast(F32)
        hiF = q_2I[:, 1024:1024 + W].bitcast(F32)
        out = q_out[:, si * 512:si * 512 + W]

        def st(inst):
            s_v.inc(inst)
            v.wait_ge(s_v.h, s_v.n)

        st(v.reduce_max(out=q_bm[:, 0:J].unsqueeze(2),
                        in_=src.rearrange("p (c s) -> p c s", s=16),
                        axis=AX.X, apply_absolute_value=True))
        st(v.tensor_scalar(out=q_sc[:, 0:J], in0=q_bm[:, 0:J], scalar1=6e-12,
                           scalar2=1.0 / 6.0, op0=ALU.max, op1=ALU.mult))
        st(v.reciprocal(out=q_bm[:, 0:J], in_=q_sc[:, 0:J]))
        bm3 = q_bm[:, 0:J].unsqueeze(2).broadcast_to([128, J, 16])
        sc3 = q_sc[:, 0:J].unsqueeze(2).broadcast_to([128, J, 16])
        st(v.tensor_mul(out=vF.rearrange("p (c s) -> p c s", s=16),
                        in0=src.rearrange("p (c s) -> p c s", s=16), in1=bm3))
        st(v.tensor_scalar_add(out=q2I, in0=vI, scalar1=0x00200000))
        st(v.tensor_single_scalar(out=q2I, in_=q2I, scalar=NEG4M, op=ALU.bitwise_and))
        st(v.tensor_scalar(out=vF, in0=vF, scalar1=M22, scalar2=M22,
                           op0=ALU.add, op1=ALU.subtract))
        st(v.tensor_scalar(out=vF, in0=vF, scalar1=-2.0, scalar2=2.0,
                           op0=ALU.max, op1=ALU.min))
        st(v.tensor_scalar(out=hiF, in0=q2F, scalar1=-2.0, scalar2=2.0,
                           op0=ALU.max, op1=ALU.min))
        st(v.tensor_sub(out=hiF, in0=q2F, in1=hiF))
        st(v.tensor_add(out=hiF, in0=hiF, in1=vF))
        st(v.tensor_mul(out=out.rearrange("p (c s) -> p c s", s=16),
                        in0=hiF.rearrange("p (c s) -> p c s", s=16), in1=sc3))

    qtasks = []
    for wsrc, wdst, rows, wid in ((w_gate, wgq_s, FFI, D), (w_up, wuq_s, FFI, D),
                                  (w_down, wdq_s, D, FFI)):
        for rt in range(rows // 128):
            c0 = 0
            while c0 < wid:
                W = min(512, wid - c0)
                dbg_ap = None
                if wsrc is w_gate and "d_wgq" in dbg_t:
                    dbg_ap = dbg_t["d_wgq"][ts(rt, 128), c0:c0 + W]
                qtasks.append((wsrc[ts(rt, 128), c0:c0 + W],
                               wdst[ts(rt, 128), c0:c0 + W], W, dbg_ap))
                c0 += W

    qst = {"ld": 0, "slots": [], "lxm": [], "vend": [], "qsm": [0],
           "qt_slot": 0, "st_slot": 0}

    def q_issue_loads(eng):
        n = min(QSLOT, len(qtasks) - qst["ld"])
        if n <= 0:
            return
        i0 = qst["ld"]
        for j in range(n):
            i = qst["ld"]
            src, dst, W, dbg_ap = qtasks[i]
            s_lq.dinc(eng.dma_start(
                out=qsrcF[:, (i % QSLOT) * 512:(i % QSLOT) * 512 + W], in_=src))
            qst["ld"] += 1
        qst["slots"].append((i0, qst["ld"]))
        qst["lxm"].append(s_lq.n)

    def q_bootstrap():
        if qst["ld"] or not qtasks:
            return
        vcur = s_v.n

        @blk.scalar
        def _(sc_, vcur=vcur):
            sc_.wait_ge(s_v.h, vcur)
            q_issue_loads(sc_)

    def q_slot_vector():
        k = qst["qt_slot"]
        if k >= len(qst["slots"]):
            return
        i0, i1 = qst["slots"][k]
        lxm_k = qst["lxm"][k]
        qsm_k = qst["qsm"][k]

        @blk.vector
        def _(v, i0=i0, i1=i1, lxm_k=lxm_k, qsm_k=qsm_k):
            v.wait_ge(s_lq.h, lxm_k)
            v.wait_ge(s_qs.h, qsm_k)
            for i in range(i0, i1):
                emit_fp4s(v, qtasks[i][2], i % QSLOT)
        qst["vend"].append(s_v.n)
        qst["qt_slot"] = k + 1

    def q_slot_scalar():
        k = qst["st_slot"]
        if k >= len(qst["vend"]):
            return
        i0, i1 = qst["slots"][k]
        vend_k = qst["vend"][k]

        @blk.scalar
        def _(sc_, i0=i0, i1=i1, vend_k=vend_k):
            sc_.wait_ge(s_v.h, vend_k)
            for i in range(i0, i1):
                src, dst, W, dbg_ap = qtasks[i]
                osl = q_out[:, (i % QSLOT) * 512:(i % QSLOT) * 512 + W]
                s_qs.dinc(sc_.dma_start(out=dst, in_=osl))
                if dbg_ap is not None:
                    s_qs.dinc(sc_.dma_start(out=dbg_ap, in_=osl))
            q_issue_loads(sc_)
        qst["qsm"].append(s_qs.n)
        qst["st_slot"] = k + 1

    def phase0a():
        @blk.sync
        def _(sync):
            for i, b in enumerate((bq, bk, bv, bo)):
                s_ld.dinc(sync.dma_start(out=bqt[:, ts(i, NCH)],
                                         in_=b.rearrange("(c p) -> p c", p=128),
                                         allow_slow_non_contiguous=True))
            s_ld.dinc(sync.dma_start(out=tiny[:, 0:1],
                                     in_=nrope.unsqueeze(0).broadcast_to([128, 1])))
            s_ld.dinc(sync.dma_start(out=ifr[:, 0:1], in_=rope_e[:, :]))
            s_ld.dinc(sync.dma_start(out=ifr[:, 1:2], in_=rope_sm[:, :]))
            s_ld.dinc(sync.dma_start(out=junk[:, 0:T], in_=iota_t[:, 0:T]))
            s_ld.dinc(sync.dma_start(out=tiny[:, 4:6],
                                     in_=sw_in.unsqueeze(0).broadcast_to([128, 2])))
            s_ld.dinc(sync.dma_start(out=e2t[:, :], in_=e2sel[:, :]))
            s_ld.dinc(sync.dma_start(out=spermt[:, :], in_=sperm[:, :]))
        ld0 = s_ld.n

        ncst = [0]

        @blk.vector
        def _(v):
            v.wait_ge(s_ld.h, ld0)
            for val, col in CONSTS.items():
                v.memset(cbc[:, ts(col, 1)], val).then_inc(s_v.h, 1)
                ncst[0] += 1
            v.memset(onesrow[:, :], 1.0).then_inc(s_v.h, 1)
            ncst[0] += 1
            v.memset(zpad[:, :], 0.0).then_inc(s_v.h, 1)
            ncst[0] += 1
        s_v.n += ncst[0]
        vc0 = s_v.n

        @blk.scalar
        def _(s):
            s.wait_ge(s_v.h, vc0)
            s.activation(out=tiny[:, 1:2], in_=tiny[:, 0:1], func=ACT.Sigmoid).then_inc(s_s.h, 1)
            s.wait_ge(s_s.h, s_s.n + 1)
            s.activation(out=tiny[:, 2:3], in_=tiny[:, 1:2], func=ACT.Ln,
                         scale=C(SDT_L), bias=C(MIN_L * BFREQ)).then_inc(s_s.h, 1)
            s.activation(out=tiny[:, 6:8], in_=tiny[:, 4:6], func=ACT.Exp).then_inc(s_s.h, 1)
        s_s.n += 3
        sc0 = s_s.n

        @blk.vector
        def _(v):
            v.wait_ge(s_s.h, sc0)
            v.reduce_sum(out=tiny[:, 10:11], in_=tiny[:, 6:8], axis=AX.X).then_inc(s_v.h, 1)
            v.wait_ge(s_v.h, s_v.n + 1)
            v.reciprocal(out=tiny[:, 11:12], in_=tiny[:, 10:11]).then_inc(s_v.h, 1)
            v.wait_ge(s_v.h, s_v.n + 2)
            v.tensor_mul(out=tiny[:, 8:10], in0=tiny[:, 6:8],
                         in1=tiny[:, 11:12].broadcast_to([128, 2])).then_inc(s_v.h, 1)
            v.tensor_mul(out=ifr[:, 2:3], in0=ifr[:, 0:1], in1=tiny[:, 2:3]).then_inc(s_v.h, 1)
        s_v.n += 4
        vc1 = s_v.n

        @blk.scalar
        def _(s):
            s.wait_ge(s_v.h, vc1)
            s.activation(out=ifr[:, 3:4], in_=ifr[:, 2:3], func=ACT.Exp).then_inc(s_s.h, 1)
        s_s.n += 1
        sc1 = s_s.n

        @blk.vector
        def _(v):
            v.wait_ge(s_s.h, sc1)
            v.tensor_mul(out=ifr[:, 3:4], in0=ifr[:, 3:4], in1=ifr[:, 1:2]).then_inc(s_v.h, 1)
            v.wait_ge(s_v.h, s_v.n + 1)
            v.tensor_scalar(out=junk[:, 0:T], in0=junk[:, 0:T], scalar1=ifr[:, 3:4],
                            scalar2=None, op0=ALU.mult).then_inc(s_v.h, 1)
            v.wait_ge(s_v.h, s_v.n + 2)
            v.tensor_single_scalar(out=q_vI[:, 0:T].bitcast(F32), in_=junk[:, 0:T],
                                   scalar=PI, op=ALU.is_gt).then_inc(s_v.h, 1)
            v.wait_ge(s_v.h, s_v.n + 3)
            v.scalar_tensor_tensor(out=junk[:, 0:T], in0=q_vI[:, 0:T].bitcast(F32),
                                   scalar=-2.0 * PI, in1=junk[:, 0:T],
                                   op0=ALU.mult, op1=ALU.add).then_inc(s_v.h, 1)
            v.wait_ge(s_v.h, s_v.n + 4)
            v.tensor_single_scalar(out=q_vI[:, 0:T].bitcast(F32), in_=junk[:, 0:T],
                                   scalar=-PI, op=ALU.is_lt).then_inc(s_v.h, 1)
            v.wait_ge(s_v.h, s_v.n + 5)
            v.scalar_tensor_tensor(out=junk[:, 0:T], in0=q_vI[:, 0:T].bitcast(F32),
                                   scalar=2.0 * PI, in1=junk[:, 0:T],
                                   op0=ALU.mult, op1=ALU.add).then_inc(s_v.h, 1)
        s_v.n += 6
        vc2 = s_v.n

        @blk.scalar
        def _(s):
            s.wait_ge(s_v.h, vc2)
            s.activation(out=SSb[:, 0:T], in_=junk[:, 0:T], func=ACT.Sin).then_inc(s_s.h, 1)
        s_s.n += 1
        sc2 = s_s.n

        @blk.vector
        def _(v):
            v.wait_ge(s_s.h, sc2)
            v.tensor_scalar_mul(out=xg[:, 0:T], in0=junk[:, 0:T],
                                scalar1=-1.0).then_inc(s_v.h, 1)
            v.wait_ge(s_v.h, s_v.n + 1)
            v.tensor_max(out=junk[:, 0:T], in0=junk[:, 0:T],
                         in1=xg[:, 0:T]).then_inc(s_v.h, 1)
            v.wait_ge(s_v.h, s_v.n + 2)
            v.tensor_scalar(out=junk[:, 0:T], in0=junk[:, 0:T], scalar1=-1.0,
                            scalar2=PI / 2, op0=ALU.mult, op1=ALU.add).then_inc(s_v.h, 1)
        s_v.n += 3
        vc3 = s_v.n

        @blk.scalar
        def _(s):
            s.wait_ge(s_v.h, vc3)
            s.activation(out=CC[:, 0:T], in_=junk[:, 0:T], func=ACT.Sin).then_inc(s_s.h, 1)
        s_s.n += 1

    def phase0b():
        for wsrc, wdst in ((Wq, wq_bs), (Wk, wk_bs), (Wv, wv_bs), (Wo, wo_bs)):
            for c in range(NCH):
                @blk.sync
                def _(sync, wsrc=wsrc, c=c):
                    sync.wait_ge(s_v.h, s_v.n)
                    s_ld.dinc(sync.dma_start(out=wfp[:, :], in_=wsrc[ts(c, 128), :]))
                ldc = s_ld.n

                @blk.vector
                def _(v):
                    v.wait_ge(s_ld.h, ldc)
                    v.tensor_copy(out=wbuf[:, 0:1024], in_=wfp[:, :]).then_inc(s_v.h, 1)
                s_v.n += 1
                vcc = s_v.n

                @blk.scalar
                def _(s, wdst=wdst, c=c):
                    s.wait_ge(s_v.h, vcc)
                    s_st.dinc(s.dma_start(out=wdst[ts(c, 128), :], in_=wbuf[:, 0:1024]))
                stc = s_st.n

                @blk.vector
                def _(v):
                    v.wait_ge(s_st.h, stc)
                    v.sem_inc(s_v.h, 1)
                s_v.n += 1

        @blk.vector
        def _(v):
            v.tensor_copy(out=wbuf[:, 0:NCH], in_=bqt[:, ts(2, NCH)]).then_inc(s_v.h, 1)
        s_v.n += 1
        vb = s_v.n
        for c in range(NCH):
            @blk.sync
            def _(sync, c=c):
                sync.wait_ge(s_v.h, vb)
                if c > 0:
                    sync.wait_ge(s_t.h, s_t.n)
                s_tr.dinc(sync.dma_start(out=wbuf[:, 1024:2048],
                                         in_=wo_bs[ts(c, 128), :]))
            trc = s_tr.n

            @blk.tensor
            def _(t, c=c):
                t.wait_ge(s_tr.h, trc)
                t.matmul(pw0[0:1, 0:512], wbuf[:, c:c + 1], wbuf[:, 1024:1536],
                         start=(c == 0), stop=(c == NCH - 1))
                t.matmul(pw0[0:1, 512:1024], wbuf[:, c:c + 1], wbuf[:, 1536:2048],
                         start=(c == 0), stop=(c == NCH - 1)).then_inc(s_t.h, 1)
            s_t.n += 1
        tb = s_t.n

        @blk.sync
        def _(sync):
            sync.wait_ge(s_t.h, tb)
            s_tr.dinc(sync.dma_start(out=wfp[0:1, 0:D], in_=bo.unsqueeze(0)))
        trb = s_tr.n

        @blk.vector
        def _(v):
            v.wait_ge(s_tr.h, trb)
            v.tensor_add(out=wfp[0:1, 0:D], in0=wfp[0:1, 0:D],
                         in1=pw0[0:1, 0:1024]).then_inc(s_v.h, 1)
            v.wait_ge(s_v.h, s_v.n + 1)
            v.tensor_copy(out=biasrow[0:1, :], in_=wfp[0:1, 0:D]).then_inc(s_v.h, 1)
        s_v.n += 2

    def phase_ln(src_dram, xnbf_dram, q_dram, layer):
        ngrp = NT // LNG
        for g in range(ngrp):
            @blk.sync
            def _(sync, g=g):
                sync.wait_ge(s_v.h, s_v.n)
                for i in range(LNG):
                    s_lx.dinc(sync.dma_start(out=xg[:, ts(i, D)],
                                             in_=src_dram[ts(g * LNG + i, 128), :]))
            ldg = s_lx.n

            @blk.vector
            def _(v):
                v.wait_ge(s_lx.h, ldg)
                for i in range(LNG):
                    v.reduce_sum(out=stat[:, ts(i, 1)], in_=xg[:, ts(i, D)],
                                 axis=AX.X).then_inc(s_v.h, 1)
            s_v.n += LNG
            vred = s_v.n

            @blk.scalar
            def _(s):
                s.wait_ge(s_lx.h, ldg)
                base = s_s.n
                for i in range(LNG):
                    if i > 0:
                        s.wait_ge(s_s.h, base + i)
                    s.activation(out=junk[:, 0:D], in_=xg[:, ts(i, D)], func=ACT.Square,
                                 accum_out=stat[:, ts(LNG + i, 1)]).then_inc(s_s.h, 1)
            s_s.n += LNG
            ssq = s_s.n

            @blk.vector
            def _(v):
                v.wait_ge(s_v.h, vred)
                v.wait_ge(s_s.h, ssq)
                L = slice(0, LNG)
                v.tensor_scalar_mul(out=stat[:, L], in0=stat[:, L],
                                    scalar1=1.0 / D).then_inc(s_v.h, 1)
                v.wait_ge(s_v.h, s_v.n + 1)
                v.tensor_mul(out=stat[:, 2 * LNG:3 * LNG], in0=stat[:, L],
                             in1=stat[:, L]).then_inc(s_v.h, 1)
                v.wait_ge(s_v.h, s_v.n + 2)
                v.scalar_tensor_tensor(out=stat[:, LNG:2 * LNG],
                                       in0=stat[:, LNG:2 * LNG], scalar=1.0 / D,
                                       in1=stat[:, 2 * LNG:3 * LNG],
                                       op0=ALU.mult, op1=ALU.subtract).then_inc(s_v.h, 1)
                v.wait_ge(s_v.h, s_v.n + 3)
                v.tensor_scalar_add(out=stat[:, LNG:2 * LNG], in0=stat[:, LNG:2 * LNG],
                                    scalar1=EPS_LN).then_inc(s_v.h, 1)
            s_v.n += 4
            vstat = s_v.n

            @blk.scalar
            def _(s):
                s.wait_ge(s_v.h, vstat)
                s.activation(out=stat[:, LNG:2 * LNG], in_=stat[:, LNG:2 * LNG],
                             func=ACT.Sqrt).then_inc(s_s.h, 1)
            s_s.n += 1
            ssd = s_s.n

            @blk.vector
            def _(v):
                v.wait_ge(s_s.h, ssd)
                v.reciprocal(out=stat[:, LNG:2 * LNG],
                             in_=stat[:, LNG:2 * LNG]).then_inc(s_v.h, 1)
                v.wait_ge(s_v.h, s_v.n + 1)
                v.tensor_scalar(out=stat[:, 2 * LNG:3 * LNG], in0=stat[:, 0:LNG],
                                scalar1=-1.0, scalar2=None,
                                op0=ALU.mult).then_inc(s_v.h, 1)
                v.wait_ge(s_v.h, s_v.n + 2)
                v.tensor_mul(out=stat[:, 2 * LNG:3 * LNG],
                             in0=stat[:, 2 * LNG:3 * LNG],
                             in1=stat[:, LNG:2 * LNG]).then_inc(s_v.h, 1)
            s_v.n += 3
            vrs = s_v.n

            for i in range(LNG):
                it = g * LNG + i

                @blk.scalar
                def _(s, i=i):
                    s.wait_ge(s_v.h, vrs)
                    s.wait_ge(s_v.h, s_v.n)
                    s.activation(out=xnt[:, :], in_=xg[:, ts(i, D)], func=ACT.Identity,
                                 bias=stat[:, ts(2 * LNG + i, 1)],
                                 scale=stat[:, ts(LNG + i, 1)]).then_inc(s_s.h, 1)
                s_s.n += 1
                sxn = s_s.n

                @blk.vector
                def _(v):
                    v.wait_ge(s_s.h, sxn)
                    v.tensor_copy(out=q_out[:, 0:D], in_=xnt[:, :]).then_inc(s_v.h, 1)
                s_v.n += 1
                vxb = s_v.n

                @blk.scalar
                def _(s, it=it):
                    s.wait_ge(s_v.h, vxb)
                    s_st.dinc(s.dma_start(out=xnbf_dram[ts(it, 128), :],
                                          in_=q_out[:, 0:D]))
                    if layer == 0 and "d_xn" in dbg_t:
                        s_st.dinc(s.dma_start(out=dbg_t["d_xn"][ts(it, 128), :],
                                              in_=xnt[:, :]))
                    if layer == 1 and "d_xn2" in dbg_t:
                        s_st.dinc(s.dma_start(out=dbg_t["d_xn2"][ts(it, 128), :],
                                              in_=xnt[:, :]))
                stq = s_st.n

                if layer == 0:
                    @blk.vector
                    def _(v):
                        v.wait_ge(s_st.h, stq)
                        emit_fp4(v, D, xnt[:, 0:D])
                    vq = s_v.n

                    @blk.scalar
                    def _(s, it=it):
                        s.wait_ge(s_v.h, vq)
                        s_st.dinc(s.dma_start(out=q_dram[ts(it, 128), :],
                                              in_=q_out[:, 0:D]))
                        if "d_xq" in dbg_t:
                            s_st.dinc(s.dma_start(out=dbg_t["d_xq"][ts(it, 128), :],
                                                  in_=q_out[:, 0:D]))
                else:
                    @blk.vector
                    def _(v, it=it):
                        v.wait_ge(s_st.h, stq)

                        def st(inst):
                            s_v.inc(inst)
                            v.wait_ge(s_v.h, s_v.n)
                        c0 = 3 * LNG
                        st(v.reduce_max(out=stat[:, ts(c0, 1)], in_=xnt[:, :],
                                        axis=AX.X, apply_absolute_value=True))
                        st(v.tensor_scalar_add(out=stat[:, ts(c0, 1)],
                                               in0=stat[:, ts(c0, 1)], scalar1=1e-8))
                        st(v.reciprocal(out=stat[:, ts(c0 + 1, 1)],
                                        in_=stat[:, ts(c0, 1)]))
                        st(v.tensor_scalar_mul(out=stat[:, ts(c0 + 1, 1)],
                                               in0=stat[:, ts(c0 + 1, 1)], scalar1=127.0))
                        st(v.tensor_mul(out=asws[:, ts(it, 1)],
                                        in0=stat[:, ts(c0, 1)], in1=wl1s[:, 0:1]))
                        st(v.tensor_scalar_mul(out=asws[:, ts(it, 1)],
                                               in0=asws[:, ts(it, 1)], scalar1=1.0 / 127.0))
                        st(v.tensor_scalar(out=q_vI[:, 0:D].bitcast(F32), in0=xnt[:, :],
                                           scalar1=stat[:, ts(c0 + 1, 1)], scalar2=-128.0,
                                           op0=ALU.mult, op1=ALU.max))
                        st(v.tensor_scalar(out=q_vI[:, 0:D].bitcast(F32),
                                           in0=q_vI[:, 0:D].bitcast(F32),
                                           scalar1=127.0, scalar2=M24,
                                           op0=ALU.min, op1=ALU.add))
                        st(v.tensor_scalar(out=q_out[:, 0:D],
                                           in0=q_vI[:, 0:D].bitcast(F32),
                                           scalar1=M24, scalar2=None, op0=ALU.subtract))
                    vq = s_v.n

                    @blk.scalar
                    def _(s, it=it):
                        s.wait_ge(s_v.h, vq)
                        s_st.dinc(s.dma_start(out=q_dram[ts(it, 128), :],
                                              in_=q_out[:, 0:D]))
                        if "d_aq" in dbg_t:
                            s_st.dinc(s.dma_start(out=dbg_t["d_aq"][ts(it, 128), :],
                                                  in_=q_out[:, 0:D]))
                stq2 = s_st.n

                @blk.vector
                def _(v):
                    v.wait_ge(s_st.h, stq2)
                    v.sem_inc(s_v.h, 1)
                s_v.n += 1

            @blk.vector
            def _(v):
                v.wait_ge(s_st.h, s_st.n)
                v.sem_inc(s_v.h, 1)
            s_v.n += 1

    def phase234(sq):
        t0 = sq * T
        psl = [pw0[:, 0:512], pw0[:, 512:1024], pw1[:, 0:512], pw1[:, 512:1024]]
        NB5 = T // 512
        UprH = R4[:, 0:4 * T]
        q_bootstrap()

        @blk.sync
        def _(sync):
            sync.wait_ge(s_st.h, s_st.n)
            sync.wait_ge(s_v.h, s_v.n)
            sync.wait_ge(s_out.h, s_out.n)
            for c in range(NCH):
                s_tr.dinc(sync.dma_start(out=xnT[:, ts(c, T)],
                                         in_=xn_s[t0:t0 + T, ts(c, 128)],
                                         transpose=True))
            for k in range(NCH):
                s_lw.dinc(sync.dma_start(out=R4[:, ts(k, 1024)],
                                         in_=wv_bs[ts(k, 128), :]))
        trx = s_tr.n
        ldv = s_lw.n

        @blk.vector
        def _(v):
            v.wait_ge(s_st.h, s_st.n)
            va3 = vA[:, 0:NKT * NH * 65].rearrange("p (t x) -> p t x", x=65)
            v.memset(va3[:, :, 64:65], 1.0).then_inc(s_v.h, 1)
            v.memset(Rr[:, :], 1.0).then_inc(s_v.h, 1)
        s_v.n += 2
        vones = s_v.n

        tv_l = []
        base_vt = s_t.n
        for tt in range(NTS):
            s_t.n += 1
            tv_l.append(s_t.n)
        sv_base = s_s.n

        @blk.tensor
        def _(t, tv_l=tv_l):
            t.wait_ge(s_tr.h, trx)
            t.wait_ge(s_lw.h, ldv)
            for tt in range(NTS):
                if tt > 0:
                    t.wait_ge(s_s.h, sv_base + tt)
                for k in range(NCH):
                    sl = xnT[:, k * T + tt * 128: k * T + tt * 128 + 128]
                    t.matmul(pw1[:, 0:512], sl, R4[:, k * 1024:k * 1024 + 512],
                             start=(k == 0), stop=(k == NCH - 1))
                    mm = t.matmul(pw1[:, 512:1024], sl, R4[:, k * 1024 + 512:k * 1024 + 1024],
                                  start=(k == 0), stop=(k == NCH - 1))
                mm.then_inc(s_t.h, 1)

        @blk.scalar
        def _(s, tv_l=tv_l):
            s.wait_ge(s_v.h, vones)
            for tt in range(NTS):
                s.wait_ge(s_t.h, tv_l[tt])
                dst = vA[:, ts(tt, NH * 65)].rearrange("p (h x) -> p h x", x=65)[:, :, 0:64]
                s.copy(out=dst,
                       in_=pw1[:, 0:1024].rearrange("p (h x) -> p h x", x=64)).then_inc(s_s.h, 1)
        s_s.n += NTS

        if "d_v" in dbg_t and sq == 0:
            @blk.scalar
            def _(s):
                s.wait_ge(s_s.h, s_s.n)
                s_st.dinc(s.dma_start(out=dbg_t["d_v"][:, :], in_=vA[:, 0:NH * 65]))
        vdone_s = s_s.n

        for half in range(2):
            for ci in range(4):
                c = half * 4 + ci

                @blk.sync
                def _(sync, c=c):
                    sync.wait_ge(s_v.h, s_v.n)
                    sync.wait_ge(s_t.h, s_t.n)
                    for k in range(NCH):
                        s_lw.dinc(sync.dma_start(out=R4[:, 8192 + k * 128:8192 + k * 128 + 128],
                                                 in_=wq_bs[ts(k, 128), ts(c, 128)]))
                        s_lw.dinc(sync.dma_start(out=R4[:, 12288 + k * 128:12288 + k * 128 + 128],
                                                 in_=wk_bs[ts(k, 128), ts(c, 128)]))
                ldqk = s_lw.n

                for hf, woff, boff in ((0, 8192, 0), (1, 12288, NCH)):
                    vready = s_v.n
                    q_slot_vector()

                    @blk.tensor
                    def _(t, c=c, hf=hf, woff=woff, vready=vready):
                        t.wait_ge(s_lw.h, ldqk)
                        t.wait_ge(s_tr.h, trx)
                        t.wait_ge(s_st.h, s_st.n)
                        t.wait_ge(s_v.h, vready)
                        for k in range(NCH):
                            for tb in range(NB5):
                                mm = t.matmul(psl[tb],
                                              R4[:, woff + k * 128:woff + k * 128 + 128],
                                              xnT[:, k * T + tb * 512:k * T + (tb + 1) * 512],
                                              start=(k == 0), stop=(k == NCH - 1))
                        mm.then_inc(s_t.h, 1)
                    s_t.n += 1
                    tqk = s_t.n

                    @blk.vector
                    def _(v, c=c, hf=hf, boff=boff):
                        v.wait_ge(s_t.h, tqk)
                        for tb in range(NB5):
                            v.tensor_scalar(out=qkbuf[:, hf * T + tb * 512: hf * T + (tb + 1) * 512],
                                            in0=psl[tb], scalar1=bqt[:, ts(boff + c, 1)],
                                            scalar2=None, op0=ALU.add).then_inc(s_v.h, 1)
                    s_v.n += NB5
                    sqk = s_v.n

                    @blk.tensor
                    def _(t, hf=hf):
                        t.wait_ge(s_v.h, sqk)
                        for tb in range(NB5):
                            t.matmul(p6[:, 0:512], spermt[:, :],
                                     qkbuf[:, hf * T + tb * 512: hf * T + (tb + 1) * 512],
                                     start=True, stop=True).then_inc(s_t.h, 1)
                            t.wait_ge(s_v.h, s_v.n + tb + 1)
                    s_t.n += NB5
                    trp = s_t.n

                    @blk.vector
                    def _(v, hf=hf):
                        for tb in range(NB5):
                            v.wait_ge(s_t.h, trp - NB5 + tb + 1)
                            v.tensor_copy(out=R4[:, 8192 + T + tb * 512: 8192 + T + (tb + 1) * 512],
                                          in_=p6[:, 0:512]).then_inc(s_v.h, 1)
                    s_v.n += NB5
                    srp = s_v.n

                    @blk.vector
                    def _(v, hf=hf):
                        v.wait_ge(s_v.h, srp)
                        hb = hf * T
                        v.tensor_mul(out=R4[:, 8192:8192 + T],
                                     in0=qkbuf[:, hb:hb + T], in1=CC[:, 0:T]).then_inc(s_v.h, 1)
                        v.wait_ge(s_v.h, s_v.n + 1)
                        v.tensor_mul(out=R4[:, 8192 + T:8192 + 2 * T],
                                     in0=R4[:, 8192 + T:8192 + 2 * T],
                                     in1=SSb[:, 0:T]).then_inc(s_v.h, 1)
                        v.wait_ge(s_v.h, s_v.n + 2)
                        v.tensor_add(out=qkbuf[:, hb:hb + T], in0=R4[:, 8192:8192 + T],
                                     in1=R4[:, 8192 + T:8192 + 2 * T]).then_inc(s_v.h, 1)
                    s_v.n += 3
                    q_slot_scalar()

                vrope = s_v.n
                if "d_qT" in dbg_t and sq == 0:
                    @blk.scalar
                    def _(s, c=c):
                        s.wait_ge(s_v.h, vrope)
                        s_st.dinc(s.dma_start(out=dbg_t["d_qT"][ts(c, 128), :],
                                              in_=qkbuf[:, 0:T]))
                        s_st.dinc(s.dma_start(out=dbg_t["d_kT"][ts(c, 128), :],
                                              in_=qkbuf[:, T:2 * T]))

                tsc_l = []
                sexp_l = []
                tav_l = []
                vust_l = []
                smv_l = []
                base_s = s_s.n
                smv_prev = s_mv.n
                vust_prev = s_v.n
                for qb in range(NQB):
                    for kt in range(NKT):
                        s_t.n += 1
                        tsc_l.append(s_t.n)
                        s_s.n += 1
                        sexp_l.append(s_s.n)
                    s_t.n += 1
                    tav_l.append(s_t.n)
                    s_v.n += 2
                    vust_l.append(s_v.n)
                    s_mv.n += 64
                    smv_l.append(s_mv.n)

                @blk.tensor
                def _(t, c=c, vrope=vrope, tsc_l=tsc_l, sexp_l=sexp_l,
                      tav_l=tav_l, vust_l=vust_l, vust_prev=vust_prev):
                    t.wait_ge(s_s.h, vdone_s)
                    t.wait_ge(s_v.h, vrope)
                    for qb in range(NQB):
                        for kt in range(NKT):
                            i = qb * NKT + kt
                            pq = (pw0, pw1)[kt % 2]
                            if i >= 2:
                                t.wait_ge(s_s.h, sexp_l[i - 2])
                            t.matmul(pq[:, 0:QB],
                                     qkbuf[0:64, T + kt * 128:T + kt * 128 + 128],
                                     qkbuf[0:64, qb * QB:(qb + 1) * QB],
                                     start=True, stop=True,
                                     tile_position=(0, 0))
                            t.matmul(pq[:, QB:2 * QB],
                                     qkbuf[64:128, T + kt * 128:T + kt * 128 + 128],
                                     qkbuf[64:128, qb * QB:(qb + 1) * QB],
                                     start=True, stop=True,
                                     tile_position=(64, 0)).then_inc(s_t.h, 1)
                        t.wait_ge(s_s.h, sexp_l[qb * NKT + NKT - 1])
                        t.wait_ge(s_v.h, vust_l[qb - 1] if qb > 0 else vust_prev)
                        for head in range(2):
                            pav = p6 if head == 0 else p7
                            h = 2 * c + head
                            for kt in range(NKT):
                                mm = t.matmul(pav[0:65, 0:QB],
                                              vA[:, (kt * NH + h) * 65:(kt * NH + h) * 65 + 65],
                                              ET[:, ts(head * NKT + kt, QB)],
                                              start=(kt == 0), stop=(kt == NKT - 1))
                        mm.then_inc(s_t.h, 1)

                @blk.scalar
                def _(s, c=c, tsc_l=tsc_l):
                    for qb in range(NQB):
                        for kt in range(NKT):
                            i = qb * NKT + kt
                            pq = (pw0, pw1)[kt % 2]
                            s.wait_ge(s_t.h, tsc_l[i])
                            dst = bass.AP(ET, kt * QB,
                                          [[ET.shape[1], 128], [NKT * QB, 2], [1, QB]])
                            s.activation(out=dst, in_=pq[:, 0:2 * QB].rearrange(
                                             "p (h q) -> p h q", h=2),
                                         func=ACT.Exp, scale=C(0.125),
                                         bias=C(-8.0)).then_inc(s_s.h, 1)

                @blk.vector
                def _(v, tav_l=tav_l, smv_l=smv_l, smv_prev=smv_prev):
                    for qb in range(NQB):
                        v.wait_ge(s_t.h, tav_l[qb])
                        v.wait_ge(s_mv.h, smv_l[qb - 1] if qb > 0 else smv_prev)
                        v.tensor_copy(out=R5[0:65, 3 * T:3 * T + QB],
                                      in_=p6[0:65, 0:QB]).then_inc(s_v.h, 1)
                        v.tensor_copy(out=R5[0:65, 3 * T + QB:3 * T + 2 * QB],
                                      in_=p7[0:65, 0:QB]).then_inc(s_v.h, 1)

                @blk.sync
                def _(sync, ci=ci, c=c, vust_l=vust_l):
                    for qb in range(NQB):
                        sync.wait_ge(s_v.h, vust_l[qb])
                        sync.dma_start(
                            out=UprH[0:64, ci * T + qb * QB: ci * T + qb * QB + QB],
                            in_=R5[0:64, 3 * T:3 * T + QB]).then_inc(s_mv.h, 16)
                        sync.dma_start(
                            out=UprH[64:128, ci * T + qb * QB: ci * T + qb * QB + QB],
                            in_=R5[0:64, 3 * T + QB:3 * T + 2 * QB]).then_inc(s_mv.h, 16)
                        sync.dma_start(out=Rr[2 * c:2 * c + 1, ts(qb, QB)],
                                       in_=R5[64:65, 3 * T:3 * T + QB]).then_inc(s_mv.h, 16)
                        sync.dma_start(out=Rr[2 * c + 1:2 * c + 2, ts(qb, QB)],
                                       in_=R5[64:65, 3 * T + QB:3 * T + 2 * QB]).then_inc(s_mv.h, 16)

            @blk.vector
            def _(v):
                v.wait_ge(s_mv.h, s_mv.n)
                with nc.allow_low_precision("1/r applied to bf16 matmul operands"):
                    v.reciprocal(out=Rrb[:, :], in_=Rr[:, :]).then_inc(s_v.h, 1)
            s_v.n += 1
            vr = s_v.n

            for ci in range(4):
                c = half * 4 + ci

                @blk.tensor
                def _(t, c=c):
                    t.wait_ge(s_v.h, vr)
                    t.wait_ge(s_v.h, s_v.n)
                    t.wait_ge(s_s.h, s_s.n)
                    for tb in range(T // 512):
                        t.matmul(p6[:, 0:512], e2t[:, ts(c, 128)],
                                 Rrb[:, tb * 512:(tb + 1) * 512],
                                 start=True, stop=True).then_inc(s_t.h, 1)
                        t.wait_ge(s_s.h, s_s.n + tb + 1)
                s_t.n += T // 512
                trb = s_t.n

                @blk.scalar
                def _(s, c=c):
                    for tb in range(T // 512):
                        s.wait_ge(s_t.h, trb - T // 512 + tb + 1)
                        s.copy(out=rbb[:, tb * 512:(tb + 1) * 512],
                               in_=p6[:, 0:512]).then_inc(s_s.h, 1)
                s_s.n += T // 512
                srb = s_s.n

                @blk.vector
                def _(v, ci=ci):
                    v.wait_ge(s_s.h, srb)
                    v.tensor_mul(out=UprH[:, ts(ci, T)], in0=UprH[:, ts(ci, T)],
                                 in1=rbb[:, 0:T]).then_inc(s_v.h, 1)
                s_v.n += 1
            unorm = s_v.n

            @blk.sync
            def _(sync, half=half):
                sync.wait_ge(s_v.h, unorm)
                for kc in range(4):
                    s_lw.dinc(sync.dma_start(out=R4[:, 8192 + kc * 1024:8192 + (kc + 1) * 1024],
                                             in_=wo_bs[ts(half * 4 + kc, 128), :]))
            ldo = s_lw.n

            for tt in range(NTS):
                @blk.sync
                def _(sync, tt=tt):
                    sync.wait_ge(s_v.h, s_v.n)
                    if half == 0:
                        s_lx.dinc(sync.dma_start(out=x1t[:, :],
                                                 in_=x_in[t0 + tt * 128:t0 + (tt + 1) * 128, :]))
                    else:
                        s_lx.dinc(sync.dma_start(out=x1t[:, :],
                                                 in_=x1_s[t0 + tt * 128:t0 + (tt + 1) * 128, :]))
                ldx = s_lx.n

                @blk.tensor
                def _(t, tt=tt, half=half):
                    t.wait_ge(s_lw.h, ldo)
                    if tt > 0:
                        t.wait_ge(s_v.h, s_v.n)
                    for kc in range(4):
                        t.matmul(pw0[:, 0:512],
                                 UprH[:, kc * T + tt * 128:kc * T + tt * 128 + 128],
                                 R4[:, 8192 + kc * 1024:8192 + kc * 1024 + 512],
                                 start=(kc == 0), stop=(half == 1 and kc == 3))
                        mm = t.matmul(pw0[:, 512:1024],
                                      UprH[:, kc * T + tt * 128:kc * T + tt * 128 + 128],
                                      R4[:, 8192 + kc * 1024 + 512:8192 + (kc + 1) * 1024],
                                      start=(kc == 0), stop=(half == 1 and kc == 3))
                    if half == 0:
                        t.matmul(pw0[:, 0:512], onesrow[0:1, :], biasrow[0:1, 0:512],
                                 start=False, stop=True)
                        mm = t.matmul(pw0[:, 512:1024], onesrow[0:1, :],
                                      biasrow[0:1, 512:1024], start=False, stop=True)
                    mm.then_inc(s_t.h, 1)
                s_t.n += 1
                two = s_t.n

                @blk.vector
                def _(v, tt=tt):
                    v.wait_ge(s_t.h, two)
                    v.wait_ge(s_lx.h, ldx)
                    v.tensor_add(out=x1t[:, :], in0=x1t[:, :],
                                 in1=pw0[:, 0:1024]).then_inc(s_v.h, 1)
                s_v.n += 1
                vx1 = s_v.n

                @blk.scalar
                def _(s, tt=tt, half=half):
                    s.wait_ge(s_v.h, vx1)
                    s_st.dinc(s.dma_start(out=x1_s[t0 + tt * 128:t0 + (tt + 1) * 128, :],
                                          in_=x1t[:, :]))
                    if "d_att" in dbg_t and half == 1:
                        s_st.dinc(s.dma_start(
                            out=dbg_t["d_att"][t0 + tt * 128:t0 + (tt + 1) * 128, :],
                            in_=x1t[:, :]))
                stx = s_st.n

                @blk.vector
                def _(v):
                    v.wait_ge(s_st.h, stx)
                    v.sem_inc(s_v.h, 1)
                s_v.n += 1

    def phase5():
        while True:
            if qst["qt_slot"] < len(qst["slots"]):
                q_slot_vector()
                q_slot_scalar()
                continue
            if qst["ld"] < len(qtasks):
                vcur = s_v.n

                @blk.scalar
                def _(sc_, vcur=vcur):
                    sc_.wait_ge(s_v.h, vcur)
                    q_issue_loads(sc_)
                continue
            break

    def phase6():
        @blk.sync
        def _(sync):
            sync.wait_ge(s_bar.h, s_bar.n)
            sync.wait_ge(s_qs.h, s_qs.n)
            for c in range(NCH):
                dstbuf = xqT_lo if c < 4 else xqT_hi
                s_tr.dinc(sync.dma_start(out=dstbuf[:, (c % 4) * N:(c % 4) * N + N],
                                         in_=xq_s[0:N, ts(c, 128)], transpose=True))
        trq = s_tr.n

        NPS = FFI // 1024 + (1 if FFI % 1024 else 0)
        for np_ in range(NPS):
            n0 = np_ * 1024
            nw = min(1024, FFI - n0)

            @blk.sync
            def _(sync, n0=n0, nw=nw):
                sync.wait_ge(s_t.h, s_t.n)
                sync.wait_ge(s_v.h, s_v.n)
                for k in range(NCH):
                    s_lw.dinc(sync.dma_start(out=wstr[:, k * 2048:k * 2048 + nw],
                                             in_=wgq_s[n0:n0 + nw, ts(k, 128)],
                                             transpose=True))
                    s_lw.dinc(sync.dma_start(out=wstr[:, k * 2048 + 1024:k * 2048 + 1024 + nw],
                                             in_=wuq_s[n0:n0 + nw, ts(k, 128)],
                                             transpose=True))
            ldgu = s_lw.n

            tgu_l = []
            sgs_l = []
            vh_l = []
            sth_l = []
            base6s = s_st.n
            for it in range(NT):
                s_t.n += 1
                tgu_l.append(s_t.n)
                s_s.n += 1
                sgs_l.append(s_s.n)
                s_v.n += 1
                vh_l.append(s_v.n)
                s_st.n += 16
                sth_l.append(s_st.n)

            @blk.tensor
            def _(t, n0=n0, nw=nw, tgu_l=tgu_l, vh_l=vh_l):
                t.wait_ge(s_lw.h, ldgu)
                t.wait_ge(s_tr.h, trq)
                for it in range(NT):
                    if it > 0:
                        t.wait_ge(s_v.h, vh_l[it - 1])
                    for k in range(NCH):
                        src = xqT_lo if k < 4 else xqT_hi
                        sl = src[:, (k % 4) * N + it * 128:(k % 4) * N + it * 128 + 128]
                        for nb in range(0, nw, 512):
                            ww = min(512, nw - nb)
                            t.matmul(pw0[:, nb:nb + ww], sl,
                                     wstr[:, k * 2048 + nb:k * 2048 + nb + ww],
                                     start=(k == 0), stop=(k == NCH - 1))
                            mm = t.matmul(pw1[:, nb:nb + ww], sl,
                                          wstr[:, k * 2048 + 1024 + nb:k * 2048 + 1024 + nb + ww],
                                          start=(k == 0), stop=(k == NCH - 1))
                    mm.then_inc(s_t.h, 1)

            @blk.scalar
            def _(s, n0=n0, nw=nw, tgu_l=tgu_l, vh_l=vh_l):
                for it in range(NT):
                    s.wait_ge(s_t.h, tgu_l[it])
                    s.activation(out=gsil_v[:, 0:nw], in_=pw0[:, 0:nw],
                                 func=ACT.Silu).then_inc(s_s.h, 1)
                    if it > 0:
                        s.wait_ge(s_v.h, vh_l[it - 1])
                        s.dma_start(out=h_s[ts(it - 1, 128), n0:n0 + nw],
                                    in_=junk[:, 0:nw]).then_inc(s_st.h, 16)
                s.wait_ge(s_v.h, vh_l[NT - 1])
                s.dma_start(out=h_s[ts(NT - 1, 128), n0:n0 + nw],
                            in_=junk[:, 0:nw]).then_inc(s_st.h, 16)

            @blk.vector
            def _(v, nw=nw, sgs_l=sgs_l, sth_l=sth_l, base6s=base6s):
                for it in range(NT):
                    v.wait_ge(s_s.h, sgs_l[it])
                    v.wait_ge(s_st.h, sth_l[it - 1] if it > 0 else base6s)
                    v.tensor_mul(out=junk[:, 0:nw], in0=gsil_v[:, 0:nw],
                                 in1=pw1[:, 0:nw]).then_inc(s_v.h, 1)

        for it in range(NT):
            @blk.sync
            def _(sync, it=it):
                sync.wait_ge(s_v.h, s_v.n)
                sync.wait_ge(s_st.h, s_st.n)
                s_lx.dinc(sync.dma_start(out=q_hi[:, 0:FFI], in_=h_s[ts(it, 128), :]))
            ldh = s_lx.n

            qs_tot = s_qs.n

            @blk.vector
            def _(v, qs_tot=qs_tot):
                v.wait_ge(s_qs.h, qs_tot)
                v.wait_ge(s_lx.h, ldh)
                emit_fp4(v, FFI, q_hi[:, 0:FFI])
            vhq = s_v.n

            @blk.scalar
            def _(s, it=it):
                s.wait_ge(s_v.h, vhq)
                s_st.dinc(s.dma_start(out=hq_s[ts(it, 128), :], in_=q_out[:, 0:FFI]))
                if "d_hq" in dbg_t:
                    s_st.dinc(s.dma_start(out=dbg_t["d_hq"][ts(it, 128), :],
                                          in_=q_out[:, 0:FFI]))
            sthq = s_st.n

            @blk.vector
            def _(v):
                v.wait_ge(s_st.h, sthq)
                v.sem_inc(s_v.h, 1)
            s_v.n += 1

        @blk.sync
        def _(sync):
            sync.wait_ge(s_v.h, s_v.n)
            sync.wait_ge(s_t.h, s_t.n)
            for kc in range(FFI // 128):
                if kc < 16:
                    dst = wdT_a[:, kc * 1024:(kc + 1) * 1024]
                else:
                    dst = wdT_b[:, (kc - 16) * 1024:(kc - 15) * 1024]
                s_tr.dinc(sync.dma_start(out=dst, in_=wdq_s[0:D, ts(kc, 128)],
                                         transpose=True))
        trd = s_tr.n

        NB4 = NT // 4
        for b4 in range(NB4):
            @blk.sync
            def _(sync, b4=b4):
                sync.wait_ge(s_t.h, s_t.n)
                for kc in range(FFI // 128):
                    s_th.dinc(sync.dma_start(out=hqTb[:, kc * 512:(kc + 1) * 512],
                                             in_=hq_s[b4 * 512:(b4 + 1) * 512, ts(kc, 128)],
                                             transpose=True))
            trh = s_th.n

            for i4 in range(4):
                it = b4 * 4 + i4

                @blk.sync
                def _(sync, it=it):
                    sync.wait_ge(s_v.h, s_v.n)
                    s_lx.dinc(sync.dma_start(out=x1t[:, :], in_=x1_s[ts(it, 128), :]))
                ldx1 = s_lx.n

                @blk.tensor
                def _(t, i4=i4):
                    t.wait_ge(s_th.h, trh)
                    t.wait_ge(s_tr.h, trd)
                    if True:
                        t.wait_ge(s_v.h, s_v.n)
                    for kc in range(FFI // 128):
                        if kc < 16:
                            wsl = wdT_a[:, kc * 1024:(kc + 1) * 1024]
                        else:
                            wsl = wdT_b[:, (kc - 16) * 1024:(kc - 15) * 1024]
                        hsl = hqTb[:, kc * 512 + i4 * 128:kc * 512 + i4 * 128 + 128]
                        t.matmul(pw0[:, 0:512], hsl, wsl[:, 0:512],
                                 start=(kc == 0), stop=(kc == FFI // 128 - 1))
                        mm = t.matmul(pw0[:, 512:1024], hsl, wsl[:, 512:1024],
                                      start=(kc == 0), stop=(kc == FFI // 128 - 1))
                    mm.then_inc(s_t.h, 1)
                s_t.n += 1
                tdn = s_t.n

                @blk.vector
                def _(v, it=it):
                    v.wait_ge(s_t.h, tdn)
                    v.wait_ge(s_lx.h, ldx1)
                    v.tensor_add(out=x1t[:, :], in0=x1t[:, :],
                                 in1=pw0[:, 0:1024]).then_inc(s_v.h, 1)
                s_v.n += 1
                vf = s_v.n

                @blk.scalar
                def _(s, it=it):
                    s.wait_ge(s_v.h, vf)
                    s_st.dinc(s.dma_start(out=x1_s[ts(it, 128), :], in_=x1t[:, :]))
                stf = s_st.n

                @blk.vector
                def _(v):
                    v.wait_ge(s_st.h, stf)
                    v.sem_inc(s_v.h, 1)
                s_v.n += 1

    def phase8a():
        for wsrc, wdst, rows in ((W_in, wi_s, D), (W_attn, wa_s, 656), (W_out, wo2_s, D)):
            nrt = (rows + 127) // 128
            for rt in range(nrt):
                rr = min(128, rows - rt * 128)

                @blk.sync
                def _(sync, wsrc=wsrc, rt=rt, rr=rr):
                    sync.wait_ge(s_v.h, s_v.n)
                    s_lw.dinc(sync.dma_start(out=wfp[0:rr, :],
                                             in_=wsrc[rt * 128:rt * 128 + rr, :]))
                ldr = s_lw.n

                @blk.vector
                def _(v, rr=rr):
                    v.wait_ge(s_lw.h, ldr)
                    v.tensor_copy(out=wbuf[0:rr, 0:1024], in_=wfp[0:rr, :]).then_inc(s_v.h, 1)
                s_v.n += 1
                vcp = s_v.n

                @blk.scalar
                def _(s, wdst=wdst, rt=rt, rr=rr):
                    s.wait_ge(s_v.h, vcp)
                    s_st.dinc(s.dma_start(out=wdst[rt * 128:rt * 128 + rr, :],
                                          in_=wbuf[0:rr, 0:1024]))
                str_ = s_st.n

                @blk.vector
                def _(v):
                    v.wait_ge(s_st.h, str_)
                    v.sem_inc(s_v.h, 1)
                s_v.n += 1

        for rt in range(NCH):
            @blk.sync
            def _(sync, rt=rt):
                sync.wait_ge(s_v.h, s_v.n)
                s_lw.dinc(sync.dma_start(out=wfp[:, :], in_=W_l1[ts(rt, 128), :]))
            ldr = s_lw.n

            @blk.vector
            def _(v, rt=rt):
                v.wait_ge(s_lw.h, ldr)
                v.reduce_sum(out=stat[:, ts(rt, 1)], in_=wfp[:, :], axis=AX.X,
                             apply_absolute_value=True).then_inc(s_v.h, 1)
                v.wait_ge(s_v.h, s_v.n + 1)
                v.sem_inc(s_v.h, 1)
            s_v.n += 2
        vws = s_v.n

        @blk.vector
        def _(v):
            v.wait_ge(s_v.h, vws)
            v.reduce_sum(out=stat[:, NCH:NCH + 1], in_=stat[:, 0:NCH],
                         axis=AX.X).then_inc(s_v.h, 1)
        s_v.n += 1
        vws2 = s_v.n

        @blk.tensor
        def _(t):
            t.wait_ge(s_v.h, vws2)
            t.matmul(p4[0:1, 0:1], C(1.0), stat[:, NCH:NCH + 1],
                     start=True, stop=True).then_inc(s_t.h, 1)
        s_t.n += 1
        tws = s_t.n

        @blk.vector
        def _(v):
            v.wait_ge(s_t.h, tws)
            v.tensor_scalar(out=stat[0:1, 0:1], in0=p4[0:1, 0:1],
                            scalar1=1.0 / (D * D), scalar2=1e-8,
                            op0=ALU.mult, op1=ALU.add).then_inc(s_v.h, 1)
            v.wait_ge(s_v.h, s_v.n + 1)
            v.reciprocal(out=stat[0:1, 1:2], in_=stat[0:1, 0:1]).then_inc(s_v.h, 1)
        s_v.n += 2
        vws3 = s_v.n

        @blk.scalar
        def _(s):
            s.wait_ge(s_v.h, vws3)
            s_st.dinc(s.dma_start(out=scal_s[0:2], in_=stat[0:1, 0:2]))
            if "d_ws" in dbg_t:
                s_st.dinc(s.dma_start(out=dbg_t["d_ws"][0:1, :], in_=stat[0:1, 0:2]))
        stws = s_st.n

        @blk.sync
        def _(sync):
            sync.wait_ge(s_st.h, stws)
            s_lx.dinc(sync.dma_start(out=wl1s[:, 0:1],
                                     in_=scal_s[0:1].unsqueeze(0).broadcast_to([128, 1])))
            s_lx.dinc(sync.dma_start(out=wl1r[:, 0:1],
                                     in_=scal_s[1:2].unsqueeze(0).broadcast_to([128, 1])))
        ldws = s_lx.n

        for rt in range(NCH):
            @blk.sync
            def _(sync, rt=rt):
                sync.wait_ge(s_v.h, s_v.n)
                s_lw.dinc(sync.dma_start(out=wfp[:, :], in_=W_l1[ts(rt, 128), :]))
            ldr = s_lw.n

            @blk.vector
            def _(v):
                v.wait_ge(s_lw.h, ldr)
                v.wait_ge(s_lx.h, ldws)

                def st(inst):
                    s_v.inc(inst)
                    v.wait_ge(s_v.h, s_v.n)
                st(v.tensor_scalar(out=q_vI[:, 0:D].bitcast(F32), in0=wfp[:, :],
                                   scalar1=wl1r[:, 0:1], scalar2=M24,
                                   op0=ALU.mult, op1=ALU.add))
                st(v.tensor_scalar(out=q_vI[:, 0:D].bitcast(F32),
                                   in0=q_vI[:, 0:D].bitcast(F32),
                                   scalar1=M24, scalar2=-1.0,
                                   op0=ALU.subtract, op1=ALU.max))
                st(v.tensor_scalar(out=q_out[:, 0:D], in0=q_vI[:, 0:D].bitcast(F32),
                                   scalar1=1.0, scalar2=None, op0=ALU.min))
            vqw = s_v.n

            @blk.scalar
            def _(s, rt=rt):
                s.wait_ge(s_v.h, vqw)
                s_st.dinc(s.dma_start(out=wl1q_s[ts(rt, 128), :], in_=q_out[:, 0:D]))
                if "d_wl1q" in dbg_t:
                    s_st.dinc(s.dma_start(out=dbg_t["d_wl1q"][ts(rt, 128), :],
                                          in_=q_out[:, 0:D]))
            stw = s_st.n

            @blk.vector
            def _(v):
                v.wait_ge(s_st.h, stw)
                v.sem_inc(s_v.h, 1)
            s_v.n += 1

    def phase8(sq):
        t0 = sq * T
        prs = pr_flat.rearrange("(a) -> a")

        @blk.sync
        def _(sync):
            sync.wait_ge(s_bar.h, s_bar.n)
            sync.wait_ge(s_out.h, s_out.n)
            sync.wait_ge(s_v.h, s_v.n)
            sync.wait_ge(s_st.h, s_st.n)
            s_st.dinc(sync.dma_start(out=u_s[sq, 0:D, 0:64],
                                     in_=zpad[:, 0:512]))
            s_st.dinc(sync.dma_start(out=u_s[sq, 0:D, T + 64:T + 128],
                                     in_=zpad[:, 0:512]))
            for c in range(NCH):
                s_tr.dinc(sync.dma_start(out=xn2T[:, ts(c, T)],
                                         in_=xn2_s[t0:t0 + T, ts(c, 128)], transpose=True))
            for k in range(NCH):
                s_tr.dinc(sync.dma_start(out=wiT[:, ts(k, 1024)],
                                         in_=wi_s[0:D, ts(k, 128)], transpose=True))
        trwi = s_tr.n

        psl8 = [pw0[:, 0:512], pw0[:, 512:1024], pw1[:, 0:512], pw1[:, 512:1024]]
        for e in range(NCH):
            @blk.tensor
            def _(t, e=e):
                t.wait_ge(s_tr.h, trwi)
                if e > 0:
                    t.wait_ge(s_s.h, s_s.n)
                for k in range(NCH):
                    for tb in range(T // 512):
                        mm = t.matmul(psl8[tb],
                                      wiT[:, k * 1024 + e * 128:k * 1024 + e * 128 + 128],
                                      xn2T[:, k * T + tb * 512:k * T + (tb + 1) * 512],
                                      start=(k == 0), stop=(k == NCH - 1))
                mm.then_inc(s_t.h, 1)
            s_t.n += 1
            twi = s_t.n

            @blk.scalar
            def _(s, e=e):
                s.wait_ge(s_t.h, twi)
                for tb in range(T // 512):
                    s.copy(out=uT[:, e * T + tb * 512:e * T + (tb + 1) * 512],
                           in_=psl8[tb]).then_inc(s_s.h, 1)
                    s.activation(out=ruT[:, e * T + tb * 512:e * T + (tb + 1) * 512],
                                 in_=psl8[tb], func=ACT.Relu).then_inc(s_s.h, 1)
            s_s.n += 2 * (T // 512)
            swi = s_s.n

            @blk.scalar
            def _(s, e=e):
                s.wait_ge(s_s.h, swi)
                s_st.dinc(s.dma_start(out=u_s[sq, ts(e, 128), 64:64 + T], in_=uT[:, ts(e, T)]))
                if "d_u" in dbg_t and sq == 0:
                    s_st.dinc(s.dma_start(out=dbg_t["d_u"][ts(e, 128), :],
                                          in_=uT[:, ts(e, T)]))
        uwdone = s_st.n

        @blk.sync
        def _(sync):
            sync.wait_ge(s_st.h, uwdone)
            for k in range(NCH):
                s_tr.dinc(sync.dma_start(out=waT[:, ts(k, 656)],
                                         in_=wa_s[0:656, ts(k, 128)], transpose=True))
        trwa = s_tr.n

        for tt in range(NTS):
            @blk.tensor
            def _(t, tt=tt):
                t.wait_ge(s_tr.h, trwa)
                if tt > 0:
                    t.wait_ge(s_st.h, s_st.n)
                for k in range(NCH):
                    sl = ruT[:, k * T + tt * 128:k * T + tt * 128 + 128]
                    t.matmul(pw0[:, 0:512], sl, waT[:, k * 656:k * 656 + 512],
                             start=(k == 0), stop=(k == NCH - 1))
                    mm = t.matmul(pw0[:, 512:656], sl, waT[:, k * 656 + 512:k * 656 + 656],
                                  start=(k == 0), stop=(k == NCH - 1))
                mm.then_inc(s_t.h, 1)
            s_t.n += 1
            tal = s_t.n

            @blk.vector
            def _(v, tt=tt):
                if tt == 0:
                    v.wait_ge(s_st.h, uwdone)
                    v.memset(PW[:, :], 0.0).then_inc(s_v.h, 1)
                else:
                    v.sem_inc(s_v.h, 1)
            s_v.n += 1
            vpw0_ = s_v.n

            @blk.scalar
            def _(s, tt=tt):
                s.wait_ge(s_t.h, tal)
                s.wait_ge(s_v.h, vpw0_)
                for sc_i in range(2):
                    cw = CWS[sc_i]
                    srcap = pw0[:, (0 if sc_i == 0 else 328):(168 if sc_i == 0 else 656)]
                    dst = PW[:, PRB[sc_i]:PRB[sc_i] + 8 * SLOTP[sc_i]]
                    s.activation(out=dst.rearrange("p (h w) -> p h w", w=SLOTP[sc_i])[:, :, 0:cw],
                                 in_=srcap.rearrange("p (h w) -> p h w", w=cw),
                                 func=ACT.Exp).then_inc(s_s.h, 1)
            s_s.n += 2
            spw = s_s.n

            @blk.vector
            def _(v, tt=tt):
                v.wait_ge(s_s.h, spw)

                def st(inst):
                    s_v.inc(inst)
                    v.wait_ge(s_v.h, s_v.n)
                for sc_i in range(2):
                    cw = CWS[sc_i]
                    dst = PW[:, PRB[sc_i]:PRB[sc_i] + 8 * SLOTP[sc_i]]
                    view = dst.rearrange("p (h w) -> p h w", w=SLOTP[sc_i])[:, :, 0:cw]
                    st(v.reduce_sum(out=prz[:, ts(sc_i, 8)].unsqueeze(2),
                                    in_=view, axis=AX.X))
                st(v.reciprocal(out=prz[:, 0:16], in_=prz[:, 0:16]))
                st(v.tensor_scalar(out=prz[:, 0:8], in0=prz[:, 0:8],
                                   scalar1=tiny[:, 8:9], scalar2=None, op0=ALU.mult))
                st(v.tensor_scalar(out=prz[:, 8:16], in0=prz[:, 8:16],
                                   scalar1=tiny[:, 9:10], scalar2=None, op0=ALU.mult))
                for sc_i in range(2):
                    cw = CWS[sc_i]
                    dst = PW[:, PRB[sc_i]:PRB[sc_i] + 8 * SLOTP[sc_i]]
                    view = dst.rearrange("p (h w) -> p h w", w=SLOTP[sc_i])[:, :, 0:cw]
                    st(v.tensor_mul(out=view, in0=view,
                                    in1=prz[:, ts(sc_i, 8)].unsqueeze(2).broadcast_to([128, 8, cw])))
            vpw = s_v.n

            @blk.scalar
            def _(s, tt=tt):
                s.wait_ge(s_v.h, vpw)
                base = (sq * T + tt * 128) * PRP
                dst = prs[base:base + 128 * PRP].rearrange("(p w) -> p w", w=PRP)
                s_st.dinc(s.dma_start(out=dst, in_=PW[:, :]))

        aldone = s_st.n

        @blk.sync
        def _(sync):
            sync.wait_ge(s_st.h, aldone)
            sync.wait_ge(s_t.h, s_t.n)
            for kt in range(NKTP):
                s_tr.dinc(sync.dma_start(out=upad[:, ts(kt, 1024)],
                                         in_=u_s[sq, 0:D, kt * 128:kt * 128 + 128],
                                         transpose=True))
        trup = s_tr.n

        @blk.vector
        def _(v):
            v.wait_ge(s_st.h, aldone)
            v.sem_inc(s_v.h, 1)
        s_v.n += 1
        vup = s_v.n

        phase8_window(sq, vup)

    def phase8_window(sq, vup):
        prs = pr_flat
        for c in range(NCH):
            for tg in range(NTS // 4):
                @blk.tensor
                def _(t, c=c, tg=tg):
                    t.wait_ge(s_v.h, vup)
                    t.wait_ge(s_tr.h, s_tr.n)
                    if not (c == 0 and tg == 0):
                        t.wait_ge(s_s.h, s_s.n)
                    t.sem_inc(s_t.h, 1)
                s_t.n += 1
                gat = s_t.n

                for i4 in range(4):
                    tt = tg * 4 + i4

                    @blk.sync
                    def _(sync, c=c, tt=tt, i4=i4, gat=gat):
                        sync.wait_ge(s_t.h, gat + i4)
                        for sc_i in range(2):
                            for k2 in range(2):
                                ktp = tt + k2
                                base = ((sq * T + tt * 128) * PRP
                                        + PRB[sc_i] + c * SLOTP[sc_i] + PADS[sc_i]
                                        + (ktp - tt) * 128 - UPAD)
                                ap = bass.AP(prs.tensor, base,
                                             [[PRP - 1, 128], [1, 128]])
                                s_tr.dinc(sync.dma_start(
                                    out=Btile[:, ts(sc_i * 2 + k2, 128)],
                                    in_=ap, transpose=True))
                    trbt = s_tr.n

                    @blk.tensor
                    def _(t, c=c, tt=tt, i4=i4):
                        t.wait_ge(s_tr.h, trbt)
                        first = True
                        for sc_i in range(2):
                            for k2 in range(2):
                                ktp = tt + k2
                                mm = t.matmul(p6[:, ts(i4, 128)],
                                              upad[:, ktp * 1024 + c * 128:ktp * 1024 + c * 128 + 128],
                                              Btile[:, ts(sc_i * 2 + k2, 128)],
                                              start=first, stop=(sc_i == 1 and k2 == 1))
                                first = False
                        mm.then_inc(s_t.h, 1)
                    s_t.n += 1
                tw4 = s_t.n

                @blk.scalar
                def _(s, c=c, tg=tg):
                    s.wait_ge(s_t.h, tw4)
                    s.copy(out=owT[:, c * T + tg * 512:c * T + (tg + 1) * 512],
                           in_=p6[:, 0:512]).then_inc(s_s.h, 1)
                s_s.n += 1

        if "d_ow" in dbg_t and sq == 0:
            @blk.scalar
            def _(s):
                s.wait_ge(s_s.h, s_s.n)
                for c in range(NCH):
                    s_st.dinc(s.dma_start(out=dbg_t["d_ow"][ts(c, 128), :],
                                          in_=owT[:, ts(c, T)]))

    def phase9(sq):
        t0 = sq * T

        @blk.sync
        def _(sync):
            sync.wait_ge(s_s.h, s_s.n)
            for k in range(NCH):
                s_tr.dinc(sync.dma_start(out=woutT[:, ts(k, 1024)],
                                         in_=wo2_s[0:D, ts(k, 128)], transpose=True))
                s_tr.dinc(sync.dma_start(out=wl1qT[:, ts(k, 1024)],
                                         in_=wl1q_s[0:D, ts(k, 128)], transpose=True))
                s_tr.dinc(sync.dma_start(out=aqT[:, ts(k, T)],
                                         in_=aq_s[t0:t0 + T, ts(k, 128)], transpose=True))
        tr9 = s_tr.n

        for tt in range(NTS):
            it = sq * NTS + tt

            @blk.sync
            def _(sync, it=it):
                sync.wait_ge(s_v.h, s_v.n)
                s_lx.dinc(sync.dma_start(out=x1t[:, :], in_=x1_s[ts(it, 128), :]))
            ldx1 = s_lx.n

            @blk.tensor
            def _(t, tt=tt):
                t.wait_ge(s_tr.h, tr9)
                if tt > 0:
                    t.wait_ge(s_v.h, s_v.n)
                for k in range(NCH):
                    osl = owT[:, k * T + tt * 128:k * T + tt * 128 + 128]
                    t.matmul(pw0[:, 0:512], osl, woutT[:, k * 1024:k * 1024 + 512],
                             start=(k == 0), stop=(k == NCH - 1))
                    t.matmul(pw0[:, 512:1024], osl, woutT[:, k * 1024 + 512:k * 1024 + 1024],
                             start=(k == 0), stop=(k == NCH - 1))
                for k in range(NCH):
                    asl = aqT[:, k * T + tt * 128:k * T + tt * 128 + 128]
                    t.matmul(pw1[:, 0:512], asl, wl1qT[:, k * 1024:k * 1024 + 512],
                             start=(k == 0), stop=(k == NCH - 1))
                    mm = t.matmul(pw1[:, 512:1024], asl, wl1qT[:, k * 1024 + 512:k * 1024 + 1024],
                                  start=(k == 0), stop=(k == NCH - 1))
                mm.then_inc(s_t.h, 1)
            s_t.n += 1
            t9 = s_t.n

            @blk.vector
            def _(v, it=it):
                v.wait_ge(s_t.h, t9)
                v.wait_ge(s_lx.h, ldx1)
                v.scalar_tensor_tensor(out=junk[:, 0:D], in0=pw1[:, 0:1024],
                                       scalar=asws[:, ts(it, 1)], in1=x1t[:, :],
                                       op0=ALU.mult, op1=ALU.add).then_inc(s_v.h, 1)
                v.wait_ge(s_v.h, s_v.n + 1)
                v.tensor_add(out=junk[:, 0:D], in0=junk[:, 0:D],
                             in1=pw0[:, 0:1024]).then_inc(s_v.h, 1)
            s_v.n += 2
            v9 = s_v.n

            @blk.scalar
            def _(s, it=it):
                s.wait_ge(s_v.h, v9)
                s_out.dinc(s.dma_start(out=out_t[ts(it, 128), :], in_=junk[:, 0:D]))
            o9 = s_out.n

            @blk.vector
            def _(v):
                v.wait_ge(s_out.h, o9)
                v.sem_inc(s_v.h, 1)
            s_v.n += 1

    phase0a()
    phase0b()
    if stop >= 1:
        phase_ln(x_in, xn_s, xq_s, layer=0)
    barrier()
    if stop >= 2:
        for sq in range(NSEQ):
            phase234(sq)
        barrier()
    if stop >= 5:
        phase5()
        phase6()
        barrier()
    if stop >= 7:
        phase8a()
        phase_ln(x1_s, xn2_s, aq_s, layer=1)
        barrier()
    if stop >= 8:
        for sq in range(NSEQ):
            phase8(sq)
            phase9(sq)

    if stop < 8:
        @blk.vector
        def _(v):
            v.wait_ge(s_bar.h, s_bar.n)
            v.memset(junk[:, 0:D], 0.0).then_inc(s_v.h, 1)
        s_v.n += 1
        zv = s_v.n

        @blk.scalar
        def _(s):
            s.wait_ge(s_v.h, zv)
            for it in range(NT):
                s_out.dinc(s.dma_start(out=out_t[ts(it, 128), :], in_=junk[:, 0:D]))

    @blk.gpsimd
    def _(g):
        g.wait_ge(s_out.h, s_out.n)
        g.wait_ge(s_st.h, s_st.n)
        g.wait_ge(s_mv.h, s_mv.n)
        g.wait_ge(s_qs.h, s_qs.n)

    es.close()
    return nc


def _host_consts(T):
    iota = np.broadcast_to(np.arange(T, dtype=np.float32), (128, T)).copy()
    rope_e = np.zeros((128, 1), np.float32)
    rope_sm = np.zeros((128, 1), np.float32)
    for p in range(128):
        r = p % 64
        if r >= 32:
            j = r - 32
            sgn = -1.0
            if j >= 16:
                j -= 16
                sgn = 1.0
            rope_e[p, 0] = -j / 16.0
            rope_sm[p, 0] = sgn * 2.0 * np.pi / max(T - 1, 1)
    sperm = np.zeros((128, 128), np.float32)
    for m in range(128):
        r = m % 64
        if 32 <= r < 48:
            k = m + 16
        elif 48 <= r < 64:
            k = m - 16
        else:
            k = m
        sperm[k, m] = 1.0
    e2 = np.zeros((16, 1024), np.float32)
    for c in range(8):
        for m in range(128):
            e2[2 * c + (1 if m >= 64 else 0), c * 128 + m] = 1.0
    return {
        "iota_t": iota,
        "rope_e": rope_e,
        "rope_sm": rope_sm,
        "sperm": sperm.astype(ml_dtypes.bfloat16),
        "e2sel": e2.astype(ml_dtypes.bfloat16),
    }


def make_in_maps(inputs, cfg, n_cores=8):
    T, NSEQ = cfg["T"], cfg["NSEQ"]
    consts = _host_consts(T)
    x = np.ascontiguousarray(np.asarray(inputs["L0_attn_x"]), dtype=np.float32)
    B = x.shape[0]
    shared = {
        "Wq": inputs["L0_attn_Wq"], "Wk": inputs["L0_attn_Wk"],
        "Wv": inputs["L0_attn_Wv"], "Wo": inputs["L0_attn_Wo"],
        "bq": inputs["L0_attn_bq"], "bk": inputs["L0_attn_bk"],
        "bv": inputs["L0_attn_bv"], "bo": inputs["L0_attn_bo"],
        "nrope": inputs["L0_attn_nrope_freq"],
        "w_gate": inputs["L0_ffn_w_gate"], "w_up": inputs["L0_ffn_w_up"],
        "w_down": inputs["L0_ffn_w_down"],
        "W_in": inputs["L1_attn_W_in"], "W_attn": inputs["L1_attn_W_attn"],
        "W_out": inputs["L1_attn_W_out"], "sw": inputs["L1_attn_scale_weights"],
        "W_l1": inputs["L1_ffn_weight"],
    }
    shared = {k: np.ascontiguousarray(np.asarray(v), dtype=np.float32)
              for k, v in shared.items()}
    shared.update(consts)
    per_core = B // n_cores
    maps = []
    for c in range(n_cores):
        m = dict(shared)
        m["x"] = np.ascontiguousarray(
            x[c * per_core:(c + 1) * per_core].reshape(per_core * T, D))
        maps.append(m)
    return maps


def kernel(**inputs):
    cfg = {"T": 2048, "NSEQ": 2}
    nc = build(cfg)
    maps = make_in_maps(inputs, cfg, n_cores=8)
    res = run_bass_kernel_spmd(nc, maps, core_ids=list(range(8)))
    per_core = 16 // 8
    out = np.concatenate(
        [res.results[c]["out"].reshape(per_core, cfg["T"], D) for c in range(8)],
        axis=0)
    return out.astype(np.float32)


# version 04
# speedup vs baseline: 1.3377x; 1.0441x over previous
import sys
if "/opt/trn_rl_repo" not in sys.path:
    sys.path.insert(0, "/opt/trn_rl_repo")

from contextlib import ExitStack

import numpy as np
import ml_dtypes

import concourse.bass as bass
import concourse.mybir as mybir
from concourse.bass_utils import run_bass_kernel_spmd

F32 = mybir.dt.float32
BF16 = mybir.dt.bfloat16
I32 = mybir.dt.int32
AX = mybir.AxisListType
ALU = mybir.AluOpType
ACT = mybir.ActivationFunctionType

D = 1024
NH = 16
FFI = 2816
NCH = 8
CWS = (21, 41)
PADS = (10, 20)
UPAD = 64
GAP = 192
SLOTP = [CWS[0] + GAP, CWS[1] + GAP]
PRB = [GAP // 2, GAP // 2 + 8 * SLOTP[0]]
PRP = GAP // 2 + 8 * SLOTP[0] + 8 * SLOTP[1] + GAP // 2
M22 = float(3 * 2 ** 21)
M24 = float(3 * 2 ** 22)
BFREQ = 10000.0
MIN_L = 0.83
SDT_L = (BFREQ - MIN_L * BFREQ) * 2.0
EPS_LN = 1e-5
PI = float(np.pi)
NEG4M = -4194304


def ts(i, size):
    return slice(i * size, (i + 1) * size)


class Sem:
    def __init__(self, nc, es, name):
        self.h = es.enter_context(nc.semaphore(name))
        self.n = 0

    def inc(self, inst, k=1):
        inst.then_inc(self.h, k)
        self.n += k
        return self.n

    def dinc(self, inst):
        inst.then_inc(self.h, 16)
        self.n += 16
        return self.n


def build(cfg):
    T = cfg["T"]
    NSEQ = cfg["NSEQ"]
    N = T * NSEQ
    NT = N // 128
    NTS = T // 128
    QB = min(512, T)
    NQB = T // QB
    NKT = T // 128
    NKTP = (T + 2 * UPAD) // 128
    LNG = min(8, NT)
    dbg = set(cfg.get("debug", ()))
    stop = cfg.get("stop_after", 99)

    nc = bass.Bass("TRN2", target_bir_lowering=False, debug=False)

    def inp(name, shape, dt=F32):
        return nc.dram_tensor(name, shape, dt, kind="ExternalInput").ap()

    x_in = inp("x", [N, D])
    Wq = inp("Wq", [D, D]); Wk = inp("Wk", [D, D]); Wv = inp("Wv", [D, D])
    Wo = inp("Wo", [D, D])
    bq = inp("bq", [D]); bk = inp("bk", [D]); bv = inp("bv", [D]); bo = inp("bo", [D])
    nrope = inp("nrope", [1])
    w_gate = inp("w_gate", [FFI, D]); w_up = inp("w_up", [FFI, D])
    w_down = inp("w_down", [D, FFI])
    W_in = inp("W_in", [D, D]); W_attn = inp("W_attn", [656, D])
    W_out = inp("W_out", [D, D]); sw_in = inp("sw", [2])
    W_l1 = inp("W_l1", [D, D])
    iota_t = inp("iota_t", [128, T])
    rope_e = inp("rope_e", [128, 1])
    rope_sm = inp("rope_sm", [128, 1])
    sperm = nc.dram_tensor("sperm", [128, 128], BF16, kind="ExternalInput").ap()
    e2sel = nc.dram_tensor("e2sel", [16, 1024], BF16, kind="ExternalInput").ap()

    out_t = nc.dram_tensor("out", [N, D], F32, kind="ExternalOutput").ap()
    dbg_t = {}
    for name, shape, dt in [
        ("d_xn", [N, D], F32), ("d_xq", [N, D], BF16),
        ("d_qT", [D, T], BF16), ("d_kT", [D, T], BF16),
        ("d_v", [128, NH * 65], BF16), ("d_att", [N, D], F32),
        ("d_hq", [N, FFI], BF16), ("d_xn2", [N, D], F32),
        ("d_u", [D, T], BF16), ("d_ow", [D, T], BF16),
        ("d_aq", [N, D], BF16), ("d_wgq", [FFI, D], BF16),
        ("d_wl1q", [D, D], BF16), ("d_ws", [1, 2], F32),
    ]:
        if name in dbg:
            dbg_t[name] = nc.dram_tensor(name, shape, dt, kind="ExternalOutput").ap()

    def scr(name, shape, dt=BF16):
        return nc.dram_tensor(name, shape, dt, kind="Internal").ap()

    xn_s = scr("xn_s", [N, D])
    xq_s = scr("xq_s", [N, D])
    h_s = scr("h_s", [N, FFI], F32)
    hq_s = scr("hq_s", [N, FFI])
    x1_s = scr("x1_s", [N, D], F32)
    xn2_s = scr("xn2_s", [N, D])
    aq_s = scr("aq_s", [N, D])
    wq_bs = scr("wq_bs", [D, D]); wk_bs = scr("wk_bs", [D, D])
    wv_bs = scr("wv_bs", [D, D]); wo_bs = scr("wo_bs", [D, D])
    wgq_s = scr("wgq_s", [FFI, D]); wuq_s = scr("wuq_s", [FFI, D])
    wdq_s = scr("wdq_s", [D, FFI])
    wl1q_s = scr("wl1q_s", [D, D])
    wi_s = scr("wi_s", [D, D]); wa_s = scr("wa_s", [656, D]); wo2_s = scr("wo2_s", [D, D])
    u_s = scr("u_s", [NSEQ, D, T + 128])
    pr_flat = scr("pr_s", [NSEQ * T * PRP])
    scal_s = scr("scal_s", [8], F32)

    es = ExitStack()

    def sb(name, shape, dt=F32):
        return es.enter_context(nc.sbuf_tensor(name, shape, dt))

    R1N = NCH * T
    R1 = es.enter_context(nc.sbuf_tensor("R1", [128, max(R1N, 4 * N, 8 * T, 2 * LNG * D)], BF16))
    R2 = es.enter_context(nc.sbuf_tensor("R2", [128, max(2 * NKT * QB, 4 * N, 8 * T, 8192)], BF16))
    R3 = es.enter_context(nc.sbuf_tensor("R3", [128, max(NKT * NH * 65, NKTP * 1024, 22 * 512 + 2048, NCH * T)], BF16))
    R4 = es.enter_context(nc.sbuf_tensor("R4", [128, max(NCH * T, 16384)], BF16))
    R5 = es.enter_context(nc.sbuf_tensor("R5", [128, max(4 * T, NCH * 656 + PRP, 8192 + 1024)], BF16))

    xg = R1[:].bitcast(F32)
    xnT = R1
    Upr = R1
    xqT_lo = R1
    uT = R1
    aqT = R1
    ET = R2
    xqT_hi = R2
    ruT = R2
    wl1qT = R2
    vA = R3
    hqTb = R3
    gsil_v = R3[:, 22 * 512:22 * 512 + 2048].bitcast(F32)
    xn2T = R3
    upad = R3
    WVr = R4[:, 0:NCH * 1024]
    WQs = R4[:, 8192:12288]
    WKs = R4[:, 12288:16384]
    rope_t1 = R4[:, 8192:8192 + T]
    rope_sq = R4[:, 8192 + T:8192 + 2 * T]
    WOr = R4[:, 0:NCH * 1024]
    wstr = R4
    wdT_a = R4
    owT = R4
    qkbuf = R5
    rbb = R5[:, 2 * T:3 * T]
    UstA = R5[:, 3 * T:3 * T + QB]
    UstB = R5[:, 3 * T + QB:3 * T + 2 * QB]
    wiT = R5
    waT = R5[:, 0:NCH * 656]
    PW = R5[:, NCH * 656:NCH * 656 + PRP]
    woutT = R5
    wdT_b = R5

    q_vI = es.enter_context(nc.sbuf_tensor("q_vI", [128, FFI], I32))
    q_2I = es.enter_context(nc.sbuf_tensor("q_2I", [128, FFI], I32))
    q_out = es.enter_context(nc.sbuf_tensor("q_out", [128, FFI], BF16))
    q_bm = sb("q_bm", [128, FFI // 16])
    q_sc = sb("q_sc", [128, FFI // 16])

    junk = sb("junk", [128, max(FFI + 32, T + 64)])
    junkb = junk[:].bitcast(BF16)
    q_hi = junk[:, 0:FFI]
    Rr = junk[:].bitcast(BF16)[0:16, 0:T]
    Rrb = junk[:].bitcast(BF16)[0:16, T:2 * T]
    stat = sb("stat", [128, 4 * LNG + 4])
    cbc = sb("cbc", [128, 24])
    tiny = sb("tiny", [128, 16])
    ifr = sb("ifr", [128, 4])
    bqt = sb("bqt", [128, 4 * NCH])
    CC = es.enter_context(nc.sbuf_tensor("CC", [128, T], BF16))
    SSb = es.enter_context(nc.sbuf_tensor("SSb", [128, T], BF16))
    spermt = es.enter_context(nc.sbuf_tensor("spermt", [128, 128], BF16))
    e2t = es.enter_context(nc.sbuf_tensor("e2t", [16, 1024], BF16))
    biasrow = es.enter_context(nc.sbuf_tensor("biasrow", [1, D], BF16))
    onesrow = es.enter_context(nc.sbuf_tensor("onesrow", [1, 128], BF16))
    Btile = es.enter_context(nc.sbuf_tensor("Btile", [128, 4 * 128], BF16))
    asws = sb("asws", [128, max(NT, 1)])
    x1t = sb("x1t", [128, D])
    xnt = R2[:].bitcast(F32)[:, 0:D]
    wfp = x1t
    wbuf = R1[:, 0:2048]
    prz = sb("prz", [128, 16])
    wl1r = sb("wl1r", [128, 1])
    zpad = es.enter_context(nc.sbuf_tensor("zpad", [128, 512], BF16))
    wl1s = sb("wl1s", [128, 1])

    pw0 = es.enter_context(nc.psum_tensor("pw0", [128, 1024], F32))
    pw1 = es.enter_context(nc.psum_tensor("pw1", [128, 1024], F32))
    p4 = es.enter_context(nc.psum_tensor("p4", [128, 512], F32))
    p5 = es.enter_context(nc.psum_tensor("p5", [128, 512], F32))
    p6 = es.enter_context(nc.psum_tensor("p6", [128, 512], F32))
    p7 = es.enter_context(nc.psum_tensor("p7", [128, 512], F32))

    s_ld = Sem(nc, es, "s_ld")
    s_lw = Sem(nc, es, "s_lw")
    s_lx = Sem(nc, es, "s_lx")
    s_th = Sem(nc, es, "s_th")
    s_st = Sem(nc, es, "s_st")
    s_tr = Sem(nc, es, "s_tr")
    s_mv = Sem(nc, es, "s_mv")
    s_v = Sem(nc, es, "s_v")
    s_s = Sem(nc, es, "s_s")
    s_t = Sem(nc, es, "s_t")
    s_g = Sem(nc, es, "s_g")
    s_bar = Sem(nc, es, "s_bar")
    s_out = Sem(nc, es, "s_out")
    s_qs = Sem(nc, es, "s_qs")
    s_lq = Sem(nc, es, "s_lq")

    blk = es.enter_context(nc.Block())

    CONSTS = {}

    def reg_const(*vals):
        for v in vals:
            v = float(v)
            if v not in CONSTS:
                CONSTS[v] = len(CONSTS)
                assert len(CONSTS) <= 24

    def C(v):
        return cbc[:, ts(CONSTS[float(v)], 1)]

    reg_const(0.0, 1.0, -8.0, 0.125, SDT_L, MIN_L * BFREQ)

    nbar = [0]

    def barrier():
        nbar[0] += 1
        tgt = 5 * nbar[0]
        for ename in ("sync", "vector", "scalar", "tensor", "gpsimd"):
            @getattr(blk, ename)
            def _(eng):
                eng.sem_inc(s_bar.h, 1)
                eng.wait_ge(s_bar.h, tgt)
        s_bar.n = tgt

    def emit_fp4(v, W, src_ap):
        J = W // 16
        vF = q_vI[:, 0:W].bitcast(F32)
        q2I = q_2I[:, 0:W]
        q2F = q_2I[:, 0:W].bitcast(F32)
        hiF = q_hi[:, 0:W]

        def st(inst):
            s_v.inc(inst)
            v.wait_ge(s_v.h, s_v.n)

        st(v.reduce_max(out=q_bm[:, 0:J].unsqueeze(2),
                        in_=src_ap.rearrange("p (c s) -> p c s", s=16),
                        axis=AX.X, apply_absolute_value=True))
        st(v.tensor_scalar(out=q_sc[:, 0:J], in0=q_bm[:, 0:J], scalar1=6e-12,
                           scalar2=1.0 / 6.0, op0=ALU.max, op1=ALU.mult))
        st(v.reciprocal(out=q_bm[:, 0:J], in_=q_sc[:, 0:J]))
        bm3 = q_bm[:, 0:J].unsqueeze(2).broadcast_to([128, J, 16])
        sc3 = q_sc[:, 0:J].unsqueeze(2).broadcast_to([128, J, 16])
        st(v.tensor_mul(out=vF.rearrange("p (c s) -> p c s", s=16),
                        in0=src_ap.rearrange("p (c s) -> p c s", s=16), in1=bm3))
        st(v.tensor_scalar_add(out=q2I, in0=q_vI[:, 0:W], scalar1=0x00200000))
        st(v.tensor_single_scalar(out=q2I, in_=q2I, scalar=NEG4M, op=ALU.bitwise_and))
        st(v.tensor_scalar(out=vF, in0=vF, scalar1=M22, scalar2=M22,
                           op0=ALU.add, op1=ALU.subtract))
        st(v.tensor_scalar(out=vF, in0=vF, scalar1=-2.0, scalar2=2.0,
                           op0=ALU.max, op1=ALU.min))
        st(v.tensor_scalar(out=hiF, in0=q2F, scalar1=-2.0, scalar2=2.0,
                           op0=ALU.max, op1=ALU.min))
        st(v.tensor_sub(out=hiF, in0=q2F, in1=hiF))
        st(v.tensor_add(out=hiF, in0=hiF, in1=vF))
        st(v.tensor_mul(out=q_out[:, 0:W].rearrange("p (c s) -> p c s", s=16),
                        in0=hiF.rearrange("p (c s) -> p c s", s=16), in1=sc3))

    QSLOT = 5
    qsrcF = q_vI[:, 0:QSLOT * 512].bitcast(F32)

    def emit_fp4s(v, W, si):
        J = W // 16
        src = qsrcF[:, si * 512:si * 512 + W]
        vF = q_2I[:, 0:W].bitcast(F32)
        vI = q_2I[:, 0:W]
        q2I = q_2I[:, 512:512 + W]
        q2F = q_2I[:, 512:512 + W].bitcast(F32)
        hiF = q_2I[:, 1024:1024 + W].bitcast(F32)
        out = q_out[:, si * 512:si * 512 + W]

        def st(inst):
            s_v.inc(inst)
            v.wait_ge(s_v.h, s_v.n)

        st(v.reduce_max(out=q_bm[:, 0:J].unsqueeze(2),
                        in_=src.rearrange("p (c s) -> p c s", s=16),
                        axis=AX.X, apply_absolute_value=True))
        st(v.tensor_scalar(out=q_sc[:, 0:J], in0=q_bm[:, 0:J], scalar1=6e-12,
                           scalar2=1.0 / 6.0, op0=ALU.max, op1=ALU.mult))
        st(v.reciprocal(out=q_bm[:, 0:J], in_=q_sc[:, 0:J]))
        bm3 = q_bm[:, 0:J].unsqueeze(2).broadcast_to([128, J, 16])
        sc3 = q_sc[:, 0:J].unsqueeze(2).broadcast_to([128, J, 16])
        st(v.tensor_mul(out=vF.rearrange("p (c s) -> p c s", s=16),
                        in0=src.rearrange("p (c s) -> p c s", s=16), in1=bm3))
        st(v.tensor_scalar_add(out=q2I, in0=vI, scalar1=0x00200000))
        st(v.tensor_single_scalar(out=q2I, in_=q2I, scalar=NEG4M, op=ALU.bitwise_and))
        st(v.tensor_scalar(out=vF, in0=vF, scalar1=M22, scalar2=M22,
                           op0=ALU.add, op1=ALU.subtract))
        st(v.tensor_scalar(out=vF, in0=vF, scalar1=-2.0, scalar2=2.0,
                           op0=ALU.max, op1=ALU.min))
        st(v.tensor_scalar(out=hiF, in0=q2F, scalar1=-2.0, scalar2=2.0,
                           op0=ALU.max, op1=ALU.min))
        st(v.tensor_sub(out=hiF, in0=q2F, in1=hiF))
        st(v.tensor_add(out=hiF, in0=hiF, in1=vF))
        st(v.tensor_mul(out=out.rearrange("p (c s) -> p c s", s=16),
                        in0=hiF.rearrange("p (c s) -> p c s", s=16), in1=sc3))

    qtasks = []
    for wsrc, wdst, rows, wid in ((w_gate, wgq_s, FFI, D), (w_up, wuq_s, FFI, D),
                                  (w_down, wdq_s, D, FFI)):
        for rt in range(rows // 128):
            c0 = 0
            while c0 < wid:
                W = min(512, wid - c0)
                dbg_ap = None
                if wsrc is w_gate and "d_wgq" in dbg_t:
                    dbg_ap = dbg_t["d_wgq"][ts(rt, 128), c0:c0 + W]
                qtasks.append((wsrc[ts(rt, 128), c0:c0 + W],
                               wdst[ts(rt, 128), c0:c0 + W], W, dbg_ap))
                c0 += W

    qst = {"ld": 0, "slots": [], "lxm": [], "vend": [], "qsm": [0],
           "qt_slot": 0, "st_slot": 0}

    def q_issue_loads(eng):
        n = min(QSLOT, len(qtasks) - qst["ld"])
        if n <= 0:
            return
        i0 = qst["ld"]
        for j in range(n):
            i = qst["ld"]
            src, dst, W, dbg_ap = qtasks[i]
            s_lq.dinc(eng.dma_start(
                out=qsrcF[:, (i % QSLOT) * 512:(i % QSLOT) * 512 + W], in_=src))
            qst["ld"] += 1
        qst["slots"].append((i0, qst["ld"]))
        qst["lxm"].append(s_lq.n)

    def q_bootstrap():
        if qst["ld"] or not qtasks:
            return
        vcur = s_v.n

        @blk.scalar
        def _(sc_, vcur=vcur):
            sc_.wait_ge(s_v.h, vcur)
            q_issue_loads(sc_)

    def q_slot_vector():
        k = qst["qt_slot"]
        if k >= len(qst["slots"]):
            return
        i0, i1 = qst["slots"][k]
        lxm_k = qst["lxm"][k]
        qsm_k = qst["qsm"][k]

        @blk.vector
        def _(v, i0=i0, i1=i1, lxm_k=lxm_k, qsm_k=qsm_k):
            v.wait_ge(s_lq.h, lxm_k)
            v.wait_ge(s_qs.h, qsm_k)
            for i in range(i0, i1):
                emit_fp4s(v, qtasks[i][2], i % QSLOT)
        qst["vend"].append(s_v.n)
        qst["qt_slot"] = k + 1

    def q_slot_scalar():
        k = qst["st_slot"]
        if k >= len(qst["vend"]):
            return
        i0, i1 = qst["slots"][k]
        vend_k = qst["vend"][k]

        @blk.scalar
        def _(sc_, i0=i0, i1=i1, vend_k=vend_k):
            sc_.wait_ge(s_v.h, vend_k)
            for i in range(i0, i1):
                src, dst, W, dbg_ap = qtasks[i]
                osl = q_out[:, (i % QSLOT) * 512:(i % QSLOT) * 512 + W]
                s_qs.dinc(sc_.dma_start(out=dst, in_=osl))
                if dbg_ap is not None:
                    s_qs.dinc(sc_.dma_start(out=dbg_ap, in_=osl))
            q_issue_loads(sc_)
        qst["qsm"].append(s_qs.n)
        qst["st_slot"] = k + 1

    def phase0a():
        @blk.sync
        def _(sync):
            for i, b in enumerate((bq, bk, bv, bo)):
                s_ld.dinc(sync.dma_start(out=bqt[:, ts(i, NCH)],
                                         in_=b.rearrange("(c p) -> p c", p=128),
                                         allow_slow_non_contiguous=True))
            s_ld.dinc(sync.dma_start(out=tiny[:, 0:1],
                                     in_=nrope.unsqueeze(0).broadcast_to([128, 1])))
            s_ld.dinc(sync.dma_start(out=ifr[:, 0:1], in_=rope_e[:, :]))
            s_ld.dinc(sync.dma_start(out=ifr[:, 1:2], in_=rope_sm[:, :]))
            s_ld.dinc(sync.dma_start(out=junk[:, 0:T], in_=iota_t[:, 0:T]))
            s_ld.dinc(sync.dma_start(out=tiny[:, 4:6],
                                     in_=sw_in.unsqueeze(0).broadcast_to([128, 2])))
            s_ld.dinc(sync.dma_start(out=e2t[:, :], in_=e2sel[:, :]))
            s_ld.dinc(sync.dma_start(out=spermt[:, :], in_=sperm[:, :]))
        ld0 = s_ld.n

        ncst = [0]

        @blk.vector
        def _(v):
            v.wait_ge(s_ld.h, ld0)
            for val, col in CONSTS.items():
                v.memset(cbc[:, ts(col, 1)], val).then_inc(s_v.h, 1)
                ncst[0] += 1
            v.memset(onesrow[:, :], 1.0).then_inc(s_v.h, 1)
            ncst[0] += 1
            v.memset(zpad[:, :], 0.0).then_inc(s_v.h, 1)
            ncst[0] += 1
        s_v.n += ncst[0]
        vc0 = s_v.n

        @blk.scalar
        def _(s):
            s.wait_ge(s_v.h, vc0)
            s.activation(out=tiny[:, 1:2], in_=tiny[:, 0:1], func=ACT.Sigmoid).then_inc(s_s.h, 1)
            s.wait_ge(s_s.h, s_s.n + 1)
            s.activation(out=tiny[:, 2:3], in_=tiny[:, 1:2], func=ACT.Ln,
                         scale=C(SDT_L), bias=C(MIN_L * BFREQ)).then_inc(s_s.h, 1)
            s.activation(out=tiny[:, 6:8], in_=tiny[:, 4:6], func=ACT.Exp).then_inc(s_s.h, 1)
        s_s.n += 3
        sc0 = s_s.n

        @blk.vector
        def _(v):
            v.wait_ge(s_s.h, sc0)
            v.reduce_sum(out=tiny[:, 10:11], in_=tiny[:, 6:8], axis=AX.X).then_inc(s_v.h, 1)
            v.wait_ge(s_v.h, s_v.n + 1)
            v.reciprocal(out=tiny[:, 11:12], in_=tiny[:, 10:11]).then_inc(s_v.h, 1)
            v.wait_ge(s_v.h, s_v.n + 2)
            v.tensor_mul(out=tiny[:, 8:10], in0=tiny[:, 6:8],
                         in1=tiny[:, 11:12].broadcast_to([128, 2])).then_inc(s_v.h, 1)
            v.tensor_mul(out=ifr[:, 2:3], in0=ifr[:, 0:1], in1=tiny[:, 2:3]).then_inc(s_v.h, 1)
        s_v.n += 4
        vc1 = s_v.n

        @blk.scalar
        def _(s):
            s.wait_ge(s_v.h, vc1)
            s.activation(out=ifr[:, 3:4], in_=ifr[:, 2:3], func=ACT.Exp).then_inc(s_s.h, 1)
        s_s.n += 1
        sc1 = s_s.n

        @blk.vector
        def _(v):
            v.wait_ge(s_s.h, sc1)
            v.tensor_mul(out=ifr[:, 3:4], in0=ifr[:, 3:4], in1=ifr[:, 1:2]).then_inc(s_v.h, 1)
            v.wait_ge(s_v.h, s_v.n + 1)
            v.tensor_scalar(out=junk[:, 0:T], in0=junk[:, 0:T], scalar1=ifr[:, 3:4],
                            scalar2=None, op0=ALU.mult).then_inc(s_v.h, 1)
            v.wait_ge(s_v.h, s_v.n + 2)
            v.tensor_single_scalar(out=q_vI[:, 0:T].bitcast(F32), in_=junk[:, 0:T],
                                   scalar=PI, op=ALU.is_gt).then_inc(s_v.h, 1)
            v.wait_ge(s_v.h, s_v.n + 3)
            v.scalar_tensor_tensor(out=junk[:, 0:T], in0=q_vI[:, 0:T].bitcast(F32),
                                   scalar=-2.0 * PI, in1=junk[:, 0:T],
                                   op0=ALU.mult, op1=ALU.add).then_inc(s_v.h, 1)
            v.wait_ge(s_v.h, s_v.n + 4)
            v.tensor_single_scalar(out=q_vI[:, 0:T].bitcast(F32), in_=junk[:, 0:T],
                                   scalar=-PI, op=ALU.is_lt).then_inc(s_v.h, 1)
            v.wait_ge(s_v.h, s_v.n + 5)
            v.scalar_tensor_tensor(out=junk[:, 0:T], in0=q_vI[:, 0:T].bitcast(F32),
                                   scalar=2.0 * PI, in1=junk[:, 0:T],
                                   op0=ALU.mult, op1=ALU.add).then_inc(s_v.h, 1)
        s_v.n += 6
        vc2 = s_v.n

        @blk.scalar
        def _(s):
            s.wait_ge(s_v.h, vc2)
            s.activation(out=SSb[:, 0:T], in_=junk[:, 0:T], func=ACT.Sin).then_inc(s_s.h, 1)
        s_s.n += 1
        sc2 = s_s.n

        @blk.vector
        def _(v):
            v.wait_ge(s_s.h, sc2)
            v.tensor_scalar_mul(out=xg[:, 0:T], in0=junk[:, 0:T],
                                scalar1=-1.0).then_inc(s_v.h, 1)
            v.wait_ge(s_v.h, s_v.n + 1)
            v.tensor_max(out=junk[:, 0:T], in0=junk[:, 0:T],
                         in1=xg[:, 0:T]).then_inc(s_v.h, 1)
            v.wait_ge(s_v.h, s_v.n + 2)
            v.tensor_scalar(out=junk[:, 0:T], in0=junk[:, 0:T], scalar1=-1.0,
                            scalar2=PI / 2, op0=ALU.mult, op1=ALU.add).then_inc(s_v.h, 1)
        s_v.n += 3
        vc3 = s_v.n

        @blk.scalar
        def _(s):
            s.wait_ge(s_v.h, vc3)
            s.activation(out=CC[:, 0:T], in_=junk[:, 0:T], func=ACT.Sin).then_inc(s_s.h, 1)
        s_s.n += 1

    def phase0b():
        for wsrc, wdst in ((Wq, wq_bs), (Wk, wk_bs), (Wv, wv_bs), (Wo, wo_bs)):
            for c in range(NCH):
                @blk.sync
                def _(sync, wsrc=wsrc, c=c):
                    sync.wait_ge(s_v.h, s_v.n)
                    s_ld.dinc(sync.dma_start(out=wfp[:, :], in_=wsrc[ts(c, 128), :]))
                ldc = s_ld.n

                @blk.vector
                def _(v):
                    v.wait_ge(s_ld.h, ldc)
                    v.tensor_copy(out=wbuf[:, 0:1024], in_=wfp[:, :]).then_inc(s_v.h, 1)
                s_v.n += 1
                vcc = s_v.n

                @blk.scalar
                def _(s, wdst=wdst, c=c):
                    s.wait_ge(s_v.h, vcc)
                    s_st.dinc(s.dma_start(out=wdst[ts(c, 128), :], in_=wbuf[:, 0:1024]))
                stc = s_st.n

                @blk.vector
                def _(v):
                    v.wait_ge(s_st.h, stc)
                    v.sem_inc(s_v.h, 1)
                s_v.n += 1

        @blk.vector
        def _(v):
            v.tensor_copy(out=wbuf[:, 0:NCH], in_=bqt[:, ts(2, NCH)]).then_inc(s_v.h, 1)
        s_v.n += 1
        vb = s_v.n
        for c in range(NCH):
            @blk.sync
            def _(sync, c=c):
                sync.wait_ge(s_v.h, vb)
                if c > 0:
                    sync.wait_ge(s_t.h, s_t.n)
                s_tr.dinc(sync.dma_start(out=wbuf[:, 1024:2048],
                                         in_=wo_bs[ts(c, 128), :]))
            trc = s_tr.n

            @blk.tensor
            def _(t, c=c):
                t.wait_ge(s_tr.h, trc)
                t.matmul(pw0[0:1, 0:512], wbuf[:, c:c + 1], wbuf[:, 1024:1536],
                         start=(c == 0), stop=(c == NCH - 1))
                t.matmul(pw0[0:1, 512:1024], wbuf[:, c:c + 1], wbuf[:, 1536:2048],
                         start=(c == 0), stop=(c == NCH - 1)).then_inc(s_t.h, 1)
            s_t.n += 1
        tb = s_t.n

        @blk.sync
        def _(sync):
            sync.wait_ge(s_t.h, tb)
            s_tr.dinc(sync.dma_start(out=wfp[0:1, 0:D], in_=bo.unsqueeze(0)))
        trb = s_tr.n

        @blk.vector
        def _(v):
            v.wait_ge(s_tr.h, trb)
            v.tensor_add(out=wfp[0:1, 0:D], in0=wfp[0:1, 0:D],
                         in1=pw0[0:1, 0:1024]).then_inc(s_v.h, 1)
            v.wait_ge(s_v.h, s_v.n + 1)
            v.tensor_copy(out=biasrow[0:1, :], in_=wfp[0:1, 0:D]).then_inc(s_v.h, 1)
        s_v.n += 2

    def phase_ln(src_dram, xnbf_dram, q_dram, layer):
        ngrp = NT // LNG
        for g in range(ngrp):
            @blk.sync
            def _(sync, g=g):
                sync.wait_ge(s_v.h, s_v.n)
                for i in range(LNG):
                    s_lx.dinc(sync.dma_start(out=xg[:, ts(i, D)],
                                             in_=src_dram[ts(g * LNG + i, 128), :]))
            ldg = s_lx.n

            @blk.vector
            def _(v):
                v.wait_ge(s_lx.h, ldg)
                for i in range(LNG):
                    v.reduce_sum(out=stat[:, ts(i, 1)], in_=xg[:, ts(i, D)],
                                 axis=AX.X).then_inc(s_v.h, 1)
            s_v.n += LNG
            vred = s_v.n

            @blk.scalar
            def _(s):
                s.wait_ge(s_lx.h, ldg)
                base = s_s.n
                for i in range(LNG):
                    if i > 0:
                        s.wait_ge(s_s.h, base + i)
                    s.activation(out=junk[:, 0:D], in_=xg[:, ts(i, D)], func=ACT.Square,
                                 accum_out=stat[:, ts(LNG + i, 1)]).then_inc(s_s.h, 1)
            s_s.n += LNG
            ssq = s_s.n

            @blk.vector
            def _(v):
                v.wait_ge(s_v.h, vred)
                v.wait_ge(s_s.h, ssq)
                L = slice(0, LNG)
                v.tensor_scalar_mul(out=stat[:, L], in0=stat[:, L],
                                    scalar1=1.0 / D).then_inc(s_v.h, 1)
                v.wait_ge(s_v.h, s_v.n + 1)
                v.tensor_mul(out=stat[:, 2 * LNG:3 * LNG], in0=stat[:, L],
                             in1=stat[:, L]).then_inc(s_v.h, 1)
                v.wait_ge(s_v.h, s_v.n + 2)
                v.scalar_tensor_tensor(out=stat[:, LNG:2 * LNG],
                                       in0=stat[:, LNG:2 * LNG], scalar=1.0 / D,
                                       in1=stat[:, 2 * LNG:3 * LNG],
                                       op0=ALU.mult, op1=ALU.subtract).then_inc(s_v.h, 1)
                v.wait_ge(s_v.h, s_v.n + 3)
                v.tensor_scalar_add(out=stat[:, LNG:2 * LNG], in0=stat[:, LNG:2 * LNG],
                                    scalar1=EPS_LN).then_inc(s_v.h, 1)
            s_v.n += 4
            vstat = s_v.n

            @blk.scalar
            def _(s):
                s.wait_ge(s_v.h, vstat)
                s.activation(out=stat[:, LNG:2 * LNG], in_=stat[:, LNG:2 * LNG],
                             func=ACT.Sqrt).then_inc(s_s.h, 1)
            s_s.n += 1
            ssd = s_s.n

            @blk.vector
            def _(v):
                v.wait_ge(s_s.h, ssd)
                v.reciprocal(out=stat[:, LNG:2 * LNG],
                             in_=stat[:, LNG:2 * LNG]).then_inc(s_v.h, 1)
                v.wait_ge(s_v.h, s_v.n + 1)
                v.tensor_scalar(out=stat[:, 2 * LNG:3 * LNG], in0=stat[:, 0:LNG],
                                scalar1=-1.0, scalar2=None,
                                op0=ALU.mult).then_inc(s_v.h, 1)
                v.wait_ge(s_v.h, s_v.n + 2)
                v.tensor_mul(out=stat[:, 2 * LNG:3 * LNG],
                             in0=stat[:, 2 * LNG:3 * LNG],
                             in1=stat[:, LNG:2 * LNG]).then_inc(s_v.h, 1)
            s_v.n += 3
            vrs = s_v.n

            for i in range(LNG):
                it = g * LNG + i

                @blk.scalar
                def _(s, i=i):
                    s.wait_ge(s_v.h, vrs)
                    s.wait_ge(s_v.h, s_v.n)
                    s.activation(out=xnt[:, :], in_=xg[:, ts(i, D)], func=ACT.Identity,
                                 bias=stat[:, ts(2 * LNG + i, 1)],
                                 scale=stat[:, ts(LNG + i, 1)]).then_inc(s_s.h, 1)
                s_s.n += 1
                sxn = s_s.n

                @blk.vector
                def _(v):
                    v.wait_ge(s_s.h, sxn)
                    v.tensor_copy(out=q_out[:, 0:D], in_=xnt[:, :]).then_inc(s_v.h, 1)
                s_v.n += 1
                vxb = s_v.n

                @blk.scalar
                def _(s, it=it):
                    s.wait_ge(s_v.h, vxb)
                    s_st.dinc(s.dma_start(out=xnbf_dram[ts(it, 128), :],
                                          in_=q_out[:, 0:D]))
                    if layer == 0 and "d_xn" in dbg_t:
                        s_st.dinc(s.dma_start(out=dbg_t["d_xn"][ts(it, 128), :],
                                              in_=xnt[:, :]))
                    if layer == 1 and "d_xn2" in dbg_t:
                        s_st.dinc(s.dma_start(out=dbg_t["d_xn2"][ts(it, 128), :],
                                              in_=xnt[:, :]))
                stq = s_st.n

                if layer == 0:
                    @blk.vector
                    def _(v):
                        v.wait_ge(s_st.h, stq)
                        emit_fp4(v, D, xnt[:, 0:D])
                    vq = s_v.n

                    @blk.scalar
                    def _(s, it=it):
                        s.wait_ge(s_v.h, vq)
                        s_st.dinc(s.dma_start(out=q_dram[ts(it, 128), :],
                                              in_=q_out[:, 0:D]))
                        if "d_xq" in dbg_t:
                            s_st.dinc(s.dma_start(out=dbg_t["d_xq"][ts(it, 128), :],
                                                  in_=q_out[:, 0:D]))
                else:
                    @blk.vector
                    def _(v, it=it):
                        v.wait_ge(s_st.h, stq)

                        def st(inst):
                            s_v.inc(inst)
                            v.wait_ge(s_v.h, s_v.n)
                        c0 = 3 * LNG
                        st(v.reduce_max(out=stat[:, ts(c0, 1)], in_=xnt[:, :],
                                        axis=AX.X, apply_absolute_value=True))
                        st(v.tensor_scalar_add(out=stat[:, ts(c0, 1)],
                                               in0=stat[:, ts(c0, 1)], scalar1=1e-8))
                        st(v.reciprocal(out=stat[:, ts(c0 + 1, 1)],
                                        in_=stat[:, ts(c0, 1)]))
                        st(v.tensor_scalar_mul(out=stat[:, ts(c0 + 1, 1)],
                                               in0=stat[:, ts(c0 + 1, 1)], scalar1=127.0))
                        st(v.tensor_mul(out=asws[:, ts(it, 1)],
                                        in0=stat[:, ts(c0, 1)], in1=wl1s[:, 0:1]))
                        st(v.tensor_scalar_mul(out=asws[:, ts(it, 1)],
                                               in0=asws[:, ts(it, 1)], scalar1=1.0 / 127.0))
                        st(v.tensor_scalar(out=q_vI[:, 0:D].bitcast(F32), in0=xnt[:, :],
                                           scalar1=stat[:, ts(c0 + 1, 1)], scalar2=-128.0,
                                           op0=ALU.mult, op1=ALU.max))
                        st(v.tensor_scalar(out=q_vI[:, 0:D].bitcast(F32),
                                           in0=q_vI[:, 0:D].bitcast(F32),
                                           scalar1=127.0, scalar2=M24,
                                           op0=ALU.min, op1=ALU.add))
                        st(v.tensor_scalar(out=q_out[:, 0:D],
                                           in0=q_vI[:, 0:D].bitcast(F32),
                                           scalar1=M24, scalar2=None, op0=ALU.subtract))
                    vq = s_v.n

                    @blk.scalar
                    def _(s, it=it):
                        s.wait_ge(s_v.h, vq)
                        s_st.dinc(s.dma_start(out=q_dram[ts(it, 128), :],
                                              in_=q_out[:, 0:D]))
                        if "d_aq" in dbg_t:
                            s_st.dinc(s.dma_start(out=dbg_t["d_aq"][ts(it, 128), :],
                                                  in_=q_out[:, 0:D]))
                stq2 = s_st.n

                @blk.vector
                def _(v):
                    v.wait_ge(s_st.h, stq2)
                    v.sem_inc(s_v.h, 1)
                s_v.n += 1

            @blk.vector
            def _(v):
                v.wait_ge(s_st.h, s_st.n)
                v.sem_inc(s_v.h, 1)
            s_v.n += 1

    def phase234(sq):
        t0 = sq * T
        psl = [pw0[:, 0:512], pw0[:, 512:1024], pw1[:, 0:512], pw1[:, 512:1024]]
        NB5 = T // 512
        UprH = R4[:, 0:4 * T]
        q_bootstrap()

        @blk.sync
        def _(sync):
            sync.wait_ge(s_st.h, s_st.n)
            sync.wait_ge(s_v.h, s_v.n)
            sync.wait_ge(s_out.h, s_out.n)
            for c in range(NCH):
                s_tr.dinc(sync.dma_start(out=xnT[:, ts(c, T)],
                                         in_=xn_s[t0:t0 + T, ts(c, 128)],
                                         transpose=True))
            for k in range(NCH):
                s_lw.dinc(sync.dma_start(out=R4[:, ts(k, 1024)],
                                         in_=wv_bs[ts(k, 128), :]))
        trx = s_tr.n
        ldv = s_lw.n

        @blk.vector
        def _(v):
            v.wait_ge(s_st.h, s_st.n)
            va3 = vA[:, 0:NKT * NH * 65].rearrange("p (t x) -> p t x", x=65)
            v.memset(va3[:, :, 64:65], 1.0).then_inc(s_v.h, 1)
            v.memset(Rr[:, :], 1.0).then_inc(s_v.h, 1)
        s_v.n += 2
        vones = s_v.n

        tv_l = []
        base_vt = s_t.n
        for tt in range(NTS):
            s_t.n += 1
            tv_l.append(s_t.n)
        sv_base = s_s.n

        @blk.tensor
        def _(t, tv_l=tv_l):
            t.wait_ge(s_tr.h, trx)
            t.wait_ge(s_lw.h, ldv)
            for tt in range(NTS):
                if tt > 0:
                    t.wait_ge(s_s.h, sv_base + tt)
                for k in range(NCH):
                    sl = xnT[:, k * T + tt * 128: k * T + tt * 128 + 128]
                    t.matmul(pw1[:, 0:512], sl, R4[:, k * 1024:k * 1024 + 512],
                             start=(k == 0), stop=(k == NCH - 1))
                    mm = t.matmul(pw1[:, 512:1024], sl, R4[:, k * 1024 + 512:k * 1024 + 1024],
                                  start=(k == 0), stop=(k == NCH - 1))
                mm.then_inc(s_t.h, 1)

        @blk.scalar
        def _(s, tv_l=tv_l):
            s.wait_ge(s_v.h, vones)
            for tt in range(NTS):
                s.wait_ge(s_t.h, tv_l[tt])
                dst = vA[:, ts(tt, NH * 65)].rearrange("p (h x) -> p h x", x=65)[:, :, 0:64]
                s.copy(out=dst,
                       in_=pw1[:, 0:1024].rearrange("p (h x) -> p h x", x=64)).then_inc(s_s.h, 1)
        s_s.n += NTS

        if "d_v" in dbg_t and sq == 0:
            @blk.scalar
            def _(s):
                s.wait_ge(s_s.h, s_s.n)
                s_st.dinc(s.dma_start(out=dbg_t["d_v"][:, :], in_=vA[:, 0:NH * 65]))
        vdone_s = s_s.n

        for half in range(2):
            for ci in range(4):
                c = half * 4 + ci

                @blk.sync
                def _(sync, c=c):
                    sync.wait_ge(s_v.h, s_v.n)
                    sync.wait_ge(s_t.h, s_t.n)
                    for k in range(NCH):
                        s_lw.dinc(sync.dma_start(out=R4[:, 8192 + k * 128:8192 + k * 128 + 128],
                                                 in_=wq_bs[ts(k, 128), ts(c, 128)]))
                        s_lw.dinc(sync.dma_start(out=R4[:, 12288 + k * 128:12288 + k * 128 + 128],
                                                 in_=wk_bs[ts(k, 128), ts(c, 128)]))
                ldqk = s_lw.n

                for hf, woff, boff in ((0, 8192, 0), (1, 12288, NCH)):
                    vready = s_v.n
                    q_slot_vector()

                    @blk.tensor
                    def _(t, c=c, hf=hf, woff=woff, vready=vready):
                        t.wait_ge(s_lw.h, ldqk)
                        t.wait_ge(s_tr.h, trx)
                        t.wait_ge(s_st.h, s_st.n)
                        t.wait_ge(s_v.h, vready)
                        for k in range(NCH):
                            for tb in range(NB5):
                                mm = t.matmul(psl[tb],
                                              R4[:, woff + k * 128:woff + k * 128 + 128],
                                              xnT[:, k * T + tb * 512:k * T + (tb + 1) * 512],
                                              start=(k == 0), stop=(k == NCH - 1))
                        mm.then_inc(s_t.h, 1)
                    s_t.n += 1
                    tqk = s_t.n

                    @blk.vector
                    def _(v, c=c, hf=hf, boff=boff):
                        v.wait_ge(s_t.h, tqk)
                        for tb in range(NB5):
                            v.tensor_scalar(out=qkbuf[:, hf * T + tb * 512: hf * T + (tb + 1) * 512],
                                            in0=psl[tb], scalar1=bqt[:, ts(boff + c, 1)],
                                            scalar2=None, op0=ALU.add).then_inc(s_v.h, 1)
                    s_v.n += NB5
                    sqk = s_v.n

                    @blk.tensor
                    def _(t, hf=hf):
                        t.wait_ge(s_v.h, sqk)
                        for tb in range(NB5):
                            t.matmul(p6[:, 0:512], spermt[:, :],
                                     qkbuf[:, hf * T + tb * 512: hf * T + (tb + 1) * 512],
                                     start=True, stop=True).then_inc(s_t.h, 1)
                            t.wait_ge(s_v.h, s_v.n + tb + 1)
                    s_t.n += NB5
                    trp = s_t.n

                    @blk.vector
                    def _(v, hf=hf):
                        for tb in range(NB5):
                            v.wait_ge(s_t.h, trp - NB5 + tb + 1)
                            v.tensor_copy(out=R4[:, 8192 + T + tb * 512: 8192 + T + (tb + 1) * 512],
                                          in_=p6[:, 0:512]).then_inc(s_v.h, 1)
                    s_v.n += NB5
                    srp = s_v.n

                    @blk.vector
                    def _(v, hf=hf):
                        v.wait_ge(s_v.h, srp)
                        hb = hf * T
                        v.tensor_mul(out=R4[:, 8192:8192 + T],
                                     in0=qkbuf[:, hb:hb + T], in1=CC[:, 0:T]).then_inc(s_v.h, 1)
                        v.wait_ge(s_v.h, s_v.n + 1)
                        v.tensor_mul(out=R4[:, 8192 + T:8192 + 2 * T],
                                     in0=R4[:, 8192 + T:8192 + 2 * T],
                                     in1=SSb[:, 0:T]).then_inc(s_v.h, 1)
                        v.wait_ge(s_v.h, s_v.n + 2)
                        v.tensor_add(out=qkbuf[:, hb:hb + T], in0=R4[:, 8192:8192 + T],
                                     in1=R4[:, 8192 + T:8192 + 2 * T]).then_inc(s_v.h, 1)
                    s_v.n += 3
                    q_slot_scalar()

                vrope = s_v.n
                if "d_qT" in dbg_t and sq == 0:
                    @blk.scalar
                    def _(s, c=c):
                        s.wait_ge(s_v.h, vrope)
                        s_st.dinc(s.dma_start(out=dbg_t["d_qT"][ts(c, 128), :],
                                              in_=qkbuf[:, 0:T]))
                        s_st.dinc(s.dma_start(out=dbg_t["d_kT"][ts(c, 128), :],
                                              in_=qkbuf[:, T:2 * T]))

                tsc_l = []
                sexp_l = []
                tav_l = []
                vust_l = []
                smv_l = []
                base_s = s_s.n
                smv_prev = s_mv.n
                vust_prev = s_v.n
                for qb in range(NQB):
                    for kt in range(NKT):
                        s_t.n += 1
                        tsc_l.append(s_t.n)
                        s_s.n += 1
                        sexp_l.append(s_s.n)
                    s_t.n += 1
                    tav_l.append(s_t.n)
                    s_v.n += 2
                    vust_l.append(s_v.n)
                    s_mv.n += 64
                    smv_l.append(s_mv.n)

                @blk.tensor
                def _(t, c=c, vrope=vrope, tsc_l=tsc_l, sexp_l=sexp_l,
                      tav_l=tav_l, vust_l=vust_l, vust_prev=vust_prev):
                    t.wait_ge(s_s.h, vdone_s)
                    t.wait_ge(s_v.h, vrope)
                    for qb in range(NQB):
                        for kt in range(NKT):
                            i = qb * NKT + kt
                            pq = (pw0, pw1)[kt % 2]
                            if i >= 2:
                                t.wait_ge(s_s.h, sexp_l[i - 2])
                            t.matmul(pq[:, 0:QB],
                                     qkbuf[0:64, T + kt * 128:T + kt * 128 + 128],
                                     qkbuf[0:64, qb * QB:(qb + 1) * QB],
                                     start=True, stop=True,
                                     tile_position=(0, 0))
                            t.matmul(pq[:, QB:2 * QB],
                                     qkbuf[64:128, T + kt * 128:T + kt * 128 + 128],
                                     qkbuf[64:128, qb * QB:(qb + 1) * QB],
                                     start=True, stop=True,
                                     tile_position=(64, 0)).then_inc(s_t.h, 1)
                        t.wait_ge(s_s.h, sexp_l[qb * NKT + NKT - 1])
                        t.wait_ge(s_v.h, vust_l[qb - 1] if qb > 0 else vust_prev)
                        for head in range(2):
                            pav = p6 if head == 0 else p7
                            h = 2 * c + head
                            for kt in range(NKT):
                                mm = t.matmul(pav[0:65, 0:QB],
                                              vA[:, (kt * NH + h) * 65:(kt * NH + h) * 65 + 65],
                                              ET[:, ts(head * NKT + kt, QB)],
                                              start=(kt == 0), stop=(kt == NKT - 1))
                        mm.then_inc(s_t.h, 1)

                @blk.scalar
                def _(s, c=c, tsc_l=tsc_l):
                    for qb in range(NQB):
                        for kt in range(NKT):
                            i = qb * NKT + kt
                            pq = (pw0, pw1)[kt % 2]
                            s.wait_ge(s_t.h, tsc_l[i])
                            dst = bass.AP(ET, kt * QB,
                                          [[ET.shape[1], 128], [NKT * QB, 2], [1, QB]])
                            s.activation(out=dst, in_=pq[:, 0:2 * QB].rearrange(
                                             "p (h q) -> p h q", h=2),
                                         func=ACT.Exp, scale=C(0.125),
                                         bias=C(-8.0)).then_inc(s_s.h, 1)

                @blk.vector
                def _(v, tav_l=tav_l, smv_l=smv_l, smv_prev=smv_prev):
                    for qb in range(NQB):
                        v.wait_ge(s_t.h, tav_l[qb])
                        v.wait_ge(s_mv.h, smv_l[qb - 1] if qb > 0 else smv_prev)
                        v.tensor_copy(out=R5[0:65, 3 * T:3 * T + QB],
                                      in_=p6[0:65, 0:QB]).then_inc(s_v.h, 1)
                        v.tensor_copy(out=R5[0:65, 3 * T + QB:3 * T + 2 * QB],
                                      in_=p7[0:65, 0:QB]).then_inc(s_v.h, 1)

                @blk.sync
                def _(sync, ci=ci, c=c, vust_l=vust_l):
                    for qb in range(NQB):
                        sync.wait_ge(s_v.h, vust_l[qb])
                        sync.dma_start(
                            out=UprH[0:64, ci * T + qb * QB: ci * T + qb * QB + QB],
                            in_=R5[0:64, 3 * T:3 * T + QB]).then_inc(s_mv.h, 16)
                        sync.dma_start(
                            out=UprH[64:128, ci * T + qb * QB: ci * T + qb * QB + QB],
                            in_=R5[0:64, 3 * T + QB:3 * T + 2 * QB]).then_inc(s_mv.h, 16)
                        sync.dma_start(out=Rr[2 * c:2 * c + 1, ts(qb, QB)],
                                       in_=R5[64:65, 3 * T:3 * T + QB]).then_inc(s_mv.h, 16)
                        sync.dma_start(out=Rr[2 * c + 1:2 * c + 2, ts(qb, QB)],
                                       in_=R5[64:65, 3 * T + QB:3 * T + 2 * QB]).then_inc(s_mv.h, 16)

            @blk.vector
            def _(v):
                v.wait_ge(s_mv.h, s_mv.n)
                with nc.allow_low_precision("1/r applied to bf16 matmul operands"):
                    v.reciprocal(out=Rrb[:, :], in_=Rr[:, :]).then_inc(s_v.h, 1)
            s_v.n += 1
            vr = s_v.n

            for ci in range(4):
                c = half * 4 + ci

                @blk.tensor
                def _(t, c=c):
                    t.wait_ge(s_v.h, vr)
                    t.wait_ge(s_v.h, s_v.n)
                    t.wait_ge(s_s.h, s_s.n)
                    for tb in range(T // 512):
                        t.matmul(p6[:, 0:512], e2t[:, ts(c, 128)],
                                 Rrb[:, tb * 512:(tb + 1) * 512],
                                 start=True, stop=True).then_inc(s_t.h, 1)
                        t.wait_ge(s_s.h, s_s.n + tb + 1)
                s_t.n += T // 512
                trb = s_t.n

                @blk.scalar
                def _(s, c=c):
                    for tb in range(T // 512):
                        s.wait_ge(s_t.h, trb - T // 512 + tb + 1)
                        s.copy(out=rbb[:, tb * 512:(tb + 1) * 512],
                               in_=p6[:, 0:512]).then_inc(s_s.h, 1)
                s_s.n += T // 512
                srb = s_s.n

                @blk.vector
                def _(v, ci=ci):
                    v.wait_ge(s_s.h, srb)
                    v.tensor_mul(out=UprH[:, ts(ci, T)], in0=UprH[:, ts(ci, T)],
                                 in1=rbb[:, 0:T]).then_inc(s_v.h, 1)
                s_v.n += 1
            unorm = s_v.n

            @blk.sync
            def _(sync, half=half):
                sync.wait_ge(s_v.h, unorm)
                for kc in range(4):
                    s_lw.dinc(sync.dma_start(out=R4[:, 8192 + kc * 1024:8192 + (kc + 1) * 1024],
                                             in_=wo_bs[ts(half * 4 + kc, 128), :]))
            ldo = s_lw.n

            for tt in range(NTS):
                @blk.sync
                def _(sync, tt=tt):
                    sync.wait_ge(s_v.h, s_v.n)
                    if half == 0:
                        s_lx.dinc(sync.dma_start(out=x1t[:, :],
                                                 in_=x_in[t0 + tt * 128:t0 + (tt + 1) * 128, :]))
                    else:
                        s_lx.dinc(sync.dma_start(out=x1t[:, :],
                                                 in_=x1_s[t0 + tt * 128:t0 + (tt + 1) * 128, :]))
                ldx = s_lx.n

                @blk.tensor
                def _(t, tt=tt, half=half):
                    t.wait_ge(s_lw.h, ldo)
                    if tt > 0:
                        t.wait_ge(s_v.h, s_v.n)
                    for kc in range(4):
                        t.matmul(pw0[:, 0:512],
                                 UprH[:, kc * T + tt * 128:kc * T + tt * 128 + 128],
                                 R4[:, 8192 + kc * 1024:8192 + kc * 1024 + 512],
                                 start=(kc == 0), stop=(half == 1 and kc == 3))
                        mm = t.matmul(pw0[:, 512:1024],
                                      UprH[:, kc * T + tt * 128:kc * T + tt * 128 + 128],
                                      R4[:, 8192 + kc * 1024 + 512:8192 + (kc + 1) * 1024],
                                      start=(kc == 0), stop=(half == 1 and kc == 3))
                    if half == 0:
                        t.matmul(pw0[:, 0:512], onesrow[0:1, :], biasrow[0:1, 0:512],
                                 start=False, stop=True)
                        mm = t.matmul(pw0[:, 512:1024], onesrow[0:1, :],
                                      biasrow[0:1, 512:1024], start=False, stop=True)
                    mm.then_inc(s_t.h, 1)
                s_t.n += 1
                two = s_t.n

                @blk.vector
                def _(v, tt=tt):
                    v.wait_ge(s_t.h, two)
                    v.wait_ge(s_lx.h, ldx)
                    v.tensor_add(out=x1t[:, :], in0=x1t[:, :],
                                 in1=pw0[:, 0:1024]).then_inc(s_v.h, 1)
                s_v.n += 1
                vx1 = s_v.n

                @blk.scalar
                def _(s, tt=tt, half=half):
                    s.wait_ge(s_v.h, vx1)
                    s_st.dinc(s.dma_start(out=x1_s[t0 + tt * 128:t0 + (tt + 1) * 128, :],
                                          in_=x1t[:, :]))
                    if "d_att" in dbg_t and half == 1:
                        s_st.dinc(s.dma_start(
                            out=dbg_t["d_att"][t0 + tt * 128:t0 + (tt + 1) * 128, :],
                            in_=x1t[:, :]))
                stx = s_st.n

                @blk.vector
                def _(v):
                    v.wait_ge(s_st.h, stx)
                    v.sem_inc(s_v.h, 1)
                s_v.n += 1

    def phase5():
        while True:
            if qst["qt_slot"] < len(qst["slots"]):
                q_slot_vector()
                q_slot_scalar()
                continue
            if qst["ld"] < len(qtasks):
                vcur = s_v.n

                @blk.scalar
                def _(sc_, vcur=vcur):
                    sc_.wait_ge(s_v.h, vcur)
                    q_issue_loads(sc_)
                continue
            break

    def phase6():
        @blk.sync
        def _(sync):
            sync.wait_ge(s_bar.h, s_bar.n)
            sync.wait_ge(s_qs.h, s_qs.n)
            for c in range(NCH):
                dstbuf = xqT_lo if c < 4 else xqT_hi
                s_tr.dinc(sync.dma_start(out=dstbuf[:, (c % 4) * N:(c % 4) * N + N],
                                         in_=xq_s[0:N, ts(c, 128)], transpose=True))
        trq = s_tr.n

        NPS = FFI // 1024 + (1 if FFI % 1024 else 0)
        for np_ in range(NPS):
            n0 = np_ * 1024
            nw = min(1024, FFI - n0)

            @blk.sync
            def _(sync, n0=n0, nw=nw):
                sync.wait_ge(s_t.h, s_t.n)
                sync.wait_ge(s_v.h, s_v.n)
                for k in range(NCH):
                    s_lw.dinc(sync.dma_start(out=wstr[:, k * 2048:k * 2048 + nw],
                                             in_=wgq_s[n0:n0 + nw, ts(k, 128)],
                                             transpose=True))
                    s_lw.dinc(sync.dma_start(out=wstr[:, k * 2048 + 1024:k * 2048 + 1024 + nw],
                                             in_=wuq_s[n0:n0 + nw, ts(k, 128)],
                                             transpose=True))
            ldgu = s_lw.n

            tgu_l = []
            sgs_l = []
            vh_l = []
            sth_l = []
            base6s = s_st.n
            for it in range(NT):
                s_t.n += 1
                tgu_l.append(s_t.n)
                s_s.n += 1
                sgs_l.append(s_s.n)
                s_v.n += 1
                vh_l.append(s_v.n)
                s_st.n += 16
                sth_l.append(s_st.n)

            @blk.tensor
            def _(t, n0=n0, nw=nw, tgu_l=tgu_l, vh_l=vh_l):
                t.wait_ge(s_lw.h, ldgu)
                t.wait_ge(s_tr.h, trq)
                for it in range(NT):
                    if it > 0:
                        t.wait_ge(s_v.h, vh_l[it - 1])
                    for k in range(NCH):
                        src = xqT_lo if k < 4 else xqT_hi
                        sl = src[:, (k % 4) * N + it * 128:(k % 4) * N + it * 128 + 128]
                        for nb in range(0, nw, 512):
                            ww = min(512, nw - nb)
                            t.matmul(pw0[:, nb:nb + ww], sl,
                                     wstr[:, k * 2048 + nb:k * 2048 + nb + ww],
                                     start=(k == 0), stop=(k == NCH - 1))
                            mm = t.matmul(pw1[:, nb:nb + ww], sl,
                                          wstr[:, k * 2048 + 1024 + nb:k * 2048 + 1024 + nb + ww],
                                          start=(k == 0), stop=(k == NCH - 1))
                    mm.then_inc(s_t.h, 1)

            @blk.scalar
            def _(s, n0=n0, nw=nw, tgu_l=tgu_l, vh_l=vh_l):
                for it in range(NT):
                    s.wait_ge(s_t.h, tgu_l[it])
                    s.activation(out=gsil_v[:, 0:nw], in_=pw0[:, 0:nw],
                                 func=ACT.Silu).then_inc(s_s.h, 1)
                    if it > 0:
                        s.wait_ge(s_v.h, vh_l[it - 1])
                        s.dma_start(out=h_s[ts(it - 1, 128), n0:n0 + nw],
                                    in_=junk[:, 0:nw]).then_inc(s_st.h, 16)
                s.wait_ge(s_v.h, vh_l[NT - 1])
                s.dma_start(out=h_s[ts(NT - 1, 128), n0:n0 + nw],
                            in_=junk[:, 0:nw]).then_inc(s_st.h, 16)

            @blk.vector
            def _(v, nw=nw, sgs_l=sgs_l, sth_l=sth_l, base6s=base6s):
                for it in range(NT):
                    v.wait_ge(s_s.h, sgs_l[it])
                    v.wait_ge(s_st.h, sth_l[it - 1] if it > 0 else base6s)
                    v.tensor_mul(out=junk[:, 0:nw], in0=gsil_v[:, 0:nw],
                                 in1=pw1[:, 0:nw]).then_inc(s_v.h, 1)

        for it in range(NT):
            @blk.sync
            def _(sync, it=it):
                sync.wait_ge(s_v.h, s_v.n)
                sync.wait_ge(s_st.h, s_st.n)
                s_lx.dinc(sync.dma_start(out=q_hi[:, 0:FFI], in_=h_s[ts(it, 128), :]))
            ldh = s_lx.n

            qs_tot = s_qs.n

            @blk.vector
            def _(v, qs_tot=qs_tot):
                v.wait_ge(s_qs.h, qs_tot)
                v.wait_ge(s_lx.h, ldh)
                emit_fp4(v, FFI, q_hi[:, 0:FFI])
            vhq = s_v.n

            @blk.scalar
            def _(s, it=it):
                s.wait_ge(s_v.h, vhq)
                s_st.dinc(s.dma_start(out=hq_s[ts(it, 128), :], in_=q_out[:, 0:FFI]))
                if "d_hq" in dbg_t:
                    s_st.dinc(s.dma_start(out=dbg_t["d_hq"][ts(it, 128), :],
                                          in_=q_out[:, 0:FFI]))
            sthq = s_st.n

            @blk.vector
            def _(v):
                v.wait_ge(s_st.h, sthq)
                v.sem_inc(s_v.h, 1)
            s_v.n += 1

        @blk.sync
        def _(sync):
            sync.wait_ge(s_v.h, s_v.n)
            sync.wait_ge(s_t.h, s_t.n)
            for kc in range(FFI // 128):
                if kc < 16:
                    dst = wdT_a[:, kc * 1024:(kc + 1) * 1024]
                else:
                    dst = wdT_b[:, (kc - 16) * 1024:(kc - 15) * 1024]
                s_tr.dinc(sync.dma_start(out=dst, in_=wdq_s[0:D, ts(kc, 128)],
                                         transpose=True))
        trd = s_tr.n

        NB4 = NT // 4
        for b4 in range(NB4):
            @blk.sync
            def _(sync, b4=b4):
                sync.wait_ge(s_t.h, s_t.n)
                for kc in range(FFI // 128):
                    s_th.dinc(sync.dma_start(out=hqTb[:, kc * 512:(kc + 1) * 512],
                                             in_=hq_s[b4 * 512:(b4 + 1) * 512, ts(kc, 128)],
                                             transpose=True))
            trh = s_th.n

            for i4 in range(4):
                it = b4 * 4 + i4

                @blk.sync
                def _(sync, it=it):
                    sync.wait_ge(s_v.h, s_v.n)
                    s_lx.dinc(sync.dma_start(out=x1t[:, :], in_=x1_s[ts(it, 128), :]))
                ldx1 = s_lx.n

                @blk.tensor
                def _(t, i4=i4):
                    t.wait_ge(s_th.h, trh)
                    t.wait_ge(s_tr.h, trd)
                    if True:
                        t.wait_ge(s_v.h, s_v.n)
                    for kc in range(FFI // 128):
                        if kc < 16:
                            wsl = wdT_a[:, kc * 1024:(kc + 1) * 1024]
                        else:
                            wsl = wdT_b[:, (kc - 16) * 1024:(kc - 15) * 1024]
                        hsl = hqTb[:, kc * 512 + i4 * 128:kc * 512 + i4 * 128 + 128]
                        t.matmul(pw0[:, 0:512], hsl, wsl[:, 0:512],
                                 start=(kc == 0), stop=(kc == FFI // 128 - 1))
                        mm = t.matmul(pw0[:, 512:1024], hsl, wsl[:, 512:1024],
                                      start=(kc == 0), stop=(kc == FFI // 128 - 1))
                    mm.then_inc(s_t.h, 1)
                s_t.n += 1
                tdn = s_t.n

                @blk.vector
                def _(v, it=it):
                    v.wait_ge(s_t.h, tdn)
                    v.wait_ge(s_lx.h, ldx1)
                    v.tensor_add(out=x1t[:, :], in0=x1t[:, :],
                                 in1=pw0[:, 0:1024]).then_inc(s_v.h, 1)
                s_v.n += 1
                vf = s_v.n

                @blk.scalar
                def _(s, it=it):
                    s.wait_ge(s_v.h, vf)
                    s_st.dinc(s.dma_start(out=x1_s[ts(it, 128), :], in_=x1t[:, :]))
                stf = s_st.n

                @blk.vector
                def _(v):
                    v.wait_ge(s_st.h, stf)
                    v.sem_inc(s_v.h, 1)
                s_v.n += 1

    def phase8a():
        for wsrc, wdst, rows in ((W_in, wi_s, D), (W_attn, wa_s, 656), (W_out, wo2_s, D)):
            nrt = (rows + 127) // 128
            for rt in range(nrt):
                rr = min(128, rows - rt * 128)

                @blk.sync
                def _(sync, wsrc=wsrc, rt=rt, rr=rr):
                    sync.wait_ge(s_v.h, s_v.n)
                    s_lw.dinc(sync.dma_start(out=wfp[0:rr, :],
                                             in_=wsrc[rt * 128:rt * 128 + rr, :]))
                ldr = s_lw.n

                @blk.vector
                def _(v, rr=rr):
                    v.wait_ge(s_lw.h, ldr)
                    v.tensor_copy(out=wbuf[0:rr, 0:1024], in_=wfp[0:rr, :]).then_inc(s_v.h, 1)
                s_v.n += 1
                vcp = s_v.n

                @blk.scalar
                def _(s, wdst=wdst, rt=rt, rr=rr):
                    s.wait_ge(s_v.h, vcp)
                    s_st.dinc(s.dma_start(out=wdst[rt * 128:rt * 128 + rr, :],
                                          in_=wbuf[0:rr, 0:1024]))
                str_ = s_st.n

                @blk.vector
                def _(v):
                    v.wait_ge(s_st.h, str_)
                    v.sem_inc(s_v.h, 1)
                s_v.n += 1

        for rt in range(NCH):
            @blk.sync
            def _(sync, rt=rt):
                sync.wait_ge(s_v.h, s_v.n)
                s_lw.dinc(sync.dma_start(out=wfp[:, :], in_=W_l1[ts(rt, 128), :]))
            ldr = s_lw.n

            @blk.vector
            def _(v, rt=rt):
                v.wait_ge(s_lw.h, ldr)
                v.reduce_sum(out=stat[:, ts(rt, 1)], in_=wfp[:, :], axis=AX.X,
                             apply_absolute_value=True).then_inc(s_v.h, 1)
                v.wait_ge(s_v.h, s_v.n + 1)
                v.sem_inc(s_v.h, 1)
            s_v.n += 2
        vws = s_v.n

        @blk.vector
        def _(v):
            v.wait_ge(s_v.h, vws)
            v.reduce_sum(out=stat[:, NCH:NCH + 1], in_=stat[:, 0:NCH],
                         axis=AX.X).then_inc(s_v.h, 1)
        s_v.n += 1
        vws2 = s_v.n

        @blk.tensor
        def _(t):
            t.wait_ge(s_v.h, vws2)
            t.matmul(p4[0:1, 0:1], C(1.0), stat[:, NCH:NCH + 1],
                     start=True, stop=True).then_inc(s_t.h, 1)
        s_t.n += 1
        tws = s_t.n

        @blk.vector
        def _(v):
            v.wait_ge(s_t.h, tws)
            v.tensor_scalar(out=stat[0:1, 0:1], in0=p4[0:1, 0:1],
                            scalar1=1.0 / (D * D), scalar2=1e-8,
                            op0=ALU.mult, op1=ALU.add).then_inc(s_v.h, 1)
            v.wait_ge(s_v.h, s_v.n + 1)
            v.reciprocal(out=stat[0:1, 1:2], in_=stat[0:1, 0:1]).then_inc(s_v.h, 1)
        s_v.n += 2
        vws3 = s_v.n

        @blk.scalar
        def _(s):
            s.wait_ge(s_v.h, vws3)
            s_st.dinc(s.dma_start(out=scal_s[0:2], in_=stat[0:1, 0:2]))
            if "d_ws" in dbg_t:
                s_st.dinc(s.dma_start(out=dbg_t["d_ws"][0:1, :], in_=stat[0:1, 0:2]))
        stws = s_st.n

        @blk.sync
        def _(sync):
            sync.wait_ge(s_st.h, stws)
            s_lx.dinc(sync.dma_start(out=wl1s[:, 0:1],
                                     in_=scal_s[0:1].unsqueeze(0).broadcast_to([128, 1])))
            s_lx.dinc(sync.dma_start(out=wl1r[:, 0:1],
                                     in_=scal_s[1:2].unsqueeze(0).broadcast_to([128, 1])))
        ldws = s_lx.n

        for rt in range(NCH):
            @blk.sync
            def _(sync, rt=rt):
                sync.wait_ge(s_v.h, s_v.n)
                s_lw.dinc(sync.dma_start(out=wfp[:, :], in_=W_l1[ts(rt, 128), :]))
            ldr = s_lw.n

            @blk.vector
            def _(v):
                v.wait_ge(s_lw.h, ldr)
                v.wait_ge(s_lx.h, ldws)

                def st(inst):
                    s_v.inc(inst)
                    v.wait_ge(s_v.h, s_v.n)
                st(v.tensor_scalar(out=q_vI[:, 0:D].bitcast(F32), in0=wfp[:, :],
                                   scalar1=wl1r[:, 0:1], scalar2=M24,
                                   op0=ALU.mult, op1=ALU.add))
                st(v.tensor_scalar(out=q_vI[:, 0:D].bitcast(F32),
                                   in0=q_vI[:, 0:D].bitcast(F32),
                                   scalar1=M24, scalar2=-1.0,
                                   op0=ALU.subtract, op1=ALU.max))
                st(v.tensor_scalar(out=q_out[:, 0:D], in0=q_vI[:, 0:D].bitcast(F32),
                                   scalar1=1.0, scalar2=None, op0=ALU.min))
            vqw = s_v.n

            @blk.scalar
            def _(s, rt=rt):
                s.wait_ge(s_v.h, vqw)
                s_st.dinc(s.dma_start(out=wl1q_s[ts(rt, 128), :], in_=q_out[:, 0:D]))
                if "d_wl1q" in dbg_t:
                    s_st.dinc(s.dma_start(out=dbg_t["d_wl1q"][ts(rt, 128), :],
                                          in_=q_out[:, 0:D]))
            stw = s_st.n

            @blk.vector
            def _(v):
                v.wait_ge(s_st.h, stw)
                v.sem_inc(s_v.h, 1)
            s_v.n += 1

    def phase8(sq):
        t0 = sq * T
        prs = pr_flat.rearrange("(a) -> a")

        @blk.vector
        def _(v):
            v.wait_ge(s_t.h, s_t.n)
            v.wait_ge(s_bar.h, s_bar.n)
            v.memset(zpad[:, :], 0.0).then_inc(s_v.h, 1)
        s_v.n += 1

        @blk.sync
        def _(sync):
            sync.wait_ge(s_bar.h, s_bar.n)
            sync.wait_ge(s_out.h, s_out.n)
            sync.wait_ge(s_v.h, s_v.n)
            sync.wait_ge(s_st.h, s_st.n)
            s_st.dinc(sync.dma_start(out=u_s[sq, 0:D, 0:64],
                                     in_=zpad[:, 0:512]))
            s_st.dinc(sync.dma_start(out=u_s[sq, 0:D, T + 64:T + 128],
                                     in_=zpad[:, 0:512]))
            for c in range(NCH):
                s_tr.dinc(sync.dma_start(out=xn2T[:, ts(c, T)],
                                         in_=xn2_s[t0:t0 + T, ts(c, 128)], transpose=True))
            for k in range(NCH):
                s_tr.dinc(sync.dma_start(out=wiT[:, ts(k, 1024)],
                                         in_=wi_s[0:D, ts(k, 128)], transpose=True))
        trwi = s_tr.n

        psl8 = [pw0[:, 0:512], pw0[:, 512:1024], pw1[:, 0:512], pw1[:, 512:1024]]
        for e in range(NCH):
            @blk.tensor
            def _(t, e=e):
                t.wait_ge(s_tr.h, trwi)
                if e > 0:
                    t.wait_ge(s_s.h, s_s.n)
                for k in range(NCH):
                    for tb in range(T // 512):
                        mm = t.matmul(psl8[tb],
                                      wiT[:, k * 1024 + e * 128:k * 1024 + e * 128 + 128],
                                      xn2T[:, k * T + tb * 512:k * T + (tb + 1) * 512],
                                      start=(k == 0), stop=(k == NCH - 1))
                mm.then_inc(s_t.h, 1)
            s_t.n += 1
            twi = s_t.n

            @blk.scalar
            def _(s, e=e):
                s.wait_ge(s_t.h, twi)
                for tb in range(T // 512):
                    s.copy(out=uT[:, e * T + tb * 512:e * T + (tb + 1) * 512],
                           in_=psl8[tb]).then_inc(s_s.h, 1)
                    s.activation(out=ruT[:, e * T + tb * 512:e * T + (tb + 1) * 512],
                                 in_=psl8[tb], func=ACT.Relu).then_inc(s_s.h, 1)
            s_s.n += 2 * (T // 512)
            swi = s_s.n

            @blk.scalar
            def _(s, e=e):
                s.wait_ge(s_s.h, swi)
                s_st.dinc(s.dma_start(out=u_s[sq, ts(e, 128), 64:64 + T], in_=uT[:, ts(e, T)]))
                if "d_u" in dbg_t and sq == 0:
                    s_st.dinc(s.dma_start(out=dbg_t["d_u"][ts(e, 128), :],
                                          in_=uT[:, ts(e, T)]))
        uwdone = s_st.n

        @blk.sync
        def _(sync):
            sync.wait_ge(s_st.h, uwdone)
            for k in range(NCH):
                s_tr.dinc(sync.dma_start(out=waT[:, ts(k, 656)],
                                         in_=wa_s[0:656, ts(k, 128)], transpose=True))
        trwa = s_tr.n

        for tt in range(NTS):
            @blk.tensor
            def _(t, tt=tt):
                t.wait_ge(s_tr.h, trwa)
                if tt > 0:
                    t.wait_ge(s_st.h, s_st.n)
                for k in range(NCH):
                    sl = ruT[:, k * T + tt * 128:k * T + tt * 128 + 128]
                    t.matmul(pw0[:, 0:512], sl, waT[:, k * 656:k * 656 + 512],
                             start=(k == 0), stop=(k == NCH - 1))
                    mm = t.matmul(pw0[:, 512:656], sl, waT[:, k * 656 + 512:k * 656 + 656],
                                  start=(k == 0), stop=(k == NCH - 1))
                mm.then_inc(s_t.h, 1)
            s_t.n += 1
            tal = s_t.n

            @blk.vector
            def _(v, tt=tt):
                if tt == 0:
                    v.wait_ge(s_st.h, uwdone)
                    v.memset(PW[:, :], 0.0).then_inc(s_v.h, 1)
                else:
                    v.sem_inc(s_v.h, 1)
            s_v.n += 1
            vpw0_ = s_v.n

            @blk.scalar
            def _(s, tt=tt):
                s.wait_ge(s_t.h, tal)
                s.wait_ge(s_v.h, vpw0_)
                for sc_i in range(2):
                    cw = CWS[sc_i]
                    srcap = pw0[:, (0 if sc_i == 0 else 328):(168 if sc_i == 0 else 656)]
                    dst = PW[:, PRB[sc_i]:PRB[sc_i] + 8 * SLOTP[sc_i]]
                    s.activation(out=dst.rearrange("p (h w) -> p h w", w=SLOTP[sc_i])[:, :, 0:cw],
                                 in_=srcap.rearrange("p (h w) -> p h w", w=cw),
                                 func=ACT.Exp).then_inc(s_s.h, 1)
            s_s.n += 2
            spw = s_s.n

            @blk.vector
            def _(v, tt=tt):
                v.wait_ge(s_s.h, spw)

                def st(inst):
                    s_v.inc(inst)
                    v.wait_ge(s_v.h, s_v.n)
                for sc_i in range(2):
                    cw = CWS[sc_i]
                    dst = PW[:, PRB[sc_i]:PRB[sc_i] + 8 * SLOTP[sc_i]]
                    view = dst.rearrange("p (h w) -> p h w", w=SLOTP[sc_i])[:, :, 0:cw]
                    st(v.reduce_sum(out=prz[:, ts(sc_i, 8)].unsqueeze(2),
                                    in_=view, axis=AX.X))
                st(v.reciprocal(out=prz[:, 0:16], in_=prz[:, 0:16]))
                st(v.tensor_scalar(out=prz[:, 0:8], in0=prz[:, 0:8],
                                   scalar1=tiny[:, 8:9], scalar2=None, op0=ALU.mult))
                st(v.tensor_scalar(out=prz[:, 8:16], in0=prz[:, 8:16],
                                   scalar1=tiny[:, 9:10], scalar2=None, op0=ALU.mult))
                for sc_i in range(2):
                    cw = CWS[sc_i]
                    dst = PW[:, PRB[sc_i]:PRB[sc_i] + 8 * SLOTP[sc_i]]
                    view = dst.rearrange("p (h w) -> p h w", w=SLOTP[sc_i])[:, :, 0:cw]
                    st(v.tensor_mul(out=view, in0=view,
                                    in1=prz[:, ts(sc_i, 8)].unsqueeze(2).broadcast_to([128, 8, cw])))
            vpw = s_v.n

            @blk.scalar
            def _(s, tt=tt):
                s.wait_ge(s_v.h, vpw)
                base = (sq * T + tt * 128) * PRP
                dst = prs[base:base + 128 * PRP].rearrange("(p w) -> p w", w=PRP)
                s_st.dinc(s.dma_start(out=dst, in_=PW[:, :]))

        aldone = s_st.n

        @blk.sync
        def _(sync):
            sync.wait_ge(s_st.h, aldone)
            sync.wait_ge(s_t.h, s_t.n)
            for kt in range(NKTP):
                s_tr.dinc(sync.dma_start(out=upad[:, ts(kt, 1024)],
                                         in_=u_s[sq, 0:D, kt * 128:kt * 128 + 128],
                                         transpose=True))
        trup = s_tr.n

        @blk.vector
        def _(v):
            v.wait_ge(s_st.h, aldone)
            v.sem_inc(s_v.h, 1)
        s_v.n += 1
        vup = s_v.n

        phase8_window(sq, vup)

    def phase8_window(sq, vup):
        prs = pr_flat
        NTG = NTS // 4
        btbs = (Btile[:, 0:512], zpad[:, 0:512])
        psems = (s_tr, s_th)
        t_base = s_t.n
        sev_prev = None
        items = [(c, tg, i4) for c in range(NCH) for tg in range(NTG)
                 for i4 in range(4)]
        for idx, (c, tg, i4) in enumerate(items):
            tt = tg * 4 + i4
            psem = psems[idx % 2]
            buf = btbs[idx % 2]

            @blk.sync
            def _(sync, sq=sq, c=c, tt=tt, idx=idx, buf=buf, psem=psem,
                  t_base=t_base):
                if idx >= 2:
                    sync.wait_ge(s_t.h, t_base + idx - 1)
                else:
                    sync.wait_ge(s_v.h, vup)
                for sc_i in range(2):
                    for k2 in range(2):
                        base = ((sq * T + tt * 128) * PRP
                                + PRB[sc_i] + c * SLOTP[sc_i] + PADS[sc_i]
                                + k2 * 128 - UPAD)
                        ap = bass.AP(prs.tensor, base,
                                     [[PRP - 1, 128], [1, 128]])
                        psem.dinc(sync.dma_start(
                            out=buf[:, ts(sc_i * 2 + k2, 128)],
                            in_=ap, transpose=True))
            trbt = psem.n

            @blk.tensor
            def _(t, c=c, tt=tt, i4=i4, idx=idx, buf=buf, psem=psem,
                  trbt=trbt, sev_prev=sev_prev):
                t.wait_ge(psem.h, trbt)
                if idx == 0:
                    t.wait_ge(s_v.h, vup)
                if i4 == 0 and sev_prev is not None:
                    t.wait_ge(s_s.h, sev_prev)
                first = True
                for sc_i in range(2):
                    for k2 in range(2):
                        ktp = tt + k2
                        mm = t.matmul(p6[:, ts(i4, 128)],
                                      upad[:, ktp * 1024 + c * 128:ktp * 1024 + c * 128 + 128],
                                      buf[:, ts(sc_i * 2 + k2, 128)],
                                      start=first, stop=(sc_i == 1 and k2 == 1))
                        first = False
                mm.then_inc(s_t.h, 1)
            s_t.n += 1

            if i4 == 3:
                tw4 = s_t.n

                @blk.scalar
                def _(s, c=c, tg=tg, tw4=tw4):
                    s.wait_ge(s_t.h, tw4)
                    s.copy(out=owT[:, c * T + tg * 512:c * T + (tg + 1) * 512],
                           in_=p6[:, 0:512]).then_inc(s_s.h, 1)
                s_s.n += 1
                sev_prev = s_s.n

        if "d_ow" in dbg_t and sq == 0:
            @blk.scalar
            def _(s):
                s.wait_ge(s_s.h, s_s.n)
                for c in range(NCH):
                    s_st.dinc(s.dma_start(out=dbg_t["d_ow"][ts(c, 128), :],
                                          in_=owT[:, ts(c, T)]))

    def phase9(sq):
        t0 = sq * T

        @blk.sync
        def _(sync):
            sync.wait_ge(s_s.h, s_s.n)
            for k in range(NCH):
                s_tr.dinc(sync.dma_start(out=woutT[:, ts(k, 1024)],
                                         in_=wo2_s[0:D, ts(k, 128)], transpose=True))
                s_tr.dinc(sync.dma_start(out=wl1qT[:, ts(k, 1024)],
                                         in_=wl1q_s[0:D, ts(k, 128)], transpose=True))
                s_tr.dinc(sync.dma_start(out=aqT[:, ts(k, T)],
                                         in_=aq_s[t0:t0 + T, ts(k, 128)], transpose=True))
        tr9 = s_tr.n

        for tt in range(NTS):
            it = sq * NTS + tt

            @blk.sync
            def _(sync, it=it):
                sync.wait_ge(s_v.h, s_v.n)
                s_lx.dinc(sync.dma_start(out=x1t[:, :], in_=x1_s[ts(it, 128), :]))
            ldx1 = s_lx.n

            @blk.tensor
            def _(t, tt=tt):
                t.wait_ge(s_tr.h, tr9)
                if tt > 0:
                    t.wait_ge(s_v.h, s_v.n)
                for k in range(NCH):
                    osl = owT[:, k * T + tt * 128:k * T + tt * 128 + 128]
                    t.matmul(pw0[:, 0:512], osl, woutT[:, k * 1024:k * 1024 + 512],
                             start=(k == 0), stop=(k == NCH - 1))
                    t.matmul(pw0[:, 512:1024], osl, woutT[:, k * 1024 + 512:k * 1024 + 1024],
                             start=(k == 0), stop=(k == NCH - 1))
                for k in range(NCH):
                    asl = aqT[:, k * T + tt * 128:k * T + tt * 128 + 128]
                    t.matmul(pw1[:, 0:512], asl, wl1qT[:, k * 1024:k * 1024 + 512],
                             start=(k == 0), stop=(k == NCH - 1))
                    mm = t.matmul(pw1[:, 512:1024], asl, wl1qT[:, k * 1024 + 512:k * 1024 + 1024],
                                  start=(k == 0), stop=(k == NCH - 1))
                mm.then_inc(s_t.h, 1)
            s_t.n += 1
            t9 = s_t.n

            @blk.vector
            def _(v, it=it):
                v.wait_ge(s_t.h, t9)
                v.wait_ge(s_lx.h, ldx1)
                v.scalar_tensor_tensor(out=junk[:, 0:D], in0=pw1[:, 0:1024],
                                       scalar=asws[:, ts(it, 1)], in1=x1t[:, :],
                                       op0=ALU.mult, op1=ALU.add).then_inc(s_v.h, 1)
                v.wait_ge(s_v.h, s_v.n + 1)
                v.tensor_add(out=junk[:, 0:D], in0=junk[:, 0:D],
                             in1=pw0[:, 0:1024]).then_inc(s_v.h, 1)
            s_v.n += 2
            v9 = s_v.n

            @blk.scalar
            def _(s, it=it):
                s.wait_ge(s_v.h, v9)
                s_out.dinc(s.dma_start(out=out_t[ts(it, 128), :], in_=junk[:, 0:D]))
            o9 = s_out.n

            @blk.vector
            def _(v):
                v.wait_ge(s_out.h, o9)
                v.sem_inc(s_v.h, 1)
            s_v.n += 1

    phase0a()
    phase0b()
    if stop >= 1:
        phase_ln(x_in, xn_s, xq_s, layer=0)
    barrier()
    if stop >= 2:
        for sq in range(NSEQ):
            phase234(sq)
        barrier()
    if stop >= 5:
        phase5()
        phase6()
        barrier()
    if stop >= 7:
        phase8a()
        phase_ln(x1_s, xn2_s, aq_s, layer=1)
        barrier()
    if stop >= 8:
        for sq in range(NSEQ):
            phase8(sq)
            phase9(sq)

    if stop < 8:
        @blk.vector
        def _(v):
            v.wait_ge(s_bar.h, s_bar.n)
            v.memset(junk[:, 0:D], 0.0).then_inc(s_v.h, 1)
        s_v.n += 1
        zv = s_v.n

        @blk.scalar
        def _(s):
            s.wait_ge(s_v.h, zv)
            for it in range(NT):
                s_out.dinc(s.dma_start(out=out_t[ts(it, 128), :], in_=junk[:, 0:D]))

    @blk.gpsimd
    def _(g):
        g.wait_ge(s_out.h, s_out.n)
        g.wait_ge(s_st.h, s_st.n)
        g.wait_ge(s_mv.h, s_mv.n)
        g.wait_ge(s_qs.h, s_qs.n)

    es.close()
    return nc


def _host_consts(T):
    iota = np.broadcast_to(np.arange(T, dtype=np.float32), (128, T)).copy()
    rope_e = np.zeros((128, 1), np.float32)
    rope_sm = np.zeros((128, 1), np.float32)
    for p in range(128):
        r = p % 64
        if r >= 32:
            j = r - 32
            sgn = -1.0
            if j >= 16:
                j -= 16
                sgn = 1.0
            rope_e[p, 0] = -j / 16.0
            rope_sm[p, 0] = sgn * 2.0 * np.pi / max(T - 1, 1)
    sperm = np.zeros((128, 128), np.float32)
    for m in range(128):
        r = m % 64
        if 32 <= r < 48:
            k = m + 16
        elif 48 <= r < 64:
            k = m - 16
        else:
            k = m
        sperm[k, m] = 1.0
    e2 = np.zeros((16, 1024), np.float32)
    for c in range(8):
        for m in range(128):
            e2[2 * c + (1 if m >= 64 else 0), c * 128 + m] = 1.0
    return {
        "iota_t": iota,
        "rope_e": rope_e,
        "rope_sm": rope_sm,
        "sperm": sperm.astype(ml_dtypes.bfloat16),
        "e2sel": e2.astype(ml_dtypes.bfloat16),
    }


def make_in_maps(inputs, cfg, n_cores=8):
    T, NSEQ = cfg["T"], cfg["NSEQ"]
    consts = _host_consts(T)
    x = np.ascontiguousarray(np.asarray(inputs["L0_attn_x"]), dtype=np.float32)
    B = x.shape[0]
    shared = {
        "Wq": inputs["L0_attn_Wq"], "Wk": inputs["L0_attn_Wk"],
        "Wv": inputs["L0_attn_Wv"], "Wo": inputs["L0_attn_Wo"],
        "bq": inputs["L0_attn_bq"], "bk": inputs["L0_attn_bk"],
        "bv": inputs["L0_attn_bv"], "bo": inputs["L0_attn_bo"],
        "nrope": inputs["L0_attn_nrope_freq"],
        "w_gate": inputs["L0_ffn_w_gate"], "w_up": inputs["L0_ffn_w_up"],
        "w_down": inputs["L0_ffn_w_down"],
        "W_in": inputs["L1_attn_W_in"], "W_attn": inputs["L1_attn_W_attn"],
        "W_out": inputs["L1_attn_W_out"], "sw": inputs["L1_attn_scale_weights"],
        "W_l1": inputs["L1_ffn_weight"],
    }
    shared = {k: np.ascontiguousarray(np.asarray(v), dtype=np.float32)
              for k, v in shared.items()}
    shared.update(consts)
    per_core = B // n_cores
    maps = []
    for c in range(n_cores):
        m = dict(shared)
        m["x"] = np.ascontiguousarray(
            x[c * per_core:(c + 1) * per_core].reshape(per_core * T, D))
        maps.append(m)
    return maps


def kernel(**inputs):
    cfg = {"T": 2048, "NSEQ": 2}
    nc = build(cfg)
    maps = make_in_maps(inputs, cfg, n_cores=8)
    res = run_bass_kernel_spmd(nc, maps, core_ids=list(range(8)))
    per_core = 16 // 8
    out = np.concatenate(
        [res.results[c]["out"].reshape(per_core, cfg["T"], D) for c in range(8)],
        axis=0)
    return out.astype(np.float32)


# version 05
# speedup vs baseline: 1.3517x; 1.0105x over previous
import sys
if "/opt/trn_rl_repo" not in sys.path:
    sys.path.insert(0, "/opt/trn_rl_repo")

from contextlib import ExitStack

import numpy as np
import ml_dtypes

import concourse.bass as bass
import concourse.mybir as mybir
from concourse.bass_utils import run_bass_kernel_spmd

F32 = mybir.dt.float32
BF16 = mybir.dt.bfloat16
I32 = mybir.dt.int32
AX = mybir.AxisListType
ALU = mybir.AluOpType
ACT = mybir.ActivationFunctionType

D = 1024
NH = 16
FFI = 2816
NCH = 8
CWS = (21, 41)
PADS = (10, 20)
UPAD = 64
GAP = 192
SLOTP = [CWS[0] + GAP, CWS[1] + GAP]
PRB = [GAP // 2, GAP // 2 + 8 * SLOTP[0]]
PRP = GAP // 2 + 8 * SLOTP[0] + 8 * SLOTP[1] + GAP // 2
M22 = float(3 * 2 ** 21)
M24 = float(3 * 2 ** 22)
BFREQ = 10000.0
MIN_L = 0.83
SDT_L = (BFREQ - MIN_L * BFREQ) * 2.0
EPS_LN = 1e-5
PI = float(np.pi)
NEG4M = -4194304


def ts(i, size):
    return slice(i * size, (i + 1) * size)


class Sem:
    def __init__(self, nc, es, name):
        self.h = es.enter_context(nc.semaphore(name))
        self.n = 0

    def inc(self, inst, k=1):
        inst.then_inc(self.h, k)
        self.n += k
        return self.n

    def dinc(self, inst):
        inst.then_inc(self.h, 16)
        self.n += 16
        return self.n


def build(cfg):
    T = cfg["T"]
    NSEQ = cfg["NSEQ"]
    N = T * NSEQ
    NT = N // 128
    NTS = T // 128
    QB = min(512, T)
    NQB = T // QB
    NKT = T // 128
    NKTP = (T + 2 * UPAD) // 128
    LNG = min(8, NT)
    dbg = set(cfg.get("debug", ()))
    stop = cfg.get("stop_after", 99)

    nc = bass.Bass("TRN2", target_bir_lowering=False, debug=False)

    def inp(name, shape, dt=F32):
        return nc.dram_tensor(name, shape, dt, kind="ExternalInput").ap()

    x_in = inp("x", [N, D])
    Wq = inp("Wq", [D, D]); Wk = inp("Wk", [D, D]); Wv = inp("Wv", [D, D])
    Wo = inp("Wo", [D, D])
    bq = inp("bq", [D]); bk = inp("bk", [D]); bv = inp("bv", [D]); bo = inp("bo", [D])
    nrope = inp("nrope", [1])
    w_gate = inp("w_gate", [FFI, D]); w_up = inp("w_up", [FFI, D])
    w_down = inp("w_down", [D, FFI])
    W_in = inp("W_in", [D, D]); W_attn = inp("W_attn", [656, D])
    W_out = inp("W_out", [D, D]); sw_in = inp("sw", [2])
    W_l1 = inp("W_l1", [D, D])
    iota_t = inp("iota_t", [128, T])
    rope_e = inp("rope_e", [128, 1])
    rope_sm = inp("rope_sm", [128, 1])
    sperm = nc.dram_tensor("sperm", [128, 128], BF16, kind="ExternalInput").ap()
    e2sel = nc.dram_tensor("e2sel", [16, 1024], BF16, kind="ExternalInput").ap()

    out_t = nc.dram_tensor("out", [N, D], F32, kind="ExternalOutput").ap()
    dbg_t = {}
    for name, shape, dt in [
        ("d_xn", [N, D], F32), ("d_xq", [N, D], BF16),
        ("d_qT", [D, T], BF16), ("d_kT", [D, T], BF16),
        ("d_v", [128, NH * 65], BF16), ("d_att", [N, D], F32),
        ("d_hq", [N, FFI], BF16), ("d_xn2", [N, D], F32),
        ("d_u", [D, T], BF16), ("d_ow", [D, T], BF16),
        ("d_aq", [N, D], BF16), ("d_wgq", [FFI, D], BF16),
        ("d_wl1q", [D, D], BF16), ("d_ws", [1, 2], F32),
    ]:
        if name in dbg:
            dbg_t[name] = nc.dram_tensor(name, shape, dt, kind="ExternalOutput").ap()

    def scr(name, shape, dt=BF16):
        return nc.dram_tensor(name, shape, dt, kind="Internal").ap()

    xn_s = scr("xn_s", [N, D])
    xq_s = scr("xq_s", [N, D])
    h_s = scr("h_s", [N, FFI], F32)
    hq_s = scr("hq_s", [N, FFI])
    x1_s = scr("x1_s", [N, D], F32)
    xn2_s = scr("xn2_s", [N, D])
    aq_s = scr("aq_s", [N, D])
    wq_bs = scr("wq_bs", [D, D]); wk_bs = scr("wk_bs", [D, D])
    wv_bs = scr("wv_bs", [D, D]); wo_bs = scr("wo_bs", [D, D])
    wgq_s = scr("wgq_s", [FFI, D]); wuq_s = scr("wuq_s", [FFI, D])
    wdq_s = scr("wdq_s", [D, FFI])
    wl1q_s = scr("wl1q_s", [D, D])
    wi_s = scr("wi_s", [D, D]); wa_s = scr("wa_s", [656, D]); wo2_s = scr("wo2_s", [D, D])
    u_s = scr("u_s", [NSEQ, D, T + 128])
    pr_flat = scr("pr_s", [NSEQ * T * PRP])
    scal_s = scr("scal_s", [8], F32)

    es = ExitStack()

    def sb(name, shape, dt=F32):
        return es.enter_context(nc.sbuf_tensor(name, shape, dt))

    R1N = NCH * T
    R1 = es.enter_context(nc.sbuf_tensor("R1", [128, max(R1N, 4 * N, 8 * T, 2 * LNG * D)], BF16))
    R2 = es.enter_context(nc.sbuf_tensor("R2", [128, max(2 * NKT * QB, 4 * N, 8 * T, 8192)], BF16))
    R3 = es.enter_context(nc.sbuf_tensor("R3", [128, max(NKT * NH * 65, NKTP * 1024, 22 * 512 + 2048, NCH * T)], BF16))
    R4 = es.enter_context(nc.sbuf_tensor("R4", [128, max(NCH * T, 16384)], BF16))
    R5 = es.enter_context(nc.sbuf_tensor("R5", [128, max(4 * T, NCH * 656 + PRP, 8192 + 1024)], BF16))

    xg = R1[:].bitcast(F32)
    xnT = R1
    Upr = R1
    xqT_lo = R1
    uT = R1
    aqT = R1
    ET = R2
    xqT_hi = R2
    ruT = R2
    wl1qT = R2
    vA = R3
    hqTb = R3
    gsil_v = R3[:, 22 * 512:22 * 512 + 2048].bitcast(F32)
    xn2T = R3
    upad = R3
    WVr = R4[:, 0:NCH * 1024]
    WQs = R4[:, 8192:12288]
    WKs = R4[:, 12288:16384]
    rope_t1 = R4[:, 8192:8192 + T]
    rope_sq = R4[:, 8192 + T:8192 + 2 * T]
    WOr = R4[:, 0:NCH * 1024]
    wstr = R4
    wdT_a = R4
    owT = R4
    qkbuf = R5
    rbb = R5[:, 2 * T:3 * T]
    UstA = R5[:, 3 * T:3 * T + QB]
    UstB = R5[:, 3 * T + QB:3 * T + 2 * QB]
    wiT = R5
    waT = R5[:, 0:NCH * 656]
    PW = R5[:, NCH * 656:NCH * 656 + PRP]
    woutT = R5
    wdT_b = R5

    q_vI = es.enter_context(nc.sbuf_tensor("q_vI", [128, FFI], I32))
    q_2I = es.enter_context(nc.sbuf_tensor("q_2I", [128, FFI], I32))
    q_out = es.enter_context(nc.sbuf_tensor("q_out", [128, FFI], BF16))
    q_bm = sb("q_bm", [128, FFI // 16])
    q_sc = sb("q_sc", [128, FFI // 16])

    junk = sb("junk", [128, max(FFI + 32, T + 64)])
    junkb = junk[:].bitcast(BF16)
    q_hi = junk[:, 0:FFI]
    Rr = junk[:].bitcast(BF16)[0:16, 0:T]
    Rrb = junk[:].bitcast(BF16)[0:16, T:2 * T]
    stat = sb("stat", [128, 4 * LNG + 4])
    cbc = sb("cbc", [128, 24])
    tiny = sb("tiny", [128, 16])
    ifr = sb("ifr", [128, 4])
    bqt = sb("bqt", [128, 4 * NCH])
    CC = es.enter_context(nc.sbuf_tensor("CC", [128, T], BF16))
    SSb = es.enter_context(nc.sbuf_tensor("SSb", [128, T], BF16))
    spermt = es.enter_context(nc.sbuf_tensor("spermt", [128, 128], BF16))
    e2t = es.enter_context(nc.sbuf_tensor("e2t", [16, 1024], BF16))
    biasrow = es.enter_context(nc.sbuf_tensor("biasrow", [1, D], BF16))
    onesrow = es.enter_context(nc.sbuf_tensor("onesrow", [1, 128], BF16))
    Btile = es.enter_context(nc.sbuf_tensor("Btile", [128, 4 * 128], BF16))
    asws = sb("asws", [128, max(NT, 1)])
    x1t = sb("x1t", [128, D])
    xnt = R2[:].bitcast(F32)[:, 0:D]
    wfp = x1t
    wbuf = R1[:, 0:2048]
    prz = sb("prz", [128, 16])
    wl1r = sb("wl1r", [128, 1])
    zpad = es.enter_context(nc.sbuf_tensor("zpad", [128, 512], BF16))
    wl1s = sb("wl1s", [128, 1])

    pw0 = es.enter_context(nc.psum_tensor("pw0", [128, 1024], F32))
    pw1 = es.enter_context(nc.psum_tensor("pw1", [128, 1024], F32))
    p4 = es.enter_context(nc.psum_tensor("p4", [128, 512], F32))
    p5 = es.enter_context(nc.psum_tensor("p5", [128, 512], F32))
    p6 = es.enter_context(nc.psum_tensor("p6", [128, 512], F32))
    p7 = es.enter_context(nc.psum_tensor("p7", [128, 512], F32))

    s_ld = Sem(nc, es, "s_ld")
    s_lw = Sem(nc, es, "s_lw")
    s_lx = Sem(nc, es, "s_lx")
    s_th = Sem(nc, es, "s_th")
    s_st = Sem(nc, es, "s_st")
    s_tr = Sem(nc, es, "s_tr")
    s_mv = Sem(nc, es, "s_mv")
    s_v = Sem(nc, es, "s_v")
    s_s = Sem(nc, es, "s_s")
    s_t = Sem(nc, es, "s_t")
    s_g = Sem(nc, es, "s_g")
    s_bar = Sem(nc, es, "s_bar")
    s_out = Sem(nc, es, "s_out")
    s_qs = Sem(nc, es, "s_qs")
    s_lq = Sem(nc, es, "s_lq")

    blk = es.enter_context(nc.Block())

    CONSTS = {}

    def reg_const(*vals):
        for v in vals:
            v = float(v)
            if v not in CONSTS:
                CONSTS[v] = len(CONSTS)
                assert len(CONSTS) <= 24

    def C(v):
        return cbc[:, ts(CONSTS[float(v)], 1)]

    reg_const(0.0, 1.0, -8.0, 0.125, SDT_L, MIN_L * BFREQ)

    nbar = [0]

    def barrier():
        nbar[0] += 1
        tgt = 5 * nbar[0]
        for ename in ("sync", "vector", "scalar", "tensor", "gpsimd"):
            @getattr(blk, ename)
            def _(eng):
                eng.sem_inc(s_bar.h, 1)
                eng.wait_ge(s_bar.h, tgt)
        s_bar.n = tgt

    def emit_fp4(v, W, src_ap, out_off=0, hi_ap=None):
        J = W // 16
        vF = q_vI[:, 0:W].bitcast(F32)
        q2I = q_2I[:, 0:W]
        q2F = q_2I[:, 0:W].bitcast(F32)
        hiF = q_hi[:, 0:W] if hi_ap is None else hi_ap

        def st(inst):
            s_v.inc(inst)
            v.wait_ge(s_v.h, s_v.n)

        st(v.reduce_max(out=q_bm[:, 0:J].unsqueeze(2),
                        in_=src_ap.rearrange("p (c s) -> p c s", s=16),
                        axis=AX.X, apply_absolute_value=True))
        st(v.tensor_scalar(out=q_sc[:, 0:J], in0=q_bm[:, 0:J], scalar1=6e-12,
                           scalar2=1.0 / 6.0, op0=ALU.max, op1=ALU.mult))
        st(v.reciprocal(out=q_bm[:, 0:J], in_=q_sc[:, 0:J]))
        bm3 = q_bm[:, 0:J].unsqueeze(2).broadcast_to([128, J, 16])
        sc3 = q_sc[:, 0:J].unsqueeze(2).broadcast_to([128, J, 16])
        st(v.tensor_mul(out=vF.rearrange("p (c s) -> p c s", s=16),
                        in0=src_ap.rearrange("p (c s) -> p c s", s=16), in1=bm3))
        st(v.tensor_scalar_add(out=q2I, in0=q_vI[:, 0:W], scalar1=0x00200000))
        st(v.tensor_single_scalar(out=q2I, in_=q2I, scalar=NEG4M, op=ALU.bitwise_and))
        st(v.tensor_scalar(out=vF, in0=vF, scalar1=M22, scalar2=M22,
                           op0=ALU.add, op1=ALU.subtract))
        st(v.tensor_scalar(out=vF, in0=vF, scalar1=-2.0, scalar2=2.0,
                           op0=ALU.max, op1=ALU.min))
        st(v.tensor_scalar(out=hiF, in0=q2F, scalar1=-2.0, scalar2=2.0,
                           op0=ALU.max, op1=ALU.min))
        st(v.tensor_sub(out=hiF, in0=q2F, in1=hiF))
        st(v.tensor_add(out=hiF, in0=hiF, in1=vF))
        st(v.tensor_mul(out=q_out[:, out_off:out_off + W].rearrange(
                            "p (c s) -> p c s", s=16),
                        in0=hiF.rearrange("p (c s) -> p c s", s=16), in1=sc3))

    QSLOT = 5
    qsrcF = q_vI[:, 0:QSLOT * 512].bitcast(F32)

    def emit_fp4s(v, W, si):
        J = W // 16
        src = qsrcF[:, si * 512:si * 512 + W]
        vF = q_2I[:, 0:W].bitcast(F32)
        vI = q_2I[:, 0:W]
        q2I = q_2I[:, 512:512 + W]
        q2F = q_2I[:, 512:512 + W].bitcast(F32)
        hiF = q_2I[:, 1024:1024 + W].bitcast(F32)
        out = q_out[:, si * 512:si * 512 + W]

        def st(inst):
            s_v.inc(inst)
            v.wait_ge(s_v.h, s_v.n)

        st(v.reduce_max(out=q_bm[:, 0:J].unsqueeze(2),
                        in_=src.rearrange("p (c s) -> p c s", s=16),
                        axis=AX.X, apply_absolute_value=True))
        st(v.tensor_scalar(out=q_sc[:, 0:J], in0=q_bm[:, 0:J], scalar1=6e-12,
                           scalar2=1.0 / 6.0, op0=ALU.max, op1=ALU.mult))
        st(v.reciprocal(out=q_bm[:, 0:J], in_=q_sc[:, 0:J]))
        bm3 = q_bm[:, 0:J].unsqueeze(2).broadcast_to([128, J, 16])
        sc3 = q_sc[:, 0:J].unsqueeze(2).broadcast_to([128, J, 16])
        st(v.tensor_mul(out=vF.rearrange("p (c s) -> p c s", s=16),
                        in0=src.rearrange("p (c s) -> p c s", s=16), in1=bm3))
        st(v.tensor_scalar_add(out=q2I, in0=vI, scalar1=0x00200000))
        st(v.tensor_single_scalar(out=q2I, in_=q2I, scalar=NEG4M, op=ALU.bitwise_and))
        st(v.tensor_scalar(out=vF, in0=vF, scalar1=M22, scalar2=M22,
                           op0=ALU.add, op1=ALU.subtract))
        st(v.tensor_scalar(out=vF, in0=vF, scalar1=-2.0, scalar2=2.0,
                           op0=ALU.max, op1=ALU.min))
        st(v.tensor_scalar(out=hiF, in0=q2F, scalar1=-2.0, scalar2=2.0,
                           op0=ALU.max, op1=ALU.min))
        st(v.tensor_sub(out=hiF, in0=q2F, in1=hiF))
        st(v.tensor_add(out=hiF, in0=hiF, in1=vF))
        st(v.tensor_mul(out=out.rearrange("p (c s) -> p c s", s=16),
                        in0=hiF.rearrange("p (c s) -> p c s", s=16), in1=sc3))

    qtasks = []
    for wsrc, wdst, rows, wid in ((w_gate, wgq_s, FFI, D), (w_up, wuq_s, FFI, D),
                                  (w_down, wdq_s, D, FFI)):
        for rt in range(rows // 128):
            c0 = 0
            while c0 < wid:
                W = min(512, wid - c0)
                dbg_ap = None
                if wsrc is w_gate and "d_wgq" in dbg_t:
                    dbg_ap = dbg_t["d_wgq"][ts(rt, 128), c0:c0 + W]
                qtasks.append((wsrc[ts(rt, 128), c0:c0 + W],
                               wdst[ts(rt, 128), c0:c0 + W], W, dbg_ap))
                c0 += W

    qst = {"ld": 0, "slots": [], "lxm": [], "vend": [], "qsm": [0],
           "qt_slot": 0, "st_slot": 0}

    def q_issue_loads(eng):
        n = min(QSLOT, len(qtasks) - qst["ld"])
        if n <= 0:
            return
        i0 = qst["ld"]
        for j in range(n):
            i = qst["ld"]
            src, dst, W, dbg_ap = qtasks[i]
            s_lq.dinc(eng.dma_start(
                out=qsrcF[:, (i % QSLOT) * 512:(i % QSLOT) * 512 + W], in_=src))
            qst["ld"] += 1
        qst["slots"].append((i0, qst["ld"]))
        qst["lxm"].append(s_lq.n)

    def q_bootstrap():
        if qst["ld"] or not qtasks:
            return
        vcur = s_v.n

        @blk.scalar
        def _(sc_, vcur=vcur):
            sc_.wait_ge(s_v.h, vcur)
            q_issue_loads(sc_)

    def q_slot_vector():
        k = qst["qt_slot"]
        if k >= len(qst["slots"]):
            return
        i0, i1 = qst["slots"][k]
        lxm_k = qst["lxm"][k]
        qsm_k = qst["qsm"][k]

        @blk.vector
        def _(v, i0=i0, i1=i1, lxm_k=lxm_k, qsm_k=qsm_k):
            v.wait_ge(s_lq.h, lxm_k)
            v.wait_ge(s_qs.h, qsm_k)
            for i in range(i0, i1):
                emit_fp4s(v, qtasks[i][2], i % QSLOT)
        qst["vend"].append(s_v.n)
        qst["qt_slot"] = k + 1

    def q_slot_scalar():
        k = qst["st_slot"]
        if k >= len(qst["vend"]):
            return
        i0, i1 = qst["slots"][k]
        vend_k = qst["vend"][k]

        @blk.scalar
        def _(sc_, i0=i0, i1=i1, vend_k=vend_k):
            sc_.wait_ge(s_v.h, vend_k)
            for i in range(i0, i1):
                src, dst, W, dbg_ap = qtasks[i]
                osl = q_out[:, (i % QSLOT) * 512:(i % QSLOT) * 512 + W]
                s_qs.dinc(sc_.dma_start(out=dst, in_=osl))
                if dbg_ap is not None:
                    s_qs.dinc(sc_.dma_start(out=dbg_ap, in_=osl))
            q_issue_loads(sc_)
        qst["qsm"].append(s_qs.n)
        qst["st_slot"] = k + 1

    def phase0a():
        @blk.sync
        def _(sync):
            for i, b in enumerate((bq, bk, bv, bo)):
                s_ld.dinc(sync.dma_start(out=bqt[:, ts(i, NCH)],
                                         in_=b.rearrange("(c p) -> p c", p=128),
                                         allow_slow_non_contiguous=True))
            s_ld.dinc(sync.dma_start(out=tiny[:, 0:1],
                                     in_=nrope.unsqueeze(0).broadcast_to([128, 1])))
            s_ld.dinc(sync.dma_start(out=ifr[:, 0:1], in_=rope_e[:, :]))
            s_ld.dinc(sync.dma_start(out=ifr[:, 1:2], in_=rope_sm[:, :]))
            s_ld.dinc(sync.dma_start(out=junk[:, 0:T], in_=iota_t[:, 0:T]))
            s_ld.dinc(sync.dma_start(out=tiny[:, 4:6],
                                     in_=sw_in.unsqueeze(0).broadcast_to([128, 2])))
            s_ld.dinc(sync.dma_start(out=e2t[:, :], in_=e2sel[:, :]))
            s_ld.dinc(sync.dma_start(out=spermt[:, :], in_=sperm[:, :]))
        ld0 = s_ld.n

        ncst = [0]

        @blk.vector
        def _(v):
            v.wait_ge(s_ld.h, ld0)
            for val, col in CONSTS.items():
                v.memset(cbc[:, ts(col, 1)], val).then_inc(s_v.h, 1)
                ncst[0] += 1
            v.memset(onesrow[:, :], 1.0).then_inc(s_v.h, 1)
            ncst[0] += 1
            v.memset(zpad[:, :], 0.0).then_inc(s_v.h, 1)
            ncst[0] += 1
        s_v.n += ncst[0]
        vc0 = s_v.n

        @blk.scalar
        def _(s):
            s.wait_ge(s_v.h, vc0)
            s.activation(out=tiny[:, 1:2], in_=tiny[:, 0:1], func=ACT.Sigmoid).then_inc(s_s.h, 1)
            s.wait_ge(s_s.h, s_s.n + 1)
            s.activation(out=tiny[:, 2:3], in_=tiny[:, 1:2], func=ACT.Ln,
                         scale=C(SDT_L), bias=C(MIN_L * BFREQ)).then_inc(s_s.h, 1)
            s.activation(out=tiny[:, 6:8], in_=tiny[:, 4:6], func=ACT.Exp).then_inc(s_s.h, 1)
        s_s.n += 3
        sc0 = s_s.n

        @blk.vector
        def _(v):
            v.wait_ge(s_s.h, sc0)
            v.reduce_sum(out=tiny[:, 10:11], in_=tiny[:, 6:8], axis=AX.X).then_inc(s_v.h, 1)
            v.wait_ge(s_v.h, s_v.n + 1)
            v.reciprocal(out=tiny[:, 11:12], in_=tiny[:, 10:11]).then_inc(s_v.h, 1)
            v.wait_ge(s_v.h, s_v.n + 2)
            v.tensor_mul(out=tiny[:, 8:10], in0=tiny[:, 6:8],
                         in1=tiny[:, 11:12].broadcast_to([128, 2])).then_inc(s_v.h, 1)
            v.tensor_mul(out=ifr[:, 2:3], in0=ifr[:, 0:1], in1=tiny[:, 2:3]).then_inc(s_v.h, 1)
        s_v.n += 4
        vc1 = s_v.n

        @blk.scalar
        def _(s):
            s.wait_ge(s_v.h, vc1)
            s.activation(out=ifr[:, 3:4], in_=ifr[:, 2:3], func=ACT.Exp).then_inc(s_s.h, 1)
        s_s.n += 1
        sc1 = s_s.n

        @blk.vector
        def _(v):
            v.wait_ge(s_s.h, sc1)
            v.tensor_mul(out=ifr[:, 3:4], in0=ifr[:, 3:4], in1=ifr[:, 1:2]).then_inc(s_v.h, 1)
            v.wait_ge(s_v.h, s_v.n + 1)
            v.tensor_scalar(out=junk[:, 0:T], in0=junk[:, 0:T], scalar1=ifr[:, 3:4],
                            scalar2=None, op0=ALU.mult).then_inc(s_v.h, 1)
            v.wait_ge(s_v.h, s_v.n + 2)
            v.tensor_single_scalar(out=q_vI[:, 0:T].bitcast(F32), in_=junk[:, 0:T],
                                   scalar=PI, op=ALU.is_gt).then_inc(s_v.h, 1)
            v.wait_ge(s_v.h, s_v.n + 3)
            v.scalar_tensor_tensor(out=junk[:, 0:T], in0=q_vI[:, 0:T].bitcast(F32),
                                   scalar=-2.0 * PI, in1=junk[:, 0:T],
                                   op0=ALU.mult, op1=ALU.add).then_inc(s_v.h, 1)
            v.wait_ge(s_v.h, s_v.n + 4)
            v.tensor_single_scalar(out=q_vI[:, 0:T].bitcast(F32), in_=junk[:, 0:T],
                                   scalar=-PI, op=ALU.is_lt).then_inc(s_v.h, 1)
            v.wait_ge(s_v.h, s_v.n + 5)
            v.scalar_tensor_tensor(out=junk[:, 0:T], in0=q_vI[:, 0:T].bitcast(F32),
                                   scalar=2.0 * PI, in1=junk[:, 0:T],
                                   op0=ALU.mult, op1=ALU.add).then_inc(s_v.h, 1)
        s_v.n += 6
        vc2 = s_v.n

        @blk.scalar
        def _(s):
            s.wait_ge(s_v.h, vc2)
            s.activation(out=SSb[:, 0:T], in_=junk[:, 0:T], func=ACT.Sin).then_inc(s_s.h, 1)
        s_s.n += 1
        sc2 = s_s.n

        @blk.vector
        def _(v):
            v.wait_ge(s_s.h, sc2)
            v.tensor_scalar_mul(out=xg[:, 0:T], in0=junk[:, 0:T],
                                scalar1=-1.0).then_inc(s_v.h, 1)
            v.wait_ge(s_v.h, s_v.n + 1)
            v.tensor_max(out=junk[:, 0:T], in0=junk[:, 0:T],
                         in1=xg[:, 0:T]).then_inc(s_v.h, 1)
            v.wait_ge(s_v.h, s_v.n + 2)
            v.tensor_scalar(out=junk[:, 0:T], in0=junk[:, 0:T], scalar1=-1.0,
                            scalar2=PI / 2, op0=ALU.mult, op1=ALU.add).then_inc(s_v.h, 1)
        s_v.n += 3
        vc3 = s_v.n

        @blk.scalar
        def _(s):
            s.wait_ge(s_v.h, vc3)
            s.activation(out=CC[:, 0:T], in_=junk[:, 0:T], func=ACT.Sin).then_inc(s_s.h, 1)
        s_s.n += 1

    def phase0b():
        for wsrc, wdst in ((Wq, wq_bs), (Wk, wk_bs), (Wv, wv_bs), (Wo, wo_bs)):
            for c in range(NCH):
                @blk.sync
                def _(sync, wsrc=wsrc, c=c):
                    sync.wait_ge(s_v.h, s_v.n)
                    s_ld.dinc(sync.dma_start(out=wfp[:, :], in_=wsrc[ts(c, 128), :]))
                ldc = s_ld.n

                @blk.vector
                def _(v):
                    v.wait_ge(s_ld.h, ldc)
                    v.tensor_copy(out=wbuf[:, 0:1024], in_=wfp[:, :]).then_inc(s_v.h, 1)
                s_v.n += 1
                vcc = s_v.n

                @blk.scalar
                def _(s, wdst=wdst, c=c):
                    s.wait_ge(s_v.h, vcc)
                    s_st.dinc(s.dma_start(out=wdst[ts(c, 128), :], in_=wbuf[:, 0:1024]))
                stc = s_st.n

                @blk.vector
                def _(v):
                    v.wait_ge(s_st.h, stc)
                    v.sem_inc(s_v.h, 1)
                s_v.n += 1

        @blk.vector
        def _(v):
            v.tensor_copy(out=wbuf[:, 0:NCH], in_=bqt[:, ts(2, NCH)]).then_inc(s_v.h, 1)
        s_v.n += 1
        vb = s_v.n
        for c in range(NCH):
            @blk.sync
            def _(sync, c=c):
                sync.wait_ge(s_v.h, vb)
                if c > 0:
                    sync.wait_ge(s_t.h, s_t.n)
                s_tr.dinc(sync.dma_start(out=wbuf[:, 1024:2048],
                                         in_=wo_bs[ts(c, 128), :]))
            trc = s_tr.n

            @blk.tensor
            def _(t, c=c):
                t.wait_ge(s_tr.h, trc)
                t.matmul(pw0[0:1, 0:512], wbuf[:, c:c + 1], wbuf[:, 1024:1536],
                         start=(c == 0), stop=(c == NCH - 1))
                t.matmul(pw0[0:1, 512:1024], wbuf[:, c:c + 1], wbuf[:, 1536:2048],
                         start=(c == 0), stop=(c == NCH - 1)).then_inc(s_t.h, 1)
            s_t.n += 1
        tb = s_t.n

        @blk.sync
        def _(sync):
            sync.wait_ge(s_t.h, tb)
            s_tr.dinc(sync.dma_start(out=wfp[0:1, 0:D], in_=bo.unsqueeze(0)))
        trb = s_tr.n

        @blk.vector
        def _(v):
            v.wait_ge(s_tr.h, trb)
            v.tensor_add(out=wfp[0:1, 0:D], in0=wfp[0:1, 0:D],
                         in1=pw0[0:1, 0:1024]).then_inc(s_v.h, 1)
            v.wait_ge(s_v.h, s_v.n + 1)
            v.tensor_copy(out=biasrow[0:1, :], in_=wfp[0:1, 0:D]).then_inc(s_v.h, 1)
        s_v.n += 2

    def phase_ln(src_dram, xnbf_dram, q_dram, layer):
        ngrp = NT // LNG
        for g in range(ngrp):
            @blk.sync
            def _(sync, g=g):
                sync.wait_ge(s_v.h, s_v.n)
                for i in range(LNG):
                    s_lx.dinc(sync.dma_start(out=xg[:, ts(i, D)],
                                             in_=src_dram[ts(g * LNG + i, 128), :]))
            ldg = s_lx.n

            @blk.vector
            def _(v):
                v.wait_ge(s_lx.h, ldg)
                for i in range(LNG):
                    v.reduce_sum(out=stat[:, ts(i, 1)], in_=xg[:, ts(i, D)],
                                 axis=AX.X).then_inc(s_v.h, 1)
            s_v.n += LNG
            vred = s_v.n

            @blk.scalar
            def _(s):
                s.wait_ge(s_lx.h, ldg)
                base = s_s.n
                for i in range(LNG):
                    if i > 0:
                        s.wait_ge(s_s.h, base + i)
                    s.activation(out=junk[:, 0:D], in_=xg[:, ts(i, D)], func=ACT.Square,
                                 accum_out=stat[:, ts(LNG + i, 1)]).then_inc(s_s.h, 1)
            s_s.n += LNG
            ssq = s_s.n

            @blk.vector
            def _(v):
                v.wait_ge(s_v.h, vred)
                v.wait_ge(s_s.h, ssq)
                L = slice(0, LNG)
                v.tensor_scalar_mul(out=stat[:, L], in0=stat[:, L],
                                    scalar1=1.0 / D).then_inc(s_v.h, 1)
                v.wait_ge(s_v.h, s_v.n + 1)
                v.tensor_mul(out=stat[:, 2 * LNG:3 * LNG], in0=stat[:, L],
                             in1=stat[:, L]).then_inc(s_v.h, 1)
                v.wait_ge(s_v.h, s_v.n + 2)
                v.scalar_tensor_tensor(out=stat[:, LNG:2 * LNG],
                                       in0=stat[:, LNG:2 * LNG], scalar=1.0 / D,
                                       in1=stat[:, 2 * LNG:3 * LNG],
                                       op0=ALU.mult, op1=ALU.subtract).then_inc(s_v.h, 1)
                v.wait_ge(s_v.h, s_v.n + 3)
                v.tensor_scalar_add(out=stat[:, LNG:2 * LNG], in0=stat[:, LNG:2 * LNG],
                                    scalar1=EPS_LN).then_inc(s_v.h, 1)
            s_v.n += 4
            vstat = s_v.n

            @blk.scalar
            def _(s):
                s.wait_ge(s_v.h, vstat)
                s.activation(out=stat[:, LNG:2 * LNG], in_=stat[:, LNG:2 * LNG],
                             func=ACT.Sqrt).then_inc(s_s.h, 1)
            s_s.n += 1
            ssd = s_s.n

            @blk.vector
            def _(v):
                v.wait_ge(s_s.h, ssd)
                v.reciprocal(out=stat[:, LNG:2 * LNG],
                             in_=stat[:, LNG:2 * LNG]).then_inc(s_v.h, 1)
                v.wait_ge(s_v.h, s_v.n + 1)
                v.tensor_scalar(out=stat[:, 2 * LNG:3 * LNG], in0=stat[:, 0:LNG],
                                scalar1=-1.0, scalar2=None,
                                op0=ALU.mult).then_inc(s_v.h, 1)
                v.wait_ge(s_v.h, s_v.n + 2)
                v.tensor_mul(out=stat[:, 2 * LNG:3 * LNG],
                             in0=stat[:, 2 * LNG:3 * LNG],
                             in1=stat[:, LNG:2 * LNG]).then_inc(s_v.h, 1)
            s_v.n += 3
            vrs = s_v.n

            for i in range(LNG):
                it = g * LNG + i

                @blk.scalar
                def _(s, i=i):
                    s.wait_ge(s_v.h, vrs)
                    s.wait_ge(s_v.h, s_v.n)
                    s.activation(out=xnt[:, :], in_=xg[:, ts(i, D)], func=ACT.Identity,
                                 bias=stat[:, ts(2 * LNG + i, 1)],
                                 scale=stat[:, ts(LNG + i, 1)]).then_inc(s_s.h, 1)
                s_s.n += 1
                sxn = s_s.n

                @blk.vector
                def _(v):
                    v.wait_ge(s_s.h, sxn)
                    v.tensor_copy(out=q_out[:, 0:D], in_=xnt[:, :]).then_inc(s_v.h, 1)
                s_v.n += 1
                vxb = s_v.n

                @blk.scalar
                def _(s, it=it):
                    s.wait_ge(s_v.h, vxb)
                    s_st.dinc(s.dma_start(out=xnbf_dram[ts(it, 128), :],
                                          in_=q_out[:, 0:D]))
                    if layer == 0 and "d_xn" in dbg_t:
                        s_st.dinc(s.dma_start(out=dbg_t["d_xn"][ts(it, 128), :],
                                              in_=xnt[:, :]))
                    if layer == 1 and "d_xn2" in dbg_t:
                        s_st.dinc(s.dma_start(out=dbg_t["d_xn2"][ts(it, 128), :],
                                              in_=xnt[:, :]))
                stq = s_st.n

                if layer == 0:
                    @blk.vector
                    def _(v):
                        v.wait_ge(s_st.h, stq)
                        emit_fp4(v, D, xnt[:, 0:D])
                    vq = s_v.n

                    @blk.scalar
                    def _(s, it=it):
                        s.wait_ge(s_v.h, vq)
                        s_st.dinc(s.dma_start(out=q_dram[ts(it, 128), :],
                                              in_=q_out[:, 0:D]))
                        if "d_xq" in dbg_t:
                            s_st.dinc(s.dma_start(out=dbg_t["d_xq"][ts(it, 128), :],
                                                  in_=q_out[:, 0:D]))
                else:
                    @blk.vector
                    def _(v, it=it):
                        v.wait_ge(s_st.h, stq)

                        def st(inst):
                            s_v.inc(inst)
                            v.wait_ge(s_v.h, s_v.n)
                        c0 = 3 * LNG
                        st(v.reduce_max(out=stat[:, ts(c0, 1)], in_=xnt[:, :],
                                        axis=AX.X, apply_absolute_value=True))
                        st(v.tensor_scalar_add(out=stat[:, ts(c0, 1)],
                                               in0=stat[:, ts(c0, 1)], scalar1=1e-8))
                        st(v.reciprocal(out=stat[:, ts(c0 + 1, 1)],
                                        in_=stat[:, ts(c0, 1)]))
                        st(v.tensor_scalar_mul(out=stat[:, ts(c0 + 1, 1)],
                                               in0=stat[:, ts(c0 + 1, 1)], scalar1=127.0))
                        st(v.tensor_mul(out=asws[:, ts(it, 1)],
                                        in0=stat[:, ts(c0, 1)], in1=wl1s[:, 0:1]))
                        st(v.tensor_scalar_mul(out=asws[:, ts(it, 1)],
                                               in0=asws[:, ts(it, 1)], scalar1=1.0 / 127.0))
                        st(v.tensor_scalar(out=q_vI[:, 0:D].bitcast(F32), in0=xnt[:, :],
                                           scalar1=stat[:, ts(c0 + 1, 1)], scalar2=-128.0,
                                           op0=ALU.mult, op1=ALU.max))
                        st(v.tensor_scalar(out=q_vI[:, 0:D].bitcast(F32),
                                           in0=q_vI[:, 0:D].bitcast(F32),
                                           scalar1=127.0, scalar2=M24,
                                           op0=ALU.min, op1=ALU.add))
                        st(v.tensor_scalar(out=q_out[:, 0:D],
                                           in0=q_vI[:, 0:D].bitcast(F32),
                                           scalar1=M24, scalar2=None, op0=ALU.subtract))
                    vq = s_v.n

                    @blk.scalar
                    def _(s, it=it):
                        s.wait_ge(s_v.h, vq)
                        s_st.dinc(s.dma_start(out=q_dram[ts(it, 128), :],
                                              in_=q_out[:, 0:D]))
                        if "d_aq" in dbg_t:
                            s_st.dinc(s.dma_start(out=dbg_t["d_aq"][ts(it, 128), :],
                                                  in_=q_out[:, 0:D]))
                stq2 = s_st.n

                @blk.vector
                def _(v):
                    v.wait_ge(s_st.h, stq2)
                    v.sem_inc(s_v.h, 1)
                s_v.n += 1

            @blk.vector
            def _(v):
                v.wait_ge(s_st.h, s_st.n)
                v.sem_inc(s_v.h, 1)
            s_v.n += 1

    def phase234(sq):
        t0 = sq * T
        psl = [pw0[:, 0:512], pw0[:, 512:1024], pw1[:, 0:512], pw1[:, 512:1024]]
        NB5 = T // 512
        UprH = R4[:, 0:4 * T]
        q_bootstrap()

        @blk.sync
        def _(sync):
            sync.wait_ge(s_st.h, s_st.n)
            sync.wait_ge(s_v.h, s_v.n)
            sync.wait_ge(s_out.h, s_out.n)
            for c in range(NCH):
                s_tr.dinc(sync.dma_start(out=xnT[:, ts(c, T)],
                                         in_=xn_s[t0:t0 + T, ts(c, 128)],
                                         transpose=True))
            for k in range(NCH):
                s_lw.dinc(sync.dma_start(out=R4[:, ts(k, 1024)],
                                         in_=wv_bs[ts(k, 128), :]))
        trx = s_tr.n
        ldv = s_lw.n

        @blk.vector
        def _(v):
            v.wait_ge(s_st.h, s_st.n)
            va3 = vA[:, 0:NKT * NH * 65].rearrange("p (t x) -> p t x", x=65)
            v.memset(va3[:, :, 64:65], 1.0).then_inc(s_v.h, 1)
            v.memset(Rr[:, :], 1.0).then_inc(s_v.h, 1)
        s_v.n += 2
        vones = s_v.n

        tv_l = []
        base_vt = s_t.n
        for tt in range(NTS):
            s_t.n += 1
            tv_l.append(s_t.n)
        sv_base = s_s.n

        @blk.tensor
        def _(t, tv_l=tv_l):
            t.wait_ge(s_tr.h, trx)
            t.wait_ge(s_lw.h, ldv)
            for tt in range(NTS):
                if tt > 0:
                    t.wait_ge(s_s.h, sv_base + tt)
                for k in range(NCH):
                    sl = xnT[:, k * T + tt * 128: k * T + tt * 128 + 128]
                    t.matmul(pw1[:, 0:512], sl, R4[:, k * 1024:k * 1024 + 512],
                             start=(k == 0), stop=(k == NCH - 1))
                    mm = t.matmul(pw1[:, 512:1024], sl, R4[:, k * 1024 + 512:k * 1024 + 1024],
                                  start=(k == 0), stop=(k == NCH - 1))
                mm.then_inc(s_t.h, 1)

        @blk.scalar
        def _(s, tv_l=tv_l):
            s.wait_ge(s_v.h, vones)
            for tt in range(NTS):
                s.wait_ge(s_t.h, tv_l[tt])
                dst = vA[:, ts(tt, NH * 65)].rearrange("p (h x) -> p h x", x=65)[:, :, 0:64]
                s.copy(out=dst,
                       in_=pw1[:, 0:1024].rearrange("p (h x) -> p h x", x=64)).then_inc(s_s.h, 1)
        s_s.n += NTS

        if "d_v" in dbg_t and sq == 0:
            @blk.scalar
            def _(s):
                s.wait_ge(s_s.h, s_s.n)
                s_st.dinc(s.dma_start(out=dbg_t["d_v"][:, :], in_=vA[:, 0:NH * 65]))
        vdone_s = s_s.n

        for half in range(2):
            for ci in range(4):
                c = half * 4 + ci

                @blk.sync
                def _(sync, c=c):
                    sync.wait_ge(s_v.h, s_v.n)
                    sync.wait_ge(s_t.h, s_t.n)
                    for k in range(NCH):
                        s_lw.dinc(sync.dma_start(out=R4[:, 8192 + k * 128:8192 + k * 128 + 128],
                                                 in_=wq_bs[ts(k, 128), ts(c, 128)]))
                        s_lw.dinc(sync.dma_start(out=R4[:, 12288 + k * 128:12288 + k * 128 + 128],
                                                 in_=wk_bs[ts(k, 128), ts(c, 128)]))
                ldqk = s_lw.n

                for hf, woff, boff in ((0, 8192, 0), (1, 12288, NCH)):
                    vready = s_v.n
                    q_slot_vector()

                    @blk.tensor
                    def _(t, c=c, hf=hf, woff=woff, vready=vready):
                        t.wait_ge(s_lw.h, ldqk)
                        t.wait_ge(s_tr.h, trx)
                        t.wait_ge(s_st.h, s_st.n)
                        t.wait_ge(s_v.h, vready)
                        for k in range(NCH):
                            for tb in range(NB5):
                                mm = t.matmul(psl[tb],
                                              R4[:, woff + k * 128:woff + k * 128 + 128],
                                              xnT[:, k * T + tb * 512:k * T + (tb + 1) * 512],
                                              start=(k == 0), stop=(k == NCH - 1))
                        mm.then_inc(s_t.h, 1)
                    s_t.n += 1
                    tqk = s_t.n

                    @blk.vector
                    def _(v, c=c, hf=hf, boff=boff):
                        v.wait_ge(s_t.h, tqk)
                        for tb in range(NB5):
                            v.tensor_scalar(out=qkbuf[:, hf * T + tb * 512: hf * T + (tb + 1) * 512],
                                            in0=psl[tb], scalar1=bqt[:, ts(boff + c, 1)],
                                            scalar2=None, op0=ALU.add).then_inc(s_v.h, 1)
                    s_v.n += NB5
                    sqk = s_v.n

                    @blk.tensor
                    def _(t, hf=hf):
                        t.wait_ge(s_v.h, sqk)
                        for tb in range(NB5):
                            t.matmul(p6[:, 0:512], spermt[:, :],
                                     qkbuf[:, hf * T + tb * 512: hf * T + (tb + 1) * 512],
                                     start=True, stop=True).then_inc(s_t.h, 1)
                            t.wait_ge(s_v.h, s_v.n + tb + 1)
                    s_t.n += NB5
                    trp = s_t.n

                    @blk.vector
                    def _(v, hf=hf):
                        for tb in range(NB5):
                            v.wait_ge(s_t.h, trp - NB5 + tb + 1)
                            v.tensor_copy(out=R4[:, 8192 + T + tb * 512: 8192 + T + (tb + 1) * 512],
                                          in_=p6[:, 0:512]).then_inc(s_v.h, 1)
                    s_v.n += NB5
                    srp = s_v.n

                    @blk.vector
                    def _(v, hf=hf):
                        v.wait_ge(s_v.h, srp)
                        hb = hf * T
                        v.tensor_mul(out=R4[:, 8192:8192 + T],
                                     in0=qkbuf[:, hb:hb + T], in1=CC[:, 0:T]).then_inc(s_v.h, 1)
                        v.wait_ge(s_v.h, s_v.n + 1)
                        v.tensor_mul(out=R4[:, 8192 + T:8192 + 2 * T],
                                     in0=R4[:, 8192 + T:8192 + 2 * T],
                                     in1=SSb[:, 0:T]).then_inc(s_v.h, 1)
                        v.wait_ge(s_v.h, s_v.n + 2)
                        v.tensor_add(out=qkbuf[:, hb:hb + T], in0=R4[:, 8192:8192 + T],
                                     in1=R4[:, 8192 + T:8192 + 2 * T]).then_inc(s_v.h, 1)
                    s_v.n += 3
                    q_slot_scalar()

                vrope = s_v.n
                if "d_qT" in dbg_t and sq == 0:
                    @blk.scalar
                    def _(s, c=c):
                        s.wait_ge(s_v.h, vrope)
                        s_st.dinc(s.dma_start(out=dbg_t["d_qT"][ts(c, 128), :],
                                              in_=qkbuf[:, 0:T]))
                        s_st.dinc(s.dma_start(out=dbg_t["d_kT"][ts(c, 128), :],
                                              in_=qkbuf[:, T:2 * T]))

                tsc_l = []
                sexp_l = []
                tav_l = []
                vust_l = []
                smv_l = []
                base_s = s_s.n
                smv_prev = s_mv.n
                vust_prev = s_v.n
                for qb in range(NQB):
                    for kt in range(NKT):
                        s_t.n += 1
                        tsc_l.append(s_t.n)
                        s_s.n += 1
                        sexp_l.append(s_s.n)
                    s_t.n += 1
                    tav_l.append(s_t.n)
                    s_v.n += 2
                    vust_l.append(s_v.n)
                    s_mv.n += 64
                    smv_l.append(s_mv.n)

                @blk.tensor
                def _(t, c=c, vrope=vrope, tsc_l=tsc_l, sexp_l=sexp_l,
                      tav_l=tav_l, vust_l=vust_l, vust_prev=vust_prev):
                    t.wait_ge(s_s.h, vdone_s)
                    t.wait_ge(s_v.h, vrope)
                    for qb in range(NQB):
                        for kt in range(NKT):
                            i = qb * NKT + kt
                            pq = (pw0, pw1)[kt % 2]
                            if i >= 2:
                                t.wait_ge(s_s.h, sexp_l[i - 2])
                            t.matmul(pq[:, 0:QB],
                                     qkbuf[0:64, T + kt * 128:T + kt * 128 + 128],
                                     qkbuf[0:64, qb * QB:(qb + 1) * QB],
                                     start=True, stop=True,
                                     tile_position=(0, 0))
                            t.matmul(pq[:, QB:2 * QB],
                                     qkbuf[64:128, T + kt * 128:T + kt * 128 + 128],
                                     qkbuf[64:128, qb * QB:(qb + 1) * QB],
                                     start=True, stop=True,
                                     tile_position=(64, 0)).then_inc(s_t.h, 1)
                        t.wait_ge(s_s.h, sexp_l[qb * NKT + NKT - 1])
                        t.wait_ge(s_v.h, vust_l[qb - 1] if qb > 0 else vust_prev)
                        for head in range(2):
                            pav = p6 if head == 0 else p7
                            h = 2 * c + head
                            for kt in range(NKT):
                                mm = t.matmul(pav[0:65, 0:QB],
                                              vA[:, (kt * NH + h) * 65:(kt * NH + h) * 65 + 65],
                                              ET[:, ts(head * NKT + kt, QB)],
                                              start=(kt == 0), stop=(kt == NKT - 1))
                        mm.then_inc(s_t.h, 1)

                @blk.scalar
                def _(s, c=c, tsc_l=tsc_l):
                    for qb in range(NQB):
                        for kt in range(NKT):
                            i = qb * NKT + kt
                            pq = (pw0, pw1)[kt % 2]
                            s.wait_ge(s_t.h, tsc_l[i])
                            dst = bass.AP(ET, kt * QB,
                                          [[ET.shape[1], 128], [NKT * QB, 2], [1, QB]])
                            s.activation(out=dst, in_=pq[:, 0:2 * QB].rearrange(
                                             "p (h q) -> p h q", h=2),
                                         func=ACT.Exp, scale=C(0.125),
                                         bias=C(-8.0)).then_inc(s_s.h, 1)

                @blk.vector
                def _(v, tav_l=tav_l, smv_l=smv_l, smv_prev=smv_prev):
                    for qb in range(NQB):
                        v.wait_ge(s_t.h, tav_l[qb])
                        v.wait_ge(s_mv.h, smv_l[qb - 1] if qb > 0 else smv_prev)
                        v.tensor_copy(out=R5[0:65, 3 * T:3 * T + QB],
                                      in_=p6[0:65, 0:QB]).then_inc(s_v.h, 1)
                        v.tensor_copy(out=R5[0:65, 3 * T + QB:3 * T + 2 * QB],
                                      in_=p7[0:65, 0:QB]).then_inc(s_v.h, 1)

                @blk.sync
                def _(sync, ci=ci, c=c, vust_l=vust_l):
                    for qb in range(NQB):
                        sync.wait_ge(s_v.h, vust_l[qb])
                        sync.dma_start(
                            out=UprH[0:64, ci * T + qb * QB: ci * T + qb * QB + QB],
                            in_=R5[0:64, 3 * T:3 * T + QB]).then_inc(s_mv.h, 16)
                        sync.dma_start(
                            out=UprH[64:128, ci * T + qb * QB: ci * T + qb * QB + QB],
                            in_=R5[0:64, 3 * T + QB:3 * T + 2 * QB]).then_inc(s_mv.h, 16)
                        sync.dma_start(out=Rr[2 * c:2 * c + 1, ts(qb, QB)],
                                       in_=R5[64:65, 3 * T:3 * T + QB]).then_inc(s_mv.h, 16)
                        sync.dma_start(out=Rr[2 * c + 1:2 * c + 2, ts(qb, QB)],
                                       in_=R5[64:65, 3 * T + QB:3 * T + 2 * QB]).then_inc(s_mv.h, 16)

            @blk.vector
            def _(v):
                v.wait_ge(s_mv.h, s_mv.n)
                with nc.allow_low_precision("1/r applied to bf16 matmul operands"):
                    v.reciprocal(out=Rrb[:, :], in_=Rr[:, :]).then_inc(s_v.h, 1)
            s_v.n += 1
            vr = s_v.n

            for ci in range(4):
                c = half * 4 + ci

                @blk.tensor
                def _(t, c=c):
                    t.wait_ge(s_v.h, vr)
                    t.wait_ge(s_v.h, s_v.n)
                    t.wait_ge(s_s.h, s_s.n)
                    for tb in range(T // 512):
                        t.matmul(p6[:, 0:512], e2t[:, ts(c, 128)],
                                 Rrb[:, tb * 512:(tb + 1) * 512],
                                 start=True, stop=True).then_inc(s_t.h, 1)
                        t.wait_ge(s_s.h, s_s.n + tb + 1)
                s_t.n += T // 512
                trb = s_t.n

                @blk.scalar
                def _(s, c=c):
                    for tb in range(T // 512):
                        s.wait_ge(s_t.h, trb - T // 512 + tb + 1)
                        s.copy(out=rbb[:, tb * 512:(tb + 1) * 512],
                               in_=p6[:, 0:512]).then_inc(s_s.h, 1)
                s_s.n += T // 512
                srb = s_s.n

                @blk.vector
                def _(v, ci=ci):
                    v.wait_ge(s_s.h, srb)
                    v.tensor_mul(out=UprH[:, ts(ci, T)], in0=UprH[:, ts(ci, T)],
                                 in1=rbb[:, 0:T]).then_inc(s_v.h, 1)
                s_v.n += 1
            unorm = s_v.n

            @blk.sync
            def _(sync, half=half):
                sync.wait_ge(s_v.h, unorm)
                for kc in range(4):
                    s_lw.dinc(sync.dma_start(out=R4[:, 8192 + kc * 1024:8192 + (kc + 1) * 1024],
                                             in_=wo_bs[ts(half * 4 + kc, 128), :]))
            ldo = s_lw.n

            for tt in range(NTS):
                @blk.sync
                def _(sync, tt=tt):
                    sync.wait_ge(s_v.h, s_v.n)
                    if half == 0:
                        s_lx.dinc(sync.dma_start(out=x1t[:, :],
                                                 in_=x_in[t0 + tt * 128:t0 + (tt + 1) * 128, :]))
                    else:
                        s_lx.dinc(sync.dma_start(out=x1t[:, :],
                                                 in_=x1_s[t0 + tt * 128:t0 + (tt + 1) * 128, :]))
                ldx = s_lx.n

                @blk.tensor
                def _(t, tt=tt, half=half):
                    t.wait_ge(s_lw.h, ldo)
                    if tt > 0:
                        t.wait_ge(s_v.h, s_v.n)
                    for kc in range(4):
                        t.matmul(pw0[:, 0:512],
                                 UprH[:, kc * T + tt * 128:kc * T + tt * 128 + 128],
                                 R4[:, 8192 + kc * 1024:8192 + kc * 1024 + 512],
                                 start=(kc == 0), stop=(half == 1 and kc == 3))
                        mm = t.matmul(pw0[:, 512:1024],
                                      UprH[:, kc * T + tt * 128:kc * T + tt * 128 + 128],
                                      R4[:, 8192 + kc * 1024 + 512:8192 + (kc + 1) * 1024],
                                      start=(kc == 0), stop=(half == 1 and kc == 3))
                    if half == 0:
                        t.matmul(pw0[:, 0:512], onesrow[0:1, :], biasrow[0:1, 0:512],
                                 start=False, stop=True)
                        mm = t.matmul(pw0[:, 512:1024], onesrow[0:1, :],
                                      biasrow[0:1, 512:1024], start=False, stop=True)
                    mm.then_inc(s_t.h, 1)
                s_t.n += 1
                two = s_t.n

                @blk.vector
                def _(v, tt=tt):
                    v.wait_ge(s_t.h, two)
                    v.wait_ge(s_lx.h, ldx)
                    v.tensor_add(out=x1t[:, :], in0=x1t[:, :],
                                 in1=pw0[:, 0:1024]).then_inc(s_v.h, 1)
                s_v.n += 1
                vx1 = s_v.n

                @blk.scalar
                def _(s, tt=tt, half=half):
                    s.wait_ge(s_v.h, vx1)
                    s_st.dinc(s.dma_start(out=x1_s[t0 + tt * 128:t0 + (tt + 1) * 128, :],
                                          in_=x1t[:, :]))
                    if "d_att" in dbg_t and half == 1:
                        s_st.dinc(s.dma_start(
                            out=dbg_t["d_att"][t0 + tt * 128:t0 + (tt + 1) * 128, :],
                            in_=x1t[:, :]))
                stx = s_st.n

                @blk.vector
                def _(v):
                    v.wait_ge(s_st.h, stx)
                    v.sem_inc(s_v.h, 1)
                s_v.n += 1

    def phase5():
        while True:
            if qst["qt_slot"] < len(qst["slots"]):
                q_slot_vector()
                q_slot_scalar()
                continue
            if qst["ld"] < len(qtasks):
                vcur = s_v.n

                @blk.scalar
                def _(sc_, vcur=vcur):
                    sc_.wait_ge(s_v.h, vcur)
                    q_issue_loads(sc_)
                continue
            break

    def phase6():
        @blk.sync
        def _(sync):
            sync.wait_ge(s_bar.h, s_bar.n)
            sync.wait_ge(s_qs.h, s_qs.n)
            for c in range(NCH):
                dstbuf = xqT_lo if c < 4 else xqT_hi
                s_tr.dinc(sync.dma_start(out=dstbuf[:, (c % 4) * N:(c % 4) * N + N],
                                         in_=xq_s[0:N, ts(c, 128)], transpose=True))
        trq = s_tr.n

        NPS = FFI // 1024 + (1 if FFI % 1024 else 0)
        for np_ in range(NPS):
            n0 = np_ * 1024
            nw = min(1024, FFI - n0)

            @blk.sync
            def _(sync, n0=n0, nw=nw):
                sync.wait_ge(s_t.h, s_t.n)
                sync.wait_ge(s_v.h, s_v.n)
                for k in range(NCH):
                    s_lw.dinc(sync.dma_start(out=wstr[:, k * 2048:k * 2048 + nw],
                                             in_=wgq_s[n0:n0 + nw, ts(k, 128)],
                                             transpose=True))
                    s_lw.dinc(sync.dma_start(out=wstr[:, k * 2048 + 1024:k * 2048 + 1024 + nw],
                                             in_=wuq_s[n0:n0 + nw, ts(k, 128)],
                                             transpose=True))
            ldgu = s_lw.n

            tgu_l = []
            sgs_l = []
            vh_l = []
            sth_l = []
            base6s = s_st.n
            for it in range(NT):
                s_t.n += 1
                tgu_l.append(s_t.n)
                s_s.n += 1
                sgs_l.append(s_s.n)
                s_v.n += 1
                vh_l.append(s_v.n)
                s_st.n += 16
                sth_l.append(s_st.n)

            @blk.tensor
            def _(t, n0=n0, nw=nw, tgu_l=tgu_l, vh_l=vh_l):
                t.wait_ge(s_lw.h, ldgu)
                t.wait_ge(s_tr.h, trq)
                for it in range(NT):
                    if it > 0:
                        t.wait_ge(s_v.h, vh_l[it - 1])
                    for k in range(NCH):
                        src = xqT_lo if k < 4 else xqT_hi
                        sl = src[:, (k % 4) * N + it * 128:(k % 4) * N + it * 128 + 128]
                        for nb in range(0, nw, 512):
                            ww = min(512, nw - nb)
                            t.matmul(pw0[:, nb:nb + ww], sl,
                                     wstr[:, k * 2048 + nb:k * 2048 + nb + ww],
                                     start=(k == 0), stop=(k == NCH - 1))
                            mm = t.matmul(pw1[:, nb:nb + ww], sl,
                                          wstr[:, k * 2048 + 1024 + nb:k * 2048 + 1024 + nb + ww],
                                          start=(k == 0), stop=(k == NCH - 1))
                    mm.then_inc(s_t.h, 1)

            @blk.scalar
            def _(s, n0=n0, nw=nw, tgu_l=tgu_l, vh_l=vh_l):
                for it in range(NT):
                    s.wait_ge(s_t.h, tgu_l[it])
                    s.activation(out=gsil_v[:, 0:nw], in_=pw0[:, 0:nw],
                                 func=ACT.Silu).then_inc(s_s.h, 1)
                    if it > 0:
                        s.wait_ge(s_v.h, vh_l[it - 1])
                        s.dma_start(out=h_s[ts(it - 1, 128), n0:n0 + nw],
                                    in_=junk[:, 0:nw]).then_inc(s_st.h, 16)
                s.wait_ge(s_v.h, vh_l[NT - 1])
                s.dma_start(out=h_s[ts(NT - 1, 128), n0:n0 + nw],
                            in_=junk[:, 0:nw]).then_inc(s_st.h, 16)

            @blk.vector
            def _(v, nw=nw, sgs_l=sgs_l, sth_l=sth_l, base6s=base6s):
                for it in range(NT):
                    v.wait_ge(s_s.h, sgs_l[it])
                    v.wait_ge(s_st.h, sth_l[it - 1] if it > 0 else base6s)
                    v.tensor_mul(out=junk[:, 0:nw], in0=gsil_v[:, 0:nw],
                                 in1=pw1[:, 0:nw]).then_inc(s_v.h, 1)

        HW2 = FFI // 2
        nhc = 2 * NT
        lsems = (s_lq, s_lx)
        ssems = (s_qs, s_ld)
        lq_mark = {}
        st_mark = {}
        sth_tot = s_st.n
        vre = s_v.n
        qs0 = s_qs.n

        @blk.scalar
        def _(sc_, sth_tot=sth_tot, vre=vre, qs0=qs0):
            sc_.wait_ge(s_st.h, sth_tot)
            sc_.wait_ge(s_v.h, vre)
            sc_.wait_ge(s_qs.h, qs0)
            for k in (0, 1):
                it, hh = divmod(k, 2)
                lsems[k % 2].dinc(sc_.dma_start(
                    out=junk[:, (k % 2) * HW2:(k % 2) * HW2 + HW2],
                    in_=h_s[ts(it, 128), hh * HW2:hh * HW2 + HW2]))
                lq_mark[k] = lsems[k % 2].n
        for k in range(nhc):
            it, hh = divmod(k, 2)
            lmk = lq_mark[k]
            smk = st_mark.get(k - 2)

            @blk.vector
            def _(v, k=k, lmk=lmk, smk=smk):
                v.wait_ge(lsems[k % 2].h, lmk)
                if smk is not None:
                    v.wait_ge(ssems[k % 2].h, smk)
                src = junk[:, (k % 2) * HW2:(k % 2) * HW2 + HW2]
                emit_fp4(v, HW2, src, out_off=(k % 2) * HW2, hi_ap=src)
            vend_k = s_v.n

            @blk.scalar
            def _(sc_, k=k, it=it, hh=hh, vend_k=vend_k):
                sc_.wait_ge(s_v.h, vend_k)
                osl = q_out[:, (k % 2) * HW2:(k % 2) * HW2 + HW2]
                ssems[k % 2].dinc(sc_.dma_start(
                    out=hq_s[ts(it, 128), hh * HW2:hh * HW2 + HW2], in_=osl))
                if "d_hq" in dbg_t:
                    ssems[k % 2].dinc(sc_.dma_start(
                        out=dbg_t["d_hq"][ts(it, 128), hh * HW2:hh * HW2 + HW2],
                        in_=osl))
                if k + 2 < nhc:
                    it2, hh2 = divmod(k + 2, 2)
                    lsems[k % 2].dinc(sc_.dma_start(
                        out=junk[:, (k % 2) * HW2:(k % 2) * HW2 + HW2],
                        in_=h_s[ts(it2, 128), hh2 * HW2:hh2 * HW2 + HW2]))
                    lq_mark[k + 2] = lsems[k % 2].n
            st_mark[k] = ssems[k % 2].n

        @blk.sync
        def _(sync):
            sync.wait_ge(s_v.h, s_v.n)
            sync.wait_ge(s_t.h, s_t.n)
            sync.wait_ge(s_qs.h, s_qs.n)
            sync.wait_ge(s_ld.h, s_ld.n)
            for kc in range(FFI // 128):
                if kc < 16:
                    dst = wdT_a[:, kc * 1024:(kc + 1) * 1024]
                else:
                    dst = wdT_b[:, (kc - 16) * 1024:(kc - 15) * 1024]
                s_tr.dinc(sync.dma_start(out=dst, in_=wdq_s[0:D, ts(kc, 128)],
                                         transpose=True))
        trd = s_tr.n

        NB4 = NT // 4
        for b4 in range(NB4):
            @blk.sync
            def _(sync, b4=b4):
                sync.wait_ge(s_t.h, s_t.n)
                for kc in range(FFI // 128):
                    s_th.dinc(sync.dma_start(out=hqTb[:, kc * 512:(kc + 1) * 512],
                                             in_=hq_s[b4 * 512:(b4 + 1) * 512, ts(kc, 128)],
                                             transpose=True))
            trh = s_th.n

            for i4 in range(4):
                it = b4 * 4 + i4

                @blk.sync
                def _(sync, it=it):
                    sync.wait_ge(s_v.h, s_v.n)
                    s_lx.dinc(sync.dma_start(out=x1t[:, :], in_=x1_s[ts(it, 128), :]))
                ldx1 = s_lx.n

                @blk.tensor
                def _(t, i4=i4):
                    t.wait_ge(s_th.h, trh)
                    t.wait_ge(s_tr.h, trd)
                    if True:
                        t.wait_ge(s_v.h, s_v.n)
                    for kc in range(FFI // 128):
                        if kc < 16:
                            wsl = wdT_a[:, kc * 1024:(kc + 1) * 1024]
                        else:
                            wsl = wdT_b[:, (kc - 16) * 1024:(kc - 15) * 1024]
                        hsl = hqTb[:, kc * 512 + i4 * 128:kc * 512 + i4 * 128 + 128]
                        t.matmul(pw0[:, 0:512], hsl, wsl[:, 0:512],
                                 start=(kc == 0), stop=(kc == FFI // 128 - 1))
                        mm = t.matmul(pw0[:, 512:1024], hsl, wsl[:, 512:1024],
                                      start=(kc == 0), stop=(kc == FFI // 128 - 1))
                    mm.then_inc(s_t.h, 1)
                s_t.n += 1
                tdn = s_t.n

                @blk.vector
                def _(v, it=it):
                    v.wait_ge(s_t.h, tdn)
                    v.wait_ge(s_lx.h, ldx1)
                    v.tensor_add(out=x1t[:, :], in0=x1t[:, :],
                                 in1=pw0[:, 0:1024]).then_inc(s_v.h, 1)
                s_v.n += 1
                vf = s_v.n

                @blk.scalar
                def _(s, it=it):
                    s.wait_ge(s_v.h, vf)
                    s_st.dinc(s.dma_start(out=x1_s[ts(it, 128), :], in_=x1t[:, :]))
                stf = s_st.n

                @blk.vector
                def _(v):
                    v.wait_ge(s_st.h, stf)
                    v.sem_inc(s_v.h, 1)
                s_v.n += 1

    def phase8a():
        for wsrc, wdst, rows in ((W_in, wi_s, D), (W_attn, wa_s, 656), (W_out, wo2_s, D)):
            nrt = (rows + 127) // 128
            for rt in range(nrt):
                rr = min(128, rows - rt * 128)

                @blk.sync
                def _(sync, wsrc=wsrc, rt=rt, rr=rr):
                    sync.wait_ge(s_v.h, s_v.n)
                    s_lw.dinc(sync.dma_start(out=wfp[0:rr, :],
                                             in_=wsrc[rt * 128:rt * 128 + rr, :]))
                ldr = s_lw.n

                @blk.vector
                def _(v, rr=rr):
                    v.wait_ge(s_lw.h, ldr)
                    v.tensor_copy(out=wbuf[0:rr, 0:1024], in_=wfp[0:rr, :]).then_inc(s_v.h, 1)
                s_v.n += 1
                vcp = s_v.n

                @blk.scalar
                def _(s, wdst=wdst, rt=rt, rr=rr):
                    s.wait_ge(s_v.h, vcp)
                    s_st.dinc(s.dma_start(out=wdst[rt * 128:rt * 128 + rr, :],
                                          in_=wbuf[0:rr, 0:1024]))
                str_ = s_st.n

                @blk.vector
                def _(v):
                    v.wait_ge(s_st.h, str_)
                    v.sem_inc(s_v.h, 1)
                s_v.n += 1

        for rt in range(NCH):
            @blk.sync
            def _(sync, rt=rt):
                sync.wait_ge(s_v.h, s_v.n)
                s_lw.dinc(sync.dma_start(out=wfp[:, :], in_=W_l1[ts(rt, 128), :]))
            ldr = s_lw.n

            @blk.vector
            def _(v, rt=rt):
                v.wait_ge(s_lw.h, ldr)
                v.reduce_sum(out=stat[:, ts(rt, 1)], in_=wfp[:, :], axis=AX.X,
                             apply_absolute_value=True).then_inc(s_v.h, 1)
                v.wait_ge(s_v.h, s_v.n + 1)
                v.sem_inc(s_v.h, 1)
            s_v.n += 2
        vws = s_v.n

        @blk.vector
        def _(v):
            v.wait_ge(s_v.h, vws)
            v.reduce_sum(out=stat[:, NCH:NCH + 1], in_=stat[:, 0:NCH],
                         axis=AX.X).then_inc(s_v.h, 1)
        s_v.n += 1
        vws2 = s_v.n

        @blk.tensor
        def _(t):
            t.wait_ge(s_v.h, vws2)
            t.matmul(p4[0:1, 0:1], C(1.0), stat[:, NCH:NCH + 1],
                     start=True, stop=True).then_inc(s_t.h, 1)
        s_t.n += 1
        tws = s_t.n

        @blk.vector
        def _(v):
            v.wait_ge(s_t.h, tws)
            v.tensor_scalar(out=stat[0:1, 0:1], in0=p4[0:1, 0:1],
                            scalar1=1.0 / (D * D), scalar2=1e-8,
                            op0=ALU.mult, op1=ALU.add).then_inc(s_v.h, 1)
            v.wait_ge(s_v.h, s_v.n + 1)
            v.reciprocal(out=stat[0:1, 1:2], in_=stat[0:1, 0:1]).then_inc(s_v.h, 1)
        s_v.n += 2
        vws3 = s_v.n

        @blk.scalar
        def _(s):
            s.wait_ge(s_v.h, vws3)
            s_st.dinc(s.dma_start(out=scal_s[0:2], in_=stat[0:1, 0:2]))
            if "d_ws" in dbg_t:
                s_st.dinc(s.dma_start(out=dbg_t["d_ws"][0:1, :], in_=stat[0:1, 0:2]))
        stws = s_st.n

        @blk.sync
        def _(sync):
            sync.wait_ge(s_st.h, stws)
            s_lx.dinc(sync.dma_start(out=wl1s[:, 0:1],
                                     in_=scal_s[0:1].unsqueeze(0).broadcast_to([128, 1])))
            s_lx.dinc(sync.dma_start(out=wl1r[:, 0:1],
                                     in_=scal_s[1:2].unsqueeze(0).broadcast_to([128, 1])))
        ldws = s_lx.n

        for rt in range(NCH):
            @blk.sync
            def _(sync, rt=rt):
                sync.wait_ge(s_v.h, s_v.n)
                s_lw.dinc(sync.dma_start(out=wfp[:, :], in_=W_l1[ts(rt, 128), :]))
            ldr = s_lw.n

            @blk.vector
            def _(v):
                v.wait_ge(s_lw.h, ldr)
                v.wait_ge(s_lx.h, ldws)

                def st(inst):
                    s_v.inc(inst)
                    v.wait_ge(s_v.h, s_v.n)
                st(v.tensor_scalar(out=q_vI[:, 0:D].bitcast(F32), in0=wfp[:, :],
                                   scalar1=wl1r[:, 0:1], scalar2=M24,
                                   op0=ALU.mult, op1=ALU.add))
                st(v.tensor_scalar(out=q_vI[:, 0:D].bitcast(F32),
                                   in0=q_vI[:, 0:D].bitcast(F32),
                                   scalar1=M24, scalar2=-1.0,
                                   op0=ALU.subtract, op1=ALU.max))
                st(v.tensor_scalar(out=q_out[:, 0:D], in0=q_vI[:, 0:D].bitcast(F32),
                                   scalar1=1.0, scalar2=None, op0=ALU.min))
            vqw = s_v.n

            @blk.scalar
            def _(s, rt=rt):
                s.wait_ge(s_v.h, vqw)
                s_st.dinc(s.dma_start(out=wl1q_s[ts(rt, 128), :], in_=q_out[:, 0:D]))
                if "d_wl1q" in dbg_t:
                    s_st.dinc(s.dma_start(out=dbg_t["d_wl1q"][ts(rt, 128), :],
                                          in_=q_out[:, 0:D]))
            stw = s_st.n

            @blk.vector
            def _(v):
                v.wait_ge(s_st.h, stw)
                v.sem_inc(s_v.h, 1)
            s_v.n += 1

    def phase8(sq):
        t0 = sq * T
        prs = pr_flat.rearrange("(a) -> a")

        @blk.vector
        def _(v):
            v.wait_ge(s_t.h, s_t.n)
            v.wait_ge(s_bar.h, s_bar.n)
            v.memset(zpad[:, :], 0.0).then_inc(s_v.h, 1)
        s_v.n += 1

        @blk.sync
        def _(sync):
            sync.wait_ge(s_bar.h, s_bar.n)
            sync.wait_ge(s_out.h, s_out.n)
            sync.wait_ge(s_v.h, s_v.n)
            sync.wait_ge(s_st.h, s_st.n)
            s_st.dinc(sync.dma_start(out=u_s[sq, 0:D, 0:64],
                                     in_=zpad[:, 0:512]))
            s_st.dinc(sync.dma_start(out=u_s[sq, 0:D, T + 64:T + 128],
                                     in_=zpad[:, 0:512]))
            for c in range(NCH):
                s_tr.dinc(sync.dma_start(out=xn2T[:, ts(c, T)],
                                         in_=xn2_s[t0:t0 + T, ts(c, 128)], transpose=True))
            for k in range(NCH):
                s_tr.dinc(sync.dma_start(out=wiT[:, ts(k, 1024)],
                                         in_=wi_s[0:D, ts(k, 128)], transpose=True))
        trwi = s_tr.n

        psl8 = [pw0[:, 0:512], pw0[:, 512:1024], pw1[:, 0:512], pw1[:, 512:1024]]
        for e in range(NCH):
            @blk.tensor
            def _(t, e=e):
                t.wait_ge(s_tr.h, trwi)
                if e > 0:
                    t.wait_ge(s_s.h, s_s.n)
                for k in range(NCH):
                    for tb in range(T // 512):
                        mm = t.matmul(psl8[tb],
                                      wiT[:, k * 1024 + e * 128:k * 1024 + e * 128 + 128],
                                      xn2T[:, k * T + tb * 512:k * T + (tb + 1) * 512],
                                      start=(k == 0), stop=(k == NCH - 1))
                mm.then_inc(s_t.h, 1)
            s_t.n += 1
            twi = s_t.n

            @blk.scalar
            def _(s, e=e):
                s.wait_ge(s_t.h, twi)
                for tb in range(T // 512):
                    s.copy(out=uT[:, e * T + tb * 512:e * T + (tb + 1) * 512],
                           in_=psl8[tb]).then_inc(s_s.h, 1)
                    s.activation(out=ruT[:, e * T + tb * 512:e * T + (tb + 1) * 512],
                                 in_=psl8[tb], func=ACT.Relu).then_inc(s_s.h, 1)
            s_s.n += 2 * (T // 512)
            swi = s_s.n

            @blk.scalar
            def _(s, e=e):
                s.wait_ge(s_s.h, swi)
                s_st.dinc(s.dma_start(out=u_s[sq, ts(e, 128), 64:64 + T], in_=uT[:, ts(e, T)]))
                if "d_u" in dbg_t and sq == 0:
                    s_st.dinc(s.dma_start(out=dbg_t["d_u"][ts(e, 128), :],
                                          in_=uT[:, ts(e, T)]))
        uwdone = s_st.n

        @blk.sync
        def _(sync):
            sync.wait_ge(s_st.h, uwdone)
            for k in range(NCH):
                s_tr.dinc(sync.dma_start(out=waT[:, ts(k, 656)],
                                         in_=wa_s[0:656, ts(k, 128)], transpose=True))
        trwa = s_tr.n

        for tt in range(NTS):
            @blk.tensor
            def _(t, tt=tt):
                t.wait_ge(s_tr.h, trwa)
                if tt > 0:
                    t.wait_ge(s_st.h, s_st.n)
                for k in range(NCH):
                    sl = ruT[:, k * T + tt * 128:k * T + tt * 128 + 128]
                    t.matmul(pw0[:, 0:512], sl, waT[:, k * 656:k * 656 + 512],
                             start=(k == 0), stop=(k == NCH - 1))
                    mm = t.matmul(pw0[:, 512:656], sl, waT[:, k * 656 + 512:k * 656 + 656],
                                  start=(k == 0), stop=(k == NCH - 1))
                mm.then_inc(s_t.h, 1)
            s_t.n += 1
            tal = s_t.n

            @blk.vector
            def _(v, tt=tt):
                if tt == 0:
                    v.wait_ge(s_st.h, uwdone)
                    v.memset(PW[:, :], 0.0).then_inc(s_v.h, 1)
                else:
                    v.sem_inc(s_v.h, 1)
            s_v.n += 1
            vpw0_ = s_v.n

            @blk.scalar
            def _(s, tt=tt):
                s.wait_ge(s_t.h, tal)
                s.wait_ge(s_v.h, vpw0_)
                for sc_i in range(2):
                    cw = CWS[sc_i]
                    srcap = pw0[:, (0 if sc_i == 0 else 328):(168 if sc_i == 0 else 656)]
                    dst = PW[:, PRB[sc_i]:PRB[sc_i] + 8 * SLOTP[sc_i]]
                    s.activation(out=dst.rearrange("p (h w) -> p h w", w=SLOTP[sc_i])[:, :, 0:cw],
                                 in_=srcap.rearrange("p (h w) -> p h w", w=cw),
                                 func=ACT.Exp).then_inc(s_s.h, 1)
            s_s.n += 2
            spw = s_s.n

            @blk.vector
            def _(v, tt=tt):
                v.wait_ge(s_s.h, spw)

                def st(inst):
                    s_v.inc(inst)
                    v.wait_ge(s_v.h, s_v.n)
                for sc_i in range(2):
                    cw = CWS[sc_i]
                    dst = PW[:, PRB[sc_i]:PRB[sc_i] + 8 * SLOTP[sc_i]]
                    view = dst.rearrange("p (h w) -> p h w", w=SLOTP[sc_i])[:, :, 0:cw]
                    st(v.reduce_sum(out=prz[:, ts(sc_i, 8)].unsqueeze(2),
                                    in_=view, axis=AX.X))
                st(v.reciprocal(out=prz[:, 0:16], in_=prz[:, 0:16]))
                st(v.tensor_scalar(out=prz[:, 0:8], in0=prz[:, 0:8],
                                   scalar1=tiny[:, 8:9], scalar2=None, op0=ALU.mult))
                st(v.tensor_scalar(out=prz[:, 8:16], in0=prz[:, 8:16],
                                   scalar1=tiny[:, 9:10], scalar2=None, op0=ALU.mult))
                for sc_i in range(2):
                    cw = CWS[sc_i]
                    dst = PW[:, PRB[sc_i]:PRB[sc_i] + 8 * SLOTP[sc_i]]
                    view = dst.rearrange("p (h w) -> p h w", w=SLOTP[sc_i])[:, :, 0:cw]
                    st(v.tensor_mul(out=view, in0=view,
                                    in1=prz[:, ts(sc_i, 8)].unsqueeze(2).broadcast_to([128, 8, cw])))
            vpw = s_v.n

            @blk.scalar
            def _(s, tt=tt):
                s.wait_ge(s_v.h, vpw)
                base = (sq * T + tt * 128) * PRP
                dst = prs[base:base + 128 * PRP].rearrange("(p w) -> p w", w=PRP)
                s_st.dinc(s.dma_start(out=dst, in_=PW[:, :]))

        aldone = s_st.n

        @blk.sync
        def _(sync):
            sync.wait_ge(s_st.h, aldone)
            sync.wait_ge(s_t.h, s_t.n)
            for kt in range(NKTP):
                s_tr.dinc(sync.dma_start(out=upad[:, ts(kt, 1024)],
                                         in_=u_s[sq, 0:D, kt * 128:kt * 128 + 128],
                                         transpose=True))
        trup = s_tr.n

        @blk.vector
        def _(v):
            v.wait_ge(s_st.h, aldone)
            v.sem_inc(s_v.h, 1)
        s_v.n += 1
        vup = s_v.n

        phase8_window(sq, vup)

    def phase8_window(sq, vup):
        prs = pr_flat
        NTG = NTS // 4
        btbs = (Btile[:, 0:512], zpad[:, 0:512])
        psems = (s_tr, s_th)
        t_base = s_t.n
        sev_prev = None
        items = [(c, tg, i4) for c in range(NCH) for tg in range(NTG)
                 for i4 in range(4)]
        for idx, (c, tg, i4) in enumerate(items):
            tt = tg * 4 + i4
            psem = psems[idx % 2]
            buf = btbs[idx % 2]

            @blk.sync
            def _(sync, sq=sq, c=c, tt=tt, idx=idx, buf=buf, psem=psem,
                  t_base=t_base):
                if idx >= 2:
                    sync.wait_ge(s_t.h, t_base + idx - 1)
                else:
                    sync.wait_ge(s_v.h, vup)
                for sc_i in range(2):
                    for k2 in range(2):
                        base = ((sq * T + tt * 128) * PRP
                                + PRB[sc_i] + c * SLOTP[sc_i] + PADS[sc_i]
                                + k2 * 128 - UPAD)
                        ap = bass.AP(prs.tensor, base,
                                     [[PRP - 1, 128], [1, 128]])
                        psem.dinc(sync.dma_start(
                            out=buf[:, ts(sc_i * 2 + k2, 128)],
                            in_=ap, transpose=True))
            trbt = psem.n

            @blk.tensor
            def _(t, c=c, tt=tt, i4=i4, idx=idx, buf=buf, psem=psem,
                  trbt=trbt, sev_prev=sev_prev):
                t.wait_ge(psem.h, trbt)
                if idx == 0:
                    t.wait_ge(s_v.h, vup)
                if i4 == 0 and sev_prev is not None:
                    t.wait_ge(s_s.h, sev_prev)
                first = True
                for sc_i in range(2):
                    for k2 in range(2):
                        ktp = tt + k2
                        mm = t.matmul(p6[:, ts(i4, 128)],
                                      upad[:, ktp * 1024 + c * 128:ktp * 1024 + c * 128 + 128],
                                      buf[:, ts(sc_i * 2 + k2, 128)],
                                      start=first, stop=(sc_i == 1 and k2 == 1))
                        first = False
                mm.then_inc(s_t.h, 1)
            s_t.n += 1

            if i4 == 3:
                tw4 = s_t.n

                @blk.scalar
                def _(s, c=c, tg=tg, tw4=tw4):
                    s.wait_ge(s_t.h, tw4)
                    s.copy(out=owT[:, c * T + tg * 512:c * T + (tg + 1) * 512],
                           in_=p6[:, 0:512]).then_inc(s_s.h, 1)
                s_s.n += 1
                sev_prev = s_s.n

        if "d_ow" in dbg_t and sq == 0:
            @blk.scalar
            def _(s):
                s.wait_ge(s_s.h, s_s.n)
                for c in range(NCH):
                    s_st.dinc(s.dma_start(out=dbg_t["d_ow"][ts(c, 128), :],
                                          in_=owT[:, ts(c, T)]))

    def phase9(sq):
        t0 = sq * T

        @blk.sync
        def _(sync):
            sync.wait_ge(s_s.h, s_s.n)
            for k in range(NCH):
                s_tr.dinc(sync.dma_start(out=woutT[:, ts(k, 1024)],
                                         in_=wo2_s[0:D, ts(k, 128)], transpose=True))
                s_tr.dinc(sync.dma_start(out=wl1qT[:, ts(k, 1024)],
                                         in_=wl1q_s[0:D, ts(k, 128)], transpose=True))
                s_tr.dinc(sync.dma_start(out=aqT[:, ts(k, T)],
                                         in_=aq_s[t0:t0 + T, ts(k, 128)], transpose=True))
        tr9 = s_tr.n

        for tt in range(NTS):
            it = sq * NTS + tt

            @blk.sync
            def _(sync, it=it):
                sync.wait_ge(s_v.h, s_v.n)
                s_lx.dinc(sync.dma_start(out=x1t[:, :], in_=x1_s[ts(it, 128), :]))
            ldx1 = s_lx.n

            @blk.tensor
            def _(t, tt=tt):
                t.wait_ge(s_tr.h, tr9)
                if tt > 0:
                    t.wait_ge(s_v.h, s_v.n)
                for k in range(NCH):
                    osl = owT[:, k * T + tt * 128:k * T + tt * 128 + 128]
                    t.matmul(pw0[:, 0:512], osl, woutT[:, k * 1024:k * 1024 + 512],
                             start=(k == 0), stop=(k == NCH - 1))
                    t.matmul(pw0[:, 512:1024], osl, woutT[:, k * 1024 + 512:k * 1024 + 1024],
                             start=(k == 0), stop=(k == NCH - 1))
                for k in range(NCH):
                    asl = aqT[:, k * T + tt * 128:k * T + tt * 128 + 128]
                    t.matmul(pw1[:, 0:512], asl, wl1qT[:, k * 1024:k * 1024 + 512],
                             start=(k == 0), stop=(k == NCH - 1))
                    mm = t.matmul(pw1[:, 512:1024], asl, wl1qT[:, k * 1024 + 512:k * 1024 + 1024],
                                  start=(k == 0), stop=(k == NCH - 1))
                mm.then_inc(s_t.h, 1)
            s_t.n += 1
            t9 = s_t.n

            @blk.vector
            def _(v, it=it):
                v.wait_ge(s_t.h, t9)
                v.wait_ge(s_lx.h, ldx1)
                v.scalar_tensor_tensor(out=junk[:, 0:D], in0=pw1[:, 0:1024],
                                       scalar=asws[:, ts(it, 1)], in1=x1t[:, :],
                                       op0=ALU.mult, op1=ALU.add).then_inc(s_v.h, 1)
                v.wait_ge(s_v.h, s_v.n + 1)
                v.tensor_add(out=junk[:, 0:D], in0=junk[:, 0:D],
                             in1=pw0[:, 0:1024]).then_inc(s_v.h, 1)
            s_v.n += 2
            v9 = s_v.n

            @blk.scalar
            def _(s, it=it):
                s.wait_ge(s_v.h, v9)
                s_out.dinc(s.dma_start(out=out_t[ts(it, 128), :], in_=junk[:, 0:D]))
            o9 = s_out.n

            @blk.vector
            def _(v):
                v.wait_ge(s_out.h, o9)
                v.sem_inc(s_v.h, 1)
            s_v.n += 1

    phase0a()
    phase0b()
    if stop >= 1:
        phase_ln(x_in, xn_s, xq_s, layer=0)
    barrier()
    if stop >= 2:
        for sq in range(NSEQ):
            phase234(sq)
        barrier()
    if stop >= 5:
        phase5()
        phase6()
        barrier()
    if stop >= 7:
        phase8a()
        phase_ln(x1_s, xn2_s, aq_s, layer=1)
        barrier()
    if stop >= 8:
        for sq in range(NSEQ):
            phase8(sq)
            phase9(sq)

    if stop < 8:
        @blk.vector
        def _(v):
            v.wait_ge(s_bar.h, s_bar.n)
            v.memset(junk[:, 0:D], 0.0).then_inc(s_v.h, 1)
        s_v.n += 1
        zv = s_v.n

        @blk.scalar
        def _(s):
            s.wait_ge(s_v.h, zv)
            for it in range(NT):
                s_out.dinc(s.dma_start(out=out_t[ts(it, 128), :], in_=junk[:, 0:D]))

    @blk.gpsimd
    def _(g):
        g.wait_ge(s_out.h, s_out.n)
        g.wait_ge(s_st.h, s_st.n)
        g.wait_ge(s_mv.h, s_mv.n)
        g.wait_ge(s_qs.h, s_qs.n)
        g.wait_ge(s_ld.h, s_ld.n)

    es.close()
    return nc


def _host_consts(T):
    iota = np.broadcast_to(np.arange(T, dtype=np.float32), (128, T)).copy()
    rope_e = np.zeros((128, 1), np.float32)
    rope_sm = np.zeros((128, 1), np.float32)
    for p in range(128):
        r = p % 64
        if r >= 32:
            j = r - 32
            sgn = -1.0
            if j >= 16:
                j -= 16
                sgn = 1.0
            rope_e[p, 0] = -j / 16.0
            rope_sm[p, 0] = sgn * 2.0 * np.pi / max(T - 1, 1)
    sperm = np.zeros((128, 128), np.float32)
    for m in range(128):
        r = m % 64
        if 32 <= r < 48:
            k = m + 16
        elif 48 <= r < 64:
            k = m - 16
        else:
            k = m
        sperm[k, m] = 1.0
    e2 = np.zeros((16, 1024), np.float32)
    for c in range(8):
        for m in range(128):
            e2[2 * c + (1 if m >= 64 else 0), c * 128 + m] = 1.0
    return {
        "iota_t": iota,
        "rope_e": rope_e,
        "rope_sm": rope_sm,
        "sperm": sperm.astype(ml_dtypes.bfloat16),
        "e2sel": e2.astype(ml_dtypes.bfloat16),
    }


def make_in_maps(inputs, cfg, n_cores=8):
    T, NSEQ = cfg["T"], cfg["NSEQ"]
    consts = _host_consts(T)
    x = np.ascontiguousarray(np.asarray(inputs["L0_attn_x"]), dtype=np.float32)
    B = x.shape[0]
    shared = {
        "Wq": inputs["L0_attn_Wq"], "Wk": inputs["L0_attn_Wk"],
        "Wv": inputs["L0_attn_Wv"], "Wo": inputs["L0_attn_Wo"],
        "bq": inputs["L0_attn_bq"], "bk": inputs["L0_attn_bk"],
        "bv": inputs["L0_attn_bv"], "bo": inputs["L0_attn_bo"],
        "nrope": inputs["L0_attn_nrope_freq"],
        "w_gate": inputs["L0_ffn_w_gate"], "w_up": inputs["L0_ffn_w_up"],
        "w_down": inputs["L0_ffn_w_down"],
        "W_in": inputs["L1_attn_W_in"], "W_attn": inputs["L1_attn_W_attn"],
        "W_out": inputs["L1_attn_W_out"], "sw": inputs["L1_attn_scale_weights"],
        "W_l1": inputs["L1_ffn_weight"],
    }
    shared = {k: np.ascontiguousarray(np.asarray(v), dtype=np.float32)
              for k, v in shared.items()}
    shared.update(consts)
    per_core = B // n_cores
    maps = []
    for c in range(n_cores):
        m = dict(shared)
        m["x"] = np.ascontiguousarray(
            x[c * per_core:(c + 1) * per_core].reshape(per_core * T, D))
        maps.append(m)
    return maps


def kernel(**inputs):
    cfg = {"T": 2048, "NSEQ": 2}
    nc = build(cfg)
    maps = make_in_maps(inputs, cfg, n_cores=8)
    res = run_bass_kernel_spmd(nc, maps, core_ids=list(range(8)))
    per_core = 16 // 8
    out = np.concatenate(
        [res.results[c]["out"].reshape(per_core, cfg["T"], D) for c in range(8)],
        axis=0)
    return out.astype(np.float32)
